# version 02
# baseline (speedup 1.0000x reference)
import math
import contextlib
import numpy as np
import ml_dtypes

import concourse.bass as bass
import concourse.mybir as mybir
from concourse.bass_utils import run_bass_kernel_spmd

F32 = mybir.dt.float32
BF16 = mybir.dt.bfloat16
AF = mybir.ActivationFunctionType
ALU = mybir.AluOpType
AX = mybir.AxisListType

B, S, D = 16, 2048, 1024
NCORES = 8
NSEQ = B // NCORES
DH = 64
FH, FE, FP = 8, 8, 128
SCALE_A = 1.0 / math.sqrt(DH)
SCALE_F = 1.0 / math.sqrt(FP)
EPS = 1e-5
NEG = -1.0e30
TT = S // 128
NQG = S // 512
ENGS = ("sync", "vector", "scalar", "tensor", "gpsimd")


def host_consts():
    ident = np.eye(128, dtype=np.float32)
    j = np.arange(32, dtype=np.float64)
    invf = 1.0 / (10000.0 ** (j / 32.0))
    t = np.arange(S, dtype=np.float64)
    ang = invf[:, None] * t[None, :]
    cos_rep = np.tile(np.cos(ang), (4, 1))
    sin_rep = np.tile(np.sin(ang), (4, 1))
    sgn = np.ones((128, 1))
    sgn[0:32] = -1.0
    sgn[64:96] = -1.0
    sin_rep = sin_rep * sgn
    cos_hi = cos_rep.astype(np.float32)
    cos_lo = (cos_rep - cos_hi.astype(np.float64)).astype(np.float32)
    sin_hi = sin_rep.astype(np.float32)
    sin_lo = (sin_rep - sin_hi.astype(np.float64)).astype(np.float32)
    kp = np.arange(128)[:, None]
    qf = np.arange(512)[None, :]
    masks = np.zeros((4, 128, 512), dtype=np.float32)
    for o in range(4):
        masks[o] = np.where(kp <= qf - o * 128, 0.0, NEG)
    return {
        "c_ident": ident,
        "c_cos_hi": cos_hi, "c_cos_lo": cos_lo,
        "c_sin_hi": sin_hi, "c_sin_lo": sin_lo,
        "c_masks": masks.astype(ml_dtypes.bfloat16),
    }


eT, eV, eA, eP = "eT", "eV", "eA", "eP"
dC, dX, dW, dF, dG, dO, dR = "dC", "dX", "dW", "dF", "dG", "dO", "dR"
ALL_SEMS = [eT, eV, eA, eP, dC, dX, dW, dF, dG, dO, dR]
OWN = {"tensor": eT, "vector": eV, "scalar": eA, "gpsimd": eP}


class Prog:

    def __init__(self):
        self.items = {e: [] for e in ENGS}
        self.cnt = {sm: 0 for sm in ALL_SEMS}

    def op(self, eng, fn, incs=()):
        incs = tuple(incs)
        for sm, a in incs:
            self.cnt[sm] += a
        self.items[eng].append(("op", fn, incs))
        return tuple((sm, self.cnt[sm]) for sm, _ in incs)

    def wait(self, eng, sem, v=None):
        if v is None:
            v = self.cnt[sem]
        if v > 0:
            self.items[eng].append(("wait", sem, v))

    def tok(self, sem):
        return (sem, self.cnt[sem])

    def wait_tok(self, eng, *toks):
        for sem, v in toks:
            if v > 0:
                self.items[eng].append(("wait", sem, v))


def build(taps=(), upto="final"):
    taps = set(taps)
    nc = bass.Bass("TRN2", target_bir_lowering=False, detect_race_conditions=False)

    x_d = nc.dram_tensor("x", [NSEQ, S, D], F32, kind="ExternalInput")
    wqkv_d = nc.dram_tensor("w_qkv", [D, 3 * D], F32, kind="ExternalInput")
    wout_d = nc.dram_tensor("w_out", [D, D], F32, kind="ExternalInput")
    qw_d = nc.dram_tensor("q_w", [FH * FP, D], F32, kind="ExternalInput")
    kw_d = nc.dram_tensor("k_w", [FH * FP, D], F32, kind="ExternalInput")
    vw_d = nc.dram_tensor("v_w", [FH * FE, D, FP], F32, kind="ExternalInput")
    ow_d = nc.dram_tensor("o_w", [FH * FE, FP, D], F32, kind="ExternalInput")
    selv_d = nc.dram_tensor("sel_v", [FH * FE, D], F32, kind="ExternalInput")
    selo_d = nc.dram_tensor("sel_o", [FH * FE, D], F32, kind="ExternalInput")
    rs_d = nc.dram_tensor("route_scale", [1], F32, kind="ExternalInput")
    ident_d = nc.dram_tensor("c_ident", [128, 128], F32, kind="ExternalInput")
    cos_hi_d = nc.dram_tensor("c_cos_hi", [128, S], F32, kind="ExternalInput")
    cos_lo_d = nc.dram_tensor("c_cos_lo", [128, S], F32, kind="ExternalInput")
    sin_hi_d = nc.dram_tensor("c_sin_hi", [128, S], F32, kind="ExternalInput")
    sin_lo_d = nc.dram_tensor("c_sin_lo", [128, S], F32, kind="ExternalInput")
    masks_d = nc.dram_tensor("c_masks", [4, 128, 512], BF16, kind="ExternalInput")
    out_d = nc.dram_tensor("out", [NSEQ, S, D], F32, kind="ExternalOutput")
    oscr_d = nc.dram_tensor("oscr", [8, 128, S], F32, kind="Internal")
    wqkvh_d = nc.dram_tensor("wqkvh", [D, 3 * D], BF16, kind="Internal")
    wqkvl_d = nc.dram_tensor("wqkvl", [D, 3 * D], BF16, kind="Internal")
    wouth_d = nc.dram_tensor("wouth", [D, D], BF16, kind="Internal")
    woutl_d = nc.dram_tensor("woutl", [D, D], BF16, kind="Internal")
    vwh_d = nc.dram_tensor("vwh", [FH * FE, D, FP], BF16, kind="Internal")
    vwl_d = nc.dram_tensor("vwl", [FH * FE, D, FP], BF16, kind="Internal")
    owh_d = nc.dram_tensor("owh", [FH * FE, FP, D], BF16, kind="Internal")
    owl_d = nc.dram_tensor("owl", [FH * FE, FP, D], BF16, kind="Internal")

    tap_d = {}
    tap_shapes = {
        "gv": [128, TT * 64], "go": [128, TT * 64],
        "qrot": [128, S], "krot": [128, S],
        "ctxa": [128, TT * 64], "scrA": [8, 128, S],
        "ctxf": [128, TT * 128], "scrF": [8, 128, S],
        "wqh": ([D, 3 * D], BF16), "vhb": ([128, TT * 130], BF16),
        "vlb": ([128, TT * 130], BF16), "cphb": ([128, 2048], BF16),
        "qhb": ([128, S], BF16), "qlb": ([128, S], BF16),
    }
    for t in taps:
        shp = tap_shapes[t]
        dt = F32
        if isinstance(shp, tuple):
            shp, dt = shp
        tap_d[t] = nc.dram_tensor("tap_" + t, shp, dt, kind="ExternalOutput")

    A = nc.alloc_sbuf_tensor
    xn_h = A("xn_h", [128, 8, S], BF16)
    xn_l = A("xn_l", [128, 8, S], BF16)
    cos_hi = A("cos_hi", [128, S], F32)
    sin_hi = A("sin_hi", [128, S], F32)
    gates_v = A("gates_v", [128, TT, 64], F32)
    gates_o = A("gates_o", [128, TT, 64], F32)
    selT_h = A("selT_h", [128, 2, 8, 64], BF16)
    selT_l = A("selT_l", [128, 2, 8, 64], BF16)
    masks_b = A("masks_b", [128, 4, 512], BF16)
    id_sb = A("id_sb", [128, 128], F32)
    iem9 = A("iem9", [128, 64], F32)
    rs_col = A("rs_col", [128, 1], F32)
    eps_col = A("eps_col", [128, 1], F32)
    ones_row = A("ones_row", [1, 128], F32)
    rs_sb = A("rs_sb", [1, 1], F32)
    sc = A("sc", [128, 32], F32)
    big1 = A("big1", [128, S], F32)
    big2 = A("big2", [128, S], F32)
    big3 = A("big3", [128, S], F32)
    big5 = A("big5", [128, 4096], F32)
    big8 = A("big8", [128, S], F32)
    qh = A("qh", [128, S], BF16)
    ql = A("ql", [128, S], BF16)
    kh = A("kh", [128, S], BF16)
    kl = A("kl", [128, S], BF16)
    vh = A("vh", [128, TT, 130], BF16)
    vl = A("vl", [128, TT, 130], BF16)
    wh = A("wh", [128, 4096], BF16)
    wl = A("wl", [128, 4096], BF16)
    whe_h = A("whe_h", [128, 4096], BF16)
    whe_l = A("whe_l", [128, 4096], BF16)
    pTh = A("pTh", [128, 2, 512], BF16)
    pTl = A("pTl", [128, 2, 512], BF16)
    tmpg = A("tmpg", [128, 2, 128], F32)
    sbuf_left = nc.sbuf_bytes_remaining

    gt = big5.ap()[:, 0:640]
    cos_lo = big5.ap()[:, 0:2048]
    sin_lo = big5.ap()[:, 2048:4096]
    vacc = big5.ap()[:, 0:2080].rearrange("p (t u) -> p t u", t=TT)
    ln_x = big5.ap()[:, 0:2048].rearrange("p (b u) -> p b u", b=2)
    ln_s1 = big5.ap()[:, 2048:4096].rearrange("p (b u) -> p b u", b=2)
    fin_in = big5.ap()[:, 0:2048].rearrange("p (b c u) -> p b c u", b=2, c=8)
    fin_x = big5.ap()[:, 2048:4096].rearrange("p (b u) -> p b u", b=2)
    fin_o = big5.ap()[:, 0:2048].rearrange("p (b u) -> p b u", b=2)
    raw = big8.ap()
    sq_dump = big8.ap()[:, 0:1024]
    sel_raw64 = big8.ap()[0:64, 0:1024]
    shuf = big3.ap()
    ctxn_f = big3.ap().rearrange("p (t u) -> p t u", t=TT)
    ctxn = big3.ap()[:, 0:1024].rearrange("p (t u) -> p t u", t=TT)
    evac = big1.ap()[:, 0:1024].rearrange("p (b u) -> p b u", b=2)
    wq_h = wh.ap()[:, 0:3072].rearrange("p (c j u) -> p c j u", c=8, j=3)
    wq_l = wl.ap()[:, 0:3072].rearrange("p (c j u) -> p c j u", c=8, j=3)
    wout_h = wh.ap()[:, 3072:4096].rearrange("p (c u) -> p c u", c=8)
    wout_l = wl.ap()[:, 3072:4096].rearrange("p (c u) -> p c u", c=8)
    qwT_h = wh.ap()[:, 0:1024].rearrange("p (c u) -> p c u", c=8)
    qwT_l = wl.ap()[:, 0:1024].rearrange("p (c u) -> p c u", c=8)
    kwT_h = wh.ap()[:, 1024:2048].rearrange("p (c u) -> p c u", c=8)
    kwT_l = wl.ap()[:, 1024:2048].rearrange("p (c u) -> p c u", c=8)
    vw_h = wh.ap().rearrange("p (b c e u) -> p b c e u", b=2, c=8, e=2)
    vw_l = wl.ap().rearrange("p (b c e u) -> p b c e u", b=2, c=8, e=2)
    ow_h = wh.ap()[:, 0:2048].rearrange("p (b e u) -> p b e u", b=2, e=8)
    ow_l = wl.ap()[:, 0:2048].rearrange("p (b e u) -> p b e u", b=2, e=8)
    wstg_h = wh.ap().rearrange("p (b u) -> p b u", b=2)
    wstg_l = wl.ap().rearrange("p (b u) -> p b u", b=2)
    cph = whe_h.ap()[:, 0:2048]
    cpl = whe_l.ap()[:, 0:2048]
    ctxB_h = whe_h.ap()[0:64, 2048:4096]
    ctxB_l = whe_l.ap()[0:64, 2048:4096]
    w_he_h = whe_h.ap().rearrange("p (e u) -> p e u", e=8)
    w_he_l = whe_l.ap().rearrange("p (e u) -> p e u", e=8)

    ps_a = nc.alloc_psum_tensor("ps_a", [128, 512], F32)
    ps_b = nc.alloc_psum_tensor("ps_b", [128, 512], F32)
    ps_big = nc.alloc_psum_tensor("ps_big", [128, 2, 512], F32)
    ps_ctx = nc.alloc_psum_tensor("ps_ctx", [128, 4, 512], F32)
    ps_bigf = ps_big.ap().rearrange("p a u -> p (a u)")

    p = Prog()

    def T(fn, inc=1):
        return p.op("tensor", fn, ((eT, inc),) if inc else ())

    def V(fn, inc=1):
        return p.op("vector", fn, ((eV, inc),) if inc else ())

    def Ac(fn, inc=1):
        return p.op("scalar", fn, ((eA, inc),) if inc else ())

    def G(fn, inc=1):
        return p.op("gpsimd", fn, ((eP, inc),) if inc else ())

    def dma(queue_eng, sem, out, in_, accum=False):
        def fn(eng, out=out, in_=in_, accum=accum):
            if accum:
                return eng.dma_start(out=out, in_=in_, accum_op=ALU.add)
            return eng.dma_start(out=out, in_=in_)
        return p.op(queue_eng, fn, ((sem, 16),))

    def join():
        for e in ENGS:
            own = OWN.get(e)
            for sm in ALL_SEMS:
                if sm != own:
                    p.wait(e, sm)

    def tap(name, src_ap):
        if name in tap_d:
            dma("sync", dO, tap_d[name].ap(), src_ap)

    def MM3(out, lh, rh, ll, rl, start, stop, inc=0):
        T(lambda t, out=out, lh=lh, rh=rh, start=start: t.matmul(
            out=out, lhsT=lh, rhs=rh, start=start, stop=False), inc=0)
        T(lambda t, out=out, lh=lh, rl=rl: t.matmul(
            out=out, lhsT=lh, rhs=rl, start=False, stop=False), inc=0)
        return T(lambda t, out=out, ll=ll, rh=rh, stop=stop: t.matmul(
            out=out, lhsT=ll, rhs=rh, start=False, stop=stop), inc=inc)

    def castpair(dh, dl, src, inc=1):
        ta = Ac(lambda a, dh=dh, src=src: a.copy(out=dh, in_=src))
        p.wait_tok("vector", *ta)
        return V(lambda v, dh=dh, dl=dl, src=src: v.tensor_tensor(
            out=dl, in0=src, in1=dh, op=ALU.subtract), inc=inc)

    dma("sync", dC, id_sb[:], ident_d[:])
    dma("sync", dC, cos_hi[:], cos_hi_d[:])
    dma("sync", dC, sin_hi[:], sin_hi_d[:])
    dma("sync", dC, masks_b[:], masks_d.ap().rearrange("o p u -> p o u"))
    dma("sync", dC, rs_sb[:], rs_d.ap().rearrange("(a b) -> a b", a=1))

    G(lambda g: g.iota(out=iem9[:], pattern=[[0, 8], [1, 8]], base=-9,
                       channel_multiplier=0, allow_small_or_imprecise_dtypes=True))
    G(lambda g: g.memset(eps_col[:], EPS))
    G(lambda g: g.memset(ones_row[:], 1.0))

    p.wait("tensor", dC)
    p.wait("tensor", eP)
    T(lambda t: t.matmul(out=ps_a[:, 0:1], lhsT=ones_row[:], rhs=rs_sb[:],
                         start=True, stop=True))
    p.wait("vector", eT)
    V(lambda v: v.tensor_copy(out=rs_col[:], in_=ps_a[:, 0:1]))

    for si, sd in enumerate((selv_d, selo_d)):
        p.wait("sync", eT)
        tk = dma("sync", dC, sel_raw64, sd[:])
        p.wait_tok("tensor", *tk)
        p.wait("tensor", eV)
        for c in range(8):
            ps = ps_a if c % 2 == 0 else ps_b
            if si > 0 or c >= 2:
                p.wait("tensor", eV)
                p.wait("tensor", eA)
            T(lambda t, c=c, ps=ps: t.transpose(
                out=ps[0:128, 0:64], in_=sel_raw64[:, c * 128:(c + 1) * 128],
                identity=id_sb[0:64, 0:64]))
            p.wait("scalar", eT)
            castpair(selT_h[:, si, c, :], selT_l[:, si, c, :], ps[0:128, 0:64])
        p.wait("tensor", eV)
        p.wait("tensor", eA)

    def st_wsplit():
        specs = []
        for (srcd, dh_d, dl_d, rows, cols) in (
                (wqkv_d, wqkvh_d, wqkvl_d, D, 3 * D),
                (wout_d, wouth_d, woutl_d, D, D)):
            sv = srcd.ap().rearrange("(c p) n -> p c n", p=128)
            hv = dh_d.ap().rearrange("(c p) n -> p c n", p=128)
            lv = dl_d.ap().rearrange("(c p) n -> p c n", p=128)
            nb = cols // 1024
            for c in range(rows // 128):
                for b in range(nb):
                    sl = (slice(None), c, slice(b * 1024, (b + 1) * 1024))
                    specs.append((sv[sl], hv[sl], lv[sl], 1024))
        for (srcd, dh_d, dl_d) in ((vw_d, vwh_d, vwl_d), (ow_d, owh_d, owl_d)):
            sv = srcd.ap().rearrange("e a u -> (e a) u").rearrange(
                "(c p) n -> p c n", p=128)
            hv = dh_d.ap().rearrange("e a u -> (e a) u").rearrange(
                "(c p) n -> p c n", p=128)
            lv = dl_d.ap().rearrange("e a u -> (e a) u", ).rearrange(
                "(c p) n -> p c n", p=128)
            C = sv.shape[1]
            N = sv.shape[2]
            grp = max(1, 1024 // N)
            for c0 in range(0, C, grp):
                sl = (slice(None), slice(c0, c0 + grp), slice(None))
                specs.append((sv[sl], hv[sl], lv[sl], grp * N))
        stg = ln_x
        for i, (s_ap, h_ap, l_ap, L) in enumerate(specs):
            par = i % 2
            if i >= 2:
                p.wait("sync", eV)
            tkr = dma("sync", dC, stg[:, par, 0:L], s_ap)
            hv2 = wstg_h[:, par, 0:L]
            lv2 = wstg_l[:, par, 0:L]
            p.wait_tok("scalar", *tkr)
            if i >= 2:
                p.wait("scalar", dG)
            ta = Ac(lambda a, hv2=hv2, st=stg[:, par, 0:L]: a.copy(out=hv2, in_=st))
            p.wait_tok("vector", *ta)
            if i >= 2:
                p.wait("vector", dG)
            tv = V(lambda v, hv2=hv2, lv2=lv2, st=stg[:, par, 0:L]: v.tensor_tensor(
                out=lv2, in0=st, in1=hv2, op=ALU.subtract))
            p.wait_tok("gpsimd", *ta)
            p.wait_tok("gpsimd", *tv)
            dma("gpsimd", dG, h_ap, hv2)
            dma("gpsimd", dG, l_ap, lv2)
        join()
        tap("wqh", wqkvh_d.ap())

    def st_ln(s):
        join()
        for tt in range(TT):
            par = tt % 2
            o = par * 8
            p.wait("sync", eV)
            xt = ln_x[:, par, :]
            tkx = dma("sync", dX, xt, x_d[s, tt * 128:(tt + 1) * 128, :])
            p.wait_tok("vector", *tkx)
            p.wait("vector", eT)
            V(lambda v, xt=xt, o=o: v.reduce_sum(out=sc[:, o:o + 1], in_=xt, axis=AX.X))
            p.wait("vector", eV)
            V(lambda v, o=o: v.tensor_scalar(out=sc[:, o + 1:o + 2], in0=sc[:, o:o + 1],
                                             scalar1=1.0 / D, scalar2=None, op0=ALU.mult))
            p.wait("vector", eV)
            dev = ln_s1[:, par, :]
            tkd = V(lambda v, xt=xt, dev=dev, o=o: v.tensor_scalar(
                out=dev, in0=xt, scalar1=sc[:, o + 1:o + 2], scalar2=None,
                op0=ALU.subtract))
            p.wait_tok("scalar", *tkd)
            Ac(lambda a, dev=dev, o=o: a.activation(
                out=sq_dump, in_=dev, func=AF.Square,
                accum_out=sc[:, o + 2:o + 3]))
            p.wait("scalar", eA)
            tka = Ac(lambda a, o=o: a.activation(
                out=sc[:, o + 5:o + 6], in_=sc[:, o + 2:o + 3], func=AF.Sqrt,
                scale=1.0 / D, bias=eps_col[:]))
            p.wait_tok("vector", *tka)
            V(lambda v, o=o: v.reciprocal(out=sc[:, o + 3:o + 4], in_=sc[:, o + 5:o + 6]))
            p.wait("vector", eV)
            V(lambda v, o=o: v.tensor_tensor(out=sc[:, o + 4:o + 5],
                                             in0=sc[:, o + 1:o + 2],
                                             in1=sc[:, o + 3:o + 4], op=ALU.mult))
            p.wait("vector", eV)
            tkn = V(lambda v, xt=xt, dev=dev, o=o: v.tensor_scalar(
                out=dev, in0=xt, scalar1=sc[:, o + 3:o + 4],
                scalar2=sc[:, o + 4:o + 5], op0=ALU.mult, op1=ALU.subtract))
            p.wait_tok("tensor", *tkn)
            for c in range(8):
                ps = ps_a if c % 2 == 0 else ps_b
                if tt > 0 or c >= 2:
                    p.wait("tensor", eV)
                    p.wait("tensor", eA)
                T(lambda t, dev=dev, c=c, ps=ps: t.transpose(
                    out=ps[:, 0:128], in_=dev[:, c * 128:(c + 1) * 128],
                    identity=id_sb[:]))
                p.wait("scalar", eT)
                castpair(xn_h[:, c, tt * 128:(tt + 1) * 128],
                         xn_l[:, c, tt * 128:(tt + 1) * 128], ps[:, 0:128])
        if s == 0 and "xn" in tap_d:
            join()

    def st_gates(s):
        join()

        def b8(ap):
            return ap.rearrange("p (g e) -> p g e", e=1).broadcast_to([128, 8, 8])

        IE3 = iem9.ap().rearrange("p (g e) -> p g e", e=8)

        def emit_top2(si, gout, tt):
            Pa = gt[:, si * 64:(si + 1) * 64]
            P3v = Pa.rearrange("p (g e) -> p g e", e=8)
            M1 = gt[:, 192:200]
            I1 = gt[:, 200:208]
            M2 = gt[:, 208:216]
            I2 = gt[:, 216:224]
            SS = gt[:, 224:232]
            R = gt[:, 232:240]
            W1 = gt[:, 240:248]
            W2 = gt[:, 248:256]
            E1 = gt[:, 256:320].rearrange("p (g e) -> p g e", e=8)
            OH1 = gt[:, 320:384].rearrange("p (g e) -> p g e", e=8)
            PM = gt[:, 384:448]
            PM3 = PM.rearrange("p (g e) -> p g e", e=8)
            E2 = gt[:, 448:512].rearrange("p (g e) -> p g e", e=8)
            OH2 = gt[:, 512:576].rearrange("p (g e) -> p g e", e=8)
            CT = gt[:, 576:640].rearrange("p (g e) -> p g e", e=8)
            CTf = gt[:, 576:640]
            steps = [
                lambda v: v.reduce_max(out=M1, in_=P3v, axis=AX.X),
                lambda v: v.tensor_tensor(out=E1, in0=P3v, in1=b8(M1), op=ALU.is_equal),
                lambda v: v.tensor_tensor(out=CT, in0=E1, in1=IE3, op=ALU.mult),
                lambda v: v.tensor_reduce(out=I1, in_=CT, axis=AX.X, op=ALU.min),
                lambda v: v.tensor_tensor(out=OH1, in0=IE3, in1=b8(I1), op=ALU.is_equal),
                lambda v: v.scalar_tensor_tensor(
                    out=PM, in0=OH1.rearrange("p g e -> p (g e)"), scalar=-2.0,
                    in1=Pa, op0=ALU.mult, op1=ALU.add),
                lambda v: v.reduce_max(out=M2, in_=PM3, axis=AX.X),
                lambda v: v.tensor_tensor(out=E2, in0=PM3, in1=b8(M2), op=ALU.is_equal),
                lambda v: v.tensor_tensor(out=CT, in0=E2, in1=IE3, op=ALU.mult),
                lambda v: v.tensor_reduce(out=I2, in_=CT, axis=AX.X, op=ALU.min),
                lambda v: v.tensor_tensor(out=OH2, in0=IE3, in1=b8(I2), op=ALU.is_equal),
                lambda v: v.tensor_tensor(out=SS, in0=M1, in1=M2, op=ALU.add),
                lambda v: v.tensor_scalar(out=SS, in0=SS, scalar1=1e-9,
                                          scalar2=None, op0=ALU.max),
                lambda v: v.reciprocal(out=R, in_=SS),
                lambda v: v.tensor_tensor(out=W1, in0=M1, in1=R, op=ALU.mult),
                lambda v: v.tensor_scalar(out=W1, in0=W1, scalar1=rs_col[:],
                                          scalar2=None, op0=ALU.mult),
                lambda v: v.tensor_tensor(out=W2, in0=M2, in1=R, op=ALU.mult),
                lambda v: v.tensor_scalar(out=W2, in0=W2, scalar1=rs_col[:],
                                          scalar2=None, op0=ALU.mult),
                lambda v: v.tensor_tensor(out=CT, in0=OH1, in1=b8(W1), op=ALU.mult),
                lambda v: v.tensor_tensor(out=PM3, in0=OH2, in1=b8(W2), op=ALU.mult),
                lambda v: v.tensor_tensor(out=gout[:, tt, :], in0=CTf, in1=PM, op=ALU.add),
            ]
            for st_fn in steps:
                p.wait("vector", eV)
                V(st_fn, inc=1)

        for tt in range(TT):
            if tt > 0:
                p.wait("tensor", eA)
            for si in range(2):
                psg = ps_a if si == 0 else ps_b
                for c in range(8):
                    MM3(psg[:, 0:64],
                        xn_h[:, c, tt * 128:(tt + 1) * 128],
                        selT_h[:, si, c, :],
                        xn_l[:, c, tt * 128:(tt + 1) * 128],
                        selT_l[:, si, c, :],
                        start=(c == 0), stop=(c == 7), inc=(1 if c == 7 else 0))
            p.wait("scalar", eT)
            p.wait("scalar", eV)
            Ac(lambda a: a.activation(out=gt[:, 0:64], in_=ps_a[:, 0:64],
                                      func=AF.Sigmoid), inc=0)
            Ac(lambda a: a.activation(out=gt[:, 64:128], in_=ps_b[:, 0:64],
                                      func=AF.Sigmoid))
            p.wait("vector", eA)
            for si, gout in ((0, gates_v), (1, gates_o)):
                emit_top2(si, gout, tt)
        if s == 0 and ("gv" in tap_d or "go" in tap_d):
            join()
            tap("gv", gates_v.ap().rearrange("p t u -> p (t u)"))
            tap("go", gates_o.ap().rearrange("p t u -> p (t u)"))

    def rope_one(dst):
        V(lambda v, dst=dst: v.tensor_tensor(out=dst, in0=raw, in1=cos_hi[:], op=ALU.mult), inc=0)
        V(lambda v: v.tensor_tensor(out=raw, in0=raw, in1=cos_lo, op=ALU.mult), inc=0)
        V(lambda v, dst=dst: v.tensor_tensor(out=dst, in0=dst, in1=raw, op=ALU.add), inc=0)
        V(lambda v: v.tensor_tensor(out=raw, in0=shuf, in1=sin_hi[:], op=ALU.mult), inc=0)
        V(lambda v, dst=dst: v.tensor_tensor(out=dst, in0=dst, in1=raw, op=ALU.add), inc=0)
        V(lambda v: v.tensor_tensor(out=raw, in0=shuf, in1=sin_lo, op=ALU.mult), inc=0)
        V(lambda v, dst=dst: v.tensor_tensor(out=dst, in0=dst, in1=raw, op=ALU.add), inc=1)

    def attn_core(lKh, lKl, rQh, rQl, vAh, vAl, cW, causal, ctx_out_fn):
        pTlTok = [None, None]
        ctxTok = [None, None]
        p.wait("tensor", eA)
        p.wait("tensor", eV)
        p.wait("scalar", eT)
        p.wait("vector", eT)
        for qg in range(NQG):
            nkt = (4 * qg + 4) if causal else TT
            p.wait("tensor", eV)
            for kt in range(nkt):
                pr = kt % 2
                psS = ps_big.ap()[:, pr, :]
                if pTlTok[pr] is not None:
                    p.wait_tok("tensor", pTlTok[pr])
                tb = MM3(psS, lKh(kt), rQh(qg), lKl(kt), rQl(qg),
                         start=True, stop=True, inc=1)
                lastS = tb
                o = kt - 4 * qg
                if causal and o >= 0:
                    p.wait_tok("vector", *tb)
                    lastS = V(lambda v, psS=psS, o=o: v.tensor_tensor(
                        out=psS, in0=psS, in1=masks_b[:, o, :], op=ALU.add))
                    p.wait_tok("scalar", *lastS)
                else:
                    p.wait_tok("scalar", *lastS)
                Ac(lambda a, psS=psS: a.activation(
                    out=psS, in_=psS, func=AF.Exp), inc=0)
                if ctxTok[pr] is not None:
                    p.wait_tok("scalar", ctxTok[pr])
                ta = Ac(lambda a, psS=psS, pr=pr: a.copy(out=pTh[:, pr, :], in_=psS))
                p.wait_tok("vector", *ta)
                if ctxTok[pr] is not None:
                    p.wait_tok("vector", ctxTok[pr])
                tl = V(lambda v, psS=psS, pr=pr: v.tensor_tensor(
                    out=pTl[:, pr, :], in0=psS, in1=pTh[:, pr, :], op=ALU.subtract))
                pTlTok[pr] = tl[0]
                p.wait_tok("tensor", *ta)
                p.wait_tok("tensor", *tl)
                for qtl in range(4):
                    MM3(ps_ctx[:, qtl, cW[0]:cW[0] + cW[1]],
                        pTh[:, pr, qtl * 128:(qtl + 1) * 128], vAh(kt),
                        pTl[:, pr, qtl * 128:(qtl + 1) * 128], vAl(kt),
                        start=(kt == 0), stop=(kt == nkt - 1),
                        inc=(1 if (qtl == 3 and kt == nkt - 1) else 0))
                ctxTok[pr] = p.tok(eT)
            ctx_out_fn(qg)

    def st_attn(s):
        join()
        G(lambda g: g.memset(vh.ap()[:, :, 64:65], 1.0))
        G(lambda g: g.memset(vh.ap()[:, :, 129:130], 1.0))
        G(lambda g: g.memset(vl.ap()[:, :, 64:65], 0.0))
        G(lambda g: g.memset(vl.ap()[:, :, 129:130], 0.0))
        p.wait("tensor", eP)
        dma("sync", dW, cos_lo, cos_lo_d[:])
        dma("sync", dW, sin_lo, sin_lo_d[:])
        for hp in range(8):
            p.wait("sync", eT)
            tkw = None
            for j, c0 in enumerate((hp * 128, D + hp * 128, 2 * D + hp * 128)):
                dma("sync", dW, wq_h[:, :, j, :],
                    wqkvh_d.ap().rearrange("(c p) n -> p c n", p=128)[:, :, c0:c0 + 128])
                tkw = dma("sync", dW, wq_l[:, :, j, :],
                          wqkvl_d.ap().rearrange("(c p) n -> p c n", p=128)[:, :, c0:c0 + 128])
            p.wait_tok("tensor", *tkw)
            for j in (0, 1):
                scale = SCALE_A if j == 0 else 1.0
                p.wait("scalar", eV)
                p.wait("scalar", eT)
                for nt in range(NQG):
                    ps = ps_a if nt % 2 == 0 else ps_b
                    p.wait("tensor", eA)
                    for c in range(8):
                        MM3(ps[:, :], wq_h[:, c, j, :],
                            xn_h[:, c, nt * 512:(nt + 1) * 512],
                            wq_l[:, c, j, :],
                            xn_l[:, c, nt * 512:(nt + 1) * 512],
                            start=(c == 0), stop=(c == 7), inc=(1 if c == 7 else 0))
                    p.wait("scalar", eT)
                    Ac(lambda a, nt=nt, scale=scale, ps=ps: a.mul(
                        out=raw[:, nt * 512:(nt + 1) * 512], in_=ps[:, :], mul=scale))
                p.wait("gpsimd", eA)
                p.wait("gpsimd", eV)
                p.wait("gpsimd", eT)
                for (d0, s0) in ((0, 32), (32, 0), (64, 96), (96, 64)):
                    dma("gpsimd", dG, big3.ap()[d0:d0 + 32, :], raw[s0:s0 + 32, :])
                p.wait("vector", dG)
                p.wait("vector", dW)
                p.wait("vector", eA)
                p.wait("vector", eT)
                p.wait("vector", eP)
                dstb = big1 if j == 0 else big2
                rope_one(dstb.ap())
                p.wait("scalar", eV)
                p.wait("scalar", eT)
                if j == 0:
                    castpair(qh[:], ql[:], dstb.ap())
                else:
                    castpair(kh[:], kl[:], dstb.ap())
            for tt in range(TT):
                ps = ps_a if tt % 2 == 0 else ps_b
                p.wait("tensor", eV)
                p.wait("tensor", eA)
                for c in range(8):
                    MM3(ps[:, 0:128], xn_h[:, c, tt * 128:(tt + 1) * 128],
                        wq_h[:, c, 2, :],
                        xn_l[:, c, tt * 128:(tt + 1) * 128],
                        wq_l[:, c, 2, :],
                        start=(c == 0), stop=(c == 7), inc=(1 if c == 7 else 0))
                p.wait("scalar", eT)
                vh2 = vh.ap()[:, tt, :].rearrange("p (b u) -> p b u", b=2)[:, :, 0:64]
                vl2 = vl.ap()[:, tt, :].rearrange("p (b u) -> p b u", b=2)[:, :, 0:64]
                ps2 = ps[:, 0:128].rearrange("p (b u) -> p b u", b=2)
                castpair(vh2, vl2, ps2)
            if s == 0 and hp == 0:
                join()
                tap("qrot", big1.ap())
                tap("krot", big2.ap())
                tap("qhb", qh.ap())
                tap("qlb", ql.ap())
                tap("vhb", vh.ap().rearrange("p t u -> p (t u)"))
                tap("vlb", vl.ap().rearrange("p t u -> p (t u)"))
            for i in range(2):
                R0 = slice(64 * i, 64 * i + 64)
                cA = 65 * i

                def ctx_out(qg, cA=cA, i=i):
                    p.wait("vector", eT)
                    for qtl in range(4):
                        V(lambda v, qtl=qtl, cA=cA: v.reciprocal(
                            out=sc[:, 16 + qtl:17 + qtl],
                            in_=ps_ctx[:, qtl, cA + 64:cA + 65]),
                          inc=(1 if qtl == 3 else 0))
                    p.wait("vector", eV)
                    for qtl in range(4):
                        qt = qg * 4 + qtl
                        V(lambda v, qtl=qtl, qt=qt, cA=cA: v.tensor_scalar(
                            out=ctxn[:, qt, :], in0=ps_ctx[:, qtl, cA:cA + 64],
                            scalar1=sc[:, 16 + qtl:17 + qtl], scalar2=None,
                            op0=ALU.mult), inc=(1 if qtl == 3 else 0))

                attn_core(
                    lKh=lambda kt, R0=R0: kh.ap()[R0, kt * 128:(kt + 1) * 128],
                    lKl=lambda kt, R0=R0: kl.ap()[R0, kt * 128:(kt + 1) * 128],
                    rQh=lambda qg, R0=R0: qh.ap()[R0, qg * 512:(qg + 1) * 512],
                    rQl=lambda qg, R0=R0: ql.ap()[R0, qg * 512:(qg + 1) * 512],
                    vAh=lambda kt, cA=cA: vh.ap()[:, kt, cA:cA + 65],
                    vAl=lambda kt, cA=cA: vl.ap()[:, kt, cA:cA + 65],
                    cW=(cA, 65), causal=True, ctx_out_fn=ctx_out)
                p.wait("tensor", eV)
                for qt in range(TT):
                    ps = ps_a if qt % 2 == 0 else ps_b
                    p.wait("tensor", eV)
                    p.wait("tensor", eA)
                    T(lambda t, qt=qt, ps=ps: t.transpose(
                        out=ps[0:64, 0:128], in_=ctxn[:, qt, :], identity=id_sb[:]))
                    p.wait("scalar", eT)
                    if i == 0:
                        castpair(cph[0:64, qt * 128:(qt + 1) * 128],
                                 cpl[0:64, qt * 128:(qt + 1) * 128], ps[0:64, 0:128])
                    else:
                        castpair(ctxB_h[:, qt * 128:(qt + 1) * 128],
                                 ctxB_l[:, qt * 128:(qt + 1) * 128], ps[0:64, 0:128])
                if i == 1:
                    p.wait("gpsimd", eV)
                    dma("gpsimd", dG, cph[64:128, :], ctxB_h[:, :])
                    dma("gpsimd", dG, cpl[64:128, :], ctxB_l[:, :])
            if s == 0 and hp == 0:
                join()
                tap("ctxa", ctxn.rearrange("p t u -> p (t u)"))
                tap("cphb", cph)
            p.wait("sync", eT)
            dma("sync", dW, wout_h,
                wouth_d[hp * 128:(hp + 1) * 128, :].rearrange("p (c u) -> p c u", c=8))
            tkwo = dma("sync", dW, wout_l,
                       woutl_d[hp * 128:(hp + 1) * 128, :].rearrange("p (c u) -> p c u", c=8))
            p.wait_tok("tensor", *tkwo)
            p.wait("tensor", dG)
            p.wait("tensor", eV)
            woTok = [None, None]
            for dc in range(8):
                for jj in range(4):
                    pr2 = (dc * 4 + jj) % 2
                    p.wait("tensor", eA)
                    MM3(ps_big[:, jj % 2, :], wout_h[:, dc, :],
                        cph[:, jj * 512:(jj + 1) * 512],
                        wout_l[:, dc, :],
                        cpl[:, jj * 512:(jj + 1) * 512],
                        start=True, stop=True, inc=1)
                    p.wait("scalar", eT)
                    if woTok[pr2] is not None:
                        p.wait_tok("scalar", woTok[pr2])
                    tae = Ac(lambda a, pr2=pr2, jj=jj: a.copy(
                        out=evac[:, pr2, :], in_=ps_big[:, jj % 2, :]))
                    p.wait_tok("gpsimd", *tae)
                    tkg = dma("gpsimd", dG, oscr_d[dc, :, jj * 512:(jj + 1) * 512],
                              evac[:, pr2, :], accum=(hp != 0))
                    woTok[pr2] = tkg[0]
        if s == 0 and "scrA" in tap_d:
            join()
            tap("scrA", oscr_d.ap())

    def st_ffn(s):
        join()
        G(lambda g: g.memset(vacc[:, :, 128:129], 1.0))
        p.wait("scalar", eP)
        p.wait("vector", eP)
        owTok = [None, None]
        for h in range(8):
            for wi, wd in enumerate((qw_d, kw_d)):
                p.wait("sync", eT)
                p.wait("sync", dG)
                tk = dma("sync", dF, raw[:, 0:1024], wd[h * 128:(h + 1) * 128, :])
                p.wait_tok("tensor", *tk)
                p.wait("tensor", eA)
                for c in range(8):
                    ps = ps_a if c % 2 == 0 else ps_b
                    p.wait("tensor", eV)
                    p.wait("tensor", eA)
                    T(lambda t, c=c, ps=ps: t.transpose(
                        out=ps[:, 0:128], in_=raw[:, c * 128:(c + 1) * 128],
                        identity=id_sb[:]))
                    p.wait("scalar", eT)
                    hv = (qwT_h if wi == 0 else kwT_h)[:, c, :]
                    lv = (qwT_l if wi == 0 else kwT_l)[:, c, :]
                    castpair(hv, lv, ps[:, 0:128])
            for wi in (0, 1):
                scale = SCALE_F if wi == 0 else 1.0
                for nt in range(NQG):
                    ps = ps_a if nt % 2 == 0 else ps_b
                    p.wait("tensor", eA)
                    p.wait("tensor", eV)
                    for c in range(8):
                        MM3(ps[:, :], (qwT_h if wi == 0 else kwT_h)[:, c, :],
                            xn_h[:, c, nt * 512:(nt + 1) * 512],
                            (qwT_l if wi == 0 else kwT_l)[:, c, :],
                            xn_l[:, c, nt * 512:(nt + 1) * 512],
                            start=(c == 0), stop=(c == 7), inc=(1 if c == 7 else 0))
                    p.wait("scalar", eT)
                    p.wait("scalar", eV)
                    hv = (qh if wi == 0 else kh).ap()[:, nt * 512:(nt + 1) * 512]
                    lv = (ql if wi == 0 else kl).ap()[:, nt * 512:(nt + 1) * 512]
                    ta = Ac(lambda a, hv=hv, ps=ps, scale=scale: a.mul(
                        out=hv, in_=ps[:, :], mul=scale))
                    p.wait_tok("vector", *ta)
                    V(lambda v, hv=hv, lv=lv, ps=ps, scale=scale: v.scalar_tensor_tensor(
                        out=lv, in0=ps[:, :], scalar=scale, in1=hv,
                        op0=ALU.mult, op1=ALU.subtract))
            for eg in range(4):
                par = eg % 2
                p.wait("sync", eT)
                for e2 in range(2):
                    dma("sync", dF, vw_h[:, par, :, e2, :],
                        vwh_d[h * 8 + eg * 2 + e2, :, :].rearrange("(c p) u -> p c u", p=128))
                    tk = dma("sync", dF, vw_l[:, par, :, e2, :],
                             vwl_d[h * 8 + eg * 2 + e2, :, :].rearrange("(c p) u -> p c u", p=128))
                p.wait_tok("tensor", *tk)
                for tt in range(TT):
                    ps = ps_a if tt % 2 == 0 else ps_b
                    p.wait("tensor", eV)
                    for c in range(8):
                        MM3(ps[:, 0:256], xn_h[:, c, tt * 128:(tt + 1) * 128],
                            vw_h[:, par, c, :, :],
                            xn_l[:, c, tt * 128:(tt + 1) * 128],
                            vw_l[:, par, c, :, :],
                            start=(c == 0), stop=(c == 7), inc=(1 if c == 7 else 0))
                    p.wait("vector", eT)
                    for e2 in range(2):
                        he = h * 8 + eg * 2 + e2
                        if e2 == 1:
                            p.wait("vector", eV)
                        if eg == 0 and e2 == 0:
                            V(lambda v, tt=tt, e2=e2, he=he, ps=ps: v.tensor_scalar(
                                out=vacc[:, tt, 0:128],
                                in0=ps[:, e2 * 128:(e2 + 1) * 128],
                                scalar1=gates_v[:, tt, he:he + 1], scalar2=None,
                                op0=ALU.mult), inc=1)
                        else:
                            V(lambda v, tt=tt, e2=e2, he=he, ps=ps: v.scalar_tensor_tensor(
                                out=vacc[:, tt, 0:128],
                                in0=ps[:, e2 * 128:(e2 + 1) * 128],
                                scalar=gates_v[:, tt, he:he + 1],
                                in1=vacc[:, tt, 0:128],
                                op0=ALU.mult, op1=ALU.add), inc=1)
            p.wait("scalar", eV)
            p.wait("scalar", eT)
            castpair(vh.ap()[:, :, 0:129], vl.ap()[:, :, 0:129], vacc[:, :, 0:129])

            def ctx_out_f(qg):
                p.wait("vector", eT)
                for qtl in range(4):
                    V(lambda v, qtl=qtl: v.reciprocal(
                        out=sc[:, 24 + qtl:25 + qtl],
                        in_=ps_ctx[:, qtl, 128:129]),
                      inc=(1 if qtl == 3 else 0))
                p.wait("vector", eV)
                for qtl in range(4):
                    qt = qg * 4 + qtl
                    V(lambda v, qtl=qtl, qt=qt: v.tensor_scalar(
                        out=ctxn_f[:, qt, :], in0=ps_ctx[:, qtl, 0:128],
                        scalar1=sc[:, 24 + qtl:25 + qtl], scalar2=None,
                        op0=ALU.mult), inc=(1 if qtl == 3 else 0))

            attn_core(
                lKh=lambda kt: kh.ap()[:, kt * 128:(kt + 1) * 128],
                lKl=lambda kt: kl.ap()[:, kt * 128:(kt + 1) * 128],
                rQh=lambda qg: qh.ap()[:, qg * 512:(qg + 1) * 512],
                rQl=lambda qg: ql.ap()[:, qg * 512:(qg + 1) * 512],
                vAh=lambda kt: vh.ap()[:, kt, 0:129],
                vAl=lambda kt: vl.ap()[:, kt, 0:129],
                cW=(0, 129), causal=False, ctx_out_fn=ctx_out_f)
            if s == 0 and h == 0 and "ctxf" in tap_d:
                join()
                tap("ctxf", ctxn_f.rearrange("p t u -> p (t u)"))

            for tc in range(4):
                for e in range(8):
                    he = h * 8 + e
                    for qtl in range(4):
                        qt = tc * 4 + qtl
                        pg = (e * 4 + qtl) % 2
                        p.wait("vector", eT)
                        V(lambda v, qt=qt, he=he, pg=pg: v.tensor_scalar(
                            out=tmpg[:, pg, :], in0=ctxn_f[:, qt, :],
                            scalar1=gates_o[:, qt, he:he + 1], scalar2=None,
                            op0=ALU.mult))
                        ps = ps_a if pg == 0 else ps_b
                        p.wait("tensor", eV)
                        p.wait("tensor", eA)
                        T(lambda t, pg=pg, ps=ps: t.transpose(
                            out=ps[:, 0:128], in_=tmpg[:, pg, :], identity=id_sb[:]))
                        p.wait("scalar", eT)
                        castpair(w_he_h[:, e, qtl * 128:(qtl + 1) * 128],
                                 w_he_l[:, e, qtl * 128:(qtl + 1) * 128], ps[:, 0:128])
                for dc in range(8):
                    par = dc % 2
                    p.wait("sync", eT)
                    dma("sync", dF, ow_h[:, par, :, :],
                        owh_d[h * 8:(h + 1) * 8, :, dc * 128:(dc + 1) * 128]
                        .rearrange("e p u -> p e u"))
                    tk = dma("sync", dF, ow_l[:, par, :, :],
                             owl_d[h * 8:(h + 1) * 8, :, dc * 128:(dc + 1) * 128]
                             .rearrange("e p u -> p e u"))
                    p.wait_tok("tensor", *tk)
                    p.wait("tensor", eV)
                    ps = ps_a if par == 0 else ps_b
                    p.wait("tensor", eA)
                    for e in range(8):
                        MM3(ps[:, :], ow_h[:, par, e, :], w_he_h[:, e, :],
                            ow_l[:, par, e, :], w_he_l[:, e, :],
                            start=(e == 0), stop=(e == 7), inc=(1 if e == 7 else 0))
                    p.wait("scalar", eT)
                    pr2 = (tc * 8 + dc) % 2
                    if owTok[pr2] is not None:
                        p.wait_tok("scalar", owTok[pr2])
                    tae = Ac(lambda a, pr2=pr2, ps=ps: a.copy(
                        out=evac[:, pr2, :], in_=ps[:, :]))
                    p.wait_tok("gpsimd", *tae)
                    tkg = dma("gpsimd", dG, oscr_d[dc, :, tc * 512:(tc + 1) * 512],
                              evac[:, pr2, :], accum=True)
                    owTok[pr2] = tkg[0]
        if s == 0 and "scrF" in tap_d:
            join()
            tap("scrF", oscr_d.ap())

    def st_final(s):
        join()
        for tt in range(TT):
            par = tt % 2
            if tt >= 2:
                p.wait("sync", eT)
                p.wait("sync", eV)
            tk1 = dma("sync", dR, fin_in[:, par, :, :],
                      oscr_d[:, :, tt * 128:(tt + 1) * 128].rearrange("c p u -> p c u"))
            tk2 = dma("sync", dX, fin_x[:, par, :], x_d[s, tt * 128:(tt + 1) * 128, :])
            p.wait_tok("tensor", *tk1)
            if tt >= 1:
                p.wait("tensor", eV)
            for c in range(8):
                T(lambda t, par=par, c=c: t.transpose(
                    out=ps_bigf[:, c * 128:(c + 1) * 128],
                    in_=fin_in[:, par, c, :], identity=id_sb[:]),
                  inc=(1 if c == 7 else 0))
            p.wait_tok("vector", *tk2)
            p.wait("vector", eT)
            if tt >= 2:
                p.wait("vector", dO)
            tv = V(lambda v, par=par: v.tensor_tensor(
                out=fin_o[:, par, :], in0=ps_bigf, in1=fin_x[:, par, :], op=ALU.add))
            p.wait_tok("sync", *tv)
            dma("sync", dO, out_d[s, tt * 128:(tt + 1) * 128, :], fin_o[:, par, :])

    stages = ["ln", "gates", "attn", "ffn", "final"]
    upto_i = stages.index(upto)
    nseq_eff = NSEQ if upto_i >= 4 else 1
    st_wsplit()
    for s in range(nseq_eff):
        st_ln(s)
        if upto_i >= 1:
            st_gates(s)
        if upto_i >= 2:
            st_attn(s)
        if upto_i >= 3:
            st_ffn(s)
        if upto_i >= 4:
            st_final(s)

    p.wait("sync", dO)
    p.wait("sync", dG)
    p.wait("gpsimd", dO)

    with contextlib.ExitStack() as es:
        semh = {name: es.enter_context(nc.semaphore(name)) for name in p.cnt}
        block = es.enter_context(nc.Block())

        def runner(items):
            def body(eng):
                for kind, a, b in items:
                    if kind == "wait":
                        eng.wait_ge(semh[a], b)
                    else:
                        ins = a(eng)
                        for sm, amt in b:
                            ins = ins.then_inc(semh[sm], amt)
            return body

        for ename in ENGS:
            getattr(block, ename)(runner(p.items[ename]))
    return nc, tap_d, sbuf_left


def kernel(L0_attn_x, L0_attn_w_qkv, L0_attn_w_out, L0_ffn_q_w, L0_ffn_k_w,
           L0_ffn_v_w, L0_ffn_o_w, L0_ffn_sel_v_w, L0_ffn_sel_o_w,
           L0_ffn_route_scale, _taps=(), _upto="final"):
    consts = host_consts()
    nc, tap_d, _ = build(taps=_taps, upto=_upto)
    in_maps = []
    for c in range(NCORES):
        m = {
            "x": np.ascontiguousarray(np.asarray(L0_attn_x)[c * NSEQ:(c + 1) * NSEQ]),
            "w_qkv": np.asarray(L0_attn_w_qkv), "w_out": np.asarray(L0_attn_w_out),
            "q_w": np.asarray(L0_ffn_q_w), "k_w": np.asarray(L0_ffn_k_w),
            "v_w": np.asarray(L0_ffn_v_w), "o_w": np.asarray(L0_ffn_o_w),
            "sel_v": np.asarray(L0_ffn_sel_v_w), "sel_o": np.asarray(L0_ffn_sel_o_w),
            "route_scale": np.asarray(L0_ffn_route_scale),
        }
        m.update(consts)
        in_maps.append(m)
    res = run_bass_kernel_spmd(nc, in_maps, core_ids=list(range(NCORES)))
    out = np.concatenate([r["out"] for r in res.results], axis=0)
    if _taps:
        return out, {t: res.results[0]["tap_" + t] for t in _taps}
    return out


# version 03
# speedup vs baseline: 1.2035x; 1.2035x over previous
import math
import contextlib
import numpy as np
import ml_dtypes

import concourse.bass as bass
import concourse.mybir as mybir
from concourse.bass_utils import run_bass_kernel_spmd

F32 = mybir.dt.float32
BF16 = mybir.dt.bfloat16
AF = mybir.ActivationFunctionType
ALU = mybir.AluOpType
AX = mybir.AxisListType

B, S, D = 16, 2048, 1024
NCORES = 8
NSEQ = B // NCORES
DH = 64
FH, FE, FP = 8, 8, 128
SCALE_A = 1.0 / math.sqrt(DH)
SCALE_F = 1.0 / math.sqrt(FP)
EPS = 1e-5
NEG = -1.0e30
TT = S // 128
NQG = S // 512
ENGS = ("sync", "vector", "scalar", "tensor", "gpsimd")


def host_consts():
    ident = np.eye(128, dtype=np.float32)
    j = np.arange(32, dtype=np.float64)
    invf = 1.0 / (10000.0 ** (j / 32.0))
    t = np.arange(S, dtype=np.float64)
    ang = invf[:, None] * t[None, :]
    cos_rep = np.tile(np.cos(ang), (4, 1))
    sin_rep = np.tile(np.sin(ang), (4, 1))
    sgn = np.ones((128, 1))
    sgn[0:32] = -1.0
    sgn[64:96] = -1.0
    sin_rep = sin_rep * sgn
    cos_hi = cos_rep.astype(np.float32)
    cos_lo = (cos_rep - cos_hi.astype(np.float64)).astype(np.float32)
    sin_hi = sin_rep.astype(np.float32)
    sin_lo = (sin_rep - sin_hi.astype(np.float64)).astype(np.float32)
    kp = np.arange(128)[:, None]
    qf = np.arange(512)[None, :]
    masks = np.zeros((4, 128, 512), dtype=np.float32)
    for o in range(4):
        masks[o] = np.where(kp <= qf - o * 128, 0.0, NEG)
    return {
        "c_ident": ident,
        "c_cos_hi": cos_hi, "c_cos_lo": cos_lo,
        "c_sin_hi": sin_hi, "c_sin_lo": sin_lo,
        "c_masks": masks.astype(ml_dtypes.bfloat16),
    }


eT, eV, eA, eP = "eT", "eV", "eA", "eP"
dC, dX, dW, dF, dG, dO, dR = "dC", "dX", "dW", "dF", "dG", "dO", "dR"
ALL_SEMS = [eT, eV, eA, eP, dC, dX, dW, dF, dG, dO, dR]
OWN = {"tensor": eT, "vector": eV, "scalar": eA, "gpsimd": eP}


class Prog:

    def __init__(self):
        self.items = {e: [] for e in ENGS}
        self.cnt = {sm: 0 for sm in ALL_SEMS}
        self.lab = "pre"

    def op(self, eng, fn, incs=()):
        incs = tuple(incs)
        for sm, a in incs:
            self.cnt[sm] += a
        self.items[eng].append(("op", fn, incs, self.lab))
        return tuple((sm, self.cnt[sm]) for sm, _ in incs)

    def wait(self, eng, sem, v=None):
        if v is None:
            v = self.cnt[sem]
        if v > 0:
            self.items[eng].append(("wait", sem, v, self.lab))

    def tok(self, sem):
        return (sem, self.cnt[sem])

    def wait_tok(self, eng, *toks):
        for sem, v in toks:
            if v > 0:
                self.items[eng].append(("wait", sem, v, self.lab))


def build(taps=(), upto="final"):
    taps = set(taps)
    nc = bass.Bass("TRN2", target_bir_lowering=False, detect_race_conditions=False)

    x_d = nc.dram_tensor("x", [NSEQ, S, D], F32, kind="ExternalInput")
    wqkv_d = nc.dram_tensor("w_qkv", [D, 3 * D], F32, kind="ExternalInput")
    wout_d = nc.dram_tensor("w_out", [D, D], F32, kind="ExternalInput")
    qw_d = nc.dram_tensor("q_w", [FH * FP, D], F32, kind="ExternalInput")
    kw_d = nc.dram_tensor("k_w", [FH * FP, D], F32, kind="ExternalInput")
    vw_d = nc.dram_tensor("v_w", [FH * FE, D, FP], F32, kind="ExternalInput")
    ow_d = nc.dram_tensor("o_w", [FH * FE, FP, D], F32, kind="ExternalInput")
    selv_d = nc.dram_tensor("sel_v", [FH * FE, D], F32, kind="ExternalInput")
    selo_d = nc.dram_tensor("sel_o", [FH * FE, D], F32, kind="ExternalInput")
    rs_d = nc.dram_tensor("route_scale", [1], F32, kind="ExternalInput")
    ident_d = nc.dram_tensor("c_ident", [128, 128], F32, kind="ExternalInput")
    cos_hi_d = nc.dram_tensor("c_cos_hi", [128, S], F32, kind="ExternalInput")
    cos_lo_d = nc.dram_tensor("c_cos_lo", [128, S], F32, kind="ExternalInput")
    sin_hi_d = nc.dram_tensor("c_sin_hi", [128, S], F32, kind="ExternalInput")
    sin_lo_d = nc.dram_tensor("c_sin_lo", [128, S], F32, kind="ExternalInput")
    masks_d = nc.dram_tensor("c_masks", [4, 128, 512], BF16, kind="ExternalInput")
    out_d = nc.dram_tensor("out", [NSEQ, S, D], F32, kind="ExternalOutput")
    oscr_d = nc.dram_tensor("oscr", [8, 128, S], F32, kind="Internal")
    wqkvh_d = nc.dram_tensor("wqkvh", [D, 3 * D], BF16, kind="Internal")
    wqkvl_d = nc.dram_tensor("wqkvl", [D, 3 * D], BF16, kind="Internal")
    wouth_d = nc.dram_tensor("wouth", [D, D], BF16, kind="Internal")
    woutl_d = nc.dram_tensor("woutl", [D, D], BF16, kind="Internal")
    vwh_d = nc.dram_tensor("vwh", [FH * FE, D, FP], BF16, kind="Internal")
    vwl_d = nc.dram_tensor("vwl", [FH * FE, D, FP], BF16, kind="Internal")
    owh_d = nc.dram_tensor("owh", [FH * FE, FP, D], BF16, kind="Internal")
    owl_d = nc.dram_tensor("owl", [FH * FE, FP, D], BF16, kind="Internal")

    tap_d = {}
    tap_shapes = {
        "gv": [128, TT * 64], "go": [128, TT * 64],
        "qrot": [128, S], "krot": [128, S],
        "ctxa": [128, TT * 64], "scrA": [8, 128, S],
        "ctxf": [128, TT * 128], "scrF": [8, 128, S],
        "wqh": ([D, 3 * D], BF16), "vhb": ([128, TT * 130], BF16),
        "vlb": ([128, TT * 130], BF16), "cphb": ([128, 2048], BF16),
        "qhb": ([128, S], BF16), "qlb": ([128, S], BF16),
    }
    for t in taps:
        shp = tap_shapes[t]
        dt = F32
        if isinstance(shp, tuple):
            shp, dt = shp
        tap_d[t] = nc.dram_tensor("tap_" + t, shp, dt, kind="ExternalOutput")

    A = nc.alloc_sbuf_tensor
    xn_h = A("xn_h", [128, 8, S], BF16)
    xn_l = A("xn_l", [128, 8, S], BF16)
    cos_hi = A("cos_hi", [128, S], F32)
    sin_hi = A("sin_hi", [128, S], F32)
    gates_v = A("gates_v", [128, TT, 64], F32)
    gates_o = A("gates_o", [128, TT, 64], F32)
    selT_h = A("selT_h", [128, 2, 8, 64], BF16)
    selT_l = A("selT_l", [128, 2, 8, 64], BF16)
    masks_b = A("masks_b", [128, 4, 512], BF16)
    id_sb = A("id_sb", [128, 128], F32)
    iem9 = A("iem9", [128, 64], F32)
    rs_col = A("rs_col", [128, 1], F32)
    eps_col = A("eps_col", [128, 1], F32)
    ones_row = A("ones_row", [1, 128], F32)
    rs_sb = A("rs_sb", [1, 1], F32)
    sc = A("sc", [128, 32], F32)
    big1 = A("big1", [128, S], F32)
    big2 = A("big2", [128, S], F32)
    big3 = A("big3", [128, S], F32)
    big5 = A("big5", [128, 4096], F32)
    big8 = A("big8", [128, S], F32)
    qh = A("qh", [128, S], BF16)
    ql = A("ql", [128, S], BF16)
    kh = A("kh", [128, S], BF16)
    kl = A("kl", [128, S], BF16)
    vh = A("vh", [128, TT, 130], BF16)
    vl = A("vl", [128, TT, 130], BF16)
    wh = A("wh", [128, 4096], BF16)
    wl = A("wl", [128, 4096], BF16)
    whe_h = A("whe_h", [128, 4096], BF16)
    whe_l = A("whe_l", [128, 4096], BF16)
    pTh = A("pTh", [128, 2, 512], BF16)
    pTl = A("pTl", [128, 2, 512], BF16)
    tmpg = A("tmpg", [128, 2, 128], F32)
    sbuf_left = nc.sbuf_bytes_remaining

    gt = big5.ap()[:, 0:640]
    cos_lo = big5.ap()[:, 0:2048]
    sin_lo = big5.ap()[:, 2048:4096]
    vacc = big5.ap()[:, 0:2080].rearrange("p (t u) -> p t u", t=TT)
    ln_x = big5.ap()[:, 0:2048].rearrange("p (b u) -> p b u", b=2)
    ln_s1 = big5.ap()[:, 2048:4096].rearrange("p (b u) -> p b u", b=2)
    fin_in = big5.ap()[:, 0:2048].rearrange("p (b c u) -> p b c u", b=2, c=8)
    fin_x = big5.ap()[:, 2048:4096].rearrange("p (b u) -> p b u", b=2)
    fin_o = big5.ap()[:, 0:2048].rearrange("p (b u) -> p b u", b=2)
    raw = big8.ap()
    sq_dump = big8.ap()[:, 0:1024]
    sel_raw64 = big8.ap()[0:64, 0:1024]
    shuf = big3.ap()
    ctxn_f = big3.ap().rearrange("p (t u) -> p t u", t=TT)
    ctxn = big3.ap()[:, 0:1024].rearrange("p (t u) -> p t u", t=TT)
    evac = big1.ap()[:, 0:1024].rearrange("p (b u) -> p b u", b=2)
    wq_h = wh.ap()[:, 0:3072].rearrange("p (c j u) -> p c j u", c=8, j=3)
    wq_l = wl.ap()[:, 0:3072].rearrange("p (c j u) -> p c j u", c=8, j=3)
    wout_h = wh.ap()[:, 3072:4096].rearrange("p (c u) -> p c u", c=8)
    wout_l = wl.ap()[:, 3072:4096].rearrange("p (c u) -> p c u", c=8)
    qwT_h = wh.ap()[:, 0:1024].rearrange("p (c u) -> p c u", c=8)
    qwT_l = wl.ap()[:, 0:1024].rearrange("p (c u) -> p c u", c=8)
    kwT_h = wh.ap()[:, 1024:2048].rearrange("p (c u) -> p c u", c=8)
    kwT_l = wl.ap()[:, 1024:2048].rearrange("p (c u) -> p c u", c=8)
    vw_h = wh.ap().rearrange("p (b c e u) -> p b c e u", b=2, c=8, e=2)
    vw_l = wl.ap().rearrange("p (b c e u) -> p b c e u", b=2, c=8, e=2)
    ow_h = wh.ap()[:, 0:2048].rearrange("p (b e u) -> p b e u", b=2, e=8)
    ow_l = wl.ap()[:, 0:2048].rearrange("p (b e u) -> p b e u", b=2, e=8)
    wstg_h = wh.ap().rearrange("p (b u) -> p b u", b=2)
    wstg_l = wl.ap().rearrange("p (b u) -> p b u", b=2)
    cph = whe_h.ap()[:, 0:2048]
    cpl = whe_l.ap()[:, 0:2048]
    ctxB_h = whe_h.ap()[0:64, 2048:4096]
    ctxB_l = whe_l.ap()[0:64, 2048:4096]
    w_he_h = whe_h.ap().rearrange("p (e u) -> p e u", e=8)
    w_he_l = whe_l.ap().rearrange("p (e u) -> p e u", e=8)

    ps_a = nc.alloc_psum_tensor("ps_a", [128, 512], F32)
    ps_b = nc.alloc_psum_tensor("ps_b", [128, 512], F32)
    ps_big = nc.alloc_psum_tensor("ps_big", [128, 2, 512], F32)
    ps_ctx = nc.alloc_psum_tensor("ps_ctx", [128, 4, 512], F32)
    ps_bigf = ps_big.ap().rearrange("p a u -> p (a u)")

    p = Prog()

    def T(fn, inc=1):
        return p.op("tensor", fn, ((eT, inc),) if inc else ())

    def V(fn, inc=1):
        return p.op("vector", fn, ((eV, inc),) if inc else ())

    def Ac(fn, inc=1):
        return p.op("scalar", fn, ((eA, inc),) if inc else ())

    def G(fn, inc=1):
        return p.op("gpsimd", fn, ((eP, inc),) if inc else ())

    def dma(queue_eng, sem, out, in_, accum=False):
        def fn(eng, out=out, in_=in_, accum=accum):
            if accum:
                return eng.dma_start(out=out, in_=in_, accum_op=ALU.add)
            return eng.dma_start(out=out, in_=in_)
        return p.op(queue_eng, fn, ((sem, 16),))

    def join():
        for e in ENGS:
            own = OWN.get(e)
            for sm in ALL_SEMS:
                if sm != own:
                    p.wait(e, sm)

    def tap(name, src_ap):
        if name in tap_d:
            dma("sync", dO, tap_d[name].ap(), src_ap)

    def MM3(out, lh, rh, ll, rl, start, stop, inc=0):
        T(lambda t, out=out, lh=lh, rh=rh, start=start: t.matmul(
            out=out, lhsT=lh, rhs=rh, start=start, stop=False), inc=0)
        T(lambda t, out=out, lh=lh, rl=rl: t.matmul(
            out=out, lhsT=lh, rhs=rl, start=False, stop=False), inc=0)
        return T(lambda t, out=out, ll=ll, rh=rh, stop=stop: t.matmul(
            out=out, lhsT=ll, rhs=rh, start=False, stop=stop), inc=inc)

    def castpair(dh, dl, src, inc=1):
        ta = Ac(lambda a, dh=dh, src=src: a.copy(out=dh, in_=src))
        p.wait_tok("vector", *ta)
        return V(lambda v, dh=dh, dl=dl, src=src: v.tensor_tensor(
            out=dl, in0=src, in1=dh, op=ALU.subtract), inc=inc)

    dma("sync", dC, id_sb[:], ident_d[:])
    dma("sync", dC, cos_hi[:], cos_hi_d[:])
    dma("sync", dC, sin_hi[:], sin_hi_d[:])
    dma("sync", dC, masks_b[:], masks_d.ap().rearrange("o p u -> p o u"))
    dma("sync", dC, rs_sb[:], rs_d.ap().rearrange("(a b) -> a b", a=1))

    G(lambda g: g.iota(out=iem9[:], pattern=[[0, 8], [1, 8]], base=-9,
                       channel_multiplier=0, allow_small_or_imprecise_dtypes=True))
    G(lambda g: g.memset(eps_col[:], EPS))
    G(lambda g: g.memset(ones_row[:], 1.0))

    p.wait("tensor", dC)
    p.wait("tensor", eP)
    T(lambda t: t.matmul(out=ps_a[:, 0:1], lhsT=ones_row[:], rhs=rs_sb[:],
                         start=True, stop=True))
    p.wait("vector", eT)
    V(lambda v: v.tensor_copy(out=rs_col[:], in_=ps_a[:, 0:1]))

    for si, sd in enumerate((selv_d, selo_d)):
        p.wait("sync", eT)
        tk = dma("sync", dC, sel_raw64, sd[:])
        p.wait_tok("tensor", *tk)
        p.wait("tensor", eV)
        for c in range(8):
            ps = ps_a if c % 2 == 0 else ps_b
            if si > 0 or c >= 2:
                p.wait("tensor", eV)
                p.wait("tensor", eA)
            T(lambda t, c=c, ps=ps: t.transpose(
                out=ps[0:128, 0:64], in_=sel_raw64[:, c * 128:(c + 1) * 128],
                identity=id_sb[0:64, 0:64]))
            p.wait("scalar", eT)
            castpair(selT_h[:, si, c, :], selT_l[:, si, c, :], ps[0:128, 0:64])
        p.wait("tensor", eV)
        p.wait("tensor", eA)

    def st_wsplit():
        p.lab = "wsplit"
        specs = []
        for (srcd, dh_d, dl_d, rows, cols) in (
                (wqkv_d, wqkvh_d, wqkvl_d, D, 3 * D),
                (wout_d, wouth_d, woutl_d, D, D)):
            sv = srcd.ap().rearrange("(c p) n -> p c n", p=128)
            hv = dh_d.ap().rearrange("(c p) n -> p c n", p=128)
            lv = dl_d.ap().rearrange("(c p) n -> p c n", p=128)
            nb = cols // 1024
            for c in range(rows // 128):
                for b in range(nb):
                    sl = (slice(None), c, slice(b * 1024, (b + 1) * 1024))
                    specs.append((sv[sl], hv[sl], lv[sl], 1024))
        for (srcd, dh_d, dl_d) in ((vw_d, vwh_d, vwl_d), (ow_d, owh_d, owl_d)):
            sv = srcd.ap().rearrange("e a u -> (e a) u").rearrange(
                "(c p) n -> p c n", p=128)
            hv = dh_d.ap().rearrange("e a u -> (e a) u").rearrange(
                "(c p) n -> p c n", p=128)
            lv = dl_d.ap().rearrange("e a u -> (e a) u", ).rearrange(
                "(c p) n -> p c n", p=128)
            C = sv.shape[1]
            N = sv.shape[2]
            grp = max(1, 1024 // N)
            for c0 in range(0, C, grp):
                sl = (slice(None), slice(c0, c0 + grp), slice(None))
                specs.append((sv[sl], hv[sl], lv[sl], grp * N))
        stg = ln_x
        for i, (s_ap, h_ap, l_ap, L) in enumerate(specs):
            par = i % 2
            if i >= 2:
                p.wait("sync", eV)
            tkr = dma("sync", dC, stg[:, par, 0:L], s_ap)
            hv2 = wstg_h[:, par, 0:L]
            lv2 = wstg_l[:, par, 0:L]
            p.wait_tok("scalar", *tkr)
            if i >= 2:
                p.wait("scalar", dG)
            ta = Ac(lambda a, hv2=hv2, st=stg[:, par, 0:L]: a.copy(out=hv2, in_=st))
            p.wait_tok("vector", *ta)
            if i >= 2:
                p.wait("vector", dG)
            tv = V(lambda v, hv2=hv2, lv2=lv2, st=stg[:, par, 0:L]: v.tensor_tensor(
                out=lv2, in0=st, in1=hv2, op=ALU.subtract))
            p.wait_tok("gpsimd", *ta)
            p.wait_tok("gpsimd", *tv)
            dma("gpsimd", dG, h_ap, hv2)
            dma("gpsimd", dG, l_ap, lv2)
        join()
        tap("wqh", wqkvh_d.ap())

    def st_ln(s):
        p.lab = "ln"
        join()
        for tt in range(TT):
            par = tt % 2
            o = par * 8
            p.wait("sync", eV)
            xt = ln_x[:, par, :]
            tkx = dma("sync", dX, xt, x_d[s, tt * 128:(tt + 1) * 128, :])
            p.wait_tok("vector", *tkx)
            p.wait("vector", eT)
            V(lambda v, xt=xt, o=o: v.reduce_sum(out=sc[:, o:o + 1], in_=xt, axis=AX.X))
            p.wait("vector", eV)
            V(lambda v, o=o: v.tensor_scalar(out=sc[:, o + 1:o + 2], in0=sc[:, o:o + 1],
                                             scalar1=1.0 / D, scalar2=None, op0=ALU.mult))
            p.wait("vector", eV)
            dev = ln_s1[:, par, :]
            tkd = V(lambda v, xt=xt, dev=dev, o=o: v.tensor_scalar(
                out=dev, in0=xt, scalar1=sc[:, o + 1:o + 2], scalar2=None,
                op0=ALU.subtract))
            p.wait_tok("scalar", *tkd)
            Ac(lambda a, dev=dev, o=o: a.activation(
                out=sq_dump, in_=dev, func=AF.Square,
                accum_out=sc[:, o + 2:o + 3]))
            p.wait("scalar", eA)
            tka = Ac(lambda a, o=o: a.activation(
                out=sc[:, o + 5:o + 6], in_=sc[:, o + 2:o + 3], func=AF.Sqrt,
                scale=1.0 / D, bias=eps_col[:]))
            p.wait_tok("vector", *tka)
            V(lambda v, o=o: v.reciprocal(out=sc[:, o + 3:o + 4], in_=sc[:, o + 5:o + 6]))
            p.wait("vector", eV)
            V(lambda v, o=o: v.tensor_tensor(out=sc[:, o + 4:o + 5],
                                             in0=sc[:, o + 1:o + 2],
                                             in1=sc[:, o + 3:o + 4], op=ALU.mult))
            p.wait("vector", eV)
            tkn = V(lambda v, xt=xt, dev=dev, o=o: v.tensor_scalar(
                out=dev, in0=xt, scalar1=sc[:, o + 3:o + 4],
                scalar2=sc[:, o + 4:o + 5], op0=ALU.mult, op1=ALU.subtract))
            p.wait_tok("tensor", *tkn)
            for c in range(8):
                ps = ps_a if c % 2 == 0 else ps_b
                if tt > 0 or c >= 2:
                    p.wait("tensor", eV)
                    p.wait("tensor", eA)
                T(lambda t, dev=dev, c=c, ps=ps: t.transpose(
                    out=ps[:, 0:128], in_=dev[:, c * 128:(c + 1) * 128],
                    identity=id_sb[:]))
                p.wait("scalar", eT)
                castpair(xn_h[:, c, tt * 128:(tt + 1) * 128],
                         xn_l[:, c, tt * 128:(tt + 1) * 128], ps[:, 0:128])
        if s == 0 and "xn" in tap_d:
            join()

    def st_gates(s):
        p.lab = "gates"
        join()

        def b8(ap):
            return ap.rearrange("p (g e) -> p g e", e=1).broadcast_to([128, 8, 8])

        IE3 = iem9.ap().rearrange("p (g e) -> p g e", e=8)

        def emit_top2(si, gout, tt):
            Pa = gt[:, si * 64:(si + 1) * 64]
            P3v = Pa.rearrange("p (g e) -> p g e", e=8)
            M1 = gt[:, 192:200]
            I1 = gt[:, 200:208]
            M2 = gt[:, 208:216]
            I2 = gt[:, 216:224]
            SS = gt[:, 224:232]
            R = gt[:, 232:240]
            W1 = gt[:, 240:248]
            W2 = gt[:, 248:256]
            E1 = gt[:, 256:320].rearrange("p (g e) -> p g e", e=8)
            OH1 = gt[:, 320:384].rearrange("p (g e) -> p g e", e=8)
            PM = gt[:, 384:448]
            PM3 = PM.rearrange("p (g e) -> p g e", e=8)
            E2 = gt[:, 448:512].rearrange("p (g e) -> p g e", e=8)
            OH2 = gt[:, 512:576].rearrange("p (g e) -> p g e", e=8)
            CT = gt[:, 576:640].rearrange("p (g e) -> p g e", e=8)
            CTf = gt[:, 576:640]
            steps = [
                lambda v: v.reduce_max(out=M1, in_=P3v, axis=AX.X),
                lambda v: v.tensor_tensor(out=E1, in0=P3v, in1=b8(M1), op=ALU.is_equal),
                lambda v: v.tensor_tensor(out=CT, in0=E1, in1=IE3, op=ALU.mult),
                lambda v: v.tensor_reduce(out=I1, in_=CT, axis=AX.X, op=ALU.min),
                lambda v: v.tensor_tensor(out=OH1, in0=IE3, in1=b8(I1), op=ALU.is_equal),
                lambda v: v.scalar_tensor_tensor(
                    out=PM, in0=OH1.rearrange("p g e -> p (g e)"), scalar=-2.0,
                    in1=Pa, op0=ALU.mult, op1=ALU.add),
                lambda v: v.reduce_max(out=M2, in_=PM3, axis=AX.X),
                lambda v: v.tensor_tensor(out=E2, in0=PM3, in1=b8(M2), op=ALU.is_equal),
                lambda v: v.tensor_tensor(out=CT, in0=E2, in1=IE3, op=ALU.mult),
                lambda v: v.tensor_reduce(out=I2, in_=CT, axis=AX.X, op=ALU.min),
                lambda v: v.tensor_tensor(out=OH2, in0=IE3, in1=b8(I2), op=ALU.is_equal),
                lambda v: v.tensor_tensor(out=SS, in0=M1, in1=M2, op=ALU.add),
                lambda v: v.tensor_scalar(out=SS, in0=SS, scalar1=1e-9,
                                          scalar2=None, op0=ALU.max),
                lambda v: v.reciprocal(out=R, in_=SS),
                lambda v: v.tensor_tensor(out=W1, in0=M1, in1=R, op=ALU.mult),
                lambda v: v.tensor_scalar(out=W1, in0=W1, scalar1=rs_col[:],
                                          scalar2=None, op0=ALU.mult),
                lambda v: v.tensor_tensor(out=W2, in0=M2, in1=R, op=ALU.mult),
                lambda v: v.tensor_scalar(out=W2, in0=W2, scalar1=rs_col[:],
                                          scalar2=None, op0=ALU.mult),
                lambda v: v.tensor_tensor(out=CT, in0=OH1, in1=b8(W1), op=ALU.mult),
                lambda v: v.tensor_tensor(out=PM3, in0=OH2, in1=b8(W2), op=ALU.mult),
                lambda v: v.tensor_tensor(out=gout[:, tt, :], in0=CTf, in1=PM, op=ALU.add),
            ]
            for st_fn in steps:
                p.wait("vector", eV)
                V(st_fn, inc=1)

        for tt in range(TT):
            if tt > 0:
                p.wait("tensor", eA)
            for si in range(2):
                psg = ps_a if si == 0 else ps_b
                for c in range(8):
                    MM3(psg[:, 0:64],
                        xn_h[:, c, tt * 128:(tt + 1) * 128],
                        selT_h[:, si, c, :],
                        xn_l[:, c, tt * 128:(tt + 1) * 128],
                        selT_l[:, si, c, :],
                        start=(c == 0), stop=(c == 7), inc=(1 if c == 7 else 0))
            p.wait("scalar", eT)
            p.wait("scalar", eV)
            Ac(lambda a: a.activation(out=gt[:, 0:64], in_=ps_a[:, 0:64],
                                      func=AF.Sigmoid), inc=0)
            Ac(lambda a: a.activation(out=gt[:, 64:128], in_=ps_b[:, 0:64],
                                      func=AF.Sigmoid))
            p.wait("vector", eA)
            for si, gout in ((0, gates_v), (1, gates_o)):
                emit_top2(si, gout, tt)
        if s == 0 and ("gv" in tap_d or "go" in tap_d):
            join()
            tap("gv", gates_v.ap().rearrange("p t u -> p (t u)"))
            tap("go", gates_o.ap().rearrange("p t u -> p (t u)"))

    def rope_one(dst):
        V(lambda v, dst=dst: v.tensor_tensor(out=dst, in0=raw, in1=cos_hi[:], op=ALU.mult), inc=0)
        V(lambda v: v.tensor_tensor(out=raw, in0=raw, in1=cos_lo, op=ALU.mult), inc=0)
        V(lambda v, dst=dst: v.tensor_tensor(out=dst, in0=dst, in1=raw, op=ALU.add), inc=0)
        V(lambda v: v.tensor_tensor(out=raw, in0=shuf, in1=sin_hi[:], op=ALU.mult), inc=0)
        V(lambda v, dst=dst: v.tensor_tensor(out=dst, in0=dst, in1=raw, op=ALU.add), inc=0)
        V(lambda v: v.tensor_tensor(out=raw, in0=shuf, in1=sin_lo, op=ALU.mult), inc=0)
        V(lambda v, dst=dst: v.tensor_tensor(out=dst, in0=dst, in1=raw, op=ALU.add), inc=1)

    def attn_core(lKh, lKl, rQh, rQl, vAh, vAl, cW, causal, ctx_out_fn):
        pTlTok = [None, None]
        ctxTok = [None, None]
        p.wait("tensor", eA)
        p.wait("tensor", eV)
        p.wait("scalar", eT)
        p.wait("vector", eT)

        def emit_S(qg, kt):
            pr = kt % 2
            psS = ps_big.ap()[:, pr, :]
            if pTlTok[pr] is not None:
                p.wait_tok("tensor", pTlTok[pr])
            return MM3(psS, lKh(kt), rQh(qg), lKl(kt), rQl(qg),
                       start=True, stop=True, inc=1)

        for qg in range(NQG):
            nkt = (4 * qg + 4) if causal else TT
            sToks = {0: emit_S(qg, 0)}
            for kt in range(nkt):
                pr = kt % 2
                psS = ps_big.ap()[:, pr, :]
                if kt + 1 < nkt:
                    sToks[kt + 1] = emit_S(qg, kt + 1)
                tb = sToks.pop(kt)
                lastS = tb
                o = kt - 4 * qg
                if causal and o >= 0:
                    p.wait_tok("vector", *tb)
                    lastS = V(lambda v, psS=psS, o=o: v.tensor_tensor(
                        out=psS, in0=psS, in1=masks_b[:, o, :], op=ALU.add))
                    p.wait_tok("scalar", *lastS)
                else:
                    p.wait_tok("scalar", *lastS)
                Ac(lambda a, psS=psS: a.activation(
                    out=psS, in_=psS, func=AF.Exp), inc=0)
                if ctxTok[pr] is not None:
                    p.wait_tok("scalar", ctxTok[pr])
                ta = Ac(lambda a, psS=psS, pr=pr: a.copy(out=pTh[:, pr, :], in_=psS))
                p.wait_tok("vector", *ta)
                if ctxTok[pr] is not None:
                    p.wait_tok("vector", ctxTok[pr])
                tl = V(lambda v, psS=psS, pr=pr: v.tensor_tensor(
                    out=pTl[:, pr, :], in0=psS, in1=pTh[:, pr, :], op=ALU.subtract))
                pTlTok[pr] = tl[0]
                p.wait_tok("tensor", *ta)
                p.wait_tok("tensor", *tl)
                if kt == 0:
                    p.wait("tensor", eV)
                for qtl in range(4):
                    MM3(ps_ctx[:, qtl, cW[0]:cW[0] + cW[1]],
                        pTh[:, pr, qtl * 128:(qtl + 1) * 128], vAh(kt),
                        pTl[:, pr, qtl * 128:(qtl + 1) * 128], vAl(kt),
                        start=(kt == 0), stop=(kt == nkt - 1),
                        inc=(1 if (qtl == 3 and kt == nkt - 1) else 0))
                ctxTok[pr] = p.tok(eT)
            ctx_out_fn(qg)

    def st_attn(s):
        p.lab = "attn"
        join()
        G(lambda g: g.memset(vh.ap()[:, :, 64:65], 1.0))
        G(lambda g: g.memset(vh.ap()[:, :, 129:130], 1.0))
        G(lambda g: g.memset(vl.ap()[:, :, 64:65], 0.0))
        G(lambda g: g.memset(vl.ap()[:, :, 129:130], 0.0))
        p.wait("tensor", eP)
        dma("sync", dW, cos_lo, cos_lo_d[:])
        dma("sync", dW, sin_lo, sin_lo_d[:])
        for hp in range(8):
            p.wait("sync", eT)
            tkw = None
            for j, c0 in enumerate((hp * 128, D + hp * 128, 2 * D + hp * 128)):
                dma("sync", dW, wq_h[:, :, j, :],
                    wqkvh_d.ap().rearrange("(c p) n -> p c n", p=128)[:, :, c0:c0 + 128])
                tkw = dma("sync", dW, wq_l[:, :, j, :],
                          wqkvl_d.ap().rearrange("(c p) n -> p c n", p=128)[:, :, c0:c0 + 128])
            p.wait_tok("tensor", *tkw)
            p.lab = "attn.proj"
            for j in (0, 1):
                scale = SCALE_A if j == 0 else 1.0
                p.wait("scalar", eV)
                p.wait("scalar", eT)
                for nt in range(NQG):
                    ps = ps_a if nt % 2 == 0 else ps_b
                    p.wait("tensor", eA)
                    for c in range(8):
                        MM3(ps[:, :], wq_h[:, c, j, :],
                            xn_h[:, c, nt * 512:(nt + 1) * 512],
                            wq_l[:, c, j, :],
                            xn_l[:, c, nt * 512:(nt + 1) * 512],
                            start=(c == 0), stop=(c == 7), inc=(1 if c == 7 else 0))
                    p.wait("scalar", eT)
                    Ac(lambda a, nt=nt, scale=scale, ps=ps: a.mul(
                        out=raw[:, nt * 512:(nt + 1) * 512], in_=ps[:, :], mul=scale))
                p.wait("gpsimd", eA)
                p.wait("gpsimd", eV)
                p.wait("gpsimd", eT)
                for (d0, s0) in ((0, 32), (32, 0), (64, 96), (96, 64)):
                    dma("gpsimd", dG, big3.ap()[d0:d0 + 32, :], raw[s0:s0 + 32, :])
                p.wait("vector", dG)
                p.wait("vector", dW)
                p.wait("vector", eA)
                p.wait("vector", eT)
                p.wait("vector", eP)
                dstb = big1 if j == 0 else big2
                rope_one(dstb.ap())
                p.wait("scalar", eV)
                p.wait("scalar", eT)
                if j == 0:
                    castpair(qh[:], ql[:], dstb.ap())
                else:
                    castpair(kh[:], kl[:], dstb.ap())
            p.lab = "attn.vproj"
            for tt in range(TT):
                ps = ps_a if tt % 2 == 0 else ps_b
                p.wait("tensor", eV)
                p.wait("tensor", eA)
                for c in range(8):
                    MM3(ps[:, 0:128], xn_h[:, c, tt * 128:(tt + 1) * 128],
                        wq_h[:, c, 2, :],
                        xn_l[:, c, tt * 128:(tt + 1) * 128],
                        wq_l[:, c, 2, :],
                        start=(c == 0), stop=(c == 7), inc=(1 if c == 7 else 0))
                p.wait("scalar", eT)
                vh2 = vh.ap()[:, tt, :].rearrange("p (b u) -> p b u", b=2)[:, :, 0:64]
                vl2 = vl.ap()[:, tt, :].rearrange("p (b u) -> p b u", b=2)[:, :, 0:64]
                ps2 = ps[:, 0:128].rearrange("p (b u) -> p b u", b=2)
                castpair(vh2, vl2, ps2)
            if s == 0 and hp == 0:
                join()
                tap("qrot", big1.ap())
                tap("krot", big2.ap())
                tap("qhb", qh.ap())
                tap("qlb", ql.ap())
                tap("vhb", vh.ap().rearrange("p t u -> p (t u)"))
                tap("vlb", vl.ap().rearrange("p t u -> p (t u)"))
            p.lab = "attn.core"
            for i in range(2):
                R0 = slice(64 * i, 64 * i + 64)
                cA = 65 * i

                def ctx_out(qg, cA=cA, i=i):
                    p.wait("vector", eT)
                    for qtl in range(4):
                        V(lambda v, qtl=qtl, cA=cA: v.reciprocal(
                            out=sc[:, 16 + qtl:17 + qtl],
                            in_=ps_ctx[:, qtl, cA + 64:cA + 65]),
                          inc=(1 if qtl == 3 else 0))
                    p.wait("vector", eV)
                    for qtl in range(4):
                        qt = qg * 4 + qtl
                        V(lambda v, qtl=qtl, qt=qt, cA=cA: v.tensor_scalar(
                            out=ctxn[:, qt, :], in0=ps_ctx[:, qtl, cA:cA + 64],
                            scalar1=sc[:, 16 + qtl:17 + qtl], scalar2=None,
                            op0=ALU.mult), inc=(1 if qtl == 3 else 0))

                attn_core(
                    lKh=lambda kt, R0=R0: kh.ap()[R0, kt * 128:(kt + 1) * 128],
                    lKl=lambda kt, R0=R0: kl.ap()[R0, kt * 128:(kt + 1) * 128],
                    rQh=lambda qg, R0=R0: qh.ap()[R0, qg * 512:(qg + 1) * 512],
                    rQl=lambda qg, R0=R0: ql.ap()[R0, qg * 512:(qg + 1) * 512],
                    vAh=lambda kt, cA=cA: vh.ap()[:, kt, cA:cA + 65],
                    vAl=lambda kt, cA=cA: vl.ap()[:, kt, cA:cA + 65],
                    cW=(cA, 65), causal=True, ctx_out_fn=ctx_out)
                p.wait("tensor", eV)
                for qt in range(TT):
                    ps = ps_a if qt % 2 == 0 else ps_b
                    p.wait("tensor", eV)
                    p.wait("tensor", eA)
                    T(lambda t, qt=qt, ps=ps: t.transpose(
                        out=ps[0:64, 0:128], in_=ctxn[:, qt, :], identity=id_sb[:]))
                    p.wait("scalar", eT)
                    if i == 0:
                        castpair(cph[0:64, qt * 128:(qt + 1) * 128],
                                 cpl[0:64, qt * 128:(qt + 1) * 128], ps[0:64, 0:128])
                    else:
                        castpair(ctxB_h[:, qt * 128:(qt + 1) * 128],
                                 ctxB_l[:, qt * 128:(qt + 1) * 128], ps[0:64, 0:128])
                if i == 1:
                    p.wait("gpsimd", eV)
                    dma("gpsimd", dG, cph[64:128, :], ctxB_h[:, :])
                    dma("gpsimd", dG, cpl[64:128, :], ctxB_l[:, :])
            if s == 0 and hp == 0:
                join()
                tap("ctxa", ctxn.rearrange("p t u -> p (t u)"))
                tap("cphb", cph)
            p.lab = "attn.wout"
            p.wait("sync", eT)
            dma("sync", dW, wout_h,
                wouth_d[hp * 128:(hp + 1) * 128, :].rearrange("p (c u) -> p c u", c=8))
            tkwo = dma("sync", dW, wout_l,
                       woutl_d[hp * 128:(hp + 1) * 128, :].rearrange("p (c u) -> p c u", c=8))
            p.wait_tok("tensor", *tkwo)
            p.wait("tensor", dG)
            p.wait("tensor", eV)
            woTok = [None, None]
            for dc in range(8):
                for jj in range(4):
                    pr2 = (dc * 4 + jj) % 2
                    p.wait("tensor", eA)
                    MM3(ps_big[:, jj % 2, :], wout_h[:, dc, :],
                        cph[:, jj * 512:(jj + 1) * 512],
                        wout_l[:, dc, :],
                        cpl[:, jj * 512:(jj + 1) * 512],
                        start=True, stop=True, inc=1)
                    p.wait("scalar", eT)
                    if woTok[pr2] is not None:
                        p.wait_tok("scalar", woTok[pr2])
                    tae = Ac(lambda a, pr2=pr2, jj=jj: a.copy(
                        out=evac[:, pr2, :], in_=ps_big[:, jj % 2, :]))
                    p.wait_tok("gpsimd", *tae)
                    tkg = dma("gpsimd", dG, oscr_d[dc, :, jj * 512:(jj + 1) * 512],
                              evac[:, pr2, :], accum=(hp != 0))
                    woTok[pr2] = tkg[0]
        if s == 0 and "scrA" in tap_d:
            join()
            tap("scrA", oscr_d.ap())

    def st_ffn(s):
        p.lab = "ffn"
        join()
        G(lambda g: g.memset(vacc[:, :, 128:129], 1.0))
        p.wait("scalar", eP)
        p.wait("vector", eP)
        owTok = [None, None]
        for h in range(8):
            p.lab = "ffn.qkT"
            for wi, wd in enumerate((qw_d, kw_d)):
                p.wait("sync", eT)
                p.wait("sync", dG)
                tk = dma("sync", dF, raw[:, 0:1024], wd[h * 128:(h + 1) * 128, :])
                p.wait_tok("tensor", *tk)
                p.wait("tensor", eA)
                for c in range(8):
                    ps = ps_a if c % 2 == 0 else ps_b
                    p.wait("tensor", eV)
                    p.wait("tensor", eA)
                    T(lambda t, c=c, ps=ps: t.transpose(
                        out=ps[:, 0:128], in_=raw[:, c * 128:(c + 1) * 128],
                        identity=id_sb[:]))
                    p.wait("scalar", eT)
                    hv = (qwT_h if wi == 0 else kwT_h)[:, c, :]
                    lv = (qwT_l if wi == 0 else kwT_l)[:, c, :]
                    castpair(hv, lv, ps[:, 0:128])
            p.lab = "ffn.proj"
            for wi in (0, 1):
                scale = SCALE_F if wi == 0 else 1.0
                for nt in range(NQG):
                    ps = ps_a if nt % 2 == 0 else ps_b
                    p.wait("tensor", eA)
                    p.wait("tensor", eV)
                    for c in range(8):
                        MM3(ps[:, :], (qwT_h if wi == 0 else kwT_h)[:, c, :],
                            xn_h[:, c, nt * 512:(nt + 1) * 512],
                            (qwT_l if wi == 0 else kwT_l)[:, c, :],
                            xn_l[:, c, nt * 512:(nt + 1) * 512],
                            start=(c == 0), stop=(c == 7), inc=(1 if c == 7 else 0))
                    p.wait("scalar", eT)
                    p.wait("scalar", eV)
                    hv = (qh if wi == 0 else kh).ap()[:, nt * 512:(nt + 1) * 512]
                    lv = (ql if wi == 0 else kl).ap()[:, nt * 512:(nt + 1) * 512]
                    ta = Ac(lambda a, hv=hv, ps=ps, scale=scale: a.mul(
                        out=hv, in_=ps[:, :], mul=scale))
                    p.wait_tok("vector", *ta)
                    V(lambda v, hv=hv, lv=lv, ps=ps, scale=scale: v.scalar_tensor_tensor(
                        out=lv, in0=ps[:, :], scalar=scale, in1=hv,
                        op0=ALU.mult, op1=ALU.subtract))
            p.lab = "ffn.xv"
            for eg in range(4):
                par = eg % 2
                p.wait("sync", eT)
                for e2 in range(2):
                    dma("sync", dF, vw_h[:, par, :, e2, :],
                        vwh_d[h * 8 + eg * 2 + e2, :, :].rearrange("(c p) u -> p c u", p=128))
                    tk = dma("sync", dF, vw_l[:, par, :, e2, :],
                             vwl_d[h * 8 + eg * 2 + e2, :, :].rearrange("(c p) u -> p c u", p=128))
                p.wait_tok("tensor", *tk)
                for tt in range(TT):
                    ps = ps_a if tt % 2 == 0 else ps_b
                    p.wait("tensor", eV)
                    for c in range(8):
                        MM3(ps[:, 0:256], xn_h[:, c, tt * 128:(tt + 1) * 128],
                            vw_h[:, par, c, :, :],
                            xn_l[:, c, tt * 128:(tt + 1) * 128],
                            vw_l[:, par, c, :, :],
                            start=(c == 0), stop=(c == 7), inc=(1 if c == 7 else 0))
                    p.wait("vector", eT)
                    for e2 in range(2):
                        he = h * 8 + eg * 2 + e2
                        if e2 == 1:
                            p.wait("vector", eV)
                        if eg == 0 and e2 == 0:
                            V(lambda v, tt=tt, e2=e2, he=he, ps=ps: v.tensor_scalar(
                                out=vacc[:, tt, 0:128],
                                in0=ps[:, e2 * 128:(e2 + 1) * 128],
                                scalar1=gates_v[:, tt, he:he + 1], scalar2=None,
                                op0=ALU.mult), inc=1)
                        else:
                            V(lambda v, tt=tt, e2=e2, he=he, ps=ps: v.scalar_tensor_tensor(
                                out=vacc[:, tt, 0:128],
                                in0=ps[:, e2 * 128:(e2 + 1) * 128],
                                scalar=gates_v[:, tt, he:he + 1],
                                in1=vacc[:, tt, 0:128],
                                op0=ALU.mult, op1=ALU.add), inc=1)
            p.wait("scalar", eV)
            p.wait("scalar", eT)
            castpair(vh.ap()[:, :, 0:129], vl.ap()[:, :, 0:129], vacc[:, :, 0:129])

            p.lab = "ffn.core"
            def ctx_out_f(qg):
                p.wait("vector", eT)
                for qtl in range(4):
                    V(lambda v, qtl=qtl: v.reciprocal(
                        out=sc[:, 24 + qtl:25 + qtl],
                        in_=ps_ctx[:, qtl, 128:129]),
                      inc=(1 if qtl == 3 else 0))
                p.wait("vector", eV)
                for qtl in range(4):
                    qt = qg * 4 + qtl
                    V(lambda v, qtl=qtl, qt=qt: v.tensor_scalar(
                        out=ctxn_f[:, qt, :], in0=ps_ctx[:, qtl, 0:128],
                        scalar1=sc[:, 24 + qtl:25 + qtl], scalar2=None,
                        op0=ALU.mult), inc=(1 if qtl == 3 else 0))

            attn_core(
                lKh=lambda kt: kh.ap()[:, kt * 128:(kt + 1) * 128],
                lKl=lambda kt: kl.ap()[:, kt * 128:(kt + 1) * 128],
                rQh=lambda qg: qh.ap()[:, qg * 512:(qg + 1) * 512],
                rQl=lambda qg: ql.ap()[:, qg * 512:(qg + 1) * 512],
                vAh=lambda kt: vh.ap()[:, kt, 0:129],
                vAl=lambda kt: vl.ap()[:, kt, 0:129],
                cW=(0, 129), causal=False, ctx_out_fn=ctx_out_f)
            if s == 0 and h == 0 and "ctxf" in tap_d:
                join()
                tap("ctxf", ctxn_f.rearrange("p t u -> p (t u)"))

            p.lab = "ffn.ow"
            for tc in range(4):
                for e in range(8):
                    he = h * 8 + e
                    for qtl in range(4):
                        qt = tc * 4 + qtl
                        pg = (e * 4 + qtl) % 2
                        p.wait("vector", eT)
                        V(lambda v, qt=qt, he=he, pg=pg: v.tensor_scalar(
                            out=tmpg[:, pg, :], in0=ctxn_f[:, qt, :],
                            scalar1=gates_o[:, qt, he:he + 1], scalar2=None,
                            op0=ALU.mult))
                        ps = ps_a if pg == 0 else ps_b
                        p.wait("tensor", eV)
                        p.wait("tensor", eA)
                        T(lambda t, pg=pg, ps=ps: t.transpose(
                            out=ps[:, 0:128], in_=tmpg[:, pg, :], identity=id_sb[:]))
                        p.wait("scalar", eT)
                        castpair(w_he_h[:, e, qtl * 128:(qtl + 1) * 128],
                                 w_he_l[:, e, qtl * 128:(qtl + 1) * 128], ps[:, 0:128])
                for dc in range(8):
                    par = dc % 2
                    p.wait("sync", eT)
                    dma("sync", dF, ow_h[:, par, :, :],
                        owh_d[h * 8:(h + 1) * 8, :, dc * 128:(dc + 1) * 128]
                        .rearrange("e p u -> p e u"))
                    tk = dma("sync", dF, ow_l[:, par, :, :],
                             owl_d[h * 8:(h + 1) * 8, :, dc * 128:(dc + 1) * 128]
                             .rearrange("e p u -> p e u"))
                    p.wait_tok("tensor", *tk)
                    p.wait("tensor", eV)
                    ps = ps_a if par == 0 else ps_b
                    p.wait("tensor", eA)
                    for e in range(8):
                        MM3(ps[:, :], ow_h[:, par, e, :], w_he_h[:, e, :],
                            ow_l[:, par, e, :], w_he_l[:, e, :],
                            start=(e == 0), stop=(e == 7), inc=(1 if e == 7 else 0))
                    p.wait("scalar", eT)
                    pr2 = (tc * 8 + dc) % 2
                    if owTok[pr2] is not None:
                        p.wait_tok("scalar", owTok[pr2])
                    tae = Ac(lambda a, pr2=pr2, ps=ps: a.copy(
                        out=evac[:, pr2, :], in_=ps[:, :]))
                    p.wait_tok("gpsimd", *tae)
                    tkg = dma("gpsimd", dG, oscr_d[dc, :, tc * 512:(tc + 1) * 512],
                              evac[:, pr2, :], accum=True)
                    owTok[pr2] = tkg[0]
        if s == 0 and "scrF" in tap_d:
            join()
            tap("scrF", oscr_d.ap())

    def st_final(s):
        p.lab = "final"
        join()
        for tt in range(TT):
            par = tt % 2
            if tt >= 2:
                p.wait("sync", eT)
                p.wait("sync", eV)
            tk1 = dma("sync", dR, fin_in[:, par, :, :],
                      oscr_d[:, :, tt * 128:(tt + 1) * 128].rearrange("c p u -> p c u"))
            tk2 = dma("sync", dX, fin_x[:, par, :], x_d[s, tt * 128:(tt + 1) * 128, :])
            p.wait_tok("tensor", *tk1)
            if tt >= 1:
                p.wait("tensor", eV)
            for c in range(8):
                T(lambda t, par=par, c=c: t.transpose(
                    out=ps_bigf[:, c * 128:(c + 1) * 128],
                    in_=fin_in[:, par, c, :], identity=id_sb[:]),
                  inc=(1 if c == 7 else 0))
            p.wait_tok("vector", *tk2)
            p.wait("vector", eT)
            if tt >= 2:
                p.wait("vector", dO)
            tv = V(lambda v, par=par: v.tensor_tensor(
                out=fin_o[:, par, :], in0=ps_bigf, in1=fin_x[:, par, :], op=ALU.add))
            p.wait_tok("sync", *tv)
            dma("sync", dO, out_d[s, tt * 128:(tt + 1) * 128, :], fin_o[:, par, :])

    stages = ["ln", "gates", "attn", "ffn", "final"]
    upto_i = stages.index(upto)
    nseq_eff = NSEQ if upto_i >= 4 else 1
    st_wsplit()
    for s in range(nseq_eff):
        st_ln(s)
        if upto_i >= 1:
            st_gates(s)
        if upto_i >= 2:
            st_attn(s)
        if upto_i >= 3:
            st_ffn(s)
        if upto_i >= 4:
            st_final(s)

    p.wait("sync", dO)
    p.wait("sync", dG)
    p.wait("gpsimd", dO)

    with contextlib.ExitStack() as es:
        semh = {name: es.enter_context(nc.semaphore(name)) for name in p.cnt}
        block = es.enter_context(nc.Block())

        def runner(items):
            def body(eng):
                cur = [None]
                stack = [None]

                def setlab(lab):
                    if lab != stack[0]:
                        if stack[0] is not None:
                            cm = stack.pop()
                            cm.__exit__(None, None, None)
                            stack.append(None)
                        if lab is not None:
                            cm = nc.named_scope(lab)
                            cm.__enter__()
                            stack[-1] = None
                            stack.append(cm)
                        stack[0] = lab

                open_cm = [None, None]
                def swl(lab):
                    if lab != open_cm[0]:
                        if open_cm[1] is not None:
                            open_cm[1].__exit__(None, None, None)
                            open_cm[1] = None
                        if lab:
                            open_cm[1] = nc.named_scope(lab)
                            open_cm[1].__enter__()
                        open_cm[0] = lab
                for kind, a, b, lab in items:
                    swl(lab)
                    if kind == "wait":
                        eng.wait_ge(semh[a], b)
                    else:
                        ins = a(eng)
                        for sm, amt in b:
                            ins = ins.then_inc(semh[sm], amt)
                swl(None)
            return body

        for ename in ENGS:
            getattr(block, ename)(runner(p.items[ename]))
    return nc, tap_d, sbuf_left


def kernel(L0_attn_x, L0_attn_w_qkv, L0_attn_w_out, L0_ffn_q_w, L0_ffn_k_w,
           L0_ffn_v_w, L0_ffn_o_w, L0_ffn_sel_v_w, L0_ffn_sel_o_w,
           L0_ffn_route_scale, _taps=(), _upto="final"):
    consts = host_consts()
    nc, tap_d, _ = build(taps=_taps, upto=_upto)
    in_maps = []
    for c in range(NCORES):
        m = {
            "x": np.ascontiguousarray(np.asarray(L0_attn_x)[c * NSEQ:(c + 1) * NSEQ]),
            "w_qkv": np.asarray(L0_attn_w_qkv), "w_out": np.asarray(L0_attn_w_out),
            "q_w": np.asarray(L0_ffn_q_w), "k_w": np.asarray(L0_ffn_k_w),
            "v_w": np.asarray(L0_ffn_v_w), "o_w": np.asarray(L0_ffn_o_w),
            "sel_v": np.asarray(L0_ffn_sel_v_w), "sel_o": np.asarray(L0_ffn_sel_o_w),
            "route_scale": np.asarray(L0_ffn_route_scale),
        }
        m.update(consts)
        in_maps.append(m)
    res = run_bass_kernel_spmd(nc, in_maps, core_ids=list(range(NCORES)))
    out = np.concatenate([r["out"] for r in res.results], axis=0)
    if _taps:
        return out, {t: res.results[0]["tap_" + t] for t in _taps}
    return out


# version 04
# speedup vs baseline: 1.2047x; 1.0010x over previous
import math
import contextlib
import numpy as np
import ml_dtypes

import concourse.bass as bass
import concourse.mybir as mybir
from concourse.bass_utils import run_bass_kernel_spmd

F32 = mybir.dt.float32
BF16 = mybir.dt.bfloat16
AF = mybir.ActivationFunctionType
ALU = mybir.AluOpType
AX = mybir.AxisListType

B, S, D = 16, 2048, 1024
NCORES = 8
NSEQ = B // NCORES
DH = 64
FH, FE, FP = 8, 8, 128
SCALE_A = 1.0 / math.sqrt(DH)
SCALE_F = 1.0 / math.sqrt(FP)
EPS = 1e-5
NEG = -1.0e30
TT = S // 128
NQG = S // 512
ENGS = ("sync", "vector", "scalar", "tensor", "gpsimd")


def host_consts():
    ident = np.eye(128, dtype=np.float32)
    j = np.arange(32, dtype=np.float64)
    invf = 1.0 / (10000.0 ** (j / 32.0))
    t = np.arange(S, dtype=np.float64)
    ang = invf[:, None] * t[None, :]
    cos_rep = np.tile(np.cos(ang), (4, 1))
    sin_rep = np.tile(np.sin(ang), (4, 1))
    sgn = np.ones((128, 1))
    sgn[0:32] = -1.0
    sgn[64:96] = -1.0
    sin_rep = sin_rep * sgn
    cos_hi = cos_rep.astype(np.float32)
    cos_lo = (cos_rep - cos_hi.astype(np.float64)).astype(np.float32)
    sin_hi = sin_rep.astype(np.float32)
    sin_lo = (sin_rep - sin_hi.astype(np.float64)).astype(np.float32)
    kp = np.arange(128)[:, None]
    qf = np.arange(512)[None, :]
    masks = np.zeros((4, 128, 512), dtype=np.float32)
    for o in range(4):
        masks[o] = np.where(kp <= qf - o * 128, 0.0, NEG)
    return {
        "c_ident": ident,
        "c_cos_hi": cos_hi, "c_cos_lo": cos_lo,
        "c_sin_hi": sin_hi, "c_sin_lo": sin_lo,
        "c_masks": masks.astype(ml_dtypes.bfloat16),
    }


eT, eV, eA, eP = "eT", "eV", "eA", "eP"
dC, dX, dW, dF, dG, dO, dR = "dC", "dX", "dW", "dF", "dG", "dO", "dR"
ALL_SEMS = [eT, eV, eA, eP, dC, dX, dW, dF, dG, dO, dR]
OWN = {"tensor": eT, "vector": eV, "scalar": eA, "gpsimd": eP}


class Prog:

    def __init__(self):
        self.items = {e: [] for e in ENGS}
        self.cnt = {sm: 0 for sm in ALL_SEMS}
        self.lab = "pre"

    def op(self, eng, fn, incs=()):
        incs = tuple(incs)
        for sm, a in incs:
            self.cnt[sm] += a
        self.items[eng].append(("op", fn, incs, self.lab))
        return tuple((sm, self.cnt[sm]) for sm, _ in incs)

    def wait(self, eng, sem, v=None):
        if v is None:
            v = self.cnt[sem]
        if v > 0:
            self.items[eng].append(("wait", sem, v, self.lab))

    def tok(self, sem):
        return (sem, self.cnt[sem])

    def wait_tok(self, eng, *toks):
        for sem, v in toks:
            if v > 0:
                self.items[eng].append(("wait", sem, v, self.lab))


def build(taps=(), upto="final"):
    taps = set(taps)
    nc = bass.Bass("TRN2", target_bir_lowering=False, detect_race_conditions=False)

    x_d = nc.dram_tensor("x", [NSEQ, S, D], F32, kind="ExternalInput")
    wqkv_d = nc.dram_tensor("w_qkv", [D, 3 * D], F32, kind="ExternalInput")
    wout_d = nc.dram_tensor("w_out", [D, D], F32, kind="ExternalInput")
    qw_d = nc.dram_tensor("q_w", [FH * FP, D], F32, kind="ExternalInput")
    kw_d = nc.dram_tensor("k_w", [FH * FP, D], F32, kind="ExternalInput")
    vw_d = nc.dram_tensor("v_w", [FH * FE, D, FP], F32, kind="ExternalInput")
    ow_d = nc.dram_tensor("o_w", [FH * FE, FP, D], F32, kind="ExternalInput")
    selv_d = nc.dram_tensor("sel_v", [FH * FE, D], F32, kind="ExternalInput")
    selo_d = nc.dram_tensor("sel_o", [FH * FE, D], F32, kind="ExternalInput")
    rs_d = nc.dram_tensor("route_scale", [1], F32, kind="ExternalInput")
    ident_d = nc.dram_tensor("c_ident", [128, 128], F32, kind="ExternalInput")
    cos_hi_d = nc.dram_tensor("c_cos_hi", [128, S], F32, kind="ExternalInput")
    cos_lo_d = nc.dram_tensor("c_cos_lo", [128, S], F32, kind="ExternalInput")
    sin_hi_d = nc.dram_tensor("c_sin_hi", [128, S], F32, kind="ExternalInput")
    sin_lo_d = nc.dram_tensor("c_sin_lo", [128, S], F32, kind="ExternalInput")
    masks_d = nc.dram_tensor("c_masks", [4, 128, 512], BF16, kind="ExternalInput")
    out_d = nc.dram_tensor("out", [NSEQ, S, D], F32, kind="ExternalOutput")
    oscr_d = nc.dram_tensor("oscr", [8, 128, S], F32, kind="Internal")
    wqkvh_d = nc.dram_tensor("wqkvh", [D, 3 * D], BF16, kind="Internal")
    wqkvl_d = nc.dram_tensor("wqkvl", [D, 3 * D], BF16, kind="Internal")
    wouth_d = nc.dram_tensor("wouth", [D, D], BF16, kind="Internal")
    woutl_d = nc.dram_tensor("woutl", [D, D], BF16, kind="Internal")
    vwh_d = nc.dram_tensor("vwh", [FH * FE, D, FP], BF16, kind="Internal")
    vwl_d = nc.dram_tensor("vwl", [FH * FE, D, FP], BF16, kind="Internal")
    owh_d = nc.dram_tensor("owh", [FH * FE, FP, D], BF16, kind="Internal")
    owl_d = nc.dram_tensor("owl", [FH * FE, FP, D], BF16, kind="Internal")

    tap_d = {}
    tap_shapes = {
        "gv": [128, TT * 64], "go": [128, TT * 64],
        "qrot": [128, S], "krot": [128, S],
        "ctxa": [128, TT * 64], "scrA": [8, 128, S],
        "ctxf": [128, TT * 128], "scrF": [8, 128, S],
        "wqh": ([D, 3 * D], BF16), "vhb": ([128, TT * 130], BF16),
        "vlb": ([128, TT * 130], BF16), "cphb": ([128, 2048], BF16),
        "qhb": ([128, S], BF16), "qlb": ([128, S], BF16),
    }
    for t in taps:
        shp = tap_shapes[t]
        dt = F32
        if isinstance(shp, tuple):
            shp, dt = shp
        tap_d[t] = nc.dram_tensor("tap_" + t, shp, dt, kind="ExternalOutput")

    A = nc.alloc_sbuf_tensor
    xn_h = A("xn_h", [128, 8, S], BF16)
    xn_l = A("xn_l", [128, 8, S], BF16)
    cos_hi = A("cos_hi", [128, S], F32)
    sin_hi = A("sin_hi", [128, S], F32)
    gates_v = A("gates_v", [128, TT, 64], F32)
    gates_o = A("gates_o", [128, TT, 64], F32)
    selT_h = A("selT_h", [128, 2, 8, 64], BF16)
    selT_l = A("selT_l", [128, 2, 8, 64], BF16)
    masks_b = A("masks_b", [128, 4, 512], BF16)
    id_sb = A("id_sb", [128, 128], F32)
    iem9 = A("iem9", [128, 64], F32)
    rs_col = A("rs_col", [128, 1], F32)
    eps_col = A("eps_col", [128, 1], F32)
    ones_row = A("ones_row", [1, 128], F32)
    rs_sb = A("rs_sb", [1, 1], F32)
    sc = A("sc", [128, 32], F32)
    big1 = A("big1", [128, S], F32)
    big2 = A("big2", [128, S], F32)
    big3 = A("big3", [128, S], F32)
    big5 = A("big5", [128, 4096], F32)
    big8 = A("big8", [128, S], F32)
    qh = A("qh", [128, S], BF16)
    ql = A("ql", [128, S], BF16)
    kh = A("kh", [128, S], BF16)
    kl = A("kl", [128, S], BF16)
    vh = A("vh", [128, TT, 130], BF16)
    vl = A("vl", [128, TT, 130], BF16)
    wh = A("wh", [128, 4096], BF16)
    wl = A("wl", [128, 4096], BF16)
    whe_h = A("whe_h", [128, 4096], BF16)
    whe_l = A("whe_l", [128, 4096], BF16)
    pTh = A("pTh", [128, 2, 512], BF16)
    pTl = A("pTl", [128, 2, 512], BF16)
    tmpg = A("tmpg", [128, 2, 128], F32)
    sbuf_left = nc.sbuf_bytes_remaining

    gt = big5.ap()[:, 0:640]
    cos_lo = big5.ap()[:, 0:2048]
    sin_lo = big5.ap()[:, 2048:4096]
    vacc = big5.ap()[:, 0:2080].rearrange("p (t u) -> p t u", t=TT)
    ln_x = big5.ap()[:, 0:2048].rearrange("p (b u) -> p b u", b=2)
    ln_s1 = big5.ap()[:, 2048:4096].rearrange("p (b u) -> p b u", b=2)
    fin_in = big5.ap()[:, 0:2048].rearrange("p (b c u) -> p b c u", b=2, c=8)
    fin_x = big5.ap()[:, 2048:4096].rearrange("p (b u) -> p b u", b=2)
    fin_o = big5.ap()[:, 0:2048].rearrange("p (b u) -> p b u", b=2)
    raw = big8.ap()
    sq_dump = big8.ap()[:, 0:1024]
    sel_raw64 = big8.ap()[0:64, 0:1024]
    shuf = big3.ap()
    ctxn_f = big3.ap().rearrange("p (t u) -> p t u", t=TT)
    ctxn = big3.ap()[:, 0:1024].rearrange("p (t u) -> p t u", t=TT)
    evac = big1.ap()[:, 0:1024].rearrange("p (b u) -> p b u", b=2)
    wq_h = wh.ap()[:, 0:3072].rearrange("p (c j u) -> p c j u", c=8, j=3)
    wq_l = wl.ap()[:, 0:3072].rearrange("p (c j u) -> p c j u", c=8, j=3)
    wout_h = wh.ap()[:, 3072:4096].rearrange("p (c u) -> p c u", c=8)
    wout_l = wl.ap()[:, 3072:4096].rearrange("p (c u) -> p c u", c=8)
    qwT_h = wh.ap()[:, 0:1024].rearrange("p (c u) -> p c u", c=8)
    qwT_l = wl.ap()[:, 0:1024].rearrange("p (c u) -> p c u", c=8)
    kwT_h = wh.ap()[:, 1024:2048].rearrange("p (c u) -> p c u", c=8)
    kwT_l = wl.ap()[:, 1024:2048].rearrange("p (c u) -> p c u", c=8)
    vw_h = wh.ap().rearrange("p (c e u) -> p c e u", c=8, e=4)
    vw_l = wl.ap().rearrange("p (c e u) -> p c e u", c=8, e=4)
    ow_h = wh.ap()[:, 0:2048].rearrange("p (b e u) -> p b e u", b=2, e=8)
    ow_l = wl.ap()[:, 0:2048].rearrange("p (b e u) -> p b e u", b=2, e=8)
    wstg_h = wh.ap().rearrange("p (b u) -> p b u", b=2)
    wstg_l = wl.ap().rearrange("p (b u) -> p b u", b=2)
    cph = whe_h.ap()[:, 0:2048]
    cpl = whe_l.ap()[:, 0:2048]
    ctxB_h = whe_h.ap()[0:64, 2048:4096]
    ctxB_l = whe_l.ap()[0:64, 2048:4096]
    w_he_h = whe_h.ap().rearrange("p (e u) -> p e u", e=8)
    w_he_l = whe_l.ap().rearrange("p (e u) -> p e u", e=8)

    ps_a = nc.alloc_psum_tensor("ps_a", [128, 512], F32)
    ps_b = nc.alloc_psum_tensor("ps_b", [128, 512], F32)
    ps_big = nc.alloc_psum_tensor("ps_big", [128, 2, 512], F32)
    ps_ctx = nc.alloc_psum_tensor("ps_ctx", [128, 4, 512], F32)
    ps_bigf = ps_big.ap().rearrange("p a u -> p (a u)")

    p = Prog()

    def T(fn, inc=1):
        return p.op("tensor", fn, ((eT, inc),) if inc else ())

    def V(fn, inc=1):
        return p.op("vector", fn, ((eV, inc),) if inc else ())

    def Ac(fn, inc=1):
        return p.op("scalar", fn, ((eA, inc),) if inc else ())

    def G(fn, inc=1):
        return p.op("gpsimd", fn, ((eP, inc),) if inc else ())

    def dma(queue_eng, sem, out, in_, accum=False):
        def fn(eng, out=out, in_=in_, accum=accum):
            if accum:
                return eng.dma_start(out=out, in_=in_, accum_op=ALU.add)
            return eng.dma_start(out=out, in_=in_)
        return p.op(queue_eng, fn, ((sem, 16),))

    def join():
        for e in ENGS:
            own = OWN.get(e)
            for sm in ALL_SEMS:
                if sm != own:
                    p.wait(e, sm)

    def tap(name, src_ap):
        if name in tap_d:
            dma("sync", dO, tap_d[name].ap(), src_ap)

    def MM3(out, lh, rh, ll, rl, start, stop, inc=0):
        T(lambda t, out=out, lh=lh, rh=rh, start=start: t.matmul(
            out=out, lhsT=lh, rhs=rh, start=start, stop=False), inc=0)
        T(lambda t, out=out, lh=lh, rl=rl: t.matmul(
            out=out, lhsT=lh, rhs=rl, start=False, stop=False), inc=0)
        return T(lambda t, out=out, ll=ll, rh=rh, stop=stop: t.matmul(
            out=out, lhsT=ll, rhs=rh, start=False, stop=stop), inc=inc)

    def castpair(dh, dl, src, inc=1):
        ta = Ac(lambda a, dh=dh, src=src: a.copy(out=dh, in_=src))
        p.wait_tok("vector", *ta)
        return V(lambda v, dh=dh, dl=dl, src=src: v.tensor_tensor(
            out=dl, in0=src, in1=dh, op=ALU.subtract), inc=inc)

    dma("sync", dC, id_sb[:], ident_d[:])
    dma("sync", dC, cos_hi[:], cos_hi_d[:])
    dma("sync", dC, sin_hi[:], sin_hi_d[:])
    dma("sync", dC, masks_b[:], masks_d.ap().rearrange("o p u -> p o u"))
    dma("sync", dC, rs_sb[:], rs_d.ap().rearrange("(a b) -> a b", a=1))

    G(lambda g: g.iota(out=iem9[:], pattern=[[0, 8], [1, 8]], base=-9,
                       channel_multiplier=0, allow_small_or_imprecise_dtypes=True))
    G(lambda g: g.memset(eps_col[:], EPS))
    G(lambda g: g.memset(ones_row[:], 1.0))

    p.wait("tensor", dC)
    p.wait("tensor", eP)
    T(lambda t: t.matmul(out=ps_a[:, 0:1], lhsT=ones_row[:], rhs=rs_sb[:],
                         start=True, stop=True))
    p.wait("vector", eT)
    V(lambda v: v.tensor_copy(out=rs_col[:], in_=ps_a[:, 0:1]))

    for si, sd in enumerate((selv_d, selo_d)):
        p.wait("sync", eT)
        tk = dma("sync", dC, sel_raw64, sd[:])
        p.wait_tok("tensor", *tk)
        p.wait("tensor", eV)
        for c in range(8):
            ps = ps_a if c % 2 == 0 else ps_b
            if si > 0 or c >= 2:
                p.wait("tensor", eV)
                p.wait("tensor", eA)
            T(lambda t, c=c, ps=ps: t.transpose(
                out=ps[0:128, 0:64], in_=sel_raw64[:, c * 128:(c + 1) * 128],
                identity=id_sb[0:64, 0:64]))
            p.wait("scalar", eT)
            castpair(selT_h[:, si, c, :], selT_l[:, si, c, :], ps[0:128, 0:64])
        p.wait("tensor", eV)
        p.wait("tensor", eA)

    def st_wsplit():
        p.lab = "wsplit"
        specs = []
        for (srcd, dh_d, dl_d, rows, cols) in (
                (wqkv_d, wqkvh_d, wqkvl_d, D, 3 * D),
                (wout_d, wouth_d, woutl_d, D, D)):
            sv = srcd.ap().rearrange("(c p) n -> p c n", p=128)
            hv = dh_d.ap().rearrange("(c p) n -> p c n", p=128)
            lv = dl_d.ap().rearrange("(c p) n -> p c n", p=128)
            nb = cols // 1024
            for c in range(rows // 128):
                for b in range(nb):
                    sl = (slice(None), c, slice(b * 1024, (b + 1) * 1024))
                    specs.append((sv[sl], hv[sl], lv[sl], 1024))
        for (srcd, dh_d, dl_d) in ((vw_d, vwh_d, vwl_d), (ow_d, owh_d, owl_d)):
            sv = srcd.ap().rearrange("e a u -> (e a) u").rearrange(
                "(c p) n -> p c n", p=128)
            hv = dh_d.ap().rearrange("e a u -> (e a) u").rearrange(
                "(c p) n -> p c n", p=128)
            lv = dl_d.ap().rearrange("e a u -> (e a) u", ).rearrange(
                "(c p) n -> p c n", p=128)
            C = sv.shape[1]
            N = sv.shape[2]
            grp = max(1, 1024 // N)
            for c0 in range(0, C, grp):
                sl = (slice(None), slice(c0, c0 + grp), slice(None))
                specs.append((sv[sl], hv[sl], lv[sl], grp * N))
        stg = ln_x
        for i, (s_ap, h_ap, l_ap, L) in enumerate(specs):
            par = i % 2
            if i >= 2:
                p.wait("sync", eV)
            tkr = dma("sync", dC, stg[:, par, 0:L], s_ap)
            hv2 = wstg_h[:, par, 0:L]
            lv2 = wstg_l[:, par, 0:L]
            p.wait_tok("scalar", *tkr)
            if i >= 2:
                p.wait("scalar", dG)
            ta = Ac(lambda a, hv2=hv2, st=stg[:, par, 0:L]: a.copy(out=hv2, in_=st))
            p.wait_tok("vector", *ta)
            if i >= 2:
                p.wait("vector", dG)
            tv = V(lambda v, hv2=hv2, lv2=lv2, st=stg[:, par, 0:L]: v.tensor_tensor(
                out=lv2, in0=st, in1=hv2, op=ALU.subtract))
            p.wait_tok("gpsimd", *ta)
            p.wait_tok("gpsimd", *tv)
            dma("gpsimd", dG, h_ap, hv2)
            dma("gpsimd", dG, l_ap, lv2)
        join()
        tap("wqh", wqkvh_d.ap())

    def st_ln(s):
        p.lab = "ln"
        join()
        for tt in range(TT):
            par = tt % 2
            o = par * 8
            p.wait("sync", eV)
            xt = ln_x[:, par, :]
            tkx = dma("sync", dX, xt, x_d[s, tt * 128:(tt + 1) * 128, :])
            p.wait_tok("vector", *tkx)
            p.wait("vector", eT)
            V(lambda v, xt=xt, o=o: v.reduce_sum(out=sc[:, o:o + 1], in_=xt, axis=AX.X))
            p.wait("vector", eV)
            V(lambda v, o=o: v.tensor_scalar(out=sc[:, o + 1:o + 2], in0=sc[:, o:o + 1],
                                             scalar1=1.0 / D, scalar2=None, op0=ALU.mult))
            p.wait("vector", eV)
            dev = ln_s1[:, par, :]
            tkd = V(lambda v, xt=xt, dev=dev, o=o: v.tensor_scalar(
                out=dev, in0=xt, scalar1=sc[:, o + 1:o + 2], scalar2=None,
                op0=ALU.subtract))
            p.wait_tok("scalar", *tkd)
            Ac(lambda a, dev=dev, o=o: a.activation(
                out=sq_dump, in_=dev, func=AF.Square,
                accum_out=sc[:, o + 2:o + 3]))
            p.wait("scalar", eA)
            tka = Ac(lambda a, o=o: a.activation(
                out=sc[:, o + 5:o + 6], in_=sc[:, o + 2:o + 3], func=AF.Sqrt,
                scale=1.0 / D, bias=eps_col[:]))
            p.wait_tok("vector", *tka)
            V(lambda v, o=o: v.reciprocal(out=sc[:, o + 3:o + 4], in_=sc[:, o + 5:o + 6]))
            p.wait("vector", eV)
            V(lambda v, o=o: v.tensor_tensor(out=sc[:, o + 4:o + 5],
                                             in0=sc[:, o + 1:o + 2],
                                             in1=sc[:, o + 3:o + 4], op=ALU.mult))
            p.wait("vector", eV)
            tkn = V(lambda v, xt=xt, dev=dev, o=o: v.tensor_scalar(
                out=dev, in0=xt, scalar1=sc[:, o + 3:o + 4],
                scalar2=sc[:, o + 4:o + 5], op0=ALU.mult, op1=ALU.subtract))
            p.wait_tok("tensor", *tkn)
            for c in range(8):
                ps = ps_a if c % 2 == 0 else ps_b
                if tt > 0 or c >= 2:
                    p.wait("tensor", eV)
                    p.wait("tensor", eA)
                T(lambda t, dev=dev, c=c, ps=ps: t.transpose(
                    out=ps[:, 0:128], in_=dev[:, c * 128:(c + 1) * 128],
                    identity=id_sb[:]))
                p.wait("scalar", eT)
                castpair(xn_h[:, c, tt * 128:(tt + 1) * 128],
                         xn_l[:, c, tt * 128:(tt + 1) * 128], ps[:, 0:128])
        if s == 0 and "xn" in tap_d:
            join()

    def st_gates(s):
        p.lab = "gates"
        join()

        def b8(ap):
            return ap.rearrange("p (g e) -> p g e", e=1).broadcast_to([128, 8, 8])

        IE3 = iem9.ap().rearrange("p (g e) -> p g e", e=8)

        def emit_top2(si, gout, tt):
            Pa = gt[:, si * 64:(si + 1) * 64]
            P3v = Pa.rearrange("p (g e) -> p g e", e=8)
            M1 = gt[:, 192:200]
            I1 = gt[:, 200:208]
            M2 = gt[:, 208:216]
            I2 = gt[:, 216:224]
            SS = gt[:, 224:232]
            R = gt[:, 232:240]
            W1 = gt[:, 240:248]
            W2 = gt[:, 248:256]
            E1 = gt[:, 256:320].rearrange("p (g e) -> p g e", e=8)
            OH1 = gt[:, 320:384].rearrange("p (g e) -> p g e", e=8)
            PM = gt[:, 384:448]
            PM3 = PM.rearrange("p (g e) -> p g e", e=8)
            E2 = gt[:, 448:512].rearrange("p (g e) -> p g e", e=8)
            OH2 = gt[:, 512:576].rearrange("p (g e) -> p g e", e=8)
            CT = gt[:, 576:640].rearrange("p (g e) -> p g e", e=8)
            CTf = gt[:, 576:640]
            steps = [
                lambda v: v.reduce_max(out=M1, in_=P3v, axis=AX.X),
                lambda v: v.tensor_tensor(out=E1, in0=P3v, in1=b8(M1), op=ALU.is_equal),
                lambda v: v.tensor_tensor(out=CT, in0=E1, in1=IE3, op=ALU.mult),
                lambda v: v.tensor_reduce(out=I1, in_=CT, axis=AX.X, op=ALU.min),
                lambda v: v.tensor_tensor(out=OH1, in0=IE3, in1=b8(I1), op=ALU.is_equal),
                lambda v: v.scalar_tensor_tensor(
                    out=PM, in0=OH1.rearrange("p g e -> p (g e)"), scalar=-2.0,
                    in1=Pa, op0=ALU.mult, op1=ALU.add),
                lambda v: v.reduce_max(out=M2, in_=PM3, axis=AX.X),
                lambda v: v.tensor_tensor(out=E2, in0=PM3, in1=b8(M2), op=ALU.is_equal),
                lambda v: v.tensor_tensor(out=CT, in0=E2, in1=IE3, op=ALU.mult),
                lambda v: v.tensor_reduce(out=I2, in_=CT, axis=AX.X, op=ALU.min),
                lambda v: v.tensor_tensor(out=OH2, in0=IE3, in1=b8(I2), op=ALU.is_equal),
                lambda v: v.tensor_tensor(out=SS, in0=M1, in1=M2, op=ALU.add),
                lambda v: v.tensor_scalar(out=SS, in0=SS, scalar1=1e-9,
                                          scalar2=None, op0=ALU.max),
                lambda v: v.reciprocal(out=R, in_=SS),
                lambda v: v.tensor_tensor(out=W1, in0=M1, in1=R, op=ALU.mult),
                lambda v: v.tensor_scalar(out=W1, in0=W1, scalar1=rs_col[:],
                                          scalar2=None, op0=ALU.mult),
                lambda v: v.tensor_tensor(out=W2, in0=M2, in1=R, op=ALU.mult),
                lambda v: v.tensor_scalar(out=W2, in0=W2, scalar1=rs_col[:],
                                          scalar2=None, op0=ALU.mult),
                lambda v: v.tensor_tensor(out=CT, in0=OH1, in1=b8(W1), op=ALU.mult),
                lambda v: v.tensor_tensor(out=PM3, in0=OH2, in1=b8(W2), op=ALU.mult),
                lambda v: v.tensor_tensor(out=gout[:, tt, :], in0=CTf, in1=PM, op=ALU.add),
            ]
            for st_fn in steps:
                p.wait("vector", eV)
                V(st_fn, inc=1)

        for tt in range(TT):
            if tt > 0:
                p.wait("tensor", eA)
            for si in range(2):
                psg = ps_a if si == 0 else ps_b
                for c in range(8):
                    MM3(psg[:, 0:64],
                        xn_h[:, c, tt * 128:(tt + 1) * 128],
                        selT_h[:, si, c, :],
                        xn_l[:, c, tt * 128:(tt + 1) * 128],
                        selT_l[:, si, c, :],
                        start=(c == 0), stop=(c == 7), inc=(1 if c == 7 else 0))
            p.wait("scalar", eT)
            p.wait("scalar", eV)
            Ac(lambda a: a.activation(out=gt[:, 0:64], in_=ps_a[:, 0:64],
                                      func=AF.Sigmoid), inc=0)
            Ac(lambda a: a.activation(out=gt[:, 64:128], in_=ps_b[:, 0:64],
                                      func=AF.Sigmoid))
            p.wait("vector", eA)
            for si, gout in ((0, gates_v), (1, gates_o)):
                emit_top2(si, gout, tt)
        if s == 0 and ("gv" in tap_d or "go" in tap_d):
            join()
            tap("gv", gates_v.ap().rearrange("p t u -> p (t u)"))
            tap("go", gates_o.ap().rearrange("p t u -> p (t u)"))

    def rope_one(dst):
        V(lambda v, dst=dst: v.tensor_tensor(out=dst, in0=raw, in1=cos_hi[:], op=ALU.mult), inc=0)
        V(lambda v: v.tensor_tensor(out=raw, in0=raw, in1=cos_lo, op=ALU.mult), inc=0)
        V(lambda v, dst=dst: v.tensor_tensor(out=dst, in0=dst, in1=raw, op=ALU.add), inc=0)
        V(lambda v: v.tensor_tensor(out=raw, in0=shuf, in1=sin_hi[:], op=ALU.mult), inc=0)
        V(lambda v, dst=dst: v.tensor_tensor(out=dst, in0=dst, in1=raw, op=ALU.add), inc=0)
        V(lambda v: v.tensor_tensor(out=raw, in0=shuf, in1=sin_lo, op=ALU.mult), inc=0)
        V(lambda v, dst=dst: v.tensor_tensor(out=dst, in0=dst, in1=raw, op=ALU.add), inc=1)

    def attn_core(lKh, lKl, rQh, rQl, vAh, vAl, cW, causal, ctx_out_fn):
        pTlTok = [None, None]
        ctxTok = [None, None]
        p.wait("tensor", eA)
        p.wait("tensor", eV)
        p.wait("scalar", eT)
        p.wait("vector", eT)

        def emit_S(qg, kt):
            pr = kt % 2
            psS = ps_big.ap()[:, pr, :]
            if pTlTok[pr] is not None:
                p.wait_tok("tensor", pTlTok[pr])
            return MM3(psS, lKh(kt), rQh(qg), lKl(kt), rQl(qg),
                       start=True, stop=True, inc=1)

        for qg in range(NQG):
            nkt = (4 * qg + 4) if causal else TT
            sToks = {0: emit_S(qg, 0)}
            for kt in range(nkt):
                pr = kt % 2
                psS = ps_big.ap()[:, pr, :]
                if kt + 1 < nkt:
                    sToks[kt + 1] = emit_S(qg, kt + 1)
                tb = sToks.pop(kt)
                lastS = tb
                o = kt - 4 * qg
                if causal and o >= 0:
                    p.wait_tok("vector", *tb)
                    lastS = V(lambda v, psS=psS, o=o: v.tensor_tensor(
                        out=psS, in0=psS, in1=masks_b[:, o, :], op=ALU.add))
                    p.wait_tok("scalar", *lastS)
                else:
                    p.wait_tok("scalar", *lastS)
                Ac(lambda a, psS=psS: a.activation(
                    out=psS, in_=psS, func=AF.Exp), inc=0)
                if ctxTok[pr] is not None:
                    p.wait_tok("scalar", ctxTok[pr])
                ta = Ac(lambda a, psS=psS, pr=pr: a.copy(out=pTh[:, pr, :], in_=psS))
                p.wait_tok("vector", *ta)
                if ctxTok[pr] is not None:
                    p.wait_tok("vector", ctxTok[pr])
                tl = V(lambda v, psS=psS, pr=pr: v.tensor_tensor(
                    out=pTl[:, pr, :], in0=psS, in1=pTh[:, pr, :], op=ALU.subtract))
                pTlTok[pr] = tl[0]
                p.wait_tok("tensor", *ta)
                p.wait_tok("tensor", *tl)
                if kt == 0:
                    p.wait("tensor", eV)
                for qtl in range(4):
                    MM3(ps_ctx[:, qtl, cW[0]:cW[0] + cW[1]],
                        pTh[:, pr, qtl * 128:(qtl + 1) * 128], vAh(kt),
                        pTl[:, pr, qtl * 128:(qtl + 1) * 128], vAl(kt),
                        start=(kt == 0), stop=(kt == nkt - 1),
                        inc=(1 if (qtl == 3 and kt == nkt - 1) else 0))
                ctxTok[pr] = p.tok(eT)
            ctx_out_fn(qg)

    def st_attn(s):
        p.lab = "attn"
        join()
        G(lambda g: g.memset(vh.ap()[:, :, 64:65], 1.0))
        G(lambda g: g.memset(vh.ap()[:, :, 129:130], 1.0))
        G(lambda g: g.memset(vl.ap()[:, :, 64:65], 0.0))
        G(lambda g: g.memset(vl.ap()[:, :, 129:130], 0.0))
        p.wait("tensor", eP)
        dma("sync", dW, cos_lo, cos_lo_d[:])
        dma("sync", dW, sin_lo, sin_lo_d[:])
        for hp in range(8):
            p.wait("sync", eT)
            tkw = None
            for j, c0 in enumerate((hp * 128, D + hp * 128, 2 * D + hp * 128)):
                dma("sync", dW, wq_h[:, :, j, :],
                    wqkvh_d.ap().rearrange("(c p) n -> p c n", p=128)[:, :, c0:c0 + 128])
                tkw = dma("sync", dW, wq_l[:, :, j, :],
                          wqkvl_d.ap().rearrange("(c p) n -> p c n", p=128)[:, :, c0:c0 + 128])
            p.wait_tok("tensor", *tkw)
            p.lab = "attn.proj"
            for j in (0, 1):
                scale = SCALE_A if j == 0 else 1.0
                p.wait("scalar", eV)
                p.wait("scalar", eT)
                for nt in range(NQG):
                    ps = ps_a if nt % 2 == 0 else ps_b
                    p.wait("tensor", eA)
                    for c in range(8):
                        MM3(ps[:, :], wq_h[:, c, j, :],
                            xn_h[:, c, nt * 512:(nt + 1) * 512],
                            wq_l[:, c, j, :],
                            xn_l[:, c, nt * 512:(nt + 1) * 512],
                            start=(c == 0), stop=(c == 7), inc=(1 if c == 7 else 0))
                    p.wait("scalar", eT)
                    Ac(lambda a, nt=nt, scale=scale, ps=ps: a.mul(
                        out=raw[:, nt * 512:(nt + 1) * 512], in_=ps[:, :], mul=scale))
                p.wait("gpsimd", eA)
                p.wait("gpsimd", eV)
                p.wait("gpsimd", eT)
                for (d0, s0) in ((0, 32), (32, 0), (64, 96), (96, 64)):
                    dma("gpsimd", dG, big3.ap()[d0:d0 + 32, :], raw[s0:s0 + 32, :])
                p.wait("vector", dG)
                p.wait("vector", dW)
                p.wait("vector", eA)
                p.wait("vector", eT)
                p.wait("vector", eP)
                dstb = big1 if j == 0 else big2
                rope_one(dstb.ap())
                p.wait("scalar", eV)
                p.wait("scalar", eT)
                if j == 0:
                    castpair(qh[:], ql[:], dstb.ap())
                else:
                    castpair(kh[:], kl[:], dstb.ap())
            p.lab = "attn.vproj"
            for tt in range(TT):
                ps = ps_a if tt % 2 == 0 else ps_b
                p.wait("tensor", eV)
                p.wait("tensor", eA)
                for c in range(8):
                    MM3(ps[:, 0:128], xn_h[:, c, tt * 128:(tt + 1) * 128],
                        wq_h[:, c, 2, :],
                        xn_l[:, c, tt * 128:(tt + 1) * 128],
                        wq_l[:, c, 2, :],
                        start=(c == 0), stop=(c == 7), inc=(1 if c == 7 else 0))
                p.wait("scalar", eT)
                vh2 = vh.ap()[:, tt, :].rearrange("p (b u) -> p b u", b=2)[:, :, 0:64]
                vl2 = vl.ap()[:, tt, :].rearrange("p (b u) -> p b u", b=2)[:, :, 0:64]
                ps2 = ps[:, 0:128].rearrange("p (b u) -> p b u", b=2)
                castpair(vh2, vl2, ps2)
            if s == 0 and hp == 0:
                join()
                tap("qrot", big1.ap())
                tap("krot", big2.ap())
                tap("qhb", qh.ap())
                tap("qlb", ql.ap())
                tap("vhb", vh.ap().rearrange("p t u -> p (t u)"))
                tap("vlb", vl.ap().rearrange("p t u -> p (t u)"))
            p.lab = "attn.core"
            for i in range(2):
                R0 = slice(64 * i, 64 * i + 64)
                cA = 65 * i

                def ctx_out(qg, cA=cA, i=i):
                    p.wait("vector", eT)
                    for qtl in range(4):
                        V(lambda v, qtl=qtl, cA=cA: v.reciprocal(
                            out=sc[:, 16 + qtl:17 + qtl],
                            in_=ps_ctx[:, qtl, cA + 64:cA + 65]),
                          inc=(1 if qtl == 3 else 0))
                    p.wait("vector", eV)
                    for qtl in range(4):
                        qt = qg * 4 + qtl
                        V(lambda v, qtl=qtl, qt=qt, cA=cA: v.tensor_scalar(
                            out=ctxn[:, qt, :], in0=ps_ctx[:, qtl, cA:cA + 64],
                            scalar1=sc[:, 16 + qtl:17 + qtl], scalar2=None,
                            op0=ALU.mult), inc=(1 if qtl == 3 else 0))

                attn_core(
                    lKh=lambda kt, R0=R0: kh.ap()[R0, kt * 128:(kt + 1) * 128],
                    lKl=lambda kt, R0=R0: kl.ap()[R0, kt * 128:(kt + 1) * 128],
                    rQh=lambda qg, R0=R0: qh.ap()[R0, qg * 512:(qg + 1) * 512],
                    rQl=lambda qg, R0=R0: ql.ap()[R0, qg * 512:(qg + 1) * 512],
                    vAh=lambda kt, cA=cA: vh.ap()[:, kt, cA:cA + 65],
                    vAl=lambda kt, cA=cA: vl.ap()[:, kt, cA:cA + 65],
                    cW=(cA, 65), causal=True, ctx_out_fn=ctx_out)
                p.wait("tensor", eV)
                for qt in range(TT):
                    ps = ps_a if qt % 2 == 0 else ps_b
                    p.wait("tensor", eV)
                    p.wait("tensor", eA)
                    T(lambda t, qt=qt, ps=ps: t.transpose(
                        out=ps[0:64, 0:128], in_=ctxn[:, qt, :], identity=id_sb[:]))
                    p.wait("scalar", eT)
                    if i == 0:
                        castpair(cph[0:64, qt * 128:(qt + 1) * 128],
                                 cpl[0:64, qt * 128:(qt + 1) * 128], ps[0:64, 0:128])
                    else:
                        castpair(ctxB_h[:, qt * 128:(qt + 1) * 128],
                                 ctxB_l[:, qt * 128:(qt + 1) * 128], ps[0:64, 0:128])
                if i == 1:
                    p.wait("gpsimd", eV)
                    dma("gpsimd", dG, cph[64:128, :], ctxB_h[:, :])
                    dma("gpsimd", dG, cpl[64:128, :], ctxB_l[:, :])
            if s == 0 and hp == 0:
                join()
                tap("ctxa", ctxn.rearrange("p t u -> p (t u)"))
                tap("cphb", cph)
            p.lab = "attn.wout"
            p.wait("sync", eT)
            dma("sync", dW, wout_h,
                wouth_d[hp * 128:(hp + 1) * 128, :].rearrange("p (c u) -> p c u", c=8))
            tkwo = dma("sync", dW, wout_l,
                       woutl_d[hp * 128:(hp + 1) * 128, :].rearrange("p (c u) -> p c u", c=8))
            p.wait_tok("tensor", *tkwo)
            p.wait("tensor", dG)
            p.wait("tensor", eV)
            woTok = [None, None]
            for dc in range(8):
                for jj in range(4):
                    pr2 = (dc * 4 + jj) % 2
                    p.wait("tensor", eA)
                    MM3(ps_big[:, jj % 2, :], wout_h[:, dc, :],
                        cph[:, jj * 512:(jj + 1) * 512],
                        wout_l[:, dc, :],
                        cpl[:, jj * 512:(jj + 1) * 512],
                        start=True, stop=True, inc=1)
                    p.wait("scalar", eT)
                    if woTok[pr2] is not None:
                        p.wait_tok("scalar", woTok[pr2])
                    tae = Ac(lambda a, pr2=pr2, jj=jj: a.copy(
                        out=evac[:, pr2, :], in_=ps_big[:, jj % 2, :]))
                    p.wait_tok("gpsimd", *tae)
                    tkg = dma("gpsimd", dG, oscr_d[dc, :, jj * 512:(jj + 1) * 512],
                              evac[:, pr2, :], accum=(hp != 0))
                    woTok[pr2] = tkg[0]
        if s == 0 and "scrA" in tap_d:
            join()
            tap("scrA", oscr_d.ap())

    def st_ffn(s):
        p.lab = "ffn"
        join()
        G(lambda g: g.memset(vacc[:, :, 128:129], 1.0))
        p.wait("scalar", eP)
        p.wait("vector", eP)
        owTok = [None, None]
        for h in range(8):
            p.lab = "ffn.qkT"
            for wi, wd in enumerate((qw_d, kw_d)):
                p.wait("sync", eT)
                p.wait("sync", dG)
                tk = dma("sync", dF, raw[:, 0:1024], wd[h * 128:(h + 1) * 128, :])
                p.wait_tok("tensor", *tk)
                p.wait("tensor", eA)
                for c in range(8):
                    ps = ps_a if c % 2 == 0 else ps_b
                    p.wait("tensor", eV)
                    p.wait("tensor", eA)
                    T(lambda t, c=c, ps=ps: t.transpose(
                        out=ps[:, 0:128], in_=raw[:, c * 128:(c + 1) * 128],
                        identity=id_sb[:]))
                    p.wait("scalar", eT)
                    hv = (qwT_h if wi == 0 else kwT_h)[:, c, :]
                    lv = (qwT_l if wi == 0 else kwT_l)[:, c, :]
                    castpair(hv, lv, ps[:, 0:128])
            p.lab = "ffn.proj"
            for wi in (0, 1):
                scale = SCALE_F if wi == 0 else 1.0
                for nt in range(NQG):
                    ps = ps_a if nt % 2 == 0 else ps_b
                    p.wait("tensor", eA)
                    p.wait("tensor", eV)
                    for c in range(8):
                        MM3(ps[:, :], (qwT_h if wi == 0 else kwT_h)[:, c, :],
                            xn_h[:, c, nt * 512:(nt + 1) * 512],
                            (qwT_l if wi == 0 else kwT_l)[:, c, :],
                            xn_l[:, c, nt * 512:(nt + 1) * 512],
                            start=(c == 0), stop=(c == 7), inc=(1 if c == 7 else 0))
                    p.wait("scalar", eT)
                    p.wait("scalar", eV)
                    hv = (qh if wi == 0 else kh).ap()[:, nt * 512:(nt + 1) * 512]
                    lv = (ql if wi == 0 else kl).ap()[:, nt * 512:(nt + 1) * 512]
                    ta = Ac(lambda a, hv=hv, ps=ps, scale=scale: a.mul(
                        out=hv, in_=ps[:, :], mul=scale))
                    p.wait_tok("vector", *ta)
                    V(lambda v, hv=hv, lv=lv, ps=ps, scale=scale: v.scalar_tensor_tensor(
                        out=lv, in0=ps[:, :], scalar=scale, in1=hv,
                        op0=ALU.mult, op1=ALU.subtract))
            p.lab = "ffn.xv"
            for half in range(2):
                p.wait("sync", eT)
                for e4 in range(4):
                    dma("sync", dF, vw_h[:, :, e4, :],
                        vwh_d[h * 8 + half * 4 + e4, :, :].rearrange("(c p) u -> p c u", p=128))
                    tk = dma("sync", dF, vw_l[:, :, e4, :],
                             vwl_d[h * 8 + half * 4 + e4, :, :].rearrange("(c p) u -> p c u", p=128))
                p.wait_tok("tensor", *tk)
                for tt in range(TT):
                    ps = ps_a if tt % 2 == 0 else ps_b
                    p.wait("tensor", eV)
                    for c in range(8):
                        MM3(ps[:, :], xn_h[:, c, tt * 128:(tt + 1) * 128],
                            vw_h[:, c, :, :],
                            xn_l[:, c, tt * 128:(tt + 1) * 128],
                            vw_l[:, c, :, :],
                            start=(c == 0), stop=(c == 7), inc=(1 if c == 7 else 0))
                    p.wait("vector", eT)
                    for e4 in range(4):
                        he = h * 8 + half * 4 + e4
                        if e4 > 0:
                            p.wait("vector", eV)
                        if half == 0 and e4 == 0:
                            V(lambda v, tt=tt, e4=e4, he=he, ps=ps: v.tensor_scalar(
                                out=vacc[:, tt, 0:128],
                                in0=ps[:, e4 * 128:(e4 + 1) * 128],
                                scalar1=gates_v[:, tt, he:he + 1], scalar2=None,
                                op0=ALU.mult), inc=1)
                        else:
                            V(lambda v, tt=tt, e4=e4, he=he, ps=ps: v.scalar_tensor_tensor(
                                out=vacc[:, tt, 0:128],
                                in0=ps[:, e4 * 128:(e4 + 1) * 128],
                                scalar=gates_v[:, tt, he:he + 1],
                                in1=vacc[:, tt, 0:128],
                                op0=ALU.mult, op1=ALU.add), inc=1)
            p.wait("scalar", eV)
            p.wait("scalar", eT)
            castpair(vh.ap()[:, :, 0:129], vl.ap()[:, :, 0:129], vacc[:, :, 0:129])

            p.lab = "ffn.core"
            def ctx_out_f(qg):
                p.wait("vector", eT)
                for qtl in range(4):
                    V(lambda v, qtl=qtl: v.reciprocal(
                        out=sc[:, 24 + qtl:25 + qtl],
                        in_=ps_ctx[:, qtl, 128:129]),
                      inc=(1 if qtl == 3 else 0))
                p.wait("vector", eV)
                for qtl in range(4):
                    qt = qg * 4 + qtl
                    V(lambda v, qtl=qtl, qt=qt: v.tensor_scalar(
                        out=ctxn_f[:, qt, :], in0=ps_ctx[:, qtl, 0:128],
                        scalar1=sc[:, 24 + qtl:25 + qtl], scalar2=None,
                        op0=ALU.mult), inc=(1 if qtl == 3 else 0))

            attn_core(
                lKh=lambda kt: kh.ap()[:, kt * 128:(kt + 1) * 128],
                lKl=lambda kt: kl.ap()[:, kt * 128:(kt + 1) * 128],
                rQh=lambda qg: qh.ap()[:, qg * 512:(qg + 1) * 512],
                rQl=lambda qg: ql.ap()[:, qg * 512:(qg + 1) * 512],
                vAh=lambda kt: vh.ap()[:, kt, 0:129],
                vAl=lambda kt: vl.ap()[:, kt, 0:129],
                cW=(0, 129), causal=False, ctx_out_fn=ctx_out_f)
            if s == 0 and h == 0 and "ctxf" in tap_d:
                join()
                tap("ctxf", ctxn_f.rearrange("p t u -> p (t u)"))

            p.lab = "ffn.ow"
            for tc in range(4):
                for e in range(8):
                    he = h * 8 + e
                    for qtl in range(4):
                        qt = tc * 4 + qtl
                        pg = (e * 4 + qtl) % 2
                        p.wait("vector", eT)
                        V(lambda v, qt=qt, he=he, pg=pg: v.tensor_scalar(
                            out=tmpg[:, pg, :], in0=ctxn_f[:, qt, :],
                            scalar1=gates_o[:, qt, he:he + 1], scalar2=None,
                            op0=ALU.mult))
                        ps = ps_a if pg == 0 else ps_b
                        p.wait("tensor", eV)
                        p.wait("tensor", eA)
                        T(lambda t, pg=pg, ps=ps: t.transpose(
                            out=ps[:, 0:128], in_=tmpg[:, pg, :], identity=id_sb[:]))
                        p.wait("scalar", eT)
                        castpair(w_he_h[:, e, qtl * 128:(qtl + 1) * 128],
                                 w_he_l[:, e, qtl * 128:(qtl + 1) * 128], ps[:, 0:128])
                for dc in range(8):
                    par = dc % 2
                    p.wait("sync", eT)
                    dma("sync", dF, ow_h[:, par, :, :],
                        owh_d[h * 8:(h + 1) * 8, :, dc * 128:(dc + 1) * 128]
                        .rearrange("e p u -> p e u"))
                    tk = dma("sync", dF, ow_l[:, par, :, :],
                             owl_d[h * 8:(h + 1) * 8, :, dc * 128:(dc + 1) * 128]
                             .rearrange("e p u -> p e u"))
                    p.wait_tok("tensor", *tk)
                    p.wait("tensor", eV)
                    ps = ps_a if par == 0 else ps_b
                    p.wait("tensor", eA)
                    for e in range(8):
                        MM3(ps[:, :], ow_h[:, par, e, :], w_he_h[:, e, :],
                            ow_l[:, par, e, :], w_he_l[:, e, :],
                            start=(e == 0), stop=(e == 7), inc=(1 if e == 7 else 0))
                    p.wait("scalar", eT)
                    pr2 = (tc * 8 + dc) % 2
                    if owTok[pr2] is not None:
                        p.wait_tok("scalar", owTok[pr2])
                    tae = Ac(lambda a, pr2=pr2, ps=ps: a.copy(
                        out=evac[:, pr2, :], in_=ps[:, :]))
                    p.wait_tok("gpsimd", *tae)
                    tkg = dma("gpsimd", dG, oscr_d[dc, :, tc * 512:(tc + 1) * 512],
                              evac[:, pr2, :], accum=True)
                    owTok[pr2] = tkg[0]
        if s == 0 and "scrF" in tap_d:
            join()
            tap("scrF", oscr_d.ap())

    def st_final(s):
        p.lab = "final"
        join()
        for tt in range(TT):
            par = tt % 2
            if tt >= 2:
                p.wait("sync", eT)
                p.wait("sync", eV)
            tk1 = dma("sync", dR, fin_in[:, par, :, :],
                      oscr_d[:, :, tt * 128:(tt + 1) * 128].rearrange("c p u -> p c u"))
            tk2 = dma("sync", dX, fin_x[:, par, :], x_d[s, tt * 128:(tt + 1) * 128, :])
            p.wait_tok("tensor", *tk1)
            if tt >= 1:
                p.wait("tensor", eV)
            for c in range(8):
                T(lambda t, par=par, c=c: t.transpose(
                    out=ps_bigf[:, c * 128:(c + 1) * 128],
                    in_=fin_in[:, par, c, :], identity=id_sb[:]),
                  inc=(1 if c == 7 else 0))
            p.wait_tok("vector", *tk2)
            p.wait("vector", eT)
            if tt >= 2:
                p.wait("vector", dO)
            tv = V(lambda v, par=par: v.tensor_tensor(
                out=fin_o[:, par, :], in0=ps_bigf, in1=fin_x[:, par, :], op=ALU.add))
            p.wait_tok("sync", *tv)
            dma("sync", dO, out_d[s, tt * 128:(tt + 1) * 128, :], fin_o[:, par, :])

    stages = ["ln", "gates", "attn", "ffn", "final"]
    upto_i = stages.index(upto)
    nseq_eff = NSEQ if upto_i >= 4 else 1
    st_wsplit()
    for s in range(nseq_eff):
        st_ln(s)
        if upto_i >= 1:
            st_gates(s)
        if upto_i >= 2:
            st_attn(s)
        if upto_i >= 3:
            st_ffn(s)
        if upto_i >= 4:
            st_final(s)

    p.wait("sync", dO)
    p.wait("sync", dG)
    p.wait("gpsimd", dO)

    with contextlib.ExitStack() as es:
        semh = {name: es.enter_context(nc.semaphore(name)) for name in p.cnt}
        block = es.enter_context(nc.Block())

        def runner(items):
            def body(eng):
                cur = [None]
                stack = [None]

                def setlab(lab):
                    if lab != stack[0]:
                        if stack[0] is not None:
                            cm = stack.pop()
                            cm.__exit__(None, None, None)
                            stack.append(None)
                        if lab is not None:
                            cm = nc.named_scope(lab)
                            cm.__enter__()
                            stack[-1] = None
                            stack.append(cm)
                        stack[0] = lab

                open_cm = [None, None]
                def swl(lab):
                    if lab != open_cm[0]:
                        if open_cm[1] is not None:
                            open_cm[1].__exit__(None, None, None)
                            open_cm[1] = None
                        if lab:
                            open_cm[1] = nc.named_scope(lab)
                            open_cm[1].__enter__()
                        open_cm[0] = lab
                for kind, a, b, lab in items:
                    swl(lab)
                    if kind == "wait":
                        eng.wait_ge(semh[a], b)
                    else:
                        ins = a(eng)
                        for sm, amt in b:
                            ins = ins.then_inc(semh[sm], amt)
                swl(None)
            return body

        for ename in ENGS:
            getattr(block, ename)(runner(p.items[ename]))
    return nc, tap_d, sbuf_left


def kernel(L0_attn_x, L0_attn_w_qkv, L0_attn_w_out, L0_ffn_q_w, L0_ffn_k_w,
           L0_ffn_v_w, L0_ffn_o_w, L0_ffn_sel_v_w, L0_ffn_sel_o_w,
           L0_ffn_route_scale, _taps=(), _upto="final"):
    consts = host_consts()
    nc, tap_d, _ = build(taps=_taps, upto=_upto)
    in_maps = []
    for c in range(NCORES):
        m = {
            "x": np.ascontiguousarray(np.asarray(L0_attn_x)[c * NSEQ:(c + 1) * NSEQ]),
            "w_qkv": np.asarray(L0_attn_w_qkv), "w_out": np.asarray(L0_attn_w_out),
            "q_w": np.asarray(L0_ffn_q_w), "k_w": np.asarray(L0_ffn_k_w),
            "v_w": np.asarray(L0_ffn_v_w), "o_w": np.asarray(L0_ffn_o_w),
            "sel_v": np.asarray(L0_ffn_sel_v_w), "sel_o": np.asarray(L0_ffn_sel_o_w),
            "route_scale": np.asarray(L0_ffn_route_scale),
        }
        m.update(consts)
        in_maps.append(m)
    res = run_bass_kernel_spmd(nc, in_maps, core_ids=list(range(NCORES)))
    out = np.concatenate([r["out"] for r in res.results], axis=0)
    if _taps:
        return out, {t: res.results[0]["tap_" + t] for t in _taps}
    return out


# version 06
# speedup vs baseline: 1.2318x; 1.0225x over previous
import math
import contextlib
import numpy as np
import ml_dtypes

import concourse.bass as bass
import concourse.mybir as mybir
from concourse.bass_utils import run_bass_kernel_spmd

F32 = mybir.dt.float32
BF16 = mybir.dt.bfloat16
AF = mybir.ActivationFunctionType
ALU = mybir.AluOpType
AX = mybir.AxisListType

B, S, D = 16, 2048, 1024
NCORES = 8
NSEQ = B // NCORES
DH = 64
FH, FE, FP = 8, 8, 128
SCALE_A = 1.0 / math.sqrt(DH)
SCALE_F = 1.0 / math.sqrt(FP)
EPS = 1e-5
NEG = -1.0e30
TT = S // 128
NQG = S // 512
ENGS = ("sync", "vector", "scalar", "tensor", "gpsimd")


def host_consts():
    ident = np.eye(128, dtype=np.float32)
    j = np.arange(32, dtype=np.float64)
    invf = 1.0 / (10000.0 ** (j / 32.0))
    t = np.arange(S, dtype=np.float64)
    ang = invf[:, None] * t[None, :]
    cos_rep = np.tile(np.cos(ang), (4, 1))
    sin_rep = np.tile(np.sin(ang), (4, 1))
    sgn = np.ones((128, 1))
    sgn[0:32] = -1.0
    sgn[64:96] = -1.0
    sin_rep = sin_rep * sgn
    cos_hi = cos_rep.astype(np.float32)
    cos_lo = (cos_rep - cos_hi.astype(np.float64)).astype(np.float32)
    sin_hi = sin_rep.astype(np.float32)
    sin_lo = (sin_rep - sin_hi.astype(np.float64)).astype(np.float32)
    kp = np.arange(128)[:, None]
    qf = np.arange(512)[None, :]
    masks = np.zeros((4, 128, 512), dtype=np.float32)
    for o in range(4):
        masks[o] = np.where(kp <= qf - o * 128, 0.0, NEG)
    esel = np.kron(np.eye(8), np.ones((1, 128))).astype(np.float32)
    return {
        "c_ident": ident, "c_esel": esel,
        "c_cos_hi": cos_hi, "c_cos_lo": cos_lo,
        "c_sin_hi": sin_hi, "c_sin_lo": sin_lo,
        "c_masks": masks.astype(ml_dtypes.bfloat16),
    }


eT, eV, eA, eP = "eT", "eV", "eA", "eP"
dC, dX, dW, dF, dG, dO, dR = "dC", "dX", "dW", "dF", "dG", "dO", "dR"
ALL_SEMS = [eT, eV, eA, eP, dC, dX, dW, dF, dG, dO, dR]
OWN = {"tensor": eT, "vector": eV, "scalar": eA, "gpsimd": eP}


class Prog:

    def __init__(self):
        self.items = {e: [] for e in ENGS}
        self.cnt = {sm: 0 for sm in ALL_SEMS}
        self.lab = "pre"

    def op(self, eng, fn, incs=()):
        incs = tuple(incs)
        for sm, a in incs:
            self.cnt[sm] += a
        self.items[eng].append(("op", fn, incs, self.lab))
        return tuple((sm, self.cnt[sm]) for sm, _ in incs)

    def wait(self, eng, sem, v=None):
        if v is None:
            v = self.cnt[sem]
        if v > 0:
            self.items[eng].append(("wait", sem, v, self.lab))

    def tok(self, sem):
        return (sem, self.cnt[sem])

    def wait_tok(self, eng, *toks):
        for sem, v in toks:
            if v > 0:
                self.items[eng].append(("wait", sem, v, self.lab))


def build(taps=(), upto="final"):
    taps = set(taps)
    nc = bass.Bass("TRN2", target_bir_lowering=False, detect_race_conditions=False)

    x_d = nc.dram_tensor("x", [NSEQ, S, D], F32, kind="ExternalInput")
    wqkv_d = nc.dram_tensor("w_qkv", [D, 3 * D], F32, kind="ExternalInput")
    wout_d = nc.dram_tensor("w_out", [D, D], F32, kind="ExternalInput")
    qw_d = nc.dram_tensor("q_w", [FH * FP, D], F32, kind="ExternalInput")
    kw_d = nc.dram_tensor("k_w", [FH * FP, D], F32, kind="ExternalInput")
    vw_d = nc.dram_tensor("v_w", [FH * FE, D, FP], F32, kind="ExternalInput")
    ow_d = nc.dram_tensor("o_w", [FH * FE, FP, D], F32, kind="ExternalInput")
    selv_d = nc.dram_tensor("sel_v", [FH * FE, D], F32, kind="ExternalInput")
    selo_d = nc.dram_tensor("sel_o", [FH * FE, D], F32, kind="ExternalInput")
    rs_d = nc.dram_tensor("route_scale", [1], F32, kind="ExternalInput")
    ident_d = nc.dram_tensor("c_ident", [128, 128], F32, kind="ExternalInput")
    cos_hi_d = nc.dram_tensor("c_cos_hi", [128, S], F32, kind="ExternalInput")
    cos_lo_d = nc.dram_tensor("c_cos_lo", [128, S], F32, kind="ExternalInput")
    sin_hi_d = nc.dram_tensor("c_sin_hi", [128, S], F32, kind="ExternalInput")
    sin_lo_d = nc.dram_tensor("c_sin_lo", [128, S], F32, kind="ExternalInput")
    masks_d = nc.dram_tensor("c_masks", [4, 128, 512], BF16, kind="ExternalInput")
    esel_d = nc.dram_tensor("c_esel", [8, 1024], F32, kind="ExternalInput")
    out_d = nc.dram_tensor("out", [NSEQ, S, D], F32, kind="ExternalOutput")
    oscr_d = nc.dram_tensor("oscr", [8, 128, S], F32, kind="Internal")
    wqkvh_d = nc.dram_tensor("wqkvh", [D, 3 * D], BF16, kind="Internal")
    wqkvl_d = nc.dram_tensor("wqkvl", [D, 3 * D], BF16, kind="Internal")
    wouth_d = nc.dram_tensor("wouth", [D, D], BF16, kind="Internal")
    woutl_d = nc.dram_tensor("woutl", [D, D], BF16, kind="Internal")
    vwh_d = nc.dram_tensor("vwh", [FH * FE, D, FP], BF16, kind="Internal")
    vwl_d = nc.dram_tensor("vwl", [FH * FE, D, FP], BF16, kind="Internal")
    owh_d = nc.dram_tensor("owh", [FH * FE, FP, D], BF16, kind="Internal")
    owl_d = nc.dram_tensor("owl", [FH * FE, FP, D], BF16, kind="Internal")

    tap_d = {}
    tap_shapes = {
        "gv": [128, TT * 64], "go": [128, TT * 64],
        "qrot": [128, S], "krot": [128, S],
        "ctxa": [128, TT * 64], "scrA": [8, 128, S],
        "ctxf": [128, TT * 128], "scrF": [8, 128, S],
        "wqh": ([D, 3 * D], BF16), "vhb": ([128, TT * 130], BF16),
        "vlb": ([128, TT * 130], BF16), "cphb": ([128, 2048], BF16),
        "qhb": ([128, S], BF16), "qlb": ([128, S], BF16),
    }
    for t in taps:
        shp = tap_shapes[t]
        dt = F32
        if isinstance(shp, tuple):
            shp, dt = shp
        tap_d[t] = nc.dram_tensor("tap_" + t, shp, dt, kind="ExternalOutput")

    A = nc.alloc_sbuf_tensor
    xn_h = A("xn_h", [128, 8, S], BF16)
    xn_l = A("xn_l", [128, 8, S], BF16)
    cos_hi = A("cos_hi", [128, S], F32)
    sin_hi = A("sin_hi", [128, S], F32)
    gates_v = A("gates_v", [128, TT, 64], F32)
    gates_o = A("gates_o", [128, TT, 64], F32)
    selT_h = A("selT_h", [128, 2, 8, 64], BF16)
    selT_l = A("selT_l", [128, 2, 8, 64], BF16)
    masks_b = A("masks_b", [128, 4, 512], BF16)
    id_sb = A("id_sb", [128, 128], F32)
    iem9 = A("iem9", [128, 64], F32)
    rs_col = A("rs_col", [128, 1], F32)
    eps_col = A("eps_col", [128, 1], F32)
    ones_row = A("ones_row", [1, 128], F32)
    ones_b = A("ones_b", [128, 1], BF16)
    rs_sb = A("rs_sb", [1, 1], F32)
    sc = A("sc", [128, 32], F32)
    big1 = A("big1", [128, S], F32)
    big2 = A("big2", [128, S], F32)
    big3 = A("big3", [128, S], F32)
    big5 = A("big5", [128, 4096], F32)
    big8 = A("big8", [128, S], F32)
    qh = A("qh", [128, S], BF16)
    ql = A("ql", [128, S], BF16)
    kh = A("kh", [128, S], BF16)
    kl = A("kl", [128, S], BF16)
    vh = A("vh", [128, TT, 130], BF16)
    vl = A("vl", [128, TT, 130], BF16)
    wh = A("wh", [128, 4096], BF16)
    wl = A("wl", [128, 4096], BF16)
    whe_h = A("whe_h", [128, 4096], BF16)
    whe_l = A("whe_l", [128, 4096], BF16)
    pTh = A("pTh", [128, 3, 512], BF16)
    pTl = A("pTl", [128, 3, 512], BF16)
    sbuf_left = nc.sbuf_bytes_remaining

    gt = big5.ap()[:, 0:640]
    cos_lo = big5.ap()[:, 0:2048]
    sin_lo = big5.ap()[:, 2048:4096]
    vacc = big5.ap()[:, 0:2080].rearrange("p (t u) -> p t u", t=TT)
    ln_x = big5.ap()[:, 0:2048].rearrange("p (b u) -> p b u", b=2)
    ln_s1 = big5.ap()[:, 2048:4096].rearrange("p (b u) -> p b u", b=2)
    fin_in = big5.ap()[:, 0:2048].rearrange("p (b c u) -> p b c u", b=2, c=8)
    fin_x = big5.ap()[:, 2048:4096].rearrange("p (b u) -> p b u", b=2)
    fin_o = big5.ap()[:, 0:2048].rearrange("p (b u) -> p b u", b=2)
    raw = big8.ap()
    sq_dump = big8.ap()[:, 0:1024]
    sel_raw64 = big8.ap()[0:64, 0:1024]
    shuf = big3.ap()
    ctxn_f = big3.ap().rearrange("p (t u) -> p t u", t=TT)
    ctxn = big3.ap()[:, 0:1024].rearrange("p (t u) -> p t u", t=TT)
    evac = big1.ap()[:, 0:1024].rearrange("p (b u) -> p b u", b=2)
    wq_h = wh.ap()[:, 0:3072].rearrange("p (c j u) -> p c j u", c=8, j=3)
    wq_l = wl.ap()[:, 0:3072].rearrange("p (c j u) -> p c j u", c=8, j=3)
    wout_h = wh.ap()[:, 3072:4096].rearrange("p (c u) -> p c u", c=8)
    wout_l = wl.ap()[:, 3072:4096].rearrange("p (c u) -> p c u", c=8)
    qwT_h = wh.ap()[:, 0:1024].rearrange("p (c u) -> p c u", c=8)
    qwT_l = wl.ap()[:, 0:1024].rearrange("p (c u) -> p c u", c=8)
    kwT_h = wh.ap()[:, 1024:2048].rearrange("p (c u) -> p c u", c=8)
    kwT_l = wl.ap()[:, 1024:2048].rearrange("p (c u) -> p c u", c=8)
    vw_h = wh.ap().rearrange("p (c e u) -> p c e u", c=8, e=4)
    vw_l = wl.ap().rearrange("p (c e u) -> p c e u", c=8, e=4)
    ow_h = wh.ap()[:, 0:2048].rearrange("p (b e u) -> p b e u", b=2, e=8)
    ow_l = wl.ap()[:, 0:2048].rearrange("p (b e u) -> p b e u", b=2, e=8)
    wstg_h = wh.ap().rearrange("p (b u) -> p b u", b=2)
    wstg_l = wl.ap().rearrange("p (b u) -> p b u", b=2)
    cph = whe_h.ap()[:, 0:2048]
    cpl = whe_l.ap()[:, 0:2048]
    ctxB_h = whe_h.ap()[0:64, 2048:4096]
    ctxB_l = whe_l.ap()[0:64, 2048:4096]
    w_he_h = whe_h.ap().rearrange("p (e u) -> p e u", e=8)
    w_he_l = whe_l.ap().rearrange("p (e u) -> p e u", e=8)

    ps_a = nc.alloc_psum_tensor("ps_a", [128, 512], F32)
    ps_b = nc.alloc_psum_tensor("ps_b", [128, 512], F32)
    ps_big = nc.alloc_psum_tensor("ps_big", [128, 2, 512], F32)
    ps_ctx = nc.alloc_psum_tensor("ps_ctx", [128, 4, 512], F32)
    ps_bigf = ps_big.ap().rearrange("p a u -> p (a u)")

    p = Prog()

    def T(fn, inc=1):
        return p.op("tensor", fn, ((eT, inc),) if inc else ())

    def V(fn, inc=1):
        return p.op("vector", fn, ((eV, inc),) if inc else ())

    def Ac(fn, inc=1):
        return p.op("scalar", fn, ((eA, inc),) if inc else ())

    def G(fn, inc=1):
        return p.op("gpsimd", fn, ((eP, inc),) if inc else ())

    def dma(queue_eng, sem, out, in_, accum=False):
        def fn(eng, out=out, in_=in_, accum=accum):
            if accum:
                return eng.dma_start(out=out, in_=in_, accum_op=ALU.add)
            return eng.dma_start(out=out, in_=in_)
        return p.op(queue_eng, fn, ((sem, 16),))

    def join():
        for e in ENGS:
            own = OWN.get(e)
            for sm in ALL_SEMS:
                if sm != own:
                    p.wait(e, sm)

    def tap(name, src_ap):
        if name in tap_d:
            dma("sync", dO, tap_d[name].ap(), src_ap)

    def MM3(out, lh, rh, ll, rl, start, stop, inc=0):
        T(lambda t, out=out, lh=lh, rh=rh, start=start: t.matmul(
            out=out, lhsT=lh, rhs=rh, start=start, stop=False), inc=0)
        T(lambda t, out=out, lh=lh, rl=rl: t.matmul(
            out=out, lhsT=lh, rhs=rl, start=False, stop=False), inc=0)
        return T(lambda t, out=out, ll=ll, rh=rh, stop=stop: t.matmul(
            out=out, lhsT=ll, rhs=rh, start=False, stop=stop), inc=inc)

    def castpair(dh, dl, src, inc=1):
        ta = Ac(lambda a, dh=dh, src=src: a.copy(out=dh, in_=src))
        p.wait_tok("vector", *ta)
        return V(lambda v, dh=dh, dl=dl, src=src: v.tensor_tensor(
            out=dl, in0=src, in1=dh, op=ALU.subtract), inc=inc)

    dma("sync", dC, id_sb[:], ident_d[:])
    dma("sync", dC, cos_hi[:], cos_hi_d[:])
    dma("sync", dC, sin_hi[:], sin_hi_d[:])
    dma("sync", dC, masks_b[:], masks_d.ap().rearrange("o p u -> p o u"))
    dma("sync", dC, rs_sb[:], rs_d.ap().rearrange("(a b) -> a b", a=1))

    G(lambda g: g.iota(out=iem9[:], pattern=[[0, 8], [1, 8]], base=-9,
                       channel_multiplier=0, allow_small_or_imprecise_dtypes=True))
    G(lambda g: g.memset(eps_col[:], EPS))
    G(lambda g: g.memset(ones_row[:], 1.0))
    G(lambda g: g.memset(ones_b[:], 1.0))

    p.wait("tensor", dC)
    p.wait("tensor", eP)
    T(lambda t: t.matmul(out=ps_a[:, 0:1], lhsT=ones_row[:], rhs=rs_sb[:],
                         start=True, stop=True))
    p.wait("vector", eT)
    V(lambda v: v.tensor_copy(out=rs_col[:], in_=ps_a[:, 0:1]))

    for si, sd in enumerate((selv_d, selo_d)):
        p.wait("sync", eT)
        tk = dma("sync", dC, sel_raw64, sd[:])
        p.wait_tok("tensor", *tk)
        p.wait("tensor", eV)
        for c in range(8):
            ps = ps_a if c % 2 == 0 else ps_b
            if si > 0 or c >= 2:
                p.wait("tensor", eV)
                p.wait("tensor", eA)
            T(lambda t, c=c, ps=ps: t.transpose(
                out=ps[0:128, 0:64], in_=sel_raw64[:, c * 128:(c + 1) * 128],
                identity=id_sb[0:64, 0:64]))
            p.wait("scalar", eT)
            castpair(selT_h[:, si, c, :], selT_l[:, si, c, :], ps[0:128, 0:64])
        p.wait("tensor", eV)
        p.wait("tensor", eA)

    def st_wsplit():
        p.lab = "wsplit"
        specs = []
        for (srcd, dh_d, dl_d, rows, cols) in (
                (wqkv_d, wqkvh_d, wqkvl_d, D, 3 * D),
                (wout_d, wouth_d, woutl_d, D, D)):
            sv = srcd.ap().rearrange("(c p) n -> p c n", p=128)
            hv = dh_d.ap().rearrange("(c p) n -> p c n", p=128)
            lv = dl_d.ap().rearrange("(c p) n -> p c n", p=128)
            nb = cols // 1024
            for c in range(rows // 128):
                for b in range(nb):
                    sl = (slice(None), c, slice(b * 1024, (b + 1) * 1024))
                    specs.append((sv[sl], hv[sl], lv[sl], 1024))
        for (srcd, dh_d, dl_d) in ((vw_d, vwh_d, vwl_d), (ow_d, owh_d, owl_d)):
            sv = srcd.ap().rearrange("e a u -> (e a) u").rearrange(
                "(c p) n -> p c n", p=128)
            hv = dh_d.ap().rearrange("e a u -> (e a) u").rearrange(
                "(c p) n -> p c n", p=128)
            lv = dl_d.ap().rearrange("e a u -> (e a) u", ).rearrange(
                "(c p) n -> p c n", p=128)
            C = sv.shape[1]
            N = sv.shape[2]
            grp = max(1, 1024 // N)
            for c0 in range(0, C, grp):
                sl = (slice(None), slice(c0, c0 + grp), slice(None))
                specs.append((sv[sl], hv[sl], lv[sl], grp * N))
        for i, (s_ap, h_ap, l_ap, L) in enumerate(specs):
            par = i % 2
            stgb = big1 if par == 0 else big2
            if i >= 2:
                p.wait("gpsimd", eV)
            tkr = dma("gpsimd", dG, stgb.ap()[:, 0:L], s_ap)
            hv2 = wstg_h[:, par, 0:L]
            lv2 = wstg_l[:, par, 0:L]
            p.wait_tok("scalar", *tkr)
            if i >= 2:
                p.wait("scalar", dG)
            ta = Ac(lambda a, hv2=hv2, st=stgb.ap()[:, 0:L]: a.copy(out=hv2, in_=st))
            p.wait_tok("vector", *ta)
            if i >= 2:
                p.wait("vector", dG)
            tv = V(lambda v, hv2=hv2, lv2=lv2, st=stgb.ap()[:, 0:L]: v.tensor_tensor(
                out=lv2, in0=st, in1=hv2, op=ALU.subtract))
            p.wait_tok("gpsimd", *ta)
            p.wait_tok("gpsimd", *tv)
            dma("gpsimd", dG, h_ap, hv2)
            dma("gpsimd", dG, l_ap, lv2)
        tap("wqh", wqkvh_d.ap())

    def st_ln(s):
        p.lab = "ln"
        join()
        for tt in range(TT):
            par = tt % 2
            o = par * 8
            p.wait("sync", eV)
            xt = ln_x[:, par, :]
            tkx = dma("sync", dX, xt, x_d[s, tt * 128:(tt + 1) * 128, :])
            p.wait_tok("vector", *tkx)
            p.wait("vector", eT)
            V(lambda v, xt=xt, o=o: v.reduce_sum(out=sc[:, o:o + 1], in_=xt, axis=AX.X))
            p.wait("vector", eV)
            V(lambda v, o=o: v.tensor_scalar(out=sc[:, o + 1:o + 2], in0=sc[:, o:o + 1],
                                             scalar1=1.0 / D, scalar2=None, op0=ALU.mult))
            p.wait("vector", eV)
            dev = ln_s1[:, par, :]
            tkd = V(lambda v, xt=xt, dev=dev, o=o: v.tensor_scalar(
                out=dev, in0=xt, scalar1=sc[:, o + 1:o + 2], scalar2=None,
                op0=ALU.subtract))
            p.wait_tok("scalar", *tkd)
            Ac(lambda a, dev=dev, o=o: a.activation(
                out=sq_dump, in_=dev, func=AF.Square,
                accum_out=sc[:, o + 2:o + 3]))
            p.wait("scalar", eA)
            tka = Ac(lambda a, o=o: a.activation(
                out=sc[:, o + 5:o + 6], in_=sc[:, o + 2:o + 3], func=AF.Sqrt,
                scale=1.0 / D, bias=eps_col[:]))
            p.wait_tok("vector", *tka)
            V(lambda v, o=o: v.reciprocal(out=sc[:, o + 3:o + 4], in_=sc[:, o + 5:o + 6]))
            p.wait("vector", eV)
            V(lambda v, o=o: v.tensor_tensor(out=sc[:, o + 4:o + 5],
                                             in0=sc[:, o + 1:o + 2],
                                             in1=sc[:, o + 3:o + 4], op=ALU.mult))
            p.wait("vector", eV)
            tkn = V(lambda v, xt=xt, dev=dev, o=o: v.tensor_scalar(
                out=dev, in0=xt, scalar1=sc[:, o + 3:o + 4],
                scalar2=sc[:, o + 4:o + 5], op0=ALU.mult, op1=ALU.subtract))
            p.wait_tok("tensor", *tkn)
            for c in range(8):
                ps = ps_a if c % 2 == 0 else ps_b
                if tt > 0 or c >= 2:
                    p.wait("tensor", eV)
                    p.wait("tensor", eA)
                T(lambda t, dev=dev, c=c, ps=ps: t.transpose(
                    out=ps[:, 0:128], in_=dev[:, c * 128:(c + 1) * 128],
                    identity=id_sb[:]))
                p.wait("scalar", eT)
                castpair(xn_h[:, c, tt * 128:(tt + 1) * 128],
                         xn_l[:, c, tt * 128:(tt + 1) * 128], ps[:, 0:128])
        if s == 0 and "xn" in tap_d:
            join()

    def st_gates(s):
        p.lab = "gates"
        join()

        def b8(ap):
            return ap.rearrange("p (g e) -> p g e", e=1).broadcast_to([128, 8, 8])

        IE3 = iem9.ap().rearrange("p (g e) -> p g e", e=8)

        def emit_top2(si, gout, tt):
            Pa = gt[:, si * 64:(si + 1) * 64]
            P3v = Pa.rearrange("p (g e) -> p g e", e=8)
            M1 = gt[:, 192:200]
            I1 = gt[:, 200:208]
            M2 = gt[:, 208:216]
            I2 = gt[:, 216:224]
            SS = gt[:, 224:232]
            R = gt[:, 232:240]
            W1 = gt[:, 240:248]
            W2 = gt[:, 248:256]
            E1 = gt[:, 256:320].rearrange("p (g e) -> p g e", e=8)
            OH1 = gt[:, 320:384].rearrange("p (g e) -> p g e", e=8)
            PM = gt[:, 384:448]
            PM3 = PM.rearrange("p (g e) -> p g e", e=8)
            E2 = gt[:, 448:512].rearrange("p (g e) -> p g e", e=8)
            OH2 = gt[:, 512:576].rearrange("p (g e) -> p g e", e=8)
            CT = gt[:, 576:640].rearrange("p (g e) -> p g e", e=8)
            CTf = gt[:, 576:640]
            steps = [
                lambda v: v.reduce_max(out=M1, in_=P3v, axis=AX.X),
                lambda v: v.tensor_tensor(out=E1, in0=P3v, in1=b8(M1), op=ALU.is_equal),
                lambda v: v.tensor_tensor(out=CT, in0=E1, in1=IE3, op=ALU.mult),
                lambda v: v.tensor_reduce(out=I1, in_=CT, axis=AX.X, op=ALU.min),
                lambda v: v.tensor_tensor(out=OH1, in0=IE3, in1=b8(I1), op=ALU.is_equal),
                lambda v: v.scalar_tensor_tensor(
                    out=PM, in0=OH1.rearrange("p g e -> p (g e)"), scalar=-2.0,
                    in1=Pa, op0=ALU.mult, op1=ALU.add),
                lambda v: v.reduce_max(out=M2, in_=PM3, axis=AX.X),
                lambda v: v.tensor_tensor(out=E2, in0=PM3, in1=b8(M2), op=ALU.is_equal),
                lambda v: v.tensor_tensor(out=CT, in0=E2, in1=IE3, op=ALU.mult),
                lambda v: v.tensor_reduce(out=I2, in_=CT, axis=AX.X, op=ALU.min),
                lambda v: v.tensor_tensor(out=OH2, in0=IE3, in1=b8(I2), op=ALU.is_equal),
                lambda v: v.tensor_tensor(out=SS, in0=M1, in1=M2, op=ALU.add),
                lambda v: v.tensor_scalar(out=SS, in0=SS, scalar1=1e-9,
                                          scalar2=None, op0=ALU.max),
                lambda v: v.reciprocal(out=R, in_=SS),
                lambda v: v.tensor_tensor(out=W1, in0=M1, in1=R, op=ALU.mult),
                lambda v: v.tensor_scalar(out=W1, in0=W1, scalar1=rs_col[:],
                                          scalar2=None, op0=ALU.mult),
                lambda v: v.tensor_tensor(out=W2, in0=M2, in1=R, op=ALU.mult),
                lambda v: v.tensor_scalar(out=W2, in0=W2, scalar1=rs_col[:],
                                          scalar2=None, op0=ALU.mult),
                lambda v: v.tensor_tensor(out=CT, in0=OH1, in1=b8(W1), op=ALU.mult),
                lambda v: v.tensor_tensor(out=PM3, in0=OH2, in1=b8(W2), op=ALU.mult),
                lambda v: v.tensor_tensor(out=gout[:, tt, :], in0=CTf, in1=PM, op=ALU.add),
            ]
            for st_fn in steps:
                p.wait("vector", eV)
                V(st_fn, inc=1)

        for tt in range(TT):
            if tt > 0:
                p.wait("tensor", eA)
            for si in range(2):
                psg = ps_a if si == 0 else ps_b
                for c in range(8):
                    MM3(psg[:, 0:64],
                        xn_h[:, c, tt * 128:(tt + 1) * 128],
                        selT_h[:, si, c, :],
                        xn_l[:, c, tt * 128:(tt + 1) * 128],
                        selT_l[:, si, c, :],
                        start=(c == 0), stop=(c == 7), inc=(1 if c == 7 else 0))
            p.wait("scalar", eT)
            p.wait("scalar", eV)
            Ac(lambda a: a.activation(out=gt[:, 0:64], in_=ps_a[:, 0:64],
                                      func=AF.Sigmoid), inc=0)
            Ac(lambda a: a.activation(out=gt[:, 64:128], in_=ps_b[:, 0:64],
                                      func=AF.Sigmoid))
            p.wait("vector", eA)
            for si, gout in ((0, gates_v), (1, gates_o)):
                emit_top2(si, gout, tt)
        if s == 0 and ("gv" in tap_d or "go" in tap_d):
            join()
            tap("gv", gates_v.ap().rearrange("p t u -> p (t u)"))
            tap("go", gates_o.ap().rearrange("p t u -> p (t u)"))

    def rope_one(dst):
        V(lambda v, dst=dst: v.tensor_tensor(out=dst, in0=raw, in1=cos_hi[:], op=ALU.mult), inc=0)
        V(lambda v: v.tensor_tensor(out=raw, in0=raw, in1=cos_lo, op=ALU.mult), inc=0)
        V(lambda v, dst=dst: v.tensor_tensor(out=dst, in0=dst, in1=raw, op=ALU.add), inc=0)
        V(lambda v: v.tensor_tensor(out=raw, in0=shuf, in1=sin_hi[:], op=ALU.mult), inc=0)
        V(lambda v, dst=dst: v.tensor_tensor(out=dst, in0=dst, in1=raw, op=ALU.add), inc=0)
        V(lambda v: v.tensor_tensor(out=raw, in0=shuf, in1=sin_lo, op=ALU.mult), inc=0)
        V(lambda v, dst=dst: v.tensor_tensor(out=dst, in0=dst, in1=raw, op=ALU.add), inc=1)

    def attn_core(lKh, lKl, rQh, rQl, vAh, vAl, cM, sums, causal, ctx_out_fn):
        pTlTok = [None, None, None]
        ctxTok = [None, None, None]
        p.wait("tensor", eA)
        p.wait("tensor", eV)
        p.wait("scalar", eT)
        p.wait("vector", eT)

        def psS_of(kt):
            pr = kt % 3
            if pr < 2:
                return pr, ps_big.ap()[:, pr, :]
            return pr, ps_ctx.ap()[:, 3, :]

        def emit_S(qg, kt):
            pr, psS = psS_of(kt)
            if pTlTok[pr] is not None:
                p.wait_tok("tensor", pTlTok[pr])
            return MM3(psS, lKh(kt), rQh(qg), lKl(kt), rQl(qg),
                       start=True, stop=True, inc=1)

        DEPTH = 2
        for qg in range(NQG):
            nkt = (4 * qg + 4) if causal else TT
            p.wait("tensor", eV)
            sToks = {}
            for k0 in range(min(DEPTH, nkt)):
                sToks[k0] = emit_S(qg, k0)
            for kt in range(nkt):
                pr, psS = psS_of(kt)
                if kt + DEPTH < nkt:
                    sToks[kt + DEPTH] = emit_S(qg, kt + DEPTH)
                tb = sToks.pop(kt)
                lastS = tb
                o = kt - 4 * qg
                if causal and o >= 0:
                    p.wait_tok("vector", *tb)
                    lastS = V(lambda v, psS=psS, o=o: v.tensor_tensor(
                        out=psS, in0=psS, in1=masks_b[:, o, :], op=ALU.add))
                    p.wait_tok("scalar", *lastS)
                else:
                    p.wait_tok("scalar", *lastS)
                Ac(lambda a, psS=psS: a.activation(
                    out=psS, in_=psS, func=AF.Exp), inc=0)
                if ctxTok[pr] is not None:
                    p.wait_tok("scalar", ctxTok[pr])
                ta = Ac(lambda a, psS=psS, pr=pr: a.copy(out=pTh[:, pr, :], in_=psS))
                p.wait_tok("vector", *ta)
                if ctxTok[pr] is not None:
                    p.wait_tok("vector", ctxTok[pr])
                tl = V(lambda v, psS=psS, pr=pr: v.tensor_tensor(
                    out=pTl[:, pr, :], in0=psS, in1=pTh[:, pr, :], op=ALU.subtract))
                pTlTok[pr] = tl[0]
                p.wait_tok("tensor", *ta)
                p.wait_tok("tensor", *tl)
                if kt == 0:
                    p.wait("tensor", eV)
                MM3(ps_ctx.ap()[0:cM, 0, :], vAh(kt), pTh[:, pr, :],
                    vAl(kt), pTl[:, pr, :],
                    start=(kt == 0), stop=(kt == nkt - 1),
                    inc=(0 if sums else 1))
                if sums:
                    T(lambda t, pr=pr, kt=kt, nkt=nkt: t.matmul(
                        out=ps_ctx.ap()[0:1, 1, :], lhsT=ones_b[:, 0:1],
                        rhs=pTh[:, pr, :], start=(kt == 0), stop=False), inc=0)
                    T(lambda t, pr=pr, kt=kt, nkt=nkt: t.matmul(
                        out=ps_ctx.ap()[0:1, 1, :], lhsT=ones_b[:, 0:1],
                        rhs=pTl[:, pr, :], start=False, stop=(kt == nkt - 1)), inc=1)
                ctxTok[pr] = p.tok(eT)
            ctx_out_fn(qg)

    def st_attn(s):
        p.lab = "attn"
        join()
        G(lambda g: g.memset(vh.ap()[:, :, 64:65], 1.0))
        G(lambda g: g.memset(vh.ap()[:, :, 129:130], 1.0))
        G(lambda g: g.memset(vl.ap()[:, :, 64:65], 0.0))
        G(lambda g: g.memset(vl.ap()[:, :, 129:130], 0.0))
        p.wait("tensor", eP)
        dma("sync", dW, cos_lo, cos_lo_d[:])
        dma("sync", dW, sin_lo, sin_lo_d[:])
        for hp in range(8):
            p.wait("sync", eT)
            tkw = None
            for j, c0 in enumerate((hp * 128, D + hp * 128, 2 * D + hp * 128)):
                dma("sync", dW, wq_h[:, :, j, :],
                    wqkvh_d.ap().rearrange("(c p) n -> p c n", p=128)[:, :, c0:c0 + 128])
                tkw = dma("sync", dW, wq_l[:, :, j, :],
                          wqkvl_d.ap().rearrange("(c p) n -> p c n", p=128)[:, :, c0:c0 + 128])
            p.wait_tok("tensor", *tkw)
            p.lab = "attn.proj"
            for j in (0, 1):
                scale = SCALE_A if j == 0 else 1.0
                p.wait("scalar", eV)
                p.wait("scalar", eT)
                for nt in range(NQG):
                    ps = ps_a if nt % 2 == 0 else ps_b
                    p.wait("tensor", eA)
                    for c in range(8):
                        MM3(ps[:, :], wq_h[:, c, j, :],
                            xn_h[:, c, nt * 512:(nt + 1) * 512],
                            wq_l[:, c, j, :],
                            xn_l[:, c, nt * 512:(nt + 1) * 512],
                            start=(c == 0), stop=(c == 7), inc=(1 if c == 7 else 0))
                    p.wait("scalar", eT)
                    Ac(lambda a, nt=nt, scale=scale, ps=ps: a.mul(
                        out=raw[:, nt * 512:(nt + 1) * 512], in_=ps[:, :], mul=scale))
                p.wait("gpsimd", eA)
                p.wait("gpsimd", eV)
                p.wait("gpsimd", eT)
                for (d0, s0) in ((0, 32), (32, 0), (64, 96), (96, 64)):
                    dma("gpsimd", dG, big3.ap()[d0:d0 + 32, :], raw[s0:s0 + 32, :])
                p.wait("vector", dG)
                p.wait("vector", dW)
                p.wait("vector", eA)
                p.wait("vector", eT)
                p.wait("vector", eP)
                dstb = big1 if j == 0 else big2
                rope_one(dstb.ap())
                p.wait("scalar", eV)
                p.wait("scalar", eT)
                if j == 0:
                    castpair(qh[:], ql[:], dstb.ap())
                else:
                    castpair(kh[:], kl[:], dstb.ap())
            p.lab = "attn.vproj"
            vTok = [None, None]
            for tt in range(TT):
                ps = ps_a if tt % 2 == 0 else ps_b
                if vTok[tt % 2] is not None:
                    p.wait_tok("tensor", vTok[tt % 2])
                for c in range(8):
                    MM3(ps[:, 0:128], xn_h[:, c, tt * 128:(tt + 1) * 128],
                        wq_h[:, c, 2, :],
                        xn_l[:, c, tt * 128:(tt + 1) * 128],
                        wq_l[:, c, 2, :],
                        start=(c == 0), stop=(c == 7), inc=(1 if c == 7 else 0))
                p.wait("scalar", eT)
                vh2 = vh.ap()[:, tt, :].rearrange("p (b u) -> p b u", b=2)[:, :, 0:64]
                vl2 = vl.ap()[:, tt, :].rearrange("p (b u) -> p b u", b=2)[:, :, 0:64]
                ps2 = ps[:, 0:128].rearrange("p (b u) -> p b u", b=2)
                vtk = castpair(vh2, vl2, ps2)
                vTok[tt % 2] = vtk[0]
            if s == 0 and hp == 0:
                join()
                tap("qrot", big1.ap())
                tap("krot", big2.ap())
                tap("qhb", qh.ap())
                tap("qlb", ql.ap())
                tap("vhb", vh.ap().rearrange("p t u -> p (t u)"))
                tap("vlb", vl.ap().rearrange("p t u -> p (t u)"))
            p.lab = "attn.core"
            for i in range(2):
                R0 = slice(64 * i, 64 * i + 64)
                cA = 65 * i

                def ctx_out(qg, i=i):
                    p.wait("vector", eT)
                    tv1 = V(lambda v: v.reciprocal(
                        out=big3.ap()[0:1, 0:512], in_=ps_ctx.ap()[64:65, 0, :]))
                    p.wait_tok("tensor", *tv1)
                    p.wait("tensor", eV)
                    p.wait("tensor", eA)
                    tr = T(lambda t: t.matmul(
                        out=ps_ctx.ap()[0:64, 1, :], lhsT=ones_row[0:1, 0:64],
                        rhs=big3.ap()[0:1, 0:512], start=True, stop=True))
                    p.wait_tok("scalar", *tr)
                    trc = Ac(lambda a: a.copy(out=big3.ap()[0:64, 1536:2048],
                                              in_=ps_ctx.ap()[0:64, 1, :]))
                    p.wait_tok("vector", *trc)
                    tv2 = V(lambda v: v.tensor_tensor(
                        out=big3.ap()[0:64, 512:1024], in0=ps_ctx.ap()[0:64, 0, :],
                        in1=big3.ap()[0:64, 1536:2048], op=ALU.mult))
                    p.wait("scalar", eV)
                    if i == 0:
                        castpair(cph[0:64, qg * 512:(qg + 1) * 512],
                                 cpl[0:64, qg * 512:(qg + 1) * 512],
                                 big3.ap()[0:64, 512:1024])
                    else:
                        castpair(ctxB_h[:, qg * 512:(qg + 1) * 512],
                                 ctxB_l[:, qg * 512:(qg + 1) * 512],
                                 big3.ap()[0:64, 512:1024])

                attn_core(
                    lKh=lambda kt, R0=R0: kh.ap()[R0, kt * 128:(kt + 1) * 128],
                    lKl=lambda kt, R0=R0: kl.ap()[R0, kt * 128:(kt + 1) * 128],
                    rQh=lambda qg, R0=R0: qh.ap()[R0, qg * 512:(qg + 1) * 512],
                    rQl=lambda qg, R0=R0: ql.ap()[R0, qg * 512:(qg + 1) * 512],
                    vAh=lambda kt, cA=cA: vh.ap()[:, kt, cA:cA + 65],
                    vAl=lambda kt, cA=cA: vl.ap()[:, kt, cA:cA + 65],
                    cM=65, sums=False, causal=True, ctx_out_fn=ctx_out)
                if i == 1:
                    p.wait("gpsimd", eV)
                    dma("gpsimd", dG, cph[64:128, :], ctxB_h[:, :])
                    dma("gpsimd", dG, cpl[64:128, :], ctxB_l[:, :])
            if s == 0 and hp == 0:
                join()
                tap("cphb", cph)
            p.lab = "attn.wout"
            p.wait("sync", eT)
            dma("sync", dW, wout_h,
                wouth_d[hp * 128:(hp + 1) * 128, :].rearrange("p (c u) -> p c u", c=8))
            tkwo = dma("sync", dW, wout_l,
                       woutl_d[hp * 128:(hp + 1) * 128, :].rearrange("p (c u) -> p c u", c=8))
            p.wait_tok("tensor", *tkwo)
            p.wait("tensor", dG)
            p.wait("tensor", eV)
            woTok = [None, None]
            for dc in range(8):
                for jj in range(4):
                    pr2 = (dc * 4 + jj) % 2
                    p.wait("tensor", eA)
                    MM3(ps_big[:, jj % 2, :], wout_h[:, dc, :],
                        cph[:, jj * 512:(jj + 1) * 512],
                        wout_l[:, dc, :],
                        cpl[:, jj * 512:(jj + 1) * 512],
                        start=True, stop=True, inc=1)
                    p.wait("scalar", eT)
                    if woTok[pr2] is not None:
                        p.wait_tok("scalar", woTok[pr2])
                    tae = Ac(lambda a, pr2=pr2, jj=jj: a.copy(
                        out=evac[:, pr2, :], in_=ps_big[:, jj % 2, :]))
                    p.wait_tok("gpsimd", *tae)
                    tkg = dma("gpsimd", dG, oscr_d[dc, :, jj * 512:(jj + 1) * 512],
                              evac[:, pr2, :], accum=(hp != 0))
                    woTok[pr2] = tkg[0]
        if s == 0 and "scrA" in tap_d:
            join()
            tap("scrA", oscr_d.ap())

    def st_ffn(s):
        p.lab = "ffn"
        join()
        G(lambda g: g.memset(vacc[:, :, 128:129], 1.0))
        p.wait("scalar", eP)
        p.wait("vector", eP)
        esel = big5.ap()[0:8, 2304:3328]
        dma("sync", dF, esel, esel_d[:])
        p.wait("tensor", dF)
        owTok = [None, None]
        for h in range(8):
            p.lab = "ffn.qkT"
            for wi, wd in enumerate((qw_d, kw_d)):
                p.wait("sync", eT)
                p.wait("sync", dG)
                tk = dma("sync", dF, raw[:, 0:1024], wd[h * 128:(h + 1) * 128, :])
                p.wait_tok("tensor", *tk)
                p.wait("tensor", eA)
                for c in range(8):
                    ps = ps_a if c % 2 == 0 else ps_b
                    p.wait("tensor", eV)
                    p.wait("tensor", eA)
                    T(lambda t, c=c, ps=ps: t.transpose(
                        out=ps[:, 0:128], in_=raw[:, c * 128:(c + 1) * 128],
                        identity=id_sb[:]))
                    p.wait("scalar", eT)
                    hv = (qwT_h if wi == 0 else kwT_h)[:, c, :]
                    lv = (qwT_l if wi == 0 else kwT_l)[:, c, :]
                    castpair(hv, lv, ps[:, 0:128])
            for tt in range(TT):
                ps = ps_a if tt % 2 == 0 else ps_b
                p.wait("tensor", eV)
                p.wait("tensor", eA)
                T(lambda t, tt=tt, ps=ps, h=h: t.transpose(
                    out=ps[0:8, 0:128], in_=gates_o[:, tt, h * 8:(h + 1) * 8],
                    identity=id_sb[:]))
                p.wait("vector", eT)
                dstg = big1 if tt < 8 else big2
                col = 1024 + (tt % 8) * 128
                V(lambda v, ps=ps, dstg=dstg, col=col: v.tensor_copy(
                    out=dstg.ap()[0:8, col:col + 128], in_=ps[0:8, 0:128]))
            p.lab = "ffn.proj"
            for wi in (0, 1):
                scale = SCALE_F if wi == 0 else 1.0
                for nt in range(NQG):
                    ps = ps_a if nt % 2 == 0 else ps_b
                    p.wait("tensor", eA)
                    p.wait("tensor", eV)
                    for c in range(8):
                        MM3(ps[:, :], (qwT_h if wi == 0 else kwT_h)[:, c, :],
                            xn_h[:, c, nt * 512:(nt + 1) * 512],
                            (qwT_l if wi == 0 else kwT_l)[:, c, :],
                            xn_l[:, c, nt * 512:(nt + 1) * 512],
                            start=(c == 0), stop=(c == 7), inc=(1 if c == 7 else 0))
                    p.wait("scalar", eT)
                    p.wait("scalar", eV)
                    hv = (qh if wi == 0 else kh).ap()[:, nt * 512:(nt + 1) * 512]
                    lv = (ql if wi == 0 else kl).ap()[:, nt * 512:(nt + 1) * 512]
                    ta = Ac(lambda a, hv=hv, ps=ps, scale=scale: a.mul(
                        out=hv, in_=ps[:, :], mul=scale))
                    p.wait_tok("vector", *ta)
                    V(lambda v, hv=hv, lv=lv, ps=ps, scale=scale: v.scalar_tensor_tensor(
                        out=lv, in0=ps[:, :], scalar=scale, in1=hv,
                        op0=ALU.mult, op1=ALU.subtract))
            p.lab = "ffn.xv"
            for half in range(2):
                p.wait("sync", eT)
                for e4 in range(4):
                    dma("sync", dF, vw_h[:, :, e4, :],
                        vwh_d[h * 8 + half * 4 + e4, :, :].rearrange("(c p) u -> p c u", p=128))
                    tk = dma("sync", dF, vw_l[:, :, e4, :],
                             vwl_d[h * 8 + half * 4 + e4, :, :].rearrange("(c p) u -> p c u", p=128))
                p.wait_tok("tensor", *tk)
                for tt in range(TT):
                    ps = ps_a if tt % 2 == 0 else ps_b
                    p.wait("tensor", eV)
                    for c in range(8):
                        MM3(ps[:, :], xn_h[:, c, tt * 128:(tt + 1) * 128],
                            vw_h[:, c, :, :],
                            xn_l[:, c, tt * 128:(tt + 1) * 128],
                            vw_l[:, c, :, :],
                            start=(c == 0), stop=(c == 7), inc=(1 if c == 7 else 0))
                    p.wait("vector", eT)
                    for e4 in range(4):
                        he = h * 8 + half * 4 + e4
                        if e4 > 0:
                            p.wait("vector", eV)
                        if half == 0 and e4 == 0:
                            V(lambda v, tt=tt, e4=e4, he=he, ps=ps: v.tensor_scalar(
                                out=vacc[:, tt, 0:128],
                                in0=ps[:, e4 * 128:(e4 + 1) * 128],
                                scalar1=gates_v[:, tt, he:he + 1], scalar2=None,
                                op0=ALU.mult), inc=1)
                        else:
                            V(lambda v, tt=tt, e4=e4, he=he, ps=ps: v.scalar_tensor_tensor(
                                out=vacc[:, tt, 0:128],
                                in0=ps[:, e4 * 128:(e4 + 1) * 128],
                                scalar=gates_v[:, tt, he:he + 1],
                                in1=vacc[:, tt, 0:128],
                                op0=ALU.mult, op1=ALU.add), inc=1)
            p.wait("scalar", eV)
            p.wait("scalar", eT)
            castpair(vh.ap()[:, :, 0:129], vl.ap()[:, :, 0:129], vacc[:, :, 0:129])

            p.lab = "ffn.core"

            def g_blk(qg):
                src = big1 if qg < 2 else big2
                c0 = 1024 + (qg % 2) * 512
                return src.ap()[0:8, c0:c0 + 512]

            def ctx_out_f(qg):
                p.lab = "ffn.ow"
                p.wait("vector", eT)
                tv1 = V(lambda v: v.reciprocal(
                    out=big3.ap()[0:1, 0:512], in_=ps_ctx.ap()[0:1, 1, :]))
                p.wait_tok("tensor", *tv1)
                p.wait("tensor", eV)
                p.wait("tensor", eA)
                tr = T(lambda t: t.matmul(
                    out=ps_ctx.ap()[0:128, 2, :], lhsT=ones_row[0:1, 0:128],
                    rhs=big3.ap()[0:1, 0:512], start=True, stop=True))
                p.wait_tok("scalar", *tr)
                trc = Ac(lambda a: a.copy(out=big3.ap()[:, 1536:2048],
                                          in_=ps_ctx.ap()[0:128, 2, :]))
                p.wait_tok("vector", *trc)
                tcn = V(lambda v: v.tensor_tensor(
                    out=big3.ap()[:, 512:1024], in0=ps_ctx.ap()[0:128, 0, :],
                    in1=big3.ap()[:, 1536:2048], op=ALU.mult))
                p.wait("scalar", eT)
                for e in range(8):
                    p.wait("tensor", eV)
                    p.wait("tensor", eA)
                    tg = T(lambda t, e=e, qg=qg: t.matmul(
                        out=ps_ctx.ap()[0:128, 2, :], lhsT=esel[:, e * 128:(e + 1) * 128],
                        rhs=g_blk(qg), start=True, stop=True))
                    p.wait_tok("vector", *tg)
                    p.wait("vector", eA)
                    stg = big3.ap()[:, 1024:1536]
                    tv2 = V(lambda v, stg=stg: v.tensor_tensor(
                        out=stg, in0=big3.ap()[:, 512:1024],
                        in1=ps_ctx.ap()[0:128, 2, :], op=ALU.mult))
                    p.wait("scalar", eV)
                    castpair(w_he_h[:, e, :], w_he_l[:, e, :], stg)
                tc = qg
                for dc in range(8):
                    par = dc % 2
                    p.wait("sync", eT)
                    dma("sync", dF, ow_h[:, par, :, :],
                        owh_d[h * 8:(h + 1) * 8, :, dc * 128:(dc + 1) * 128]
                        .rearrange("e p u -> p e u"))
                    tk = dma("sync", dF, ow_l[:, par, :, :],
                             owl_d[h * 8:(h + 1) * 8, :, dc * 128:(dc + 1) * 128]
                             .rearrange("e p u -> p e u"))
                    p.wait_tok("tensor", *tk)
                    p.wait("tensor", eV)
                    ps = ps_a if par == 0 else ps_b
                    p.wait("tensor", eA)
                    for e in range(8):
                        MM3(ps[:, :], ow_h[:, par, e, :], w_he_h[:, e, :],
                            ow_l[:, par, e, :], w_he_l[:, e, :],
                            start=(e == 0), stop=(e == 7), inc=(1 if e == 7 else 0))
                    p.wait("scalar", eT)
                    pr2 = dc % 2
                    if owTok[pr2] is not None:
                        p.wait_tok("scalar", owTok[pr2])
                    tae = Ac(lambda a, pr2=pr2, ps=ps: a.copy(
                        out=evac[:, pr2, :], in_=ps[:, :]))
                    p.wait_tok("gpsimd", *tae)
                    tkg = dma("gpsimd", dG, oscr_d[dc, :, tc * 512:(tc + 1) * 512],
                              evac[:, pr2, :], accum=True)
                    owTok[pr2] = tkg[0]
                p.lab = "ffn.core"

            attn_core(
                lKh=lambda kt: kh.ap()[:, kt * 128:(kt + 1) * 128],
                lKl=lambda kt: kl.ap()[:, kt * 128:(kt + 1) * 128],
                rQh=lambda qg: qh.ap()[:, qg * 512:(qg + 1) * 512],
                rQl=lambda qg: ql.ap()[:, qg * 512:(qg + 1) * 512],
                vAh=lambda kt: vh.ap()[:, kt, 0:128],
                vAl=lambda kt: vl.ap()[:, kt, 0:128],
                cM=128, sums=True, causal=False, ctx_out_fn=ctx_out_f)
        if s == 0 and "scrF" in tap_d:
            join()
            tap("scrF", oscr_d.ap())

    def st_final(s):
        p.lab = "final"
        join()
        for tt in range(TT):
            par = tt % 2
            if tt >= 2:
                p.wait("sync", eT)
                p.wait("sync", eV)
            tk1 = dma("sync", dR, fin_in[:, par, :, :],
                      oscr_d[:, :, tt * 128:(tt + 1) * 128].rearrange("c p u -> p c u"))
            tk2 = dma("sync", dX, fin_x[:, par, :], x_d[s, tt * 128:(tt + 1) * 128, :])
            p.wait_tok("tensor", *tk1)
            if tt >= 1:
                p.wait("tensor", eV)
            for c in range(8):
                T(lambda t, par=par, c=c: t.transpose(
                    out=ps_bigf[:, c * 128:(c + 1) * 128],
                    in_=fin_in[:, par, c, :], identity=id_sb[:]),
                  inc=(1 if c == 7 else 0))
            p.wait_tok("vector", *tk2)
            p.wait("vector", eT)
            if tt >= 2:
                p.wait("vector", dO)
            tv = V(lambda v, par=par: v.tensor_tensor(
                out=fin_o[:, par, :], in0=ps_bigf, in1=fin_x[:, par, :], op=ALU.add))
            p.wait_tok("sync", *tv)
            dma("sync", dO, out_d[s, tt * 128:(tt + 1) * 128, :], fin_o[:, par, :])

    stages = ["ln", "gates", "attn", "ffn", "final"]
    upto_i = stages.index(upto)
    nseq_eff = NSEQ if upto_i >= 4 else 1
    st_wsplit()
    for s in range(nseq_eff):
        st_ln(s)
        if upto_i >= 1:
            st_gates(s)
        if upto_i >= 2:
            st_attn(s)
        if upto_i >= 3:
            st_ffn(s)
        if upto_i >= 4:
            st_final(s)

    p.wait("sync", dO)
    p.wait("sync", dG)
    p.wait("gpsimd", dO)

    with contextlib.ExitStack() as es:
        semh = {name: es.enter_context(nc.semaphore(name)) for name in p.cnt}
        block = es.enter_context(nc.Block())

        def runner(items):
            def body(eng):
                cur = [None]
                stack = [None]

                def setlab(lab):
                    if lab != stack[0]:
                        if stack[0] is not None:
                            cm = stack.pop()
                            cm.__exit__(None, None, None)
                            stack.append(None)
                        if lab is not None:
                            cm = nc.named_scope(lab)
                            cm.__enter__()
                            stack[-1] = None
                            stack.append(cm)
                        stack[0] = lab

                open_cm = [None, None]
                def swl(lab):
                    if lab != open_cm[0]:
                        if open_cm[1] is not None:
                            open_cm[1].__exit__(None, None, None)
                            open_cm[1] = None
                        if lab:
                            open_cm[1] = nc.named_scope(lab)
                            open_cm[1].__enter__()
                        open_cm[0] = lab
                for kind, a, b, lab in items:
                    swl(lab)
                    if kind == "wait":
                        eng.wait_ge(semh[a], b)
                    else:
                        ins = a(eng)
                        for sm, amt in b:
                            ins = ins.then_inc(semh[sm], amt)
                swl(None)
            return body

        for ename in ENGS:
            getattr(block, ename)(runner(p.items[ename]))
    return nc, tap_d, sbuf_left


def kernel(L0_attn_x, L0_attn_w_qkv, L0_attn_w_out, L0_ffn_q_w, L0_ffn_k_w,
           L0_ffn_v_w, L0_ffn_o_w, L0_ffn_sel_v_w, L0_ffn_sel_o_w,
           L0_ffn_route_scale, _taps=(), _upto="final"):
    consts = host_consts()
    nc, tap_d, _ = build(taps=_taps, upto=_upto)
    in_maps = []
    for c in range(NCORES):
        m = {
            "x": np.ascontiguousarray(np.asarray(L0_attn_x)[c * NSEQ:(c + 1) * NSEQ]),
            "w_qkv": np.asarray(L0_attn_w_qkv), "w_out": np.asarray(L0_attn_w_out),
            "q_w": np.asarray(L0_ffn_q_w), "k_w": np.asarray(L0_ffn_k_w),
            "v_w": np.asarray(L0_ffn_v_w), "o_w": np.asarray(L0_ffn_o_w),
            "sel_v": np.asarray(L0_ffn_sel_v_w), "sel_o": np.asarray(L0_ffn_sel_o_w),
            "route_scale": np.asarray(L0_ffn_route_scale),
        }
        m.update(consts)
        in_maps.append(m)
    res = run_bass_kernel_spmd(nc, in_maps, core_ids=list(range(NCORES)))
    out = np.concatenate([r["out"] for r in res.results], axis=0)
    if _taps:
        return out, {t: res.results[0]["tap_" + t] for t in _taps}
    return out


# version 07
# speedup vs baseline: 1.2641x; 1.0262x over previous
import math
import contextlib
import numpy as np
import ml_dtypes

import concourse.bass as bass
import concourse.mybir as mybir
from concourse.bass_utils import run_bass_kernel_spmd

F32 = mybir.dt.float32
BF16 = mybir.dt.bfloat16
AF = mybir.ActivationFunctionType
ALU = mybir.AluOpType
AX = mybir.AxisListType

B, S, D = 16, 2048, 1024
NCORES = 8
NSEQ = B // NCORES
DH = 64
FH, FE, FP = 8, 8, 128
SCALE_A = 1.0 / math.sqrt(DH)
SCALE_F = 1.0 / math.sqrt(FP)
EPS = 1e-5
NEG = -1.0e30
TT = S // 128
NQG = S // 512
ENGS = ("sync", "vector", "scalar", "tensor", "gpsimd")


def host_consts():
    ident = np.eye(128, dtype=np.float32)
    j = np.arange(32, dtype=np.float64)
    invf = 1.0 / (10000.0 ** (j / 32.0))
    t = np.arange(S, dtype=np.float64)
    ang = invf[:, None] * t[None, :]
    cos_rep = np.tile(np.cos(ang), (4, 1))
    sin_rep = np.tile(np.sin(ang), (4, 1))
    sgn = np.ones((128, 1))
    sgn[0:32] = -1.0
    sgn[64:96] = -1.0
    sin_rep = sin_rep * sgn
    cos_hi = cos_rep.astype(np.float32)
    cos_lo = (cos_rep - cos_hi.astype(np.float64)).astype(np.float32)
    sin_hi = sin_rep.astype(np.float32)
    sin_lo = (sin_rep - sin_hi.astype(np.float64)).astype(np.float32)
    kp = np.arange(128)[:, None]
    qf = np.arange(512)[None, :]
    masks = np.zeros((4, 128, 512), dtype=np.float32)
    for o in range(4):
        masks[o] = np.where(kp <= qf - o * 128, 0.0, NEG)
    esel = np.kron(np.eye(8), np.ones((1, 128))).astype(np.float32)
    return {
        "c_ident": ident, "c_esel": esel,
        "c_cos_hi": cos_hi, "c_cos_lo": cos_lo,
        "c_sin_hi": sin_hi, "c_sin_lo": sin_lo,
        "c_masks": masks.astype(ml_dtypes.bfloat16),
    }


eT, eV, eA, eP = "eT", "eV", "eA", "eP"
dC, dX, dW, dF, dG, dO, dR = "dC", "dX", "dW", "dF", "dG", "dO", "dR"
dP1, dP2 = "dP1", "dP2"
ALL_SEMS = [eT, eV, eA, eP, dC, dX, dW, dF, dG, dO, dR, dP1, dP2]
OWN = {"tensor": eT, "vector": eV, "scalar": eA, "gpsimd": eP}


class Prog:

    def __init__(self):
        self.items = {e: [] for e in ENGS}
        self.cnt = {sm: 0 for sm in ALL_SEMS}
        self.lab = "pre"

    def op(self, eng, fn, incs=()):
        incs = tuple(incs)
        for sm, a in incs:
            self.cnt[sm] += a
        self.items[eng].append(("op", fn, incs, self.lab))
        return tuple((sm, self.cnt[sm]) for sm, _ in incs)

    def wait(self, eng, sem, v=None):
        if v is None:
            v = self.cnt[sem]
        if v > 0:
            self.items[eng].append(("wait", sem, v, self.lab))

    def tok(self, sem):
        return (sem, self.cnt[sem])

    def wait_tok(self, eng, *toks):
        for sem, v in toks:
            if v > 0:
                self.items[eng].append(("wait", sem, v, self.lab))


def build(taps=(), upto="final"):
    taps = set(taps)
    nc = bass.Bass("TRN2", target_bir_lowering=False, detect_race_conditions=False)

    x_d = nc.dram_tensor("x", [NSEQ, S, D], F32, kind="ExternalInput")
    wqkv_d = nc.dram_tensor("w_qkv", [D, 3 * D], F32, kind="ExternalInput")
    wout_d = nc.dram_tensor("w_out", [D, D], F32, kind="ExternalInput")
    qw_d = nc.dram_tensor("q_w", [FH * FP, D], F32, kind="ExternalInput")
    kw_d = nc.dram_tensor("k_w", [FH * FP, D], F32, kind="ExternalInput")
    vw_d = nc.dram_tensor("v_w", [FH * FE, D, FP], F32, kind="ExternalInput")
    ow_d = nc.dram_tensor("o_w", [FH * FE, FP, D], F32, kind="ExternalInput")
    selv_d = nc.dram_tensor("sel_v", [FH * FE, D], F32, kind="ExternalInput")
    selo_d = nc.dram_tensor("sel_o", [FH * FE, D], F32, kind="ExternalInput")
    rs_d = nc.dram_tensor("route_scale", [1], F32, kind="ExternalInput")
    ident_d = nc.dram_tensor("c_ident", [128, 128], F32, kind="ExternalInput")
    cos_hi_d = nc.dram_tensor("c_cos_hi", [128, S], F32, kind="ExternalInput")
    cos_lo_d = nc.dram_tensor("c_cos_lo", [128, S], F32, kind="ExternalInput")
    sin_hi_d = nc.dram_tensor("c_sin_hi", [128, S], F32, kind="ExternalInput")
    sin_lo_d = nc.dram_tensor("c_sin_lo", [128, S], F32, kind="ExternalInput")
    masks_d = nc.dram_tensor("c_masks", [4, 128, 512], BF16, kind="ExternalInput")
    esel_d = nc.dram_tensor("c_esel", [8, 1024], F32, kind="ExternalInput")
    out_d = nc.dram_tensor("out", [NSEQ, S, D], F32, kind="ExternalOutput")
    oscr_d = nc.dram_tensor("oscr", [8, 128, S], F32, kind="Internal")
    wqkvh_d = nc.dram_tensor("wqkvh", [D, 3 * D], BF16, kind="Internal")
    wqkvl_d = nc.dram_tensor("wqkvl", [D, 3 * D], BF16, kind="Internal")
    wouth_d = nc.dram_tensor("wouth", [D, D], BF16, kind="Internal")
    woutl_d = nc.dram_tensor("woutl", [D, D], BF16, kind="Internal")
    vwh_d = nc.dram_tensor("vwh", [FH * FE, D, FP], BF16, kind="Internal")
    vwl_d = nc.dram_tensor("vwl", [FH * FE, D, FP], BF16, kind="Internal")
    owh_d = nc.dram_tensor("owh", [FH * FE, FP, D], BF16, kind="Internal")
    owl_d = nc.dram_tensor("owl", [FH * FE, FP, D], BF16, kind="Internal")

    tap_d = {}
    tap_shapes = {
        "gv": [128, TT * 64], "go": [128, TT * 64],
        "qrot": [128, S], "krot": [128, S],
        "ctxa": [128, TT * 64], "scrA": [8, 128, S],
        "ctxf": [128, TT * 128], "scrF": [8, 128, S],
        "wqh": ([D, 3 * D], BF16), "vhb": ([128, TT * 130], BF16),
        "vlb": ([128, TT * 130], BF16), "cphb": ([128, 2048], BF16),
        "qhb": ([128, S], BF16), "qlb": ([128, S], BF16),
    }
    for t in taps:
        shp = tap_shapes[t]
        dt = F32
        if isinstance(shp, tuple):
            shp, dt = shp
        tap_d[t] = nc.dram_tensor("tap_" + t, shp, dt, kind="ExternalOutput")

    A = nc.alloc_sbuf_tensor
    xn_h = A("xn_h", [128, 8, S], BF16)
    xn_l = A("xn_l", [128, 8, S], BF16)
    cos_hi = A("cos_hi", [128, S], F32)
    sin_hi = A("sin_hi", [128, S], F32)
    gates_v = A("gates_v", [128, TT, 64], F32)
    gates_o = A("gates_o", [128, TT, 64], F32)
    selT_h = A("selT_h", [128, 2, 8, 64], BF16)
    selT_l = A("selT_l", [128, 2, 8, 64], BF16)
    masks_b = A("masks_b", [128, 4, 512], BF16)
    id_sb = A("id_sb", [128, 128], F32)
    iem9 = A("iem9", [128, 64], F32)
    rs_col = A("rs_col", [128, 1], F32)
    eps_col = A("eps_col", [128, 1], F32)
    ones_row = A("ones_row", [1, 128], F32)
    ones_b = A("ones_b", [128, 1], BF16)
    rs_sb = A("rs_sb", [1, 1], F32)
    sc = A("sc", [128, 32], F32)
    big1 = A("big1", [128, S], F32)
    big2 = A("big2", [128, S], F32)
    big3 = A("big3", [128, S], F32)
    big5 = A("big5", [128, 4096], F32)
    big8 = A("big8", [128, S], F32)
    qh = A("qh", [128, S], BF16)
    ql = A("ql", [128, S], BF16)
    kh = A("kh", [128, S], BF16)
    kl = A("kl", [128, S], BF16)
    vh = A("vh", [128, TT, 130], BF16)
    vl = A("vl", [128, TT, 130], BF16)
    wh = A("wh", [128, 4096], BF16)
    wl = A("wl", [128, 4096], BF16)
    whe_h = A("whe_h", [128, 4096], BF16)
    whe_l = A("whe_l", [128, 4096], BF16)
    pTh = A("pTh", [128, 3, 512], BF16)
    pTl = A("pTl", [128, 3, 512], BF16)
    sbuf_left = nc.sbuf_bytes_remaining

    gt = big3.ap()[:, 0:640]
    cos_lo = big5.ap()[:, 0:2048]
    sin_lo = big5.ap()[:, 2048:4096]
    vacc = big5.ap()[:, 0:2080].rearrange("p (t u) -> p t u", t=TT)
    ln_x = big5.ap()[:, 0:2048].rearrange("p (b u) -> p b u", b=2)
    ln_s1 = big5.ap()[:, 2048:4096].rearrange("p (b u) -> p b u", b=2)
    fin_in = big5.ap()[:, 0:2048].rearrange("p (b c u) -> p b c u", b=2, c=8)
    fin_x = big5.ap()[:, 2048:4096].rearrange("p (b u) -> p b u", b=2)
    fin_o = big5.ap()[:, 0:2048].rearrange("p (b u) -> p b u", b=2)
    raw = big8.ap()
    sq_dump = big8.ap()[:, 0:1024]
    sel_raw64 = big8.ap()[0:64, 0:1024]
    shuf = big3.ap()
    ctxn_f = big3.ap().rearrange("p (t u) -> p t u", t=TT)
    ctxn = big3.ap()[:, 0:1024].rearrange("p (t u) -> p t u", t=TT)
    evac = big1.ap()[:, 0:1024].rearrange("p (b u) -> p b u", b=2)
    wq_h = wh.ap()[:, 0:3072].rearrange("p (c j u) -> p c j u", c=8, j=3)
    wq_l = wl.ap()[:, 0:3072].rearrange("p (c j u) -> p c j u", c=8, j=3)
    wout_h = wh.ap()[:, 3072:4096].rearrange("p (c u) -> p c u", c=8)
    wout_l = wl.ap()[:, 3072:4096].rearrange("p (c u) -> p c u", c=8)
    qwT_h = wh.ap()[:, 0:1024].rearrange("p (c u) -> p c u", c=8)
    qwT_l = wl.ap()[:, 0:1024].rearrange("p (c u) -> p c u", c=8)
    kwT_h = wh.ap()[:, 1024:2048].rearrange("p (c u) -> p c u", c=8)
    kwT_l = wl.ap()[:, 1024:2048].rearrange("p (c u) -> p c u", c=8)
    vw_h = wh.ap().rearrange("p (c e u) -> p c e u", c=8, e=4)
    vw_l = wl.ap().rearrange("p (c e u) -> p c e u", c=8, e=4)
    ow_h = wh.ap()[:, 0:2048].rearrange("p (b e u) -> p b e u", b=2, e=8)
    ow_l = wl.ap()[:, 0:2048].rearrange("p (b e u) -> p b e u", b=2, e=8)
    wstg_h = wh.ap().rearrange("p (b u) -> p b u", b=4)
    wstg_l = wl.ap().rearrange("p (b u) -> p b u", b=4)
    cph = whe_h.ap()[:, 0:2048]
    cpl = whe_l.ap()[:, 0:2048]
    ctxB_h = whe_h.ap()[0:64, 2048:4096]
    ctxB_l = whe_l.ap()[0:64, 2048:4096]
    w_he_h = whe_h.ap().rearrange("p (e u) -> p e u", e=8)
    w_he_l = whe_l.ap().rearrange("p (e u) -> p e u", e=8)

    ps_a = nc.alloc_psum_tensor("ps_a", [128, 512], F32)
    ps_b = nc.alloc_psum_tensor("ps_b", [128, 512], F32)
    ps_big = nc.alloc_psum_tensor("ps_big", [128, 2, 512], F32)
    ps_ctx = nc.alloc_psum_tensor("ps_ctx", [128, 4, 512], F32)
    ps_bigf = ps_big.ap().rearrange("p a u -> p (a u)")

    p = Prog()

    def T(fn, inc=1):
        return p.op("tensor", fn, ((eT, inc),) if inc else ())

    def V(fn, inc=1):
        return p.op("vector", fn, ((eV, inc),) if inc else ())

    def Ac(fn, inc=1):
        return p.op("scalar", fn, ((eA, inc),) if inc else ())

    def G(fn, inc=1):
        return p.op("gpsimd", fn, ((eP, inc),) if inc else ())

    def dma(queue_eng, sem, out, in_, accum=False):
        def fn(eng, out=out, in_=in_, accum=accum):
            if accum:
                return eng.dma_start(out=out, in_=in_, accum_op=ALU.add)
            return eng.dma_start(out=out, in_=in_)
        return p.op(queue_eng, fn, ((sem, 16),))

    def join():
        for e in ENGS:
            own = OWN.get(e)
            for sm in ALL_SEMS:
                if sm != own:
                    p.wait(e, sm)

    def tap(name, src_ap):
        if name in tap_d:
            dma("sync", dO, tap_d[name].ap(), src_ap)

    def MM3(out, lh, rh, ll, rl, start, stop, inc=0):
        T(lambda t, out=out, lh=lh, rh=rh, start=start: t.matmul(
            out=out, lhsT=lh, rhs=rh, start=start, stop=False), inc=0)
        T(lambda t, out=out, lh=lh, rl=rl: t.matmul(
            out=out, lhsT=lh, rhs=rl, start=False, stop=False), inc=0)
        return T(lambda t, out=out, ll=ll, rh=rh, stop=stop: t.matmul(
            out=out, lhsT=ll, rhs=rh, start=False, stop=stop), inc=inc)

    def castpair(dh, dl, src, inc=1):
        ta = Ac(lambda a, dh=dh, src=src: a.copy(out=dh, in_=src))
        p.wait_tok("vector", *ta)
        return V(lambda v, dh=dh, dl=dl, src=src: v.tensor_tensor(
            out=dl, in0=src, in1=dh, op=ALU.subtract), inc=inc)

    dma("sync", dC, id_sb[:], ident_d[:])
    dma("sync", dC, cos_hi[:], cos_hi_d[:])
    dma("sync", dC, sin_hi[:], sin_hi_d[:])
    dma("sync", dC, masks_b[:], masks_d.ap().rearrange("o p u -> p o u"))
    dma("sync", dC, rs_sb[:], rs_d.ap().rearrange("(a b) -> a b", a=1))

    G(lambda g: g.iota(out=iem9[:], pattern=[[0, 8], [1, 8]], base=-9,
                       channel_multiplier=0, allow_small_or_imprecise_dtypes=True))
    G(lambda g: g.memset(eps_col[:], EPS))
    G(lambda g: g.memset(ones_row[:], 1.0))
    G(lambda g: g.memset(ones_b[:], 1.0))

    p.wait("tensor", dC)
    p.wait("tensor", eP)
    T(lambda t: t.matmul(out=ps_a[:, 0:1], lhsT=ones_row[:], rhs=rs_sb[:],
                         start=True, stop=True))
    p.wait("vector", eT)
    V(lambda v: v.tensor_copy(out=rs_col[:], in_=ps_a[:, 0:1]))

    for si, sd in enumerate((selv_d, selo_d)):
        p.wait("sync", eT)
        tk = dma("sync", dC, sel_raw64, sd[:])
        p.wait_tok("tensor", *tk)
        p.wait("tensor", eV)
        for c in range(8):
            ps = ps_a if c % 2 == 0 else ps_b
            if si > 0 or c >= 2:
                p.wait("tensor", eV)
                p.wait("tensor", eA)
            T(lambda t, c=c, ps=ps: t.transpose(
                out=ps[0:128, 0:64], in_=sel_raw64[:, c * 128:(c + 1) * 128],
                identity=id_sb[0:64, 0:64]))
            p.wait("scalar", eT)
            castpair(selT_h[:, si, c, :], selT_l[:, si, c, :], ps[0:128, 0:64])
        p.wait("tensor", eV)
        p.wait("tensor", eA)

    def st_wsplit(inter=()):
        p.lab = "wsplit"
        specs = []
        for (srcd, dh_d, dl_d, rows, cols) in (
                (wqkv_d, wqkvh_d, wqkvl_d, D, 3 * D),
                (wout_d, wouth_d, woutl_d, D, D)):
            sv = srcd.ap().rearrange("(c p) n -> p c n", p=128)
            hv = dh_d.ap().rearrange("(c p) n -> p c n", p=128)
            lv = dl_d.ap().rearrange("(c p) n -> p c n", p=128)
            nb = cols // 1024
            for c in range(rows // 128):
                for b in range(nb):
                    sl = (slice(None), c, slice(b * 1024, (b + 1) * 1024))
                    specs.append((sv[sl], hv[sl], lv[sl], 1024))
        for (srcd, dh_d, dl_d) in ((vw_d, vwh_d, vwl_d), (ow_d, owh_d, owl_d)):
            sv = srcd.ap().rearrange("e a u -> (e a) u").rearrange(
                "(c p) n -> p c n", p=128)
            hv = dh_d.ap().rearrange("e a u -> (e a) u").rearrange(
                "(c p) n -> p c n", p=128)
            lv = dl_d.ap().rearrange("e a u -> (e a) u", ).rearrange(
                "(c p) n -> p c n", p=128)
            C = sv.shape[1]
            N = sv.shape[2]
            grp = max(1, 1024 // N)
            for c0 in range(0, C, grp):
                sl = (slice(None), slice(c0, c0 + grp), slice(None))
                specs.append((sv[sl], hv[sl], lv[sl], grp * N))
        subTok = [None] * 4
        whTok = [None] * 4
        wlTok = [None] * 4
        inter_q = list(inter)
        for i, (s_ap, h_ap, l_ap, L) in enumerate(specs):
            par = i % 4
            rq = ("sync", dC) if par % 2 == 0 else ("gpsimd", dP1)
            stgb = (big1 if par < 2 else big2).ap()[:, (par % 2) * 1024:]
            if subTok[par] is not None:
                p.wait_tok(rq[0], subTok[par])
            tkr = dma(rq[0], rq[1], stgb[:, 0:L], s_ap)
            hv2 = wstg_h[:, par, 0:L]
            lv2 = wstg_l[:, par, 0:L]
            p.wait_tok("scalar", *tkr)
            if whTok[par] is not None:
                p.wait_tok("scalar", whTok[par])
            ta = Ac(lambda a, hv2=hv2, st=stgb[:, 0:L]: a.copy(out=hv2, in_=st))
            p.wait_tok("vector", *ta)
            if wlTok[par] is not None:
                p.wait_tok("vector", wlTok[par])
            tv = V(lambda v, hv2=hv2, lv2=lv2, st=stgb[:, 0:L]: v.tensor_tensor(
                out=lv2, in0=st, in1=hv2, op=ALU.subtract))
            subTok[par] = tv[0]
            whTok[par] = dma("scalar", dP2, h_ap, hv2)[0]
            p.wait_tok("scalar", *tv)
            wlTok[par] = dma("scalar", dP2, l_ap, lv2)[0]
            if inter_q and i % 10 == 9:
                inter_q.pop(0)()
        while inter_q:
            inter_q.pop(0)()
        tap("wqh", wqkvh_d.ap())

    def ln_tile(s, tt):
            p.lab = "ln"
            par = tt % 2
            o = par * 8
            p.wait("sync", eV)
            xt = ln_x[:, par, :]
            tkx = dma("sync", dX, xt, x_d[s, tt * 128:(tt + 1) * 128, :])
            p.wait_tok("vector", *tkx)
            p.wait("vector", eT)
            V(lambda v, xt=xt, o=o: v.reduce_sum(out=sc[:, o:o + 1], in_=xt, axis=AX.X))
            p.wait("vector", eV)
            V(lambda v, o=o: v.tensor_scalar(out=sc[:, o + 1:o + 2], in0=sc[:, o:o + 1],
                                             scalar1=1.0 / D, scalar2=None, op0=ALU.mult))
            p.wait("vector", eV)
            dev = ln_s1[:, par, :]
            tkd = V(lambda v, xt=xt, dev=dev, o=o: v.tensor_scalar(
                out=dev, in0=xt, scalar1=sc[:, o + 1:o + 2], scalar2=None,
                op0=ALU.subtract))
            p.wait_tok("scalar", *tkd)
            Ac(lambda a, dev=dev, o=o: a.activation(
                out=sq_dump, in_=dev, func=AF.Square,
                accum_out=sc[:, o + 2:o + 3]))
            p.wait("scalar", eA)
            tka = Ac(lambda a, o=o: a.activation(
                out=sc[:, o + 5:o + 6], in_=sc[:, o + 2:o + 3], func=AF.Sqrt,
                scale=1.0 / D, bias=eps_col[:]))
            p.wait_tok("vector", *tka)
            V(lambda v, o=o: v.reciprocal(out=sc[:, o + 3:o + 4], in_=sc[:, o + 5:o + 6]))
            p.wait("vector", eV)
            V(lambda v, o=o: v.tensor_tensor(out=sc[:, o + 4:o + 5],
                                             in0=sc[:, o + 1:o + 2],
                                             in1=sc[:, o + 3:o + 4], op=ALU.mult))
            p.wait("vector", eV)
            tkn = V(lambda v, xt=xt, dev=dev, o=o: v.tensor_scalar(
                out=dev, in0=xt, scalar1=sc[:, o + 3:o + 4],
                scalar2=sc[:, o + 4:o + 5], op0=ALU.mult, op1=ALU.subtract))
            p.wait_tok("tensor", *tkn)
            for c in range(8):
                ps = ps_a if c % 2 == 0 else ps_b
                if tt > 0 or c >= 2:
                    p.wait("tensor", eV)
                    p.wait("tensor", eA)
                T(lambda t, dev=dev, c=c, ps=ps: t.transpose(
                    out=ps[:, 0:128], in_=dev[:, c * 128:(c + 1) * 128],
                    identity=id_sb[:]))
                p.wait("scalar", eT)
                castpair(xn_h[:, c, tt * 128:(tt + 1) * 128],
                         xn_l[:, c, tt * 128:(tt + 1) * 128], ps[:, 0:128])

    def st_ln(s):
        p.lab = "ln"
        join()
        for tt in range(TT):
            ln_tile(s, tt)

    def st_gates(s):
        p.lab = "gates"
        if s == 0:
            p.wait("tensor", eA)
            p.wait("tensor", eV)
        else:
            join()
        for tt in range(TT):
            gates_tile(s, tt)
        if s == 0 and ("gv" in tap_d or "go" in tap_d):
            join()
            tap("gv", gates_v.ap().rearrange("p t u -> p (t u)"))
            tap("go", gates_o.ap().rearrange("p t u -> p (t u)"))

    def b8(ap):
        return ap.rearrange("p (g e) -> p g e", e=1).broadcast_to([128, 8, 8])

    IE3 = iem9.ap().rearrange("p (g e) -> p g e", e=8)

    def emit_top2(si, gout, tt):
            Pa = gt[:, si * 64:(si + 1) * 64]
            P3v = Pa.rearrange("p (g e) -> p g e", e=8)
            M1 = gt[:, 192:200]
            I1 = gt[:, 200:208]
            M2 = gt[:, 208:216]
            I2 = gt[:, 216:224]
            SS = gt[:, 224:232]
            R = gt[:, 232:240]
            W1 = gt[:, 240:248]
            W2 = gt[:, 248:256]
            E1 = gt[:, 256:320].rearrange("p (g e) -> p g e", e=8)
            OH1 = gt[:, 320:384].rearrange("p (g e) -> p g e", e=8)
            PM = gt[:, 384:448]
            PM3 = PM.rearrange("p (g e) -> p g e", e=8)
            E2 = gt[:, 448:512].rearrange("p (g e) -> p g e", e=8)
            OH2 = gt[:, 512:576].rearrange("p (g e) -> p g e", e=8)
            CT = gt[:, 576:640].rearrange("p (g e) -> p g e", e=8)
            CTf = gt[:, 576:640]
            steps = [
                lambda v: v.reduce_max(out=M1, in_=P3v, axis=AX.X),
                lambda v: v.tensor_tensor(out=E1, in0=P3v, in1=b8(M1), op=ALU.is_equal),
                lambda v: v.tensor_tensor(out=CT, in0=E1, in1=IE3, op=ALU.mult),
                lambda v: v.tensor_reduce(out=I1, in_=CT, axis=AX.X, op=ALU.min),
                lambda v: v.tensor_tensor(out=OH1, in0=IE3, in1=b8(I1), op=ALU.is_equal),
                lambda v: v.scalar_tensor_tensor(
                    out=PM, in0=OH1.rearrange("p g e -> p (g e)"), scalar=-2.0,
                    in1=Pa, op0=ALU.mult, op1=ALU.add),
                lambda v: v.reduce_max(out=M2, in_=PM3, axis=AX.X),
                lambda v: v.tensor_tensor(out=E2, in0=PM3, in1=b8(M2), op=ALU.is_equal),
                lambda v: v.tensor_tensor(out=CT, in0=E2, in1=IE3, op=ALU.mult),
                lambda v: v.tensor_reduce(out=I2, in_=CT, axis=AX.X, op=ALU.min),
                lambda v: v.tensor_tensor(out=OH2, in0=IE3, in1=b8(I2), op=ALU.is_equal),
                lambda v: v.tensor_tensor(out=SS, in0=M1, in1=M2, op=ALU.add),
                lambda v: v.tensor_scalar(out=SS, in0=SS, scalar1=1e-9,
                                          scalar2=None, op0=ALU.max),
                lambda v: v.reciprocal(out=R, in_=SS),
                lambda v: v.tensor_tensor(out=W1, in0=M1, in1=R, op=ALU.mult),
                lambda v: v.tensor_scalar(out=W1, in0=W1, scalar1=rs_col[:],
                                          scalar2=None, op0=ALU.mult),
                lambda v: v.tensor_tensor(out=W2, in0=M2, in1=R, op=ALU.mult),
                lambda v: v.tensor_scalar(out=W2, in0=W2, scalar1=rs_col[:],
                                          scalar2=None, op0=ALU.mult),
                lambda v: v.tensor_tensor(out=CT, in0=OH1, in1=b8(W1), op=ALU.mult),
                lambda v: v.tensor_tensor(out=PM3, in0=OH2, in1=b8(W2), op=ALU.mult),
                lambda v: v.tensor_tensor(out=gout[:, tt, :], in0=CTf, in1=PM, op=ALU.add),
            ]
            for st_fn in steps:
                p.wait("vector", eV)
                V(st_fn, inc=1)

    def gates_tile(s, tt):
            p.lab = "gates"
            p.wait("tensor", eA)
            p.wait("tensor", eV)
            for si in range(2):
                psg = ps_a if si == 0 else ps_b
                for c in range(8):
                    MM3(psg[:, 0:64],
                        xn_h[:, c, tt * 128:(tt + 1) * 128],
                        selT_h[:, si, c, :],
                        xn_l[:, c, tt * 128:(tt + 1) * 128],
                        selT_l[:, si, c, :],
                        start=(c == 0), stop=(c == 7), inc=(1 if c == 7 else 0))
            p.wait("scalar", eT)
            p.wait("scalar", eV)
            Ac(lambda a: a.activation(out=gt[:, 0:64], in_=ps_a[:, 0:64],
                                      func=AF.Sigmoid), inc=0)
            Ac(lambda a: a.activation(out=gt[:, 64:128], in_=ps_b[:, 0:64],
                                      func=AF.Sigmoid))
            p.wait("vector", eA)
            for si, gout in ((0, gates_v), (1, gates_o)):
                emit_top2(si, gout, tt)

    def rope_one(dst):
        V(lambda v, dst=dst: v.tensor_tensor(out=dst, in0=raw, in1=cos_hi[:], op=ALU.mult), inc=0)
        V(lambda v: v.tensor_tensor(out=raw, in0=raw, in1=cos_lo, op=ALU.mult), inc=0)
        V(lambda v, dst=dst: v.tensor_tensor(out=dst, in0=dst, in1=raw, op=ALU.add), inc=0)
        V(lambda v: v.tensor_tensor(out=raw, in0=shuf, in1=sin_hi[:], op=ALU.mult), inc=0)
        V(lambda v, dst=dst: v.tensor_tensor(out=dst, in0=dst, in1=raw, op=ALU.add), inc=0)
        V(lambda v: v.tensor_tensor(out=raw, in0=shuf, in1=sin_lo, op=ALU.mult), inc=0)
        V(lambda v, dst=dst: v.tensor_tensor(out=dst, in0=dst, in1=raw, op=ALU.add), inc=1)

    def attn_core(lKh, lKl, rQh, rQl, vAh, vAl, cM, sums, causal, ctx_out_fn,
                  psum4=None):
        ctxTok = [None, None, None]
        psumFree = psum4 if psum4 is not None else [None] * 4
        p.wait("tensor", eA)
        p.wait("tensor", eV)
        p.wait("scalar", eT)
        p.wait("vector", eT)

        def psS_of(kt):
            pb = kt % 4
            if pb < 2:
                return pb, ps_big.ap()[:, pb, :]
            if pb == 2:
                return pb, ps_ctx.ap()[:, 3, :]
            return pb, ps_a[:, :]

        def emit_S(qg, kt):
            pb, psS = psS_of(kt)
            if psumFree[pb] is not None:
                p.wait_tok("tensor", psumFree[pb])
            return MM3(psS, lKh(kt), rQh(qg), lKl(kt), rQl(qg),
                       start=True, stop=True, inc=1)

        DEPTH = 3
        for qg in range(NQG):
            nkt = (4 * qg + 4) if causal else TT
            p.wait("tensor", eV)
            sToks = {}
            for k0 in range(min(DEPTH, nkt)):
                sToks[k0] = emit_S(qg, k0)
            for kt in range(nkt):
                pb, psS = psS_of(kt)
                pr = kt % 3
                if kt + DEPTH < nkt:
                    sToks[kt + DEPTH] = emit_S(qg, kt + DEPTH)
                tb = sToks.pop(kt)
                lastS = tb
                o = kt - 4 * qg
                if causal and o >= 0:
                    p.wait_tok("vector", *tb)
                    lastS = V(lambda v, psS=psS, o=o: v.tensor_tensor(
                        out=psS, in0=psS, in1=masks_b[:, o, :], op=ALU.add))
                    p.wait_tok("scalar", *lastS)
                else:
                    p.wait_tok("scalar", *lastS)
                Ac(lambda a, psS=psS: a.activation(
                    out=psS, in_=psS, func=AF.Exp), inc=0)
                if ctxTok[pr] is not None:
                    p.wait_tok("scalar", ctxTok[pr])
                ta = Ac(lambda a, psS=psS, pr=pr: a.copy(out=pTh[:, pr, :], in_=psS))
                p.wait_tok("vector", *ta)
                if ctxTok[pr] is not None:
                    p.wait_tok("vector", ctxTok[pr])
                tl = V(lambda v, psS=psS, pr=pr: v.tensor_tensor(
                    out=pTl[:, pr, :], in0=psS, in1=pTh[:, pr, :], op=ALU.subtract))
                psumFree[pb] = tl[0]
                p.wait_tok("tensor", *ta)
                p.wait_tok("tensor", *tl)
                if kt == 0:
                    p.wait("tensor", eV)
                MM3(ps_ctx.ap()[0:cM, 0, :], vAh(kt), pTh[:, pr, :],
                    vAl(kt), pTl[:, pr, :],
                    start=(kt == 0), stop=(kt == nkt - 1),
                    inc=(0 if sums else 1))
                if sums:
                    T(lambda t, pr=pr, kt=kt, nkt=nkt: t.matmul(
                        out=ps_ctx.ap()[0:1, 1, :], lhsT=ones_b[:, 0:1],
                        rhs=pTh[:, pr, :], start=(kt == 0), stop=False), inc=0)
                    T(lambda t, pr=pr, kt=kt, nkt=nkt: t.matmul(
                        out=ps_ctx.ap()[0:1, 1, :], lhsT=ones_b[:, 0:1],
                        rhs=pTl[:, pr, :], start=False, stop=(kt == nkt - 1)), inc=1)
                ctxTok[pr] = p.tok(eT)
            ctx_out_fn(qg)

    def st_attn(s):
        p.lab = "attn"
        join()
        G(lambda g: g.memset(vh.ap()[:, :, 64:65], 1.0))
        G(lambda g: g.memset(vh.ap()[:, :, 129:130], 1.0))
        G(lambda g: g.memset(vl.ap()[:, :, 64:65], 0.0))
        G(lambda g: g.memset(vl.ap()[:, :, 129:130], 0.0))
        p.wait("tensor", eP)
        dma("sync", dW, cos_lo, cos_lo_d[:])
        dma("sync", dW, sin_lo, sin_lo_d[:])
        for hp in range(8):
            p.wait("sync", eT)
            tkw = None
            for j, c0 in enumerate((hp * 128, D + hp * 128, 2 * D + hp * 128)):
                dma("sync", dW, wq_h[:, :, j, :],
                    wqkvh_d.ap().rearrange("(c p) n -> p c n", p=128)[:, :, c0:c0 + 128])
                tkw = dma("sync", dW, wq_l[:, :, j, :],
                          wqkvl_d.ap().rearrange("(c p) n -> p c n", p=128)[:, :, c0:c0 + 128])
            p.wait_tok("tensor", *tkw)
            p.lab = "attn.proj"
            for j in (0, 1):
                scale = SCALE_A if j == 0 else 1.0
                p.wait("scalar", eV)
                p.wait("scalar", eT)
                for nt in range(NQG):
                    ps = ps_a if nt % 2 == 0 else ps_b
                    p.wait("tensor", eA)
                    for c in range(8):
                        MM3(ps[:, :], wq_h[:, c, j, :],
                            xn_h[:, c, nt * 512:(nt + 1) * 512],
                            wq_l[:, c, j, :],
                            xn_l[:, c, nt * 512:(nt + 1) * 512],
                            start=(c == 0), stop=(c == 7), inc=(1 if c == 7 else 0))
                    p.wait("scalar", eT)
                    Ac(lambda a, nt=nt, scale=scale, ps=ps: a.mul(
                        out=raw[:, nt * 512:(nt + 1) * 512], in_=ps[:, :], mul=scale))
                p.wait("gpsimd", eA)
                p.wait("gpsimd", eV)
                p.wait("gpsimd", eT)
                for (d0, s0) in ((0, 32), (32, 0), (64, 96), (96, 64)):
                    dma("gpsimd", dG, big3.ap()[d0:d0 + 32, :], raw[s0:s0 + 32, :])
                p.wait("vector", dG)
                p.wait("vector", dW)
                p.wait("vector", eA)
                p.wait("vector", eT)
                p.wait("vector", eP)
                dstb = big1 if j == 0 else big2
                rope_one(dstb.ap())
                p.wait("scalar", eV)
                p.wait("scalar", eT)
                if j == 0:
                    castpair(qh[:], ql[:], dstb.ap())
                else:
                    castpair(kh[:], kl[:], dstb.ap())
            p.lab = "attn.vproj"
            vTok = [None, None]
            for tt in range(TT):
                ps = ps_a if tt % 2 == 0 else ps_b
                if vTok[tt % 2] is not None:
                    p.wait_tok("tensor", vTok[tt % 2])
                for c in range(8):
                    MM3(ps[:, 0:128], xn_h[:, c, tt * 128:(tt + 1) * 128],
                        wq_h[:, c, 2, :],
                        xn_l[:, c, tt * 128:(tt + 1) * 128],
                        wq_l[:, c, 2, :],
                        start=(c == 0), stop=(c == 7), inc=(1 if c == 7 else 0))
                p.wait("scalar", eT)
                vh2 = vh.ap()[:, tt, :].rearrange("p (b u) -> p b u", b=2)[:, :, 0:64]
                vl2 = vl.ap()[:, tt, :].rearrange("p (b u) -> p b u", b=2)[:, :, 0:64]
                ps2 = ps[:, 0:128].rearrange("p (b u) -> p b u", b=2)
                vtk = castpair(vh2, vl2, ps2)
                vTok[tt % 2] = vtk[0]
            if s == 0 and hp == 0:
                join()
                tap("qrot", big1.ap())
                tap("krot", big2.ap())
                tap("qhb", qh.ap())
                tap("qlb", ql.ap())
                tap("vhb", vh.ap().rearrange("p t u -> p (t u)"))
                tap("vlb", vl.ap().rearrange("p t u -> p (t u)"))
            p.lab = "attn.core"
            for i in range(2):
                R0 = slice(64 * i, 64 * i + 64)
                cA = 65 * i

                def ctx_out(qg, i=i):
                    p.wait("vector", eT)
                    tv1 = V(lambda v: v.reciprocal(
                        out=big3.ap()[0:1, 0:512], in_=ps_ctx.ap()[64:65, 0, :]))
                    p.wait_tok("tensor", *tv1)
                    p.wait("tensor", eV)
                    p.wait("tensor", eA)
                    tr = T(lambda t: t.matmul(
                        out=ps_ctx.ap()[0:64, 1, :], lhsT=ones_row[0:1, 0:64],
                        rhs=big3.ap()[0:1, 0:512], start=True, stop=True))
                    p.wait_tok("scalar", *tr)
                    trc = Ac(lambda a: a.copy(out=big3.ap()[0:64, 1536:2048],
                                              in_=ps_ctx.ap()[0:64, 1, :]))
                    p.wait_tok("vector", *trc)
                    tv2 = V(lambda v: v.tensor_tensor(
                        out=big3.ap()[0:64, 512:1024], in0=ps_ctx.ap()[0:64, 0, :],
                        in1=big3.ap()[0:64, 1536:2048], op=ALU.mult))
                    p.wait("scalar", eV)
                    if i == 0:
                        castpair(cph[0:64, qg * 512:(qg + 1) * 512],
                                 cpl[0:64, qg * 512:(qg + 1) * 512],
                                 big3.ap()[0:64, 512:1024])
                    else:
                        castpair(ctxB_h[:, qg * 512:(qg + 1) * 512],
                                 ctxB_l[:, qg * 512:(qg + 1) * 512],
                                 big3.ap()[0:64, 512:1024])

                attn_core(
                    lKh=lambda kt, R0=R0: kh.ap()[R0, kt * 128:(kt + 1) * 128],
                    lKl=lambda kt, R0=R0: kl.ap()[R0, kt * 128:(kt + 1) * 128],
                    rQh=lambda qg, R0=R0: qh.ap()[R0, qg * 512:(qg + 1) * 512],
                    rQl=lambda qg, R0=R0: ql.ap()[R0, qg * 512:(qg + 1) * 512],
                    vAh=lambda kt, cA=cA: vh.ap()[:, kt, cA:cA + 65],
                    vAl=lambda kt, cA=cA: vl.ap()[:, kt, cA:cA + 65],
                    cM=65, sums=False, causal=True, ctx_out_fn=ctx_out)
                if i == 1:
                    p.wait("gpsimd", eV)
                    dma("gpsimd", dG, cph[64:128, :], ctxB_h[:, :])
                    dma("gpsimd", dG, cpl[64:128, :], ctxB_l[:, :])
            if s == 0 and hp == 0:
                join()
                tap("cphb", cph)
            p.lab = "attn.wout"
            p.wait("sync", eT)
            dma("sync", dW, wout_h,
                wouth_d[hp * 128:(hp + 1) * 128, :].rearrange("p (c u) -> p c u", c=8))
            tkwo = dma("sync", dW, wout_l,
                       woutl_d[hp * 128:(hp + 1) * 128, :].rearrange("p (c u) -> p c u", c=8))
            p.wait_tok("tensor", *tkwo)
            p.wait("tensor", dG)
            p.wait("tensor", eV)
            woTok = [None, None]
            for dc in range(8):
                for jj in range(4):
                    pr2 = (dc * 4 + jj) % 2
                    p.wait("tensor", eA)
                    MM3(ps_big[:, jj % 2, :], wout_h[:, dc, :],
                        cph[:, jj * 512:(jj + 1) * 512],
                        wout_l[:, dc, :],
                        cpl[:, jj * 512:(jj + 1) * 512],
                        start=True, stop=True, inc=1)
                    p.wait("scalar", eT)
                    if woTok[pr2] is not None:
                        p.wait_tok("scalar", woTok[pr2])
                    tae = Ac(lambda a, pr2=pr2, jj=jj: a.copy(
                        out=evac[:, pr2, :], in_=ps_big[:, jj % 2, :]))
                    p.wait_tok("gpsimd", *tae)
                    tkg = dma("gpsimd", dG, oscr_d[dc, :, jj * 512:(jj + 1) * 512],
                              evac[:, pr2, :], accum=(hp != 0))
                    woTok[pr2] = tkg[0]
        if s == 0 and "scrA" in tap_d:
            join()
            tap("scrA", oscr_d.ap())

    def st_ffn(s):
        p.lab = "ffn"
        join()
        G(lambda g: g.memset(vacc[:, :, 128:129], 1.0))
        p.wait("scalar", eP)
        p.wait("vector", eP)
        esel = big5.ap()[0:8, 2304:3328]
        dma("sync", dF, esel, esel_d[:])
        p.wait("tensor", dF)
        owTok = [None, None]
        for h in range(8):
            p.lab = "ffn.qkT"
            for wi, wd in enumerate((qw_d, kw_d)):
                p.wait("sync", eT)
                p.wait("sync", dG)
                tk = dma("sync", dF, raw[:, 0:1024], wd[h * 128:(h + 1) * 128, :])
                p.wait_tok("tensor", *tk)
                p.wait("tensor", eA)
                for c in range(8):
                    ps = ps_a if c % 2 == 0 else ps_b
                    p.wait("tensor", eV)
                    p.wait("tensor", eA)
                    T(lambda t, c=c, ps=ps: t.transpose(
                        out=ps[:, 0:128], in_=raw[:, c * 128:(c + 1) * 128],
                        identity=id_sb[:]))
                    p.wait("scalar", eT)
                    hv = (qwT_h if wi == 0 else kwT_h)[:, c, :]
                    lv = (qwT_l if wi == 0 else kwT_l)[:, c, :]
                    castpair(hv, lv, ps[:, 0:128])
            for tt in range(TT):
                ps = ps_a if tt % 2 == 0 else ps_b
                p.wait("tensor", eV)
                p.wait("tensor", eA)
                T(lambda t, tt=tt, ps=ps, h=h: t.transpose(
                    out=ps[0:8, 0:128], in_=gates_o[:, tt, h * 8:(h + 1) * 8],
                    identity=id_sb[:]))
                p.wait("vector", eT)
                dstg = big1 if tt < 8 else big2
                col = 1024 + (tt % 8) * 128
                V(lambda v, ps=ps, dstg=dstg, col=col: v.tensor_copy(
                    out=dstg.ap()[0:8, col:col + 128], in_=ps[0:8, 0:128]))
            p.lab = "ffn.proj"
            for wi in (0, 1):
                scale = SCALE_F if wi == 0 else 1.0
                for nt in range(NQG):
                    ps = ps_a if nt % 2 == 0 else ps_b
                    p.wait("tensor", eA)
                    p.wait("tensor", eV)
                    for c in range(8):
                        MM3(ps[:, :], (qwT_h if wi == 0 else kwT_h)[:, c, :],
                            xn_h[:, c, nt * 512:(nt + 1) * 512],
                            (qwT_l if wi == 0 else kwT_l)[:, c, :],
                            xn_l[:, c, nt * 512:(nt + 1) * 512],
                            start=(c == 0), stop=(c == 7), inc=(1 if c == 7 else 0))
                    p.wait("scalar", eT)
                    p.wait("scalar", eV)
                    hv = (qh if wi == 0 else kh).ap()[:, nt * 512:(nt + 1) * 512]
                    lv = (ql if wi == 0 else kl).ap()[:, nt * 512:(nt + 1) * 512]
                    ta = Ac(lambda a, hv=hv, ps=ps, scale=scale: a.mul(
                        out=hv, in_=ps[:, :], mul=scale))
                    p.wait_tok("vector", *ta)
                    V(lambda v, hv=hv, lv=lv, ps=ps, scale=scale: v.scalar_tensor_tensor(
                        out=lv, in0=ps[:, :], scalar=scale, in1=hv,
                        op0=ALU.mult, op1=ALU.subtract))
            p.lab = "ffn.xv"
            for half in range(2):
                p.wait("sync", eT)
                for e4 in range(4):
                    dma("sync", dF, vw_h[:, :, e4, :],
                        vwh_d[h * 8 + half * 4 + e4, :, :].rearrange("(c p) u -> p c u", p=128))
                    tk = dma("sync", dF, vw_l[:, :, e4, :],
                             vwl_d[h * 8 + half * 4 + e4, :, :].rearrange("(c p) u -> p c u", p=128))
                p.wait_tok("tensor", *tk)
                for tt in range(TT):
                    ps = ps_a if tt % 2 == 0 else ps_b
                    p.wait("tensor", eV)
                    for c in range(8):
                        MM3(ps[:, :], xn_h[:, c, tt * 128:(tt + 1) * 128],
                            vw_h[:, c, :, :],
                            xn_l[:, c, tt * 128:(tt + 1) * 128],
                            vw_l[:, c, :, :],
                            start=(c == 0), stop=(c == 7), inc=(1 if c == 7 else 0))
                    p.wait("vector", eT)
                    for e4 in range(4):
                        he = h * 8 + half * 4 + e4
                        if e4 > 0:
                            p.wait("vector", eV)
                        if half == 0 and e4 == 0:
                            V(lambda v, tt=tt, e4=e4, he=he, ps=ps: v.tensor_scalar(
                                out=vacc[:, tt, 0:128],
                                in0=ps[:, e4 * 128:(e4 + 1) * 128],
                                scalar1=gates_v[:, tt, he:he + 1], scalar2=None,
                                op0=ALU.mult), inc=1)
                        else:
                            V(lambda v, tt=tt, e4=e4, he=he, ps=ps: v.scalar_tensor_tensor(
                                out=vacc[:, tt, 0:128],
                                in0=ps[:, e4 * 128:(e4 + 1) * 128],
                                scalar=gates_v[:, tt, he:he + 1],
                                in1=vacc[:, tt, 0:128],
                                op0=ALU.mult, op1=ALU.add), inc=1)
            p.wait("scalar", eV)
            p.wait("scalar", eT)
            castpair(vh.ap()[:, :, 0:129], vl.ap()[:, :, 0:129], vacc[:, :, 0:129])

            p.lab = "ffn.core"

            def g_blk(qg):
                src = big1 if qg < 2 else big2
                c0 = 1024 + (qg % 2) * 512
                return src.ap()[0:8, c0:c0 + 512]

            pf = [None] * 4

            def ctx_out_f(qg):
                p.lab = "ffn.ow"
                p.wait("vector", eT)
                tv1 = V(lambda v: v.reciprocal(
                    out=big3.ap()[0:1, 0:512], in_=ps_ctx.ap()[0:1, 1, :]))
                p.wait_tok("tensor", *tv1)
                p.wait("tensor", eV)
                p.wait("tensor", eA)
                tr = T(lambda t: t.matmul(
                    out=ps_ctx.ap()[0:128, 2, :], lhsT=ones_row[0:1, 0:128],
                    rhs=big3.ap()[0:1, 0:512], start=True, stop=True))
                p.wait_tok("scalar", *tr)
                trc = Ac(lambda a: a.copy(out=big3.ap()[:, 1536:2048],
                                          in_=ps_ctx.ap()[0:128, 2, :]))
                p.wait_tok("vector", *trc)
                tcn = V(lambda v: v.tensor_tensor(
                    out=big3.ap()[:, 512:1024], in0=ps_ctx.ap()[0:128, 0, :],
                    in1=big3.ap()[:, 1536:2048], op=ALU.mult))
                p.wait("scalar", eT)
                for e in range(8):
                    p.wait("tensor", eV)
                    p.wait("tensor", eA)
                    tg = T(lambda t, e=e, qg=qg: t.matmul(
                        out=ps_ctx.ap()[0:128, 2, :], lhsT=esel[:, e * 128:(e + 1) * 128],
                        rhs=g_blk(qg), start=True, stop=True))
                    p.wait_tok("vector", *tg)
                    p.wait("vector", eA)
                    stg = big3.ap()[:, 1024:1536]
                    tv2 = V(lambda v, stg=stg: v.tensor_tensor(
                        out=stg, in0=big3.ap()[:, 512:1024],
                        in1=ps_ctx.ap()[0:128, 2, :], op=ALU.mult))
                    p.wait("scalar", eV)
                    castpair(w_he_h[:, e, :], w_he_l[:, e, :], stg)
                tc = qg
                for dc in range(8):
                    par = dc % 2
                    p.wait("sync", eT)
                    dma("sync", dF, ow_h[:, par, :, :],
                        owh_d[h * 8:(h + 1) * 8, :, dc * 128:(dc + 1) * 128]
                        .rearrange("e p u -> p e u"))
                    tk = dma("sync", dF, ow_l[:, par, :, :],
                             owl_d[h * 8:(h + 1) * 8, :, dc * 128:(dc + 1) * 128]
                             .rearrange("e p u -> p e u"))
                    p.wait_tok("tensor", *tk)
                    p.wait("tensor", eV)
                    ps = ps_a if par == 0 else ps_b
                    p.wait("tensor", eA)
                    for e in range(8):
                        MM3(ps[:, :], ow_h[:, par, e, :], w_he_h[:, e, :],
                            ow_l[:, par, e, :], w_he_l[:, e, :],
                            start=(e == 0), stop=(e == 7), inc=(1 if e == 7 else 0))
                    p.wait("scalar", eT)
                    pr2 = dc % 2
                    if owTok[pr2] is not None:
                        p.wait_tok("scalar", owTok[pr2])
                    tae = Ac(lambda a, pr2=pr2, ps=ps: a.copy(
                        out=evac[:, pr2, :], in_=ps[:, :]))
                    if par == 0:
                        pf[3] = tae[0]
                    p.wait_tok("gpsimd", *tae)
                    tkg = dma("gpsimd", dG, oscr_d[dc, :, tc * 512:(tc + 1) * 512],
                              evac[:, pr2, :], accum=True)
                    owTok[pr2] = tkg[0]
                p.lab = "ffn.core"

            attn_core(
                lKh=lambda kt: kh.ap()[:, kt * 128:(kt + 1) * 128],
                lKl=lambda kt: kl.ap()[:, kt * 128:(kt + 1) * 128],
                rQh=lambda qg: qh.ap()[:, qg * 512:(qg + 1) * 512],
                rQl=lambda qg: ql.ap()[:, qg * 512:(qg + 1) * 512],
                vAh=lambda kt: vh.ap()[:, kt, 0:128],
                vAl=lambda kt: vl.ap()[:, kt, 0:128],
                cM=128, sums=True, causal=False, ctx_out_fn=ctx_out_f,
                psum4=pf)
        if s == 0 and "scrF" in tap_d:
            join()
            tap("scrF", oscr_d.ap())

    def st_final(s):
        p.lab = "final"
        join()
        for tt in range(TT):
            par = tt % 2
            if tt >= 2:
                p.wait("sync", eT)
                p.wait("sync", eV)
            tk1 = dma("sync", dR, fin_in[:, par, :, :],
                      oscr_d[:, :, tt * 128:(tt + 1) * 128].rearrange("c p u -> p c u"))
            tk2 = dma("sync", dX, fin_x[:, par, :], x_d[s, tt * 128:(tt + 1) * 128, :])
            p.wait_tok("tensor", *tk1)
            if tt >= 1:
                p.wait("tensor", eV)
            for c in range(8):
                T(lambda t, par=par, c=c: t.transpose(
                    out=ps_bigf[:, c * 128:(c + 1) * 128],
                    in_=fin_in[:, par, c, :], identity=id_sb[:]),
                  inc=(1 if c == 7 else 0))
            p.wait_tok("vector", *tk2)
            p.wait("vector", eT)
            if tt >= 2:
                p.wait("vector", dO)
            tv = V(lambda v, par=par: v.tensor_tensor(
                out=fin_o[:, par, :], in0=ps_bigf, in1=fin_x[:, par, :], op=ALU.add))
            p.wait_tok("sync", *tv)
            dma("sync", dO, out_d[s, tt * 128:(tt + 1) * 128, :], fin_o[:, par, :])

    stages = ["ln", "gates", "attn", "ffn", "final"]
    upto_i = stages.index(upto)
    nseq_eff = NSEQ if upto_i >= 4 else 1

    def seq0_tile(tt):
        ln_tile(0, tt)
        if upto_i >= 1:
            gates_tile(0, tt)

    st_wsplit(inter=[lambda tt=tt: ln_tile(0, tt) for tt in range(TT)])
    for s in range(nseq_eff):
        if s > 0:
            st_ln(s)
        if upto_i >= 1:
            st_gates(s)
        if upto_i >= 2:
            st_attn(s)
        if upto_i >= 3:
            st_ffn(s)
        if upto_i >= 4:
            st_final(s)

    p.wait("sync", dO)
    p.wait("sync", dG)
    p.wait("gpsimd", dO)

    with contextlib.ExitStack() as es:
        semh = {name: es.enter_context(nc.semaphore(name)) for name in p.cnt}
        block = es.enter_context(nc.Block())

        def runner(items):
            def body(eng):
                cur = [None]
                stack = [None]

                def setlab(lab):
                    if lab != stack[0]:
                        if stack[0] is not None:
                            cm = stack.pop()
                            cm.__exit__(None, None, None)
                            stack.append(None)
                        if lab is not None:
                            cm = nc.named_scope(lab)
                            cm.__enter__()
                            stack[-1] = None
                            stack.append(cm)
                        stack[0] = lab

                open_cm = [None, None]
                def swl(lab):
                    if lab != open_cm[0]:
                        if open_cm[1] is not None:
                            open_cm[1].__exit__(None, None, None)
                            open_cm[1] = None
                        if lab:
                            open_cm[1] = nc.named_scope(lab)
                            open_cm[1].__enter__()
                        open_cm[0] = lab
                for kind, a, b, lab in items:
                    swl(lab)
                    if kind == "wait":
                        eng.wait_ge(semh[a], b)
                    else:
                        ins = a(eng)
                        for sm, amt in b:
                            ins = ins.then_inc(semh[sm], amt)
                swl(None)
            return body

        for ename in ENGS:
            getattr(block, ename)(runner(p.items[ename]))
    return nc, tap_d, sbuf_left


def kernel(L0_attn_x, L0_attn_w_qkv, L0_attn_w_out, L0_ffn_q_w, L0_ffn_k_w,
           L0_ffn_v_w, L0_ffn_o_w, L0_ffn_sel_v_w, L0_ffn_sel_o_w,
           L0_ffn_route_scale, _taps=(), _upto="final"):
    consts = host_consts()
    nc, tap_d, _ = build(taps=_taps, upto=_upto)
    in_maps = []
    for c in range(NCORES):
        m = {
            "x": np.ascontiguousarray(np.asarray(L0_attn_x)[c * NSEQ:(c + 1) * NSEQ]),
            "w_qkv": np.asarray(L0_attn_w_qkv), "w_out": np.asarray(L0_attn_w_out),
            "q_w": np.asarray(L0_ffn_q_w), "k_w": np.asarray(L0_ffn_k_w),
            "v_w": np.asarray(L0_ffn_v_w), "o_w": np.asarray(L0_ffn_o_w),
            "sel_v": np.asarray(L0_ffn_sel_v_w), "sel_o": np.asarray(L0_ffn_sel_o_w),
            "route_scale": np.asarray(L0_ffn_route_scale),
        }
        m.update(consts)
        in_maps.append(m)
    res = run_bass_kernel_spmd(nc, in_maps, core_ids=list(range(NCORES)))
    out = np.concatenate([r["out"] for r in res.results], axis=0)
    if _taps:
        return out, {t: res.results[0]["tap_" + t] for t in _taps}
    return out


# version 08
# speedup vs baseline: 1.3791x; 1.0910x over previous
import math
import contextlib
import numpy as np
import ml_dtypes

import concourse.bass as bass
import concourse.mybir as mybir
from concourse.bass_utils import run_bass_kernel_spmd

F32 = mybir.dt.float32
BF16 = mybir.dt.bfloat16
AF = mybir.ActivationFunctionType
ALU = mybir.AluOpType
AX = mybir.AxisListType

B, S, D = 16, 2048, 1024
NCORES = 8
NSEQ = B // NCORES
DH = 64
FH, FE, FP = 8, 8, 128
SCALE_A = 1.0 / math.sqrt(DH)
SCALE_F = 1.0 / math.sqrt(FP)
EPS = 1e-5
NEG = -1.0e30
TT = S // 128
NQG = S // 512
ENGS = ("sync", "vector", "scalar", "tensor", "gpsimd")


def host_consts():
    ident = np.eye(128, dtype=np.float32)
    j = np.arange(32, dtype=np.float64)
    invf = 1.0 / (10000.0 ** (j / 32.0))
    t = np.arange(S, dtype=np.float64)
    ang = invf[:, None] * t[None, :]
    cos_rep = np.tile(np.cos(ang), (4, 1))
    sin_rep = np.tile(np.sin(ang), (4, 1))
    sgn = np.ones((128, 1))
    sgn[0:32] = -1.0
    sgn[64:96] = -1.0
    sin_rep = sin_rep * sgn
    cos_hi = cos_rep.astype(np.float32)
    cos_lo = (cos_rep - cos_hi.astype(np.float64)).astype(np.float32)
    sin_hi = sin_rep.astype(np.float32)
    sin_lo = (sin_rep - sin_hi.astype(np.float64)).astype(np.float32)
    kp = np.arange(128)[:, None]
    qf = np.arange(512)[None, :]
    masks = np.zeros((4, 128, 512), dtype=np.float32)
    for o in range(4):
        masks[o] = np.where(kp <= qf - o * 128, 0.0, NEG)
    esel = np.kron(np.eye(8), np.ones((1, 128))).astype(np.float32)
    return {
        "c_ident": ident, "c_esel": esel,
        "c_cos_hi": cos_hi, "c_cos_lo": cos_lo,
        "c_sin_hi": sin_hi, "c_sin_lo": sin_lo,
        "c_masks": masks.astype(ml_dtypes.bfloat16),
    }


eT, eV, eA, eP = "eT", "eV", "eA", "eP"
dC, dX, dW, dF, dG, dO, dR = "dC", "dX", "dW", "dF", "dG", "dO", "dR"
dP1, dP2 = "dP1", "dP2"
ALL_SEMS = [eT, eV, eA, eP, dC, dX, dW, dF, dG, dO, dR, dP1, dP2]
OWN = {"tensor": eT, "vector": eV, "scalar": eA, "gpsimd": eP}


class Prog:

    def __init__(self):
        self.items = {e: [] for e in ENGS}
        self.cnt = {sm: 0 for sm in ALL_SEMS}
        self.lab = "pre"

    def op(self, eng, fn, incs=()):
        incs = tuple(incs)
        for sm, a in incs:
            self.cnt[sm] += a
        self.items[eng].append(("op", fn, incs, self.lab))
        return tuple((sm, self.cnt[sm]) for sm, _ in incs)

    def wait(self, eng, sem, v=None):
        if v is None:
            v = self.cnt[sem]
        if v > 0:
            self.items[eng].append(("wait", sem, v, self.lab))

    def tok(self, sem):
        return (sem, self.cnt[sem])

    def wait_tok(self, eng, *toks):
        for sem, v in toks:
            if v > 0:
                self.items[eng].append(("wait", sem, v, self.lab))


def build(taps=(), upto="final"):
    taps = set(taps)
    nc = bass.Bass("TRN2", target_bir_lowering=False, detect_race_conditions=False)

    x_d = nc.dram_tensor("x", [NSEQ, S, D], F32, kind="ExternalInput")
    wqkv_d = nc.dram_tensor("w_qkv", [D, 3 * D], F32, kind="ExternalInput")
    wout_d = nc.dram_tensor("w_out", [D, D], F32, kind="ExternalInput")
    qw_d = nc.dram_tensor("q_w", [FH * FP, D], F32, kind="ExternalInput")
    kw_d = nc.dram_tensor("k_w", [FH * FP, D], F32, kind="ExternalInput")
    vw_d = nc.dram_tensor("v_w", [FH * FE, D, FP], F32, kind="ExternalInput")
    ow_d = nc.dram_tensor("o_w", [FH * FE, FP, D], F32, kind="ExternalInput")
    selv_d = nc.dram_tensor("sel_v", [FH * FE, D], F32, kind="ExternalInput")
    selo_d = nc.dram_tensor("sel_o", [FH * FE, D], F32, kind="ExternalInput")
    rs_d = nc.dram_tensor("route_scale", [1], F32, kind="ExternalInput")
    ident_d = nc.dram_tensor("c_ident", [128, 128], F32, kind="ExternalInput")
    cos_hi_d = nc.dram_tensor("c_cos_hi", [128, S], F32, kind="ExternalInput")
    cos_lo_d = nc.dram_tensor("c_cos_lo", [128, S], F32, kind="ExternalInput")
    sin_hi_d = nc.dram_tensor("c_sin_hi", [128, S], F32, kind="ExternalInput")
    sin_lo_d = nc.dram_tensor("c_sin_lo", [128, S], F32, kind="ExternalInput")
    masks_d = nc.dram_tensor("c_masks", [4, 128, 512], BF16, kind="ExternalInput")
    esel_d = nc.dram_tensor("c_esel", [8, 1024], F32, kind="ExternalInput")
    out_d = nc.dram_tensor("out", [NSEQ, S, D], F32, kind="ExternalOutput")
    oscr_d = nc.dram_tensor("oscr", [8, 128, S], F32, kind="Internal")
    wqkvh_d = nc.dram_tensor("wqkvh", [D, 3 * D], BF16, kind="Internal")
    wqkvl_d = nc.dram_tensor("wqkvl", [D, 3 * D], BF16, kind="Internal")
    wouth_d = nc.dram_tensor("wouth", [D, D], BF16, kind="Internal")
    woutl_d = nc.dram_tensor("woutl", [D, D], BF16, kind="Internal")
    vwh_d = nc.dram_tensor("vwh", [FH * FE, D, FP], BF16, kind="Internal")
    vwl_d = nc.dram_tensor("vwl", [FH * FE, D, FP], BF16, kind="Internal")
    owh_d = nc.dram_tensor("owh", [FH * FE, FP, D], BF16, kind="Internal")
    owl_d = nc.dram_tensor("owl", [FH * FE, FP, D], BF16, kind="Internal")

    tap_d = {}
    tap_shapes = {
        "gv": [128, TT * 64], "go": [128, TT * 64],
        "qrot": [128, S], "krot": [128, S],
        "ctxa": [128, TT * 64], "scrA": [8, 128, S],
        "ctxf": [128, TT * 128], "scrF": [8, 128, S],
        "wqh": ([D, 3 * D], BF16), "vhb": ([128, TT * 130], BF16),
        "vlb": ([128, TT * 130], BF16), "cphb": ([128, 2048], BF16),
        "qhb": ([128, S], BF16), "qlb": ([128, S], BF16),
    }
    for t in taps:
        shp = tap_shapes[t]
        dt = F32
        if isinstance(shp, tuple):
            shp, dt = shp
        tap_d[t] = nc.dram_tensor("tap_" + t, shp, dt, kind="ExternalOutput")

    A = nc.alloc_sbuf_tensor
    xn_h = A("xn_h", [128, 8, S], BF16)
    xn_l = A("xn_l", [128, 8, S], BF16)
    cos_hi = A("cos_hi", [128, S], F32)
    sin_hi = A("sin_hi", [128, S], F32)
    gates_v = A("gates_v", [128, TT, 64], F32)
    gates_o = A("gates_o", [128, TT, 64], F32)
    selT_h = A("selT_h", [128, 2, 8, 64], BF16)
    selT_l = A("selT_l", [128, 2, 8, 64], BF16)
    masks_b = A("masks_b", [128, 4, 512], BF16)
    id_sb = A("id_sb", [128, 128], F32)
    iem9 = A("iem9", [128, 64], F32)
    rs_col = A("rs_col", [128, 1], F32)
    eps_col = A("eps_col", [128, 1], F32)
    ones_row = A("ones_row", [1, 128], F32)
    ones_b = A("ones_b", [128, 1], BF16)
    rs_sb = A("rs_sb", [1, 1], F32)
    sc = A("sc", [128, 32], F32)
    big1 = A("big1", [128, S], F32)
    big2 = A("big2", [128, S], F32)
    big3 = A("big3", [128, S], F32)
    big5 = A("big5", [128, 4096], F32)
    big8 = A("big8", [128, S], F32)
    qh = A("qh", [128, S], BF16)
    ql = A("ql", [128, S], BF16)
    kh = A("kh", [128, S], BF16)
    kl = A("kl", [128, S], BF16)
    vh = A("vh", [128, TT, 130], BF16)
    vl = A("vl", [128, TT, 130], BF16)
    wh = A("wh", [128, 4096], BF16)
    wl = A("wl", [128, 4096], BF16)
    whe_h = A("whe_h", [128, 4096], BF16)
    whe_l = A("whe_l", [128, 4096], BF16)
    pTh = A("pTh", [128, 3, 512], BF16)
    pTl = A("pTl", [128, 3, 512], BF16)
    sbuf_left = nc.sbuf_bytes_remaining

    gt = big3.ap()[:, 0:640]
    cos_lo = big5.ap()[:, 0:2048]
    sin_lo = big5.ap()[:, 2048:4096]
    vacc = big5.ap()[:, 0:2080].rearrange("p (t u) -> p t u", t=TT)
    ln_x = big5.ap()[:, 0:2048].rearrange("p (b u) -> p b u", b=2)
    ln_s1 = big5.ap()[:, 2048:4096].rearrange("p (b u) -> p b u", b=2)
    fin_in = big5.ap()[:, 0:2048].rearrange("p (b c u) -> p b c u", b=2, c=8)
    fin_x = big5.ap()[:, 2048:4096].rearrange("p (b u) -> p b u", b=2)
    fin_o = big5.ap()[:, 0:2048].rearrange("p (b u) -> p b u", b=2)
    raw = big8.ap()
    sq_dump = big8.ap()[:, 0:1024]
    sel_raw64 = big8.ap()[0:64, 0:1024]
    shuf = big3.ap()
    ctxn_f = big3.ap().rearrange("p (t u) -> p t u", t=TT)
    ctxn = big3.ap()[:, 0:1024].rearrange("p (t u) -> p t u", t=TT)
    evac = big1.ap()[:, 0:1024].rearrange("p (b u) -> p b u", b=2)
    wq_h = wh.ap()[:, 0:3072].rearrange("p (c j u) -> p c j u", c=8, j=3)
    wq_l = wl.ap()[:, 0:3072].rearrange("p (c j u) -> p c j u", c=8, j=3)
    wout_h = wh.ap()[:, 3072:4096].rearrange("p (c u) -> p c u", c=8)
    wout_l = wl.ap()[:, 3072:4096].rearrange("p (c u) -> p c u", c=8)
    qwT_h = wh.ap()[:, 0:1024].rearrange("p (c u) -> p c u", c=8)
    qwT_l = wl.ap()[:, 0:1024].rearrange("p (c u) -> p c u", c=8)
    kwT_h = wh.ap()[:, 1024:2048].rearrange("p (c u) -> p c u", c=8)
    kwT_l = wl.ap()[:, 1024:2048].rearrange("p (c u) -> p c u", c=8)
    vw_h = wh.ap().rearrange("p (c e u) -> p c e u", c=8, e=4)
    vw_l = wl.ap().rearrange("p (c e u) -> p c e u", c=8, e=4)
    ow_h = wh.ap()[:, 0:2048].rearrange("p (b e u) -> p b e u", b=2, e=8)
    ow_l = wl.ap()[:, 0:2048].rearrange("p (b e u) -> p b e u", b=2, e=8)
    wstg_h = wh.ap().rearrange("p (b u) -> p b u", b=4)
    wstg_l = wl.ap().rearrange("p (b u) -> p b u", b=4)
    cph = whe_h.ap()[:, 0:2048]
    cpl = whe_l.ap()[:, 0:2048]
    ctxB_h = whe_h.ap()[0:64, 2048:4096]
    ctxB_l = whe_l.ap()[0:64, 2048:4096]
    w_he_h = whe_h.ap().rearrange("p (e u) -> p e u", e=8)
    w_he_l = whe_l.ap().rearrange("p (e u) -> p e u", e=8)

    ps_a = nc.alloc_psum_tensor("ps_a", [128, 512], F32)
    ps_b = nc.alloc_psum_tensor("ps_b", [128, 512], F32)
    ps_big = nc.alloc_psum_tensor("ps_big", [128, 2, 512], F32)
    ps_ctx = nc.alloc_psum_tensor("ps_ctx", [128, 4, 512], F32)
    ps_bigf = ps_big.ap().rearrange("p a u -> p (a u)")

    p = Prog()

    def T(fn, inc=1):
        return p.op("tensor", fn, ((eT, inc),) if inc else ())

    def V(fn, inc=1):
        return p.op("vector", fn, ((eV, inc),) if inc else ())

    def Ac(fn, inc=1):
        return p.op("scalar", fn, ((eA, inc),) if inc else ())

    def G(fn, inc=1):
        return p.op("gpsimd", fn, ((eP, inc),) if inc else ())

    def dma(queue_eng, sem, out, in_, accum=False):
        def fn(eng, out=out, in_=in_, accum=accum):
            if accum:
                return eng.dma_start(out=out, in_=in_, accum_op=ALU.add)
            return eng.dma_start(out=out, in_=in_)
        return p.op(queue_eng, fn, ((sem, 16),))

    def join():
        for e in ENGS:
            own = OWN.get(e)
            for sm in ALL_SEMS:
                if sm != own:
                    p.wait(e, sm)

    def tap(name, src_ap):
        if name in tap_d:
            dma("sync", dO, tap_d[name].ap(), src_ap)

    def MM3(out, lh, rh, ll, rl, start, stop, inc=0):
        T(lambda t, out=out, lh=lh, rh=rh, start=start: t.matmul(
            out=out, lhsT=lh, rhs=rh, start=start, stop=False), inc=0)
        T(lambda t, out=out, lh=lh, rl=rl: t.matmul(
            out=out, lhsT=lh, rhs=rl, start=False, stop=False), inc=0)
        return T(lambda t, out=out, ll=ll, rh=rh, stop=stop: t.matmul(
            out=out, lhsT=ll, rhs=rh, start=False, stop=stop), inc=inc)

    def castpair(dh, dl, src, inc=1):
        ta = Ac(lambda a, dh=dh, src=src: a.copy(out=dh, in_=src))
        p.wait_tok("vector", *ta)
        return V(lambda v, dh=dh, dl=dl, src=src: v.tensor_tensor(
            out=dl, in0=src, in1=dh, op=ALU.subtract), inc=inc)

    dma("sync", dC, id_sb[:], ident_d[:])
    dma("sync", dC, cos_hi[:], cos_hi_d[:])
    dma("sync", dC, sin_hi[:], sin_hi_d[:])
    dma("sync", dC, masks_b[:], masks_d.ap().rearrange("o p u -> p o u"))
    dma("sync", dC, rs_sb[:], rs_d.ap().rearrange("(a b) -> a b", a=1))

    G(lambda g: g.iota(out=iem9[:], pattern=[[0, 8], [1, 8]], base=-9,
                       channel_multiplier=0, allow_small_or_imprecise_dtypes=True))
    G(lambda g: g.memset(eps_col[:], EPS))
    G(lambda g: g.memset(ones_row[:], 1.0))
    G(lambda g: g.memset(ones_b[:], 1.0))

    p.wait("tensor", dC)
    p.wait("tensor", eP)
    T(lambda t: t.matmul(out=ps_a[:, 0:1], lhsT=ones_row[:], rhs=rs_sb[:],
                         start=True, stop=True))
    p.wait("vector", eT)
    V(lambda v: v.tensor_copy(out=rs_col[:], in_=ps_a[:, 0:1]))

    for si, sd in enumerate((selv_d, selo_d)):
        p.wait("sync", eT)
        tk = dma("sync", dC, sel_raw64, sd[:])
        p.wait_tok("tensor", *tk)
        p.wait("tensor", eV)
        for c in range(8):
            ps = ps_a if c % 2 == 0 else ps_b
            if si > 0 or c >= 2:
                p.wait("tensor", eV)
                p.wait("tensor", eA)
            T(lambda t, c=c, ps=ps: t.transpose(
                out=ps[0:128, 0:64], in_=sel_raw64[:, c * 128:(c + 1) * 128],
                identity=id_sb[0:64, 0:64]))
            p.wait("scalar", eT)
            castpair(selT_h[:, si, c, :], selT_l[:, si, c, :], ps[0:128, 0:64])
        p.wait("tensor", eV)
        p.wait("tensor", eA)

    def st_wsplit(inter=()):
        p.lab = "wsplit"
        specs = []
        for (srcd, dh_d, dl_d, rows, cols) in (
                (wqkv_d, wqkvh_d, wqkvl_d, D, 3 * D),
                (wout_d, wouth_d, woutl_d, D, D)):
            sv = srcd.ap().rearrange("(c p) n -> p c n", p=128)
            hv = dh_d.ap().rearrange("(c p) n -> p c n", p=128)
            lv = dl_d.ap().rearrange("(c p) n -> p c n", p=128)
            nb = cols // 1024
            for c in range(rows // 128):
                for b in range(nb):
                    sl = (slice(None), c, slice(b * 1024, (b + 1) * 1024))
                    specs.append((sv[sl], hv[sl], lv[sl], 1024))
        for (srcd, dh_d, dl_d) in ((vw_d, vwh_d, vwl_d), (ow_d, owh_d, owl_d)):
            sv = srcd.ap().rearrange("e a u -> (e a) u").rearrange(
                "(c p) n -> p c n", p=128)
            hv = dh_d.ap().rearrange("e a u -> (e a) u").rearrange(
                "(c p) n -> p c n", p=128)
            lv = dl_d.ap().rearrange("e a u -> (e a) u", ).rearrange(
                "(c p) n -> p c n", p=128)
            C = sv.shape[1]
            N = sv.shape[2]
            grp = max(1, 1024 // N)
            for c0 in range(0, C, grp):
                sl = (slice(None), slice(c0, c0 + grp), slice(None))
                specs.append((sv[sl], hv[sl], lv[sl], grp * N))
        subTok = [None] * 4
        whTok = [None] * 4
        wlTok = [None] * 4
        inter_q = list(inter)
        for i, (s_ap, h_ap, l_ap, L) in enumerate(specs):
            par = i % 4
            rq = ("sync", dC) if par % 2 == 0 else ("gpsimd", dP1)
            stgb = (big1 if par < 2 else big2).ap()[:, (par % 2) * 1024:]
            if subTok[par] is not None:
                p.wait_tok(rq[0], subTok[par])
            tkr = dma(rq[0], rq[1], stgb[:, 0:L], s_ap)
            hv2 = wstg_h[:, par, 0:L]
            lv2 = wstg_l[:, par, 0:L]
            p.wait_tok("scalar", *tkr)
            if whTok[par] is not None:
                p.wait_tok("scalar", whTok[par])
            ta = Ac(lambda a, hv2=hv2, st=stgb[:, 0:L]: a.copy(out=hv2, in_=st))
            p.wait_tok("vector", *ta)
            if wlTok[par] is not None:
                p.wait_tok("vector", wlTok[par])
            tv = V(lambda v, hv2=hv2, lv2=lv2, st=stgb[:, 0:L]: v.tensor_tensor(
                out=lv2, in0=st, in1=hv2, op=ALU.subtract))
            subTok[par] = tv[0]
            whTok[par] = dma("scalar", dP2, h_ap, hv2)[0]
            p.wait_tok("scalar", *tv)
            wlTok[par] = dma("scalar", dP2, l_ap, lv2)[0]
            if inter_q and i % 10 == 9:
                inter_q.pop(0)()
        while inter_q:
            inter_q.pop(0)()
        tap("wqh", wqkvh_d.ap())

    lnTok = [None, None]

    def ln_tile(s, tt):
            p.lab = "ln"
            if tt == 0:
                lnTok[0] = lnTok[1] = None
            par = tt % 2
            o = par * 8
            p.wait("sync", eV)
            xt = ln_x[:, par, :]
            tkx = dma("sync", dX, xt, x_d[s, tt * 128:(tt + 1) * 128, :])
            p.wait_tok("vector", *tkx)
            p.wait("vector", eT)
            V(lambda v, xt=xt, o=o: v.reduce_sum(out=sc[:, o:o + 1], in_=xt, axis=AX.X))
            p.wait("vector", eV)
            V(lambda v, o=o: v.tensor_scalar(out=sc[:, o + 1:o + 2], in0=sc[:, o:o + 1],
                                             scalar1=1.0 / D, scalar2=None, op0=ALU.mult))
            p.wait("vector", eV)
            dev = ln_s1[:, par, :]
            tkd = V(lambda v, xt=xt, dev=dev, o=o: v.tensor_scalar(
                out=dev, in0=xt, scalar1=sc[:, o + 1:o + 2], scalar2=None,
                op0=ALU.subtract))
            p.wait_tok("scalar", *tkd)
            Ac(lambda a, dev=dev, o=o: a.activation(
                out=sq_dump, in_=dev, func=AF.Square,
                accum_out=sc[:, o + 2:o + 3]))
            p.wait("scalar", eA)
            tka = Ac(lambda a, o=o: a.activation(
                out=sc[:, o + 5:o + 6], in_=sc[:, o + 2:o + 3], func=AF.Sqrt,
                scale=1.0 / D, bias=eps_col[:]))
            p.wait_tok("vector", *tka)
            V(lambda v, o=o: v.reciprocal(out=sc[:, o + 3:o + 4], in_=sc[:, o + 5:o + 6]))
            p.wait("vector", eV)
            V(lambda v, o=o: v.tensor_tensor(out=sc[:, o + 4:o + 5],
                                             in0=sc[:, o + 1:o + 2],
                                             in1=sc[:, o + 3:o + 4], op=ALU.mult))
            p.wait("vector", eV)
            tkn = V(lambda v, xt=xt, dev=dev, o=o: v.tensor_scalar(
                out=dev, in0=xt, scalar1=sc[:, o + 3:o + 4],
                scalar2=sc[:, o + 4:o + 5], op0=ALU.mult, op1=ALU.subtract))
            p.wait_tok("tensor", *tkn)
            for c in range(8):
                ps = ps_a if c % 2 == 0 else ps_b
                if lnTok[c % 2] is not None:
                    p.wait_tok("tensor", lnTok[c % 2])
                elif tt > 0 or c >= 2:
                    p.wait("tensor", eV)
                    p.wait("tensor", eA)
                T(lambda t, dev=dev, c=c, ps=ps: t.transpose(
                    out=ps[:, 0:128], in_=dev[:, c * 128:(c + 1) * 128],
                    identity=id_sb[:]))
                p.wait("scalar", eT)
                lnTok[c % 2] = castpair(
                    xn_h[:, c, tt * 128:(tt + 1) * 128],
                    xn_l[:, c, tt * 128:(tt + 1) * 128], ps[:, 0:128])[0]

    def st_ln(s):
        p.lab = "ln"
        join()
        for tt in range(TT):
            ln_tile(s, tt)

    def st_gates(s):
        p.lab = "gates"
        if s == 0:
            p.wait("tensor", eA)
            p.wait("tensor", eV)
        else:
            join()
        for tt in range(TT):
            gates_tile(s, tt)
        if s == 0 and ("gv" in tap_d or "go" in tap_d):
            join()
            tap("gv", gates_v.ap().rearrange("p t u -> p (t u)"))
            tap("go", gates_o.ap().rearrange("p t u -> p (t u)"))

    def b8(ap):
        return ap.rearrange("p (g e) -> p g e", e=1).broadcast_to([128, 8, 8])

    IE3 = iem9.ap().rearrange("p (g e) -> p g e", e=8)

    def emit_top2(si, gout, tt):
            Pa = gt[:, si * 64:(si + 1) * 64]
            P3v = Pa.rearrange("p (g e) -> p g e", e=8)
            M1 = gt[:, 192:200]
            I1 = gt[:, 200:208]
            M2 = gt[:, 208:216]
            I2 = gt[:, 216:224]
            SS = gt[:, 224:232]
            R = gt[:, 232:240]
            W1 = gt[:, 240:248]
            W2 = gt[:, 248:256]
            E1 = gt[:, 256:320].rearrange("p (g e) -> p g e", e=8)
            OH1 = gt[:, 320:384].rearrange("p (g e) -> p g e", e=8)
            PM = gt[:, 384:448]
            PM3 = PM.rearrange("p (g e) -> p g e", e=8)
            E2 = gt[:, 448:512].rearrange("p (g e) -> p g e", e=8)
            OH2 = gt[:, 512:576].rearrange("p (g e) -> p g e", e=8)
            CT = gt[:, 576:640].rearrange("p (g e) -> p g e", e=8)
            CTf = gt[:, 576:640]
            steps = [
                lambda v: v.reduce_max(out=M1, in_=P3v, axis=AX.X),
                lambda v: v.tensor_tensor(out=E1, in0=P3v, in1=b8(M1), op=ALU.is_equal),
                lambda v: v.tensor_tensor(out=CT, in0=E1, in1=IE3, op=ALU.mult),
                lambda v: v.tensor_reduce(out=I1, in_=CT, axis=AX.X, op=ALU.min),
                lambda v: v.tensor_tensor(out=OH1, in0=IE3, in1=b8(I1), op=ALU.is_equal),
                lambda v: v.scalar_tensor_tensor(
                    out=PM, in0=OH1.rearrange("p g e -> p (g e)"), scalar=-2.0,
                    in1=Pa, op0=ALU.mult, op1=ALU.add),
                lambda v: v.reduce_max(out=M2, in_=PM3, axis=AX.X),
                lambda v: v.tensor_tensor(out=E2, in0=PM3, in1=b8(M2), op=ALU.is_equal),
                lambda v: v.tensor_tensor(out=CT, in0=E2, in1=IE3, op=ALU.mult),
                lambda v: v.tensor_reduce(out=I2, in_=CT, axis=AX.X, op=ALU.min),
                lambda v: v.tensor_tensor(out=OH2, in0=IE3, in1=b8(I2), op=ALU.is_equal),
                lambda v: v.tensor_tensor(out=SS, in0=M1, in1=M2, op=ALU.add),
                lambda v: v.tensor_scalar(out=SS, in0=SS, scalar1=1e-9,
                                          scalar2=None, op0=ALU.max),
                lambda v: v.reciprocal(out=R, in_=SS),
                lambda v: v.tensor_tensor(out=W1, in0=M1, in1=R, op=ALU.mult),
                lambda v: v.tensor_scalar(out=W1, in0=W1, scalar1=rs_col[:],
                                          scalar2=None, op0=ALU.mult),
                lambda v: v.tensor_tensor(out=W2, in0=M2, in1=R, op=ALU.mult),
                lambda v: v.tensor_scalar(out=W2, in0=W2, scalar1=rs_col[:],
                                          scalar2=None, op0=ALU.mult),
                lambda v: v.tensor_tensor(out=CT, in0=OH1, in1=b8(W1), op=ALU.mult),
                lambda v: v.tensor_tensor(out=PM3, in0=OH2, in1=b8(W2), op=ALU.mult),
                lambda v: v.tensor_tensor(out=gout[:, tt, :], in0=CTf, in1=PM, op=ALU.add),
            ]
            for st_fn in steps:
                p.wait("vector", eV)
                V(st_fn, inc=1)

    def gates_tile(s, tt):
            p.lab = "gates"
            p.wait("tensor", eA)
            p.wait("tensor", eV)
            for si in range(2):
                psg = ps_a if si == 0 else ps_b
                for c in range(8):
                    MM3(psg[:, 0:64],
                        xn_h[:, c, tt * 128:(tt + 1) * 128],
                        selT_h[:, si, c, :],
                        xn_l[:, c, tt * 128:(tt + 1) * 128],
                        selT_l[:, si, c, :],
                        start=(c == 0), stop=(c == 7), inc=(1 if c == 7 else 0))
            p.wait("scalar", eT)
            p.wait("scalar", eV)
            Ac(lambda a: a.activation(out=gt[:, 0:64], in_=ps_a[:, 0:64],
                                      func=AF.Sigmoid), inc=0)
            Ac(lambda a: a.activation(out=gt[:, 64:128], in_=ps_b[:, 0:64],
                                      func=AF.Sigmoid))
            p.wait("vector", eA)
            for si, gout in ((0, gates_v), (1, gates_o)):
                emit_top2(si, gout, tt)

    def rope_one(dst):
        V(lambda v, dst=dst: v.tensor_tensor(out=dst, in0=raw, in1=cos_hi[:], op=ALU.mult), inc=0)
        V(lambda v: v.tensor_tensor(out=raw, in0=raw, in1=cos_lo, op=ALU.mult), inc=0)
        V(lambda v, dst=dst: v.tensor_tensor(out=dst, in0=dst, in1=raw, op=ALU.add), inc=0)
        V(lambda v: v.tensor_tensor(out=raw, in0=shuf, in1=sin_hi[:], op=ALU.mult), inc=0)
        V(lambda v, dst=dst: v.tensor_tensor(out=dst, in0=dst, in1=raw, op=ALU.add), inc=0)
        V(lambda v: v.tensor_tensor(out=raw, in0=shuf, in1=sin_lo, op=ALU.mult), inc=0)
        V(lambda v, dst=dst: v.tensor_tensor(out=dst, in0=dst, in1=raw, op=ALU.add), inc=1)

    def attn_core(lKh, lKl, rQh, rQl, vAh, vAl, cM, sums, causal, ctx_out_fn,
                  psum4=None):
        ctxTok = [None, None, None]
        psumFree = psum4 if psum4 is not None else [None] * 4
        p.wait("tensor", eA)
        p.wait("tensor", eV)
        p.wait("scalar", eT)
        p.wait("vector", eT)

        def psS_of(kt):
            pb = kt % 4
            if pb < 2:
                return pb, ps_big.ap()[:, pb, :]
            if pb == 2:
                return pb, ps_ctx.ap()[:, 3, :]
            return pb, ps_a[:, :]

        def emit_S(qg, kt):
            pb, psS = psS_of(kt)
            if psumFree[pb] is not None:
                p.wait_tok("tensor", psumFree[pb])
            return MM3(psS, lKh(kt), rQh(qg), lKl(kt), rQl(qg),
                       start=True, stop=True, inc=1)

        DEPTH = 3
        for qg in range(NQG):
            nkt = (4 * qg + 4) if causal else TT
            p.wait("tensor", eV)
            sToks = {}
            for k0 in range(min(DEPTH, nkt)):
                sToks[k0] = emit_S(qg, k0)
            for kt in range(nkt):
                pb, psS = psS_of(kt)
                pr = kt % 3
                if kt + DEPTH < nkt:
                    sToks[kt + DEPTH] = emit_S(qg, kt + DEPTH)
                tb = sToks.pop(kt)
                lastS = tb
                o = kt - 4 * qg
                if causal and o >= 0:
                    p.wait_tok("vector", *tb)
                    lastS = V(lambda v, psS=psS, o=o: v.tensor_tensor(
                        out=psS, in0=psS, in1=masks_b[:, o, :], op=ALU.add))
                    p.wait_tok("scalar", *lastS)
                else:
                    p.wait_tok("scalar", *lastS)
                Ac(lambda a, psS=psS: a.activation(
                    out=psS, in_=psS, func=AF.Exp), inc=0)
                if ctxTok[pr] is not None:
                    p.wait_tok("scalar", ctxTok[pr])
                ta = Ac(lambda a, psS=psS, pr=pr: a.copy(out=pTh[:, pr, :], in_=psS))
                p.wait_tok("vector", *ta)
                if ctxTok[pr] is not None:
                    p.wait_tok("vector", ctxTok[pr])
                tl = V(lambda v, psS=psS, pr=pr: v.tensor_tensor(
                    out=pTl[:, pr, :], in0=psS, in1=pTh[:, pr, :], op=ALU.subtract))
                psumFree[pb] = tl[0]
                p.wait_tok("tensor", *ta)
                p.wait_tok("tensor", *tl)
                if kt == 0:
                    p.wait("tensor", eV)
                MM3(ps_ctx.ap()[0:cM, 0, :], vAh(kt), pTh[:, pr, :],
                    vAl(kt), pTl[:, pr, :],
                    start=(kt == 0), stop=(kt == nkt - 1),
                    inc=(0 if sums else 1))
                if sums:
                    T(lambda t, pr=pr, kt=kt, nkt=nkt: t.matmul(
                        out=ps_ctx.ap()[0:1, 1, :], lhsT=ones_b[:, 0:1],
                        rhs=pTh[:, pr, :], start=(kt == 0), stop=False), inc=0)
                    T(lambda t, pr=pr, kt=kt, nkt=nkt: t.matmul(
                        out=ps_ctx.ap()[0:1, 1, :], lhsT=ones_b[:, 0:1],
                        rhs=pTl[:, pr, :], start=False, stop=(kt == nkt - 1)), inc=1)
                ctxTok[pr] = p.tok(eT)
            ctx_out_fn(qg)

    def st_attn(s):
        p.lab = "attn"
        join()
        G(lambda g: g.memset(vh.ap()[:, :, 64:65], 1.0))
        G(lambda g: g.memset(vh.ap()[:, :, 129:130], 1.0))
        G(lambda g: g.memset(vl.ap()[:, :, 64:65], 0.0))
        G(lambda g: g.memset(vl.ap()[:, :, 129:130], 0.0))
        p.wait("tensor", eP)
        dma("sync", dW, cos_lo, cos_lo_d[:])
        dma("sync", dW, sin_lo, sin_lo_d[:])
        for hp in range(8):
            p.wait("sync", eT)
            tkw = None
            for j, c0 in enumerate((hp * 128, D + hp * 128, 2 * D + hp * 128)):
                dma("sync", dW, wq_h[:, :, j, :],
                    wqkvh_d.ap().rearrange("(c p) n -> p c n", p=128)[:, :, c0:c0 + 128])
                tkw = dma("sync", dW, wq_l[:, :, j, :],
                          wqkvl_d.ap().rearrange("(c p) n -> p c n", p=128)[:, :, c0:c0 + 128])
            p.wait_tok("tensor", *tkw)
            p.lab = "attn.proj"
            qkTok = [None, None]
            p.wait("tensor", eV)
            p.wait("tensor", eA)
            for j in (0, 1):
                scale = SCALE_A if j == 0 else 1.0
                p.wait("scalar", eV)
                p.wait("scalar", eT)
                for nt in range(NQG):
                    ps = ps_a if nt % 2 == 0 else ps_b
                    if qkTok[nt % 2] is not None:
                        p.wait_tok("tensor", qkTok[nt % 2])
                    for c in range(8):
                        MM3(ps[:, :], wq_h[:, c, j, :],
                            xn_h[:, c, nt * 512:(nt + 1) * 512],
                            wq_l[:, c, j, :],
                            xn_l[:, c, nt * 512:(nt + 1) * 512],
                            start=(c == 0), stop=(c == 7), inc=(1 if c == 7 else 0))
                    p.wait("scalar", eT)
                    tqa = Ac(lambda a, nt=nt, scale=scale, ps=ps: a.mul(
                        out=raw[:, nt * 512:(nt + 1) * 512], in_=ps[:, :], mul=scale))
                    qkTok[nt % 2] = tqa[0]
                p.wait("gpsimd", eA)
                p.wait("gpsimd", eV)
                p.wait("gpsimd", eT)
                for (d0, s0) in ((0, 32), (32, 0), (64, 96), (96, 64)):
                    dma("gpsimd", dG, big3.ap()[d0:d0 + 32, :], raw[s0:s0 + 32, :])
                p.wait("vector", dG)
                p.wait("vector", dW)
                p.wait("vector", eA)
                p.wait("vector", eT)
                p.wait("vector", eP)
                dstb = big1 if j == 0 else big2
                rope_one(dstb.ap())
                p.wait("scalar", eV)
                p.wait("scalar", eT)
                if j == 0:
                    castpair(qh[:], ql[:], dstb.ap())
                else:
                    castpair(kh[:], kl[:], dstb.ap())
            p.lab = "attn.vproj"
            vTok = [qkTok[0], qkTok[1]]
            for tt in range(TT):
                ps = ps_a if tt % 2 == 0 else ps_b
                if vTok[tt % 2] is not None:
                    p.wait_tok("tensor", vTok[tt % 2])
                for c in range(8):
                    MM3(ps[:, 0:128], xn_h[:, c, tt * 128:(tt + 1) * 128],
                        wq_h[:, c, 2, :],
                        xn_l[:, c, tt * 128:(tt + 1) * 128],
                        wq_l[:, c, 2, :],
                        start=(c == 0), stop=(c == 7), inc=(1 if c == 7 else 0))
                p.wait("scalar", eT)
                vh2 = vh.ap()[:, tt, :].rearrange("p (b u) -> p b u", b=2)[:, :, 0:64]
                vl2 = vl.ap()[:, tt, :].rearrange("p (b u) -> p b u", b=2)[:, :, 0:64]
                ps2 = ps[:, 0:128].rearrange("p (b u) -> p b u", b=2)
                vtk = castpair(vh2, vl2, ps2)
                vTok[tt % 2] = vtk[0]
            if s == 0 and hp == 0:
                join()
                tap("qrot", big1.ap())
                tap("krot", big2.ap())
                tap("qhb", qh.ap())
                tap("qlb", ql.ap())
                tap("vhb", vh.ap().rearrange("p t u -> p (t u)"))
                tap("vlb", vl.ap().rearrange("p t u -> p (t u)"))
            p.lab = "attn.core"
            for i in range(2):
                R0 = slice(64 * i, 64 * i + 64)
                cA = 65 * i

                def ctx_out(qg, i=i):
                    p.wait("vector", eT)
                    tv1 = V(lambda v: v.reciprocal(
                        out=big3.ap()[0:1, 0:512], in_=ps_ctx.ap()[64:65, 0, :]))
                    p.wait_tok("tensor", *tv1)
                    p.wait("tensor", eV)
                    p.wait("tensor", eA)
                    tr = T(lambda t: t.matmul(
                        out=ps_ctx.ap()[0:64, 1, :], lhsT=ones_row[0:1, 0:64],
                        rhs=big3.ap()[0:1, 0:512], start=True, stop=True))
                    p.wait_tok("scalar", *tr)
                    trc = Ac(lambda a: a.copy(out=big3.ap()[0:64, 1536:2048],
                                              in_=ps_ctx.ap()[0:64, 1, :]))
                    p.wait_tok("vector", *trc)
                    tv2 = V(lambda v: v.tensor_tensor(
                        out=big3.ap()[0:64, 512:1024], in0=ps_ctx.ap()[0:64, 0, :],
                        in1=big3.ap()[0:64, 1536:2048], op=ALU.mult))
                    p.wait("scalar", eV)
                    if i == 0:
                        castpair(cph[0:64, qg * 512:(qg + 1) * 512],
                                 cpl[0:64, qg * 512:(qg + 1) * 512],
                                 big3.ap()[0:64, 512:1024])
                    else:
                        castpair(ctxB_h[:, qg * 512:(qg + 1) * 512],
                                 ctxB_l[:, qg * 512:(qg + 1) * 512],
                                 big3.ap()[0:64, 512:1024])

                attn_core(
                    lKh=lambda kt, R0=R0: kh.ap()[R0, kt * 128:(kt + 1) * 128],
                    lKl=lambda kt, R0=R0: kl.ap()[R0, kt * 128:(kt + 1) * 128],
                    rQh=lambda qg, R0=R0: qh.ap()[R0, qg * 512:(qg + 1) * 512],
                    rQl=lambda qg, R0=R0: ql.ap()[R0, qg * 512:(qg + 1) * 512],
                    vAh=lambda kt, cA=cA: vh.ap()[:, kt, cA:cA + 65],
                    vAl=lambda kt, cA=cA: vl.ap()[:, kt, cA:cA + 65],
                    cM=65, sums=False, causal=True, ctx_out_fn=ctx_out)
                if i == 1:
                    p.wait("gpsimd", eV)
                    dma("gpsimd", dG, cph[64:128, :], ctxB_h[:, :])
                    dma("gpsimd", dG, cpl[64:128, :], ctxB_l[:, :])
            if s == 0 and hp == 0:
                join()
                tap("cphb", cph)
            p.lab = "attn.wout"
            p.wait("sync", eT)
            dma("sync", dW, wout_h,
                wouth_d[hp * 128:(hp + 1) * 128, :].rearrange("p (c u) -> p c u", c=8))
            tkwo = dma("sync", dW, wout_l,
                       woutl_d[hp * 128:(hp + 1) * 128, :].rearrange("p (c u) -> p c u", c=8))
            p.wait_tok("tensor", *tkwo)
            p.wait("tensor", dG)
            p.wait("tensor", eV)
            woTok = [None, None]
            woPs = [None, None]
            for dc in range(8):
                for jj in range(4):
                    pr2 = (dc * 4 + jj) % 2
                    if woPs[jj % 2] is not None:
                        p.wait_tok("tensor", woPs[jj % 2])
                    else:
                        p.wait("tensor", eA)
                    MM3(ps_big[:, jj % 2, :], wout_h[:, dc, :],
                        cph[:, jj * 512:(jj + 1) * 512],
                        wout_l[:, dc, :],
                        cpl[:, jj * 512:(jj + 1) * 512],
                        start=True, stop=True, inc=1)
                    p.wait("scalar", eT)
                    if woTok[pr2] is not None:
                        p.wait_tok("scalar", woTok[pr2])
                    tae = Ac(lambda a, pr2=pr2, jj=jj: a.copy(
                        out=evac[:, pr2, :], in_=ps_big[:, jj % 2, :]))
                    woPs[jj % 2] = tae[0]
                    p.wait_tok("gpsimd", *tae)
                    tkg = dma("gpsimd", dG, oscr_d[dc, :, jj * 512:(jj + 1) * 512],
                              evac[:, pr2, :], accum=(hp != 0))
                    woTok[pr2] = tkg[0]
        if s == 0 and "scrA" in tap_d:
            join()
            tap("scrA", oscr_d.ap())

    def st_ffn(s):
        p.lab = "ffn"
        join()
        G(lambda g: g.memset(vacc[:, :, 128:129], 1.0))
        p.wait("scalar", eP)
        p.wait("vector", eP)
        esel = big5.ap()[0:8, 2304:3328]
        dma("sync", dF, esel, esel_d[:])
        p.wait("tensor", dF)
        owTok = [None, None]
        owPs = [None, None]
        for h in range(8):
            p.lab = "ffn.qkT"
            tpTok = [None, None]
            for wi, wd in enumerate((qw_d, kw_d)):
                p.wait("sync", eT)
                p.wait("sync", dG)
                tk = dma("sync", dF, raw[:, 0:1024], wd[h * 128:(h + 1) * 128, :])
                p.wait_tok("tensor", *tk)
                for c in range(8):
                    ps = ps_a if c % 2 == 0 else ps_b
                    if tpTok[c % 2] is not None:
                        p.wait_tok("tensor", tpTok[c % 2])
                    else:
                        p.wait("tensor", eV)
                        p.wait("tensor", eA)
                    T(lambda t, c=c, ps=ps: t.transpose(
                        out=ps[:, 0:128], in_=raw[:, c * 128:(c + 1) * 128],
                        identity=id_sb[:]))
                    p.wait("scalar", eT)
                    hv = (qwT_h if wi == 0 else kwT_h)[:, c, :]
                    lv = (qwT_l if wi == 0 else kwT_l)[:, c, :]
                    tpTok[c % 2] = castpair(hv, lv, ps[:, 0:128])[0]
            for tt in range(TT):
                ps = ps_a if tt % 2 == 0 else ps_b
                if tpTok[tt % 2] is not None:
                    p.wait_tok("tensor", tpTok[tt % 2])
                else:
                    p.wait("tensor", eV)
                    p.wait("tensor", eA)
                T(lambda t, tt=tt, ps=ps, h=h: t.transpose(
                    out=ps[0:8, 0:128], in_=gates_o[:, tt, h * 8:(h + 1) * 8],
                    identity=id_sb[:]))
                p.wait("vector", eT)
                dstg = big1 if tt < 8 else big2
                col = 1024 + (tt % 8) * 128
                tpTok[tt % 2] = V(lambda v, ps=ps, dstg=dstg, col=col: v.tensor_copy(
                    out=dstg.ap()[0:8, col:col + 128], in_=ps[0:8, 0:128]))[0]
            p.lab = "ffn.proj"
            prjTok = [tpTok[0], tpTok[1]]
            for wi in (0, 1):
                scale = SCALE_F if wi == 0 else 1.0
                for nt in range(NQG):
                    ps = ps_a if nt % 2 == 0 else ps_b
                    if prjTok[nt % 2] is not None:
                        p.wait_tok("tensor", prjTok[nt % 2])
                    else:
                        p.wait("tensor", eA)
                        p.wait("tensor", eV)
                    for c in range(8):
                        MM3(ps[:, :], (qwT_h if wi == 0 else kwT_h)[:, c, :],
                            xn_h[:, c, nt * 512:(nt + 1) * 512],
                            (qwT_l if wi == 0 else kwT_l)[:, c, :],
                            xn_l[:, c, nt * 512:(nt + 1) * 512],
                            start=(c == 0), stop=(c == 7), inc=(1 if c == 7 else 0))
                    p.wait("scalar", eT)
                    p.wait("scalar", eV)
                    hv = (qh if wi == 0 else kh).ap()[:, nt * 512:(nt + 1) * 512]
                    lv = (ql if wi == 0 else kl).ap()[:, nt * 512:(nt + 1) * 512]
                    ta = Ac(lambda a, hv=hv, ps=ps, scale=scale: a.mul(
                        out=hv, in_=ps[:, :], mul=scale))
                    p.wait_tok("vector", *ta)
                    prjTok[nt % 2] = V(
                        lambda v, hv=hv, lv=lv, ps=ps, scale=scale: v.scalar_tensor_tensor(
                            out=lv, in0=ps[:, :], scalar=scale, in1=hv,
                            op0=ALU.mult, op1=ALU.subtract))[0]
            p.lab = "ffn.xv"
            xvTok = [prjTok[0], prjTok[1]]
            for half in range(2):
                p.wait("sync", eT)
                for e4 in range(4):
                    dma("sync", dF, vw_h[:, :, e4, :],
                        vwh_d[h * 8 + half * 4 + e4, :, :].rearrange("(c p) u -> p c u", p=128))
                    tk = dma("sync", dF, vw_l[:, :, e4, :],
                             vwl_d[h * 8 + half * 4 + e4, :, :].rearrange("(c p) u -> p c u", p=128))
                p.wait_tok("tensor", *tk)
                for tt in range(TT):
                    ps = ps_a if tt % 2 == 0 else ps_b
                    if xvTok[tt % 2] is not None:
                        p.wait_tok("tensor", xvTok[tt % 2])
                    for c in range(8):
                        MM3(ps[:, :], xn_h[:, c, tt * 128:(tt + 1) * 128],
                            vw_h[:, c, :, :],
                            xn_l[:, c, tt * 128:(tt + 1) * 128],
                            vw_l[:, c, :, :],
                            start=(c == 0), stop=(c == 7), inc=(1 if c == 7 else 0))
                    p.wait("vector", eT)
                    for e4 in range(4):
                        he = h * 8 + half * 4 + e4
                        if e4 > 0:
                            p.wait("vector", eV)
                        if half == 0 and e4 == 0:
                            V(lambda v, tt=tt, e4=e4, he=he, ps=ps: v.tensor_scalar(
                                out=vacc[:, tt, 0:128],
                                in0=ps[:, e4 * 128:(e4 + 1) * 128],
                                scalar1=gates_v[:, tt, he:he + 1], scalar2=None,
                                op0=ALU.mult), inc=1)
                        else:
                            xvTok[tt % 2] = V(
                                lambda v, tt=tt, e4=e4, he=he, ps=ps: v.scalar_tensor_tensor(
                                    out=vacc[:, tt, 0:128],
                                    in0=ps[:, e4 * 128:(e4 + 1) * 128],
                                    scalar=gates_v[:, tt, he:he + 1],
                                    in1=vacc[:, tt, 0:128],
                                    op0=ALU.mult, op1=ALU.add), inc=1)[0]
            p.wait("scalar", eV)
            p.wait("scalar", eT)
            castpair(vh.ap()[:, :, 0:129], vl.ap()[:, :, 0:129], vacc[:, :, 0:129])

            p.lab = "ffn.core"

            def g_blk(qg):
                src = big1 if qg < 2 else big2
                c0 = 1024 + (qg % 2) * 512
                return src.ap()[0:8, c0:c0 + 512]

            pf = [None] * 4

            def ctx_out_f(qg):
                p.lab = "ffn.ow"
                p.wait("vector", eT)
                tv1 = V(lambda v: v.reciprocal(
                    out=big3.ap()[0:1, 0:512], in_=ps_ctx.ap()[0:1, 1, :]))
                p.wait_tok("tensor", *tv1)
                p.wait("tensor", eV)
                p.wait("tensor", eA)
                tr = T(lambda t: t.matmul(
                    out=ps_ctx.ap()[0:128, 2, :], lhsT=ones_row[0:1, 0:128],
                    rhs=big3.ap()[0:1, 0:512], start=True, stop=True))
                p.wait_tok("scalar", *tr)
                trc = Ac(lambda a: a.copy(out=big3.ap()[:, 1536:2048],
                                          in_=ps_ctx.ap()[0:128, 2, :]))
                p.wait_tok("vector", *trc)
                tcn = V(lambda v: v.tensor_tensor(
                    out=big3.ap()[:, 512:1024], in0=ps_ctx.ap()[0:128, 0, :],
                    in1=big3.ap()[:, 1536:2048], op=ALU.mult))
                p.wait("scalar", eT)
                for e in range(8):
                    p.wait("tensor", eV)
                    p.wait("tensor", eA)
                    tg = T(lambda t, e=e, qg=qg: t.matmul(
                        out=ps_ctx.ap()[0:128, 2, :], lhsT=esel[:, e * 128:(e + 1) * 128],
                        rhs=g_blk(qg), start=True, stop=True))
                    p.wait_tok("vector", *tg)
                    p.wait("vector", eA)
                    stg = big3.ap()[:, 1024:1536]
                    tv2 = V(lambda v, stg=stg: v.tensor_tensor(
                        out=stg, in0=big3.ap()[:, 512:1024],
                        in1=ps_ctx.ap()[0:128, 2, :], op=ALU.mult))
                    p.wait("scalar", eV)
                    castpair(w_he_h[:, e, :], w_he_l[:, e, :], stg)
                tc = qg
                for dc in range(8):
                    par = dc % 2
                    p.wait("sync", eT)
                    dma("sync", dF, ow_h[:, par, :, :],
                        owh_d[h * 8:(h + 1) * 8, :, dc * 128:(dc + 1) * 128]
                        .rearrange("e p u -> p e u"))
                    tk = dma("sync", dF, ow_l[:, par, :, :],
                             owl_d[h * 8:(h + 1) * 8, :, dc * 128:(dc + 1) * 128]
                             .rearrange("e p u -> p e u"))
                    p.wait_tok("tensor", *tk)
                    ps = ps_a if par == 0 else ps_b
                    if dc == 0 and pf[3] is not None:
                        p.wait_tok("tensor", pf[3])
                    if owPs[par] is not None:
                        p.wait_tok("tensor", owPs[par])
                    else:
                        p.wait("tensor", eV)
                        p.wait("tensor", eA)
                    for e in range(8):
                        MM3(ps[:, :], ow_h[:, par, e, :], w_he_h[:, e, :],
                            ow_l[:, par, e, :], w_he_l[:, e, :],
                            start=(e == 0), stop=(e == 7), inc=(1 if e == 7 else 0))
                    p.wait("scalar", eT)
                    pr2 = dc % 2
                    if owTok[pr2] is not None:
                        p.wait_tok("scalar", owTok[pr2])
                    tae = Ac(lambda a, pr2=pr2, ps=ps: a.copy(
                        out=evac[:, pr2, :], in_=ps[:, :]))
                    owPs[par] = tae[0]
                    if par == 0:
                        pf[3] = tae[0]
                    p.wait_tok("gpsimd", *tae)
                    tkg = dma("gpsimd", dG, oscr_d[dc, :, tc * 512:(tc + 1) * 512],
                              evac[:, pr2, :], accum=True)
                    owTok[pr2] = tkg[0]
                p.lab = "ffn.core"

            attn_core(
                lKh=lambda kt: kh.ap()[:, kt * 128:(kt + 1) * 128],
                lKl=lambda kt: kl.ap()[:, kt * 128:(kt + 1) * 128],
                rQh=lambda qg: qh.ap()[:, qg * 512:(qg + 1) * 512],
                rQl=lambda qg: ql.ap()[:, qg * 512:(qg + 1) * 512],
                vAh=lambda kt: vh.ap()[:, kt, 0:128],
                vAl=lambda kt: vl.ap()[:, kt, 0:128],
                cM=128, sums=True, causal=False, ctx_out_fn=ctx_out_f,
                psum4=pf)
        if s == 0 and "scrF" in tap_d:
            join()
            tap("scrF", oscr_d.ap())

    def st_final(s):
        p.lab = "final"
        join()
        for tt in range(TT):
            par = tt % 2
            if tt >= 2:
                p.wait("sync", eT)
                p.wait("sync", eV)
            tk1 = dma("sync", dR, fin_in[:, par, :, :],
                      oscr_d[:, :, tt * 128:(tt + 1) * 128].rearrange("c p u -> p c u"))
            tk2 = dma("sync", dX, fin_x[:, par, :], x_d[s, tt * 128:(tt + 1) * 128, :])
            p.wait_tok("tensor", *tk1)
            if tt >= 1:
                p.wait("tensor", eV)
            for c in range(8):
                T(lambda t, par=par, c=c: t.transpose(
                    out=ps_bigf[:, c * 128:(c + 1) * 128],
                    in_=fin_in[:, par, c, :], identity=id_sb[:]),
                  inc=(1 if c == 7 else 0))
            p.wait_tok("vector", *tk2)
            p.wait("vector", eT)
            if tt >= 2:
                p.wait("vector", dO)
            tv = V(lambda v, par=par: v.tensor_tensor(
                out=fin_o[:, par, :], in0=ps_bigf, in1=fin_x[:, par, :], op=ALU.add))
            p.wait_tok("sync", *tv)
            dma("sync", dO, out_d[s, tt * 128:(tt + 1) * 128, :], fin_o[:, par, :])

    stages = ["ln", "gates", "attn", "ffn", "final"]
    upto_i = stages.index(upto)
    nseq_eff = NSEQ if upto_i >= 4 else 1

    def seq0_tile(tt):
        ln_tile(0, tt)
        if upto_i >= 1:
            gates_tile(0, tt)

    st_wsplit(inter=[lambda tt=tt: ln_tile(0, tt) for tt in range(TT)])
    for s in range(nseq_eff):
        if s > 0:
            st_ln(s)
        if upto_i >= 1:
            st_gates(s)
        if upto_i >= 2:
            st_attn(s)
        if upto_i >= 3:
            st_ffn(s)
        if upto_i >= 4:
            st_final(s)

    p.wait("sync", dO)
    p.wait("sync", dG)
    p.wait("gpsimd", dO)

    with contextlib.ExitStack() as es:
        semh = {name: es.enter_context(nc.semaphore(name)) for name in p.cnt}
        block = es.enter_context(nc.Block())

        def runner(items):
            def body(eng):
                cur = [None]
                stack = [None]

                def setlab(lab):
                    if lab != stack[0]:
                        if stack[0] is not None:
                            cm = stack.pop()
                            cm.__exit__(None, None, None)
                            stack.append(None)
                        if lab is not None:
                            cm = nc.named_scope(lab)
                            cm.__enter__()
                            stack[-1] = None
                            stack.append(cm)
                        stack[0] = lab

                open_cm = [None, None]
                def swl(lab):
                    if lab != open_cm[0]:
                        if open_cm[1] is not None:
                            open_cm[1].__exit__(None, None, None)
                            open_cm[1] = None
                        if lab:
                            open_cm[1] = nc.named_scope(lab)
                            open_cm[1].__enter__()
                        open_cm[0] = lab
                for kind, a, b, lab in items:
                    swl(lab)
                    if kind == "wait":
                        eng.wait_ge(semh[a], b)
                    else:
                        ins = a(eng)
                        for sm, amt in b:
                            ins = ins.then_inc(semh[sm], amt)
                swl(None)
            return body

        for ename in ENGS:
            getattr(block, ename)(runner(p.items[ename]))
    return nc, tap_d, sbuf_left


def kernel(L0_attn_x, L0_attn_w_qkv, L0_attn_w_out, L0_ffn_q_w, L0_ffn_k_w,
           L0_ffn_v_w, L0_ffn_o_w, L0_ffn_sel_v_w, L0_ffn_sel_o_w,
           L0_ffn_route_scale, _taps=(), _upto="final"):
    consts = host_consts()
    nc, tap_d, _ = build(taps=_taps, upto=_upto)
    in_maps = []
    for c in range(NCORES):
        m = {
            "x": np.ascontiguousarray(np.asarray(L0_attn_x)[c * NSEQ:(c + 1) * NSEQ]),
            "w_qkv": np.asarray(L0_attn_w_qkv), "w_out": np.asarray(L0_attn_w_out),
            "q_w": np.asarray(L0_ffn_q_w), "k_w": np.asarray(L0_ffn_k_w),
            "v_w": np.asarray(L0_ffn_v_w), "o_w": np.asarray(L0_ffn_o_w),
            "sel_v": np.asarray(L0_ffn_sel_v_w), "sel_o": np.asarray(L0_ffn_sel_o_w),
            "route_scale": np.asarray(L0_ffn_route_scale),
        }
        m.update(consts)
        in_maps.append(m)
    res = run_bass_kernel_spmd(nc, in_maps, core_ids=list(range(NCORES)))
    out = np.concatenate([r["out"] for r in res.results], axis=0)
    if _taps:
        return out, {t: res.results[0]["tap_" + t] for t in _taps}
    return out


# version 09
# speedup vs baseline: 1.4187x; 1.0287x over previous
import math
import contextlib
import numpy as np
import ml_dtypes

import concourse.bass as bass
import concourse.mybir as mybir
from concourse.bass_utils import run_bass_kernel_spmd

F32 = mybir.dt.float32
BF16 = mybir.dt.bfloat16
AF = mybir.ActivationFunctionType
ALU = mybir.AluOpType
AX = mybir.AxisListType

B, S, D = 16, 2048, 1024
NCORES = 8
NSEQ = B // NCORES
DH = 64
FH, FE, FP = 8, 8, 128
SCALE_A = 1.0 / math.sqrt(DH)
SCALE_F = 1.0 / math.sqrt(FP)
EPS = 1e-5
NEG = -1.0e30
TT = S // 128
NQG = S // 512
ENGS = ("sync", "vector", "scalar", "tensor", "gpsimd")


def host_consts():
    ident = np.eye(128, dtype=np.float32)
    import jax
    import jax.numpy as jnp
    with jax.default_device(jax.devices("cpu")[0]):
        inv_freq = 1.0 / (10000.0 ** (jnp.arange(0, 64, 2, dtype=jnp.float32) / 64))
        t32 = jnp.arange(S, dtype=jnp.float32)
        freqs = t32[:, None] * inv_freq[None, :]
        cos32 = np.asarray(jnp.cos(freqs)).T.astype(np.float64)
        sin32 = np.asarray(jnp.sin(freqs)).T.astype(np.float64)
    cos_rep = np.tile(cos32, (4, 1))
    sin_rep = np.tile(sin32, (4, 1))
    sgn = np.ones((128, 1))
    sgn[0:32] = -1.0
    sgn[64:96] = -1.0
    sin_rep = sin_rep * sgn
    cos_hi = cos_rep.astype(np.float32)
    sin_hi = sin_rep.astype(np.float32)
    kp = np.arange(128)[:, None]
    qf = np.arange(512)[None, :]
    masks = np.zeros((4, 128, 512), dtype=np.float32)
    for o in range(4):
        masks[o] = np.where(kp <= qf - o * 128, 0.0, NEG)
    esel = np.kron(np.eye(8), np.ones((1, 128))).astype(np.float32)
    return {
        "c_ident": ident, "c_esel": esel,
        "c_cos_hi": cos_hi, "c_sin_hi": sin_hi,
        "c_masks": masks.astype(ml_dtypes.bfloat16),
    }


eT, eV, eA, eP = "eT", "eV", "eA", "eP"
dC, dX, dW, dF, dG, dO, dR = "dC", "dX", "dW", "dF", "dG", "dO", "dR"
dP1, dP2 = "dP1", "dP2"
ALL_SEMS = [eT, eV, eA, eP, dC, dX, dW, dF, dG, dO, dR, dP1, dP2]
OWN = {"tensor": eT, "vector": eV, "scalar": eA, "gpsimd": eP}


class Prog:

    def __init__(self):
        self.items = {e: [] for e in ENGS}
        self.cnt = {sm: 0 for sm in ALL_SEMS}
        self.lab = "pre"

    def op(self, eng, fn, incs=()):
        incs = tuple(incs)
        for sm, a in incs:
            self.cnt[sm] += a
        self.items[eng].append(("op", fn, incs, self.lab))
        return tuple((sm, self.cnt[sm]) for sm, _ in incs)

    def wait(self, eng, sem, v=None):
        if v is None:
            v = self.cnt[sem]
        if v > 0:
            self.items[eng].append(("wait", sem, v, self.lab))

    def tok(self, sem):
        return (sem, self.cnt[sem])

    def wait_tok(self, eng, *toks):
        for sem, v in toks:
            if v > 0:
                self.items[eng].append(("wait", sem, v, self.lab))


def build(taps=(), upto="final"):
    taps = set(taps)
    nc = bass.Bass("TRN2", target_bir_lowering=False, detect_race_conditions=False)

    x_d = nc.dram_tensor("x", [NSEQ, S, D], F32, kind="ExternalInput")
    wqkv_d = nc.dram_tensor("w_qkv", [D, 3 * D], F32, kind="ExternalInput")
    wout_d = nc.dram_tensor("w_out", [D, D], F32, kind="ExternalInput")
    qw_d = nc.dram_tensor("q_w", [FH * FP, D], F32, kind="ExternalInput")
    kw_d = nc.dram_tensor("k_w", [FH * FP, D], F32, kind="ExternalInput")
    vw_d = nc.dram_tensor("v_w", [FH * FE, D, FP], F32, kind="ExternalInput")
    ow_d = nc.dram_tensor("o_w", [FH * FE, FP, D], F32, kind="ExternalInput")
    selv_d = nc.dram_tensor("sel_v", [FH * FE, D], F32, kind="ExternalInput")
    selo_d = nc.dram_tensor("sel_o", [FH * FE, D], F32, kind="ExternalInput")
    rs_d = nc.dram_tensor("route_scale", [1], F32, kind="ExternalInput")
    ident_d = nc.dram_tensor("c_ident", [128, 128], F32, kind="ExternalInput")
    cos_hi_d = nc.dram_tensor("c_cos_hi", [128, S], F32, kind="ExternalInput")
    sin_hi_d = nc.dram_tensor("c_sin_hi", [128, S], F32, kind="ExternalInput")
    masks_d = nc.dram_tensor("c_masks", [4, 128, 512], BF16, kind="ExternalInput")
    esel_d = nc.dram_tensor("c_esel", [8, 1024], F32, kind="ExternalInput")
    out_d = nc.dram_tensor("out", [NSEQ, S, D], F32, kind="ExternalOutput")
    oscr_d = nc.dram_tensor("oscr", [8, 128, S], F32, kind="Internal")
    wqkvh_d = nc.dram_tensor("wqkvh", [D, 3 * D], BF16, kind="Internal")
    wqkvl_d = nc.dram_tensor("wqkvl", [D, 3 * D], BF16, kind="Internal")
    wouth_d = nc.dram_tensor("wouth", [D, D], BF16, kind="Internal")
    woutl_d = nc.dram_tensor("woutl", [D, D], BF16, kind="Internal")
    vwh_d = nc.dram_tensor("vwh", [FH * FE, D, FP], BF16, kind="Internal")
    vwl_d = nc.dram_tensor("vwl", [FH * FE, D, FP], BF16, kind="Internal")
    owh_d = nc.dram_tensor("owh", [FH * FE, FP, D], BF16, kind="Internal")
    owl_d = nc.dram_tensor("owl", [FH * FE, FP, D], BF16, kind="Internal")

    tap_d = {}
    tap_shapes = {
        "gv": [128, TT * 64], "go": [128, TT * 64],
        "qrot": [128, S], "krot": [128, S],
        "ctxa": [128, TT * 64], "scrA": [8, 128, S],
        "ctxf": [128, TT * 128], "scrF": [8, 128, S],
        "wqh": ([D, 3 * D], BF16), "vhb": ([128, TT * 130], BF16),
        "vlb": ([128, TT * 130], BF16), "cphb": ([128, 2048], BF16),
        "qhb": ([128, S], BF16), "qlb": ([128, S], BF16),
    }
    for t in taps:
        shp = tap_shapes[t]
        dt = F32
        if isinstance(shp, tuple):
            shp, dt = shp
        tap_d[t] = nc.dram_tensor("tap_" + t, shp, dt, kind="ExternalOutput")

    A = nc.alloc_sbuf_tensor
    xn_h = A("xn_h", [128, 8, S], BF16)
    xn_l = A("xn_l", [128, 8, S], BF16)
    cos_hi = A("cos_hi", [128, S], F32)
    sin_hi = A("sin_hi", [128, S], F32)
    gates_v = A("gates_v", [128, TT, 64], F32)
    gates_o = A("gates_o", [128, TT, 64], F32)
    selT_h = A("selT_h", [128, 2, 8, 64], BF16)
    selT_l = A("selT_l", [128, 2, 8, 64], BF16)
    masks_b = A("masks_b", [128, 4, 512], BF16)
    id_sb = A("id_sb", [128, 128], F32)
    iem9 = A("iem9", [128, 64], F32)
    rs_col = A("rs_col", [128, 1], F32)
    eps_col = A("eps_col", [128, 1], F32)
    ones_row = A("ones_row", [1, 128], F32)
    ones_b = A("ones_b", [128, 1], BF16)
    rs_sb = A("rs_sb", [1, 1], F32)
    sc = A("sc", [128, 32], F32)
    big1 = A("big1", [128, S], F32)
    big2 = A("big2", [128, S], F32)
    big3 = A("big3", [128, S], F32)
    big5 = A("big5", [128, 4096], F32)
    big8 = A("big8", [128, S], F32)
    qh = A("qh", [128, S], BF16)
    ql = A("ql", [128, S], BF16)
    kh = A("kh", [128, S], BF16)
    kl = A("kl", [128, S], BF16)
    vh = A("vh", [128, TT, 130], BF16)
    vl = A("vl", [128, TT, 130], BF16)
    wh = A("wh", [128, 4096], BF16)
    wl = A("wl", [128, 4096], BF16)
    whe_h = A("whe_h", [128, 4096], BF16)
    whe_l = A("whe_l", [128, 4096], BF16)
    pTh = A("pTh", [128, 3, 512], BF16)
    pTl = A("pTl", [128, 3, 512], BF16)
    sbuf_left = nc.sbuf_bytes_remaining

    gt = big3.ap()[:, 0:640]
    vacc = big5.ap()[:, 0:2080].rearrange("p (t u) -> p t u", t=TT)
    ln_x = big5.ap()[:, 0:2048].rearrange("p (b u) -> p b u", b=2)
    ln_s1 = big5.ap()[:, 2048:4096].rearrange("p (b u) -> p b u", b=2)
    fin_in = big5.ap()[:, 0:2048].rearrange("p (b c u) -> p b c u", b=2, c=8)
    fin_x = big5.ap()[:, 2048:4096].rearrange("p (b u) -> p b u", b=2)
    fin_o = big5.ap()[:, 0:2048].rearrange("p (b u) -> p b u", b=2)
    raw = big8.ap()
    sq_dump = big8.ap()[:, 0:1024]
    sel_raw64 = big8.ap()[0:64, 0:1024]
    shuf = big3.ap()
    ctxn_f = big3.ap().rearrange("p (t u) -> p t u", t=TT)
    ctxn = big3.ap()[:, 0:1024].rearrange("p (t u) -> p t u", t=TT)
    evac = big1.ap()[:, 0:1024].rearrange("p (b u) -> p b u", b=2)
    wq_h = wh.ap()[:, 0:3072].rearrange("p (c j u) -> p c j u", c=8, j=3)
    wq_l = wl.ap()[:, 0:3072].rearrange("p (c j u) -> p c j u", c=8, j=3)
    wout_h = wh.ap()[:, 3072:4096].rearrange("p (c u) -> p c u", c=8)
    wout_l = wl.ap()[:, 3072:4096].rearrange("p (c u) -> p c u", c=8)
    qwT_h = wh.ap()[:, 0:1024].rearrange("p (c u) -> p c u", c=8)
    qwT_l = wl.ap()[:, 0:1024].rearrange("p (c u) -> p c u", c=8)
    kwT_h = wh.ap()[:, 1024:2048].rearrange("p (c u) -> p c u", c=8)
    kwT_l = wl.ap()[:, 1024:2048].rearrange("p (c u) -> p c u", c=8)
    vw_h = wh.ap().rearrange("p (c e u) -> p c e u", c=8, e=4)
    vw_l = wl.ap().rearrange("p (c e u) -> p c e u", c=8, e=4)
    ow_h = wh.ap()[:, 0:2048].rearrange("p (b e u) -> p b e u", b=2, e=8)
    ow_l = wl.ap()[:, 0:2048].rearrange("p (b e u) -> p b e u", b=2, e=8)
    wstg_h = wh.ap().rearrange("p (b u) -> p b u", b=4)
    wstg_l = wl.ap().rearrange("p (b u) -> p b u", b=4)
    cph = whe_h.ap()[:, 0:2048]
    cpl = whe_l.ap()[:, 0:2048]
    ctxB_h = whe_h.ap()[0:64, 2048:4096]
    ctxB_l = whe_l.ap()[0:64, 2048:4096]
    w_he_h = whe_h.ap().rearrange("p (e u) -> p e u", e=8)
    w_he_l = whe_l.ap().rearrange("p (e u) -> p e u", e=8)

    ps_a = nc.alloc_psum_tensor("ps_a", [128, 512], F32)
    ps_b = nc.alloc_psum_tensor("ps_b", [128, 512], F32)
    ps_big = nc.alloc_psum_tensor("ps_big", [128, 2, 512], F32)
    ps_ctx = nc.alloc_psum_tensor("ps_ctx", [128, 4, 512], F32)
    ps_bigf = ps_big.ap().rearrange("p a u -> p (a u)")

    p = Prog()

    def T(fn, inc=1):
        return p.op("tensor", fn, ((eT, inc),) if inc else ())

    def V(fn, inc=1):
        return p.op("vector", fn, ((eV, inc),) if inc else ())

    def Ac(fn, inc=1):
        return p.op("scalar", fn, ((eA, inc),) if inc else ())

    def G(fn, inc=1):
        return p.op("gpsimd", fn, ((eP, inc),) if inc else ())

    def dma(queue_eng, sem, out, in_, accum=False):
        def fn(eng, out=out, in_=in_, accum=accum):
            if accum:
                return eng.dma_start(out=out, in_=in_, accum_op=ALU.add)
            return eng.dma_start(out=out, in_=in_)
        return p.op(queue_eng, fn, ((sem, 16),))

    def join():
        for e in ENGS:
            own = OWN.get(e)
            for sm in ALL_SEMS:
                if sm != own:
                    p.wait(e, sm)

    def tap(name, src_ap):
        if name in tap_d:
            dma("sync", dO, tap_d[name].ap(), src_ap)

    def MM3(out, lh, rh, ll, rl, start, stop, inc=0):
        T(lambda t, out=out, lh=lh, rh=rh, start=start: t.matmul(
            out=out, lhsT=lh, rhs=rh, start=start, stop=False), inc=0)
        T(lambda t, out=out, lh=lh, rl=rl: t.matmul(
            out=out, lhsT=lh, rhs=rl, start=False, stop=False), inc=0)
        return T(lambda t, out=out, ll=ll, rh=rh, stop=stop: t.matmul(
            out=out, lhsT=ll, rhs=rh, start=False, stop=stop), inc=inc)

    def castpair(dh, dl, src, inc=1):
        ta = Ac(lambda a, dh=dh, src=src: a.copy(out=dh, in_=src))
        p.wait_tok("vector", *ta)
        return V(lambda v, dh=dh, dl=dl, src=src: v.tensor_tensor(
            out=dl, in0=src, in1=dh, op=ALU.subtract), inc=inc)

    dma("sync", dC, id_sb[:], ident_d[:])
    dma("sync", dC, cos_hi[:], cos_hi_d[:])
    dma("sync", dC, sin_hi[:], sin_hi_d[:])
    dma("sync", dC, masks_b[:], masks_d.ap().rearrange("o p u -> p o u"))
    dma("sync", dC, rs_sb[:], rs_d.ap().rearrange("(a b) -> a b", a=1))

    G(lambda g: g.iota(out=iem9[:], pattern=[[0, 8], [1, 8]], base=-9,
                       channel_multiplier=0, allow_small_or_imprecise_dtypes=True))
    G(lambda g: g.memset(eps_col[:], EPS))
    G(lambda g: g.memset(ones_row[:], 1.0))
    G(lambda g: g.memset(ones_b[:], 1.0))

    p.wait("tensor", dC)
    p.wait("tensor", eP)
    T(lambda t: t.matmul(out=ps_a[:, 0:1], lhsT=ones_row[:], rhs=rs_sb[:],
                         start=True, stop=True))
    p.wait("vector", eT)
    V(lambda v: v.tensor_copy(out=rs_col[:], in_=ps_a[:, 0:1]))

    for si, sd in enumerate((selv_d, selo_d)):
        p.wait("sync", eT)
        tk = dma("sync", dC, sel_raw64, sd[:])
        p.wait_tok("tensor", *tk)
        p.wait("tensor", eV)
        for c in range(8):
            ps = ps_a if c % 2 == 0 else ps_b
            if si > 0 or c >= 2:
                p.wait("tensor", eV)
                p.wait("tensor", eA)
            T(lambda t, c=c, ps=ps: t.transpose(
                out=ps[0:128, 0:64], in_=sel_raw64[:, c * 128:(c + 1) * 128],
                identity=id_sb[0:64, 0:64]))
            p.wait("scalar", eT)
            castpair(selT_h[:, si, c, :], selT_l[:, si, c, :], ps[0:128, 0:64])
        p.wait("tensor", eV)
        p.wait("tensor", eA)

    def st_wsplit(inter=()):
        p.lab = "wsplit"
        specs = []
        for (srcd, dh_d, dl_d, rows, cols) in (
                (wqkv_d, wqkvh_d, wqkvl_d, D, 3 * D),
                (wout_d, wouth_d, woutl_d, D, D)):
            sv = srcd.ap().rearrange("(c p) n -> p c n", p=128)
            hv = dh_d.ap().rearrange("(c p) n -> p c n", p=128)
            lv = dl_d.ap().rearrange("(c p) n -> p c n", p=128)
            nb = cols // 1024
            for c in range(rows // 128):
                for b in range(nb):
                    sl = (slice(None), c, slice(b * 1024, (b + 1) * 1024))
                    specs.append((sv[sl], hv[sl], lv[sl], 1024))
        for (srcd, dh_d, dl_d) in ((vw_d, vwh_d, vwl_d), (ow_d, owh_d, owl_d)):
            sv = srcd.ap().rearrange("e a u -> (e a) u").rearrange(
                "(c p) n -> p c n", p=128)
            hv = dh_d.ap().rearrange("e a u -> (e a) u").rearrange(
                "(c p) n -> p c n", p=128)
            lv = dl_d.ap().rearrange("e a u -> (e a) u", ).rearrange(
                "(c p) n -> p c n", p=128)
            C = sv.shape[1]
            N = sv.shape[2]
            grp = max(1, 1024 // N)
            for c0 in range(0, C, grp):
                sl = (slice(None), slice(c0, c0 + grp), slice(None))
                specs.append((sv[sl], hv[sl], lv[sl], grp * N))
        subTok = [None] * 4
        whTok = [None] * 4
        wlTok = [None] * 4
        inter_q = list(inter)
        for i, (s_ap, h_ap, l_ap, L) in enumerate(specs):
            par = i % 4
            rq = ("sync", dC) if par % 2 == 0 else ("gpsimd", dP1)
            stgb = (big1 if par < 2 else big2).ap()[:, (par % 2) * 1024:]
            if subTok[par] is not None:
                p.wait_tok(rq[0], subTok[par])
            tkr = dma(rq[0], rq[1], stgb[:, 0:L], s_ap)
            hv2 = wstg_h[:, par, 0:L]
            lv2 = wstg_l[:, par, 0:L]
            p.wait_tok("scalar", *tkr)
            if whTok[par] is not None:
                p.wait_tok("scalar", whTok[par])
            ta = Ac(lambda a, hv2=hv2, st=stgb[:, 0:L]: a.copy(out=hv2, in_=st))
            p.wait_tok("vector", *ta)
            if wlTok[par] is not None:
                p.wait_tok("vector", wlTok[par])
            tv = V(lambda v, hv2=hv2, lv2=lv2, st=stgb[:, 0:L]: v.tensor_tensor(
                out=lv2, in0=st, in1=hv2, op=ALU.subtract))
            subTok[par] = tv[0]
            whTok[par] = dma("scalar", dP2, h_ap, hv2)[0]
            p.wait_tok("scalar", *tv)
            wlTok[par] = dma("scalar", dP2, l_ap, lv2)[0]
            if inter_q and i % 10 == 9:
                inter_q.pop(0)()
        while inter_q:
            inter_q.pop(0)()
        tap("wqh", wqkvh_d.ap())

    lnTok = [None, None]

    def ln_tile(s, tt):
            p.lab = "ln"
            if tt == 0:
                lnTok[0] = lnTok[1] = None
            par = tt % 2
            o = par * 8
            p.wait("sync", eV)
            xt = ln_x[:, par, :]
            tkx = dma("sync", dX, xt, x_d[s, tt * 128:(tt + 1) * 128, :])
            p.wait_tok("vector", *tkx)
            p.wait("vector", eT)
            V(lambda v, xt=xt, o=o: v.reduce_sum(out=sc[:, o:o + 1], in_=xt, axis=AX.X))
            p.wait("vector", eV)
            V(lambda v, o=o: v.tensor_scalar(out=sc[:, o + 1:o + 2], in0=sc[:, o:o + 1],
                                             scalar1=1.0 / D, scalar2=None, op0=ALU.mult))
            p.wait("vector", eV)
            dev = ln_s1[:, par, :]
            tkd = V(lambda v, xt=xt, dev=dev, o=o: v.tensor_scalar(
                out=dev, in0=xt, scalar1=sc[:, o + 1:o + 2], scalar2=None,
                op0=ALU.subtract))
            p.wait_tok("scalar", *tkd)
            Ac(lambda a, dev=dev, o=o: a.activation(
                out=sq_dump, in_=dev, func=AF.Square,
                accum_out=sc[:, o + 2:o + 3]))
            p.wait("scalar", eA)
            tka = Ac(lambda a, o=o: a.activation(
                out=sc[:, o + 5:o + 6], in_=sc[:, o + 2:o + 3], func=AF.Sqrt,
                scale=1.0 / D, bias=eps_col[:]))
            p.wait_tok("vector", *tka)
            V(lambda v, o=o: v.reciprocal(out=sc[:, o + 3:o + 4], in_=sc[:, o + 5:o + 6]))
            p.wait("vector", eV)
            V(lambda v, o=o: v.tensor_tensor(out=sc[:, o + 4:o + 5],
                                             in0=sc[:, o + 1:o + 2],
                                             in1=sc[:, o + 3:o + 4], op=ALU.mult))
            p.wait("vector", eV)
            tkn = V(lambda v, xt=xt, dev=dev, o=o: v.tensor_scalar(
                out=dev, in0=xt, scalar1=sc[:, o + 3:o + 4],
                scalar2=sc[:, o + 4:o + 5], op0=ALU.mult, op1=ALU.subtract))
            p.wait_tok("tensor", *tkn)
            for c in range(8):
                ps = ps_a if c % 2 == 0 else ps_b
                if lnTok[c % 2] is not None:
                    p.wait_tok("tensor", lnTok[c % 2])
                elif tt > 0 or c >= 2:
                    p.wait("tensor", eV)
                    p.wait("tensor", eA)
                T(lambda t, dev=dev, c=c, ps=ps: t.transpose(
                    out=ps[:, 0:128], in_=dev[:, c * 128:(c + 1) * 128],
                    identity=id_sb[:]))
                p.wait("scalar", eT)
                lnTok[c % 2] = castpair(
                    xn_h[:, c, tt * 128:(tt + 1) * 128],
                    xn_l[:, c, tt * 128:(tt + 1) * 128], ps[:, 0:128])[0]

    def st_ln(s):
        p.lab = "ln"
        join()
        for tt in range(TT):
            ln_tile(s, tt)

    def st_gates(s):
        p.lab = "gates"
        if s == 0:
            p.wait("tensor", eA)
            p.wait("tensor", eV)
        else:
            join()
        for tt in range(TT):
            gates_tile(s, tt)
        if s == 0 and ("gv" in tap_d or "go" in tap_d):
            join()
            tap("gv", gates_v.ap().rearrange("p t u -> p (t u)"))
            tap("go", gates_o.ap().rearrange("p t u -> p (t u)"))

    def b8(ap):
        return ap.rearrange("p (g e) -> p g e", e=1).broadcast_to([128, 8, 8])

    IE3 = iem9.ap().rearrange("p (g e) -> p g e", e=8)

    def emit_top2(si, gout, tt):
            Pa = gt[:, si * 64:(si + 1) * 64]
            P3v = Pa.rearrange("p (g e) -> p g e", e=8)
            M1 = gt[:, 192:200]
            I1 = gt[:, 200:208]
            M2 = gt[:, 208:216]
            I2 = gt[:, 216:224]
            SS = gt[:, 224:232]
            R = gt[:, 232:240]
            W1 = gt[:, 240:248]
            W2 = gt[:, 248:256]
            E1 = gt[:, 256:320].rearrange("p (g e) -> p g e", e=8)
            OH1 = gt[:, 320:384].rearrange("p (g e) -> p g e", e=8)
            PM = gt[:, 384:448]
            PM3 = PM.rearrange("p (g e) -> p g e", e=8)
            E2 = gt[:, 448:512].rearrange("p (g e) -> p g e", e=8)
            OH2 = gt[:, 512:576].rearrange("p (g e) -> p g e", e=8)
            CT = gt[:, 576:640].rearrange("p (g e) -> p g e", e=8)
            CTf = gt[:, 576:640]
            steps = [
                lambda v: v.reduce_max(out=M1, in_=P3v, axis=AX.X),
                lambda v: v.tensor_tensor(out=E1, in0=P3v, in1=b8(M1), op=ALU.is_equal),
                lambda v: v.tensor_tensor(out=CT, in0=E1, in1=IE3, op=ALU.mult),
                lambda v: v.tensor_reduce(out=I1, in_=CT, axis=AX.X, op=ALU.min),
                lambda v: v.tensor_tensor(out=OH1, in0=IE3, in1=b8(I1), op=ALU.is_equal),
                lambda v: v.scalar_tensor_tensor(
                    out=PM, in0=OH1.rearrange("p g e -> p (g e)"), scalar=-2.0,
                    in1=Pa, op0=ALU.mult, op1=ALU.add),
                lambda v: v.reduce_max(out=M2, in_=PM3, axis=AX.X),
                lambda v: v.tensor_tensor(out=E2, in0=PM3, in1=b8(M2), op=ALU.is_equal),
                lambda v: v.tensor_tensor(out=CT, in0=E2, in1=IE3, op=ALU.mult),
                lambda v: v.tensor_reduce(out=I2, in_=CT, axis=AX.X, op=ALU.min),
                lambda v: v.tensor_tensor(out=OH2, in0=IE3, in1=b8(I2), op=ALU.is_equal),
                lambda v: v.tensor_tensor(out=SS, in0=M1, in1=M2, op=ALU.add),
                lambda v: v.tensor_scalar(out=SS, in0=SS, scalar1=1e-9,
                                          scalar2=None, op0=ALU.max),
                lambda v: v.reciprocal(out=R, in_=SS),
                lambda v: v.tensor_tensor(out=W1, in0=M1, in1=R, op=ALU.mult),
                lambda v: v.tensor_scalar(out=W1, in0=W1, scalar1=rs_col[:],
                                          scalar2=None, op0=ALU.mult),
                lambda v: v.tensor_tensor(out=W2, in0=M2, in1=R, op=ALU.mult),
                lambda v: v.tensor_scalar(out=W2, in0=W2, scalar1=rs_col[:],
                                          scalar2=None, op0=ALU.mult),
                lambda v: v.tensor_tensor(out=CT, in0=OH1, in1=b8(W1), op=ALU.mult),
                lambda v: v.tensor_tensor(out=PM3, in0=OH2, in1=b8(W2), op=ALU.mult),
                lambda v: v.tensor_tensor(out=gout[:, tt, :], in0=CTf, in1=PM, op=ALU.add),
            ]
            for st_fn in steps:
                p.wait("vector", eV)
                V(st_fn, inc=1)

    def gates_tile(s, tt):
            p.lab = "gates"
            p.wait("tensor", eA)
            p.wait("tensor", eV)
            for si in range(2):
                psg = ps_a if si == 0 else ps_b
                for c in range(8):
                    MM3(psg[:, 0:64],
                        xn_h[:, c, tt * 128:(tt + 1) * 128],
                        selT_h[:, si, c, :],
                        xn_l[:, c, tt * 128:(tt + 1) * 128],
                        selT_l[:, si, c, :],
                        start=(c == 0), stop=(c == 7), inc=(1 if c == 7 else 0))
            p.wait("scalar", eT)
            p.wait("scalar", eV)
            Ac(lambda a: a.activation(out=gt[:, 0:64], in_=ps_a[:, 0:64],
                                      func=AF.Sigmoid), inc=0)
            Ac(lambda a: a.activation(out=gt[:, 64:128], in_=ps_b[:, 0:64],
                                      func=AF.Sigmoid))
            p.wait("vector", eA)
            for si, gout in ((0, gates_v), (1, gates_o)):
                emit_top2(si, gout, tt)

    def rope_one(dst):
        V(lambda v, dst=dst: v.tensor_tensor(out=dst, in0=raw, in1=cos_hi[:], op=ALU.mult), inc=0)
        V(lambda v: v.tensor_tensor(out=raw, in0=shuf, in1=sin_hi[:], op=ALU.mult), inc=0)
        V(lambda v, dst=dst: v.tensor_tensor(out=dst, in0=dst, in1=raw, op=ALU.add), inc=1)

    def attn_core(lKh, lKl, rQh, rQl, vAh, vAl, cM, sums, causal, ctx_out_fn,
                  psum4=None):
        ctxTok = [None, None, None]
        psumFree = psum4 if psum4 is not None else [None] * 4
        p.wait("tensor", eA)
        p.wait("tensor", eV)
        p.wait("scalar", eT)
        p.wait("vector", eT)

        def psS_of(kt):
            pb = kt % 4
            if pb < 2:
                return pb, ps_big.ap()[:, pb, :]
            if pb == 2:
                return pb, ps_ctx.ap()[:, 3, :]
            return pb, ps_a[:, :]

        def emit_S(qg, kt):
            pb, psS = psS_of(kt)
            if psumFree[pb] is not None:
                p.wait_tok("tensor", psumFree[pb])
            return MM3(psS, lKh(kt), rQh(qg), lKl(kt), rQl(qg),
                       start=True, stop=True, inc=1)

        DEPTH = 3
        for qg in range(NQG):
            nkt = (4 * qg + 4) if causal else TT
            p.wait("tensor", eV)
            sToks = {}
            for k0 in range(min(DEPTH, nkt)):
                sToks[k0] = emit_S(qg, k0)
            for kt in range(nkt):
                pb, psS = psS_of(kt)
                pr = kt % 3
                if kt + DEPTH < nkt:
                    sToks[kt + DEPTH] = emit_S(qg, kt + DEPTH)
                tb = sToks.pop(kt)
                lastS = tb
                o = kt - 4 * qg
                if causal and o >= 0:
                    p.wait_tok("vector", *tb)
                    lastS = V(lambda v, psS=psS, o=o: v.tensor_tensor(
                        out=psS, in0=psS, in1=masks_b[:, o, :], op=ALU.add))
                    p.wait_tok("scalar", *lastS)
                else:
                    p.wait_tok("scalar", *lastS)
                Ac(lambda a, psS=psS: a.activation(
                    out=psS, in_=psS, func=AF.Exp), inc=0)
                if ctxTok[pr] is not None:
                    p.wait_tok("scalar", ctxTok[pr])
                ta = Ac(lambda a, psS=psS, pr=pr: a.copy(out=pTh[:, pr, :], in_=psS))
                p.wait_tok("vector", *ta)
                if ctxTok[pr] is not None:
                    p.wait_tok("vector", ctxTok[pr])
                tl = V(lambda v, psS=psS, pr=pr: v.tensor_tensor(
                    out=pTl[:, pr, :], in0=psS, in1=pTh[:, pr, :], op=ALU.subtract))
                psumFree[pb] = tl[0]
                p.wait_tok("tensor", *ta)
                p.wait_tok("tensor", *tl)
                if kt == 0:
                    p.wait("tensor", eV)
                MM3(ps_ctx.ap()[0:cM, 0, :], vAh(kt), pTh[:, pr, :],
                    vAl(kt), pTl[:, pr, :],
                    start=(kt == 0), stop=(kt == nkt - 1),
                    inc=(0 if sums else 1))
                if sums:
                    T(lambda t, pr=pr, kt=kt, nkt=nkt: t.matmul(
                        out=ps_ctx.ap()[0:1, 1, :], lhsT=ones_b[:, 0:1],
                        rhs=pTh[:, pr, :], start=(kt == 0), stop=False), inc=0)
                    T(lambda t, pr=pr, kt=kt, nkt=nkt: t.matmul(
                        out=ps_ctx.ap()[0:1, 1, :], lhsT=ones_b[:, 0:1],
                        rhs=pTl[:, pr, :], start=False, stop=(kt == nkt - 1)), inc=1)
                ctxTok[pr] = p.tok(eT)
            ctx_out_fn(qg)

    def st_attn(s):
        p.lab = "attn"
        join()
        G(lambda g: g.memset(vh.ap()[:, :, 64:65], 1.0))
        G(lambda g: g.memset(vh.ap()[:, :, 129:130], 1.0))
        G(lambda g: g.memset(vl.ap()[:, :, 64:65], 0.0))
        G(lambda g: g.memset(vl.ap()[:, :, 129:130], 0.0))
        p.wait("tensor", eP)
        for hp in range(8):
            p.wait("sync", eT)
            tkw = None
            for j, c0 in enumerate((hp * 128, D + hp * 128, 2 * D + hp * 128)):
                dma("sync", dW, wq_h[:, :, j, :],
                    wqkvh_d.ap().rearrange("(c p) n -> p c n", p=128)[:, :, c0:c0 + 128])
                tkw = dma("sync", dW, wq_l[:, :, j, :],
                          wqkvl_d.ap().rearrange("(c p) n -> p c n", p=128)[:, :, c0:c0 + 128])
            p.wait_tok("tensor", *tkw)
            p.lab = "attn.proj"
            qkTok = [None, None]
            p.wait("tensor", eV)
            p.wait("tensor", eA)
            for j in (0, 1):
                scale = SCALE_A if j == 0 else 1.0
                p.wait("scalar", eV)
                p.wait("scalar", eT)
                for nt in range(NQG):
                    ps = ps_a if nt % 2 == 0 else ps_b
                    if qkTok[nt % 2] is not None:
                        p.wait_tok("tensor", qkTok[nt % 2])
                    for c in range(8):
                        MM3(ps[:, :], wq_h[:, c, j, :],
                            xn_h[:, c, nt * 512:(nt + 1) * 512],
                            wq_l[:, c, j, :],
                            xn_l[:, c, nt * 512:(nt + 1) * 512],
                            start=(c == 0), stop=(c == 7), inc=(1 if c == 7 else 0))
                    p.wait("scalar", eT)
                    tqa = Ac(lambda a, nt=nt, scale=scale, ps=ps: a.mul(
                        out=raw[:, nt * 512:(nt + 1) * 512], in_=ps[:, :], mul=scale))
                    qkTok[nt % 2] = tqa[0]
                p.wait("gpsimd", eA)
                p.wait("gpsimd", eV)
                p.wait("gpsimd", eT)
                for (d0, s0) in ((0, 32), (32, 0), (64, 96), (96, 64)):
                    dma("gpsimd", dG, big3.ap()[d0:d0 + 32, :], raw[s0:s0 + 32, :])
                p.wait("vector", dG)
                p.wait("vector", dW)
                p.wait("vector", eA)
                p.wait("vector", eT)
                p.wait("vector", eP)
                dstb = big1 if j == 0 else big2
                rope_one(dstb.ap())
                p.wait("scalar", eV)
                p.wait("scalar", eT)
                if j == 0:
                    castpair(qh[:], ql[:], dstb.ap())
                else:
                    castpair(kh[:], kl[:], dstb.ap())
            p.lab = "attn.vproj"
            vTok = [qkTok[0], qkTok[1]]
            for tt in range(TT):
                ps = ps_a if tt % 2 == 0 else ps_b
                if vTok[tt % 2] is not None:
                    p.wait_tok("tensor", vTok[tt % 2])
                for c in range(8):
                    MM3(ps[:, 0:128], xn_h[:, c, tt * 128:(tt + 1) * 128],
                        wq_h[:, c, 2, :],
                        xn_l[:, c, tt * 128:(tt + 1) * 128],
                        wq_l[:, c, 2, :],
                        start=(c == 0), stop=(c == 7), inc=(1 if c == 7 else 0))
                p.wait("scalar", eT)
                vh2 = vh.ap()[:, tt, :].rearrange("p (b u) -> p b u", b=2)[:, :, 0:64]
                vl2 = vl.ap()[:, tt, :].rearrange("p (b u) -> p b u", b=2)[:, :, 0:64]
                ps2 = ps[:, 0:128].rearrange("p (b u) -> p b u", b=2)
                vtk = castpair(vh2, vl2, ps2)
                vTok[tt % 2] = vtk[0]
            if s == 0 and hp == 0:
                join()
                tap("qrot", big1.ap())
                tap("krot", big2.ap())
                tap("qhb", qh.ap())
                tap("qlb", ql.ap())
                tap("vhb", vh.ap().rearrange("p t u -> p (t u)"))
                tap("vlb", vl.ap().rearrange("p t u -> p (t u)"))
            p.lab = "attn.core"
            for i in range(2):
                R0 = slice(64 * i, 64 * i + 64)
                cA = 65 * i

                def ctx_out(qg, i=i):
                    p.wait("vector", eT)
                    tv1 = V(lambda v: v.reciprocal(
                        out=big3.ap()[0:1, 0:512], in_=ps_ctx.ap()[64:65, 0, :]))
                    p.wait_tok("tensor", *tv1)
                    p.wait("tensor", eV)
                    p.wait("tensor", eA)
                    tr = T(lambda t: t.matmul(
                        out=ps_ctx.ap()[0:64, 1, :], lhsT=ones_row[0:1, 0:64],
                        rhs=big3.ap()[0:1, 0:512], start=True, stop=True))
                    p.wait_tok("scalar", *tr)
                    trc = Ac(lambda a: a.copy(out=big3.ap()[0:64, 1536:2048],
                                              in_=ps_ctx.ap()[0:64, 1, :]))
                    p.wait_tok("vector", *trc)
                    tv2 = V(lambda v: v.tensor_tensor(
                        out=big3.ap()[0:64, 512:1024], in0=ps_ctx.ap()[0:64, 0, :],
                        in1=big3.ap()[0:64, 1536:2048], op=ALU.mult))
                    p.wait("scalar", eV)
                    if i == 0:
                        castpair(cph[0:64, qg * 512:(qg + 1) * 512],
                                 cpl[0:64, qg * 512:(qg + 1) * 512],
                                 big3.ap()[0:64, 512:1024])
                    else:
                        castpair(ctxB_h[:, qg * 512:(qg + 1) * 512],
                                 ctxB_l[:, qg * 512:(qg + 1) * 512],
                                 big3.ap()[0:64, 512:1024])

                attn_core(
                    lKh=lambda kt, R0=R0: kh.ap()[R0, kt * 128:(kt + 1) * 128],
                    lKl=lambda kt, R0=R0: kl.ap()[R0, kt * 128:(kt + 1) * 128],
                    rQh=lambda qg, R0=R0: qh.ap()[R0, qg * 512:(qg + 1) * 512],
                    rQl=lambda qg, R0=R0: ql.ap()[R0, qg * 512:(qg + 1) * 512],
                    vAh=lambda kt, cA=cA: vh.ap()[:, kt, cA:cA + 65],
                    vAl=lambda kt, cA=cA: vl.ap()[:, kt, cA:cA + 65],
                    cM=65, sums=False, causal=True, ctx_out_fn=ctx_out)
                if i == 1:
                    p.wait("gpsimd", eV)
                    dma("gpsimd", dG, cph[64:128, :], ctxB_h[:, :])
                    dma("gpsimd", dG, cpl[64:128, :], ctxB_l[:, :])
            if s == 0 and hp == 0:
                join()
                tap("cphb", cph)
            p.lab = "attn.wout"
            p.wait("sync", eT)
            dma("sync", dW, wout_h,
                wouth_d[hp * 128:(hp + 1) * 128, :].rearrange("p (c u) -> p c u", c=8))
            tkwo = dma("sync", dW, wout_l,
                       woutl_d[hp * 128:(hp + 1) * 128, :].rearrange("p (c u) -> p c u", c=8))
            p.wait_tok("tensor", *tkwo)
            p.wait("tensor", dG)
            p.wait("tensor", eV)
            woTok = [None, None]
            woPs = [None, None]
            for dc in range(8):
                for jj in range(4):
                    pr2 = (dc * 4 + jj) % 2
                    if woPs[jj % 2] is not None:
                        p.wait_tok("tensor", woPs[jj % 2])
                    else:
                        p.wait("tensor", eA)
                    MM3(ps_big[:, jj % 2, :], wout_h[:, dc, :],
                        cph[:, jj * 512:(jj + 1) * 512],
                        wout_l[:, dc, :],
                        cpl[:, jj * 512:(jj + 1) * 512],
                        start=True, stop=True, inc=1)
                    p.wait("scalar", eT)
                    if woTok[pr2] is not None:
                        p.wait_tok("scalar", woTok[pr2])
                    tae = Ac(lambda a, pr2=pr2, jj=jj: a.copy(
                        out=evac[:, pr2, :], in_=ps_big[:, jj % 2, :]))
                    woPs[jj % 2] = tae[0]
                    p.wait_tok("gpsimd", *tae)
                    tkg = dma("gpsimd", dG, oscr_d[dc, :, jj * 512:(jj + 1) * 512],
                              evac[:, pr2, :], accum=(hp != 0))
                    woTok[pr2] = tkg[0]
        if s == 0 and "scrA" in tap_d:
            join()
            tap("scrA", oscr_d.ap())

    def st_ffn(s):
        p.lab = "ffn"
        join()
        G(lambda g: g.memset(vacc[:, :, 128:129], 1.0))
        p.wait("scalar", eP)
        p.wait("vector", eP)
        esel = big5.ap()[0:8, 2304:3328]
        dma("sync", dF, esel, esel_d[:])
        p.wait("tensor", dF)
        owTok = [None, None]
        owPs = [None, None]
        for h in range(8):
            p.lab = "ffn.qkT"
            tpTok = [None, None]
            for wi, wd in enumerate((qw_d, kw_d)):
                p.wait("sync", eT)
                p.wait("sync", dG)
                tk = dma("sync", dF, raw[:, 0:1024], wd[h * 128:(h + 1) * 128, :])
                p.wait_tok("tensor", *tk)
                for c in range(8):
                    ps = ps_a if c % 2 == 0 else ps_b
                    if tpTok[c % 2] is not None:
                        p.wait_tok("tensor", tpTok[c % 2])
                    else:
                        p.wait("tensor", eV)
                        p.wait("tensor", eA)
                    T(lambda t, c=c, ps=ps: t.transpose(
                        out=ps[:, 0:128], in_=raw[:, c * 128:(c + 1) * 128],
                        identity=id_sb[:]))
                    p.wait("scalar", eT)
                    hv = (qwT_h if wi == 0 else kwT_h)[:, c, :]
                    lv = (qwT_l if wi == 0 else kwT_l)[:, c, :]
                    tpTok[c % 2] = castpair(hv, lv, ps[:, 0:128])[0]
            for tt in range(TT):
                ps = ps_a if tt % 2 == 0 else ps_b
                if tpTok[tt % 2] is not None:
                    p.wait_tok("tensor", tpTok[tt % 2])
                else:
                    p.wait("tensor", eV)
                    p.wait("tensor", eA)
                T(lambda t, tt=tt, ps=ps, h=h: t.transpose(
                    out=ps[0:8, 0:128], in_=gates_o[:, tt, h * 8:(h + 1) * 8],
                    identity=id_sb[:]))
                p.wait("vector", eT)
                dstg = big1 if tt < 8 else big2
                col = 1024 + (tt % 8) * 128
                tpTok[tt % 2] = V(lambda v, ps=ps, dstg=dstg, col=col: v.tensor_copy(
                    out=dstg.ap()[0:8, col:col + 128], in_=ps[0:8, 0:128]))[0]
            p.lab = "ffn.proj"
            prjTok = [tpTok[0], tpTok[1]]
            for wi in (0, 1):
                scale = SCALE_F if wi == 0 else 1.0
                for nt in range(NQG):
                    ps = ps_a if nt % 2 == 0 else ps_b
                    if prjTok[nt % 2] is not None:
                        p.wait_tok("tensor", prjTok[nt % 2])
                    else:
                        p.wait("tensor", eA)
                        p.wait("tensor", eV)
                    for c in range(8):
                        MM3(ps[:, :], (qwT_h if wi == 0 else kwT_h)[:, c, :],
                            xn_h[:, c, nt * 512:(nt + 1) * 512],
                            (qwT_l if wi == 0 else kwT_l)[:, c, :],
                            xn_l[:, c, nt * 512:(nt + 1) * 512],
                            start=(c == 0), stop=(c == 7), inc=(1 if c == 7 else 0))
                    p.wait("scalar", eT)
                    p.wait("scalar", eV)
                    hv = (qh if wi == 0 else kh).ap()[:, nt * 512:(nt + 1) * 512]
                    lv = (ql if wi == 0 else kl).ap()[:, nt * 512:(nt + 1) * 512]
                    ta = Ac(lambda a, hv=hv, ps=ps, scale=scale: a.mul(
                        out=hv, in_=ps[:, :], mul=scale))
                    p.wait_tok("vector", *ta)
                    prjTok[nt % 2] = V(
                        lambda v, hv=hv, lv=lv, ps=ps, scale=scale: v.scalar_tensor_tensor(
                            out=lv, in0=ps[:, :], scalar=scale, in1=hv,
                            op0=ALU.mult, op1=ALU.subtract))[0]
            p.lab = "ffn.xv"
            xvTok = [prjTok[0], prjTok[1]]
            for half in range(2):
                p.wait("sync", eT)
                for e4 in range(4):
                    dma("sync", dF, vw_h[:, :, e4, :],
                        vwh_d[h * 8 + half * 4 + e4, :, :].rearrange("(c p) u -> p c u", p=128))
                    tk = dma("sync", dF, vw_l[:, :, e4, :],
                             vwl_d[h * 8 + half * 4 + e4, :, :].rearrange("(c p) u -> p c u", p=128))
                p.wait_tok("tensor", *tk)
                for tt in range(TT):
                    ps = ps_a if tt % 2 == 0 else ps_b
                    if xvTok[tt % 2] is not None:
                        p.wait_tok("tensor", xvTok[tt % 2])
                    for c in range(8):
                        MM3(ps[:, :], xn_h[:, c, tt * 128:(tt + 1) * 128],
                            vw_h[:, c, :, :],
                            xn_l[:, c, tt * 128:(tt + 1) * 128],
                            vw_l[:, c, :, :],
                            start=(c == 0), stop=(c == 7), inc=(1 if c == 7 else 0))
                    p.wait("vector", eT)
                    for e4 in range(4):
                        he = h * 8 + half * 4 + e4
                        if e4 > 0:
                            p.wait("vector", eV)
                        if half == 0 and e4 == 0:
                            V(lambda v, tt=tt, e4=e4, he=he, ps=ps: v.tensor_scalar(
                                out=vacc[:, tt, 0:128],
                                in0=ps[:, e4 * 128:(e4 + 1) * 128],
                                scalar1=gates_v[:, tt, he:he + 1], scalar2=None,
                                op0=ALU.mult), inc=1)
                        else:
                            xvTok[tt % 2] = V(
                                lambda v, tt=tt, e4=e4, he=he, ps=ps: v.scalar_tensor_tensor(
                                    out=vacc[:, tt, 0:128],
                                    in0=ps[:, e4 * 128:(e4 + 1) * 128],
                                    scalar=gates_v[:, tt, he:he + 1],
                                    in1=vacc[:, tt, 0:128],
                                    op0=ALU.mult, op1=ALU.add), inc=1)[0]
            p.wait("scalar", eV)
            p.wait("scalar", eT)
            castpair(vh.ap()[:, :, 0:129], vl.ap()[:, :, 0:129], vacc[:, :, 0:129])

            p.lab = "ffn.core"

            def g_blk(qg):
                src = big1 if qg < 2 else big2
                c0 = 1024 + (qg % 2) * 512
                return src.ap()[0:8, c0:c0 + 512]

            pf = [None] * 4

            def ctx_out_f(qg):
                p.lab = "ffn.ow"
                p.wait("vector", eT)
                tv1 = V(lambda v: v.reciprocal(
                    out=big3.ap()[0:1, 0:512], in_=ps_ctx.ap()[0:1, 1, :]))
                p.wait_tok("tensor", *tv1)
                p.wait("tensor", eV)
                p.wait("tensor", eA)
                tr = T(lambda t: t.matmul(
                    out=ps_ctx.ap()[0:128, 2, :], lhsT=ones_row[0:1, 0:128],
                    rhs=big3.ap()[0:1, 0:512], start=True, stop=True))
                p.wait_tok("scalar", *tr)
                trc = Ac(lambda a: a.copy(out=big3.ap()[:, 1536:2048],
                                          in_=ps_ctx.ap()[0:128, 2, :]))
                p.wait_tok("vector", *trc)
                tcn = V(lambda v: v.tensor_tensor(
                    out=big3.ap()[:, 512:1024], in0=ps_ctx.ap()[0:128, 0, :],
                    in1=big3.ap()[:, 1536:2048], op=ALU.mult))
                p.wait("scalar", eT)
                for e in range(8):
                    p.wait("tensor", eV)
                    p.wait("tensor", eA)
                    tg = T(lambda t, e=e, qg=qg: t.matmul(
                        out=ps_ctx.ap()[0:128, 2, :], lhsT=esel[:, e * 128:(e + 1) * 128],
                        rhs=g_blk(qg), start=True, stop=True))
                    p.wait_tok("vector", *tg)
                    p.wait("vector", eA)
                    stg = big3.ap()[:, 1024:1536]
                    tv2 = V(lambda v, stg=stg: v.tensor_tensor(
                        out=stg, in0=big3.ap()[:, 512:1024],
                        in1=ps_ctx.ap()[0:128, 2, :], op=ALU.mult))
                    p.wait("scalar", eV)
                    castpair(w_he_h[:, e, :], w_he_l[:, e, :], stg)
                tc = qg
                for dc in range(8):
                    par = dc % 2
                    p.wait("sync", eT)
                    dma("sync", dF, ow_h[:, par, :, :],
                        owh_d[h * 8:(h + 1) * 8, :, dc * 128:(dc + 1) * 128]
                        .rearrange("e p u -> p e u"))
                    tk = dma("sync", dF, ow_l[:, par, :, :],
                             owl_d[h * 8:(h + 1) * 8, :, dc * 128:(dc + 1) * 128]
                             .rearrange("e p u -> p e u"))
                    p.wait_tok("tensor", *tk)
                    ps = ps_a if par == 0 else ps_b
                    if dc == 0 and pf[3] is not None:
                        p.wait_tok("tensor", pf[3])
                    if owPs[par] is not None:
                        p.wait_tok("tensor", owPs[par])
                    else:
                        p.wait("tensor", eV)
                        p.wait("tensor", eA)
                    for e in range(8):
                        MM3(ps[:, :], ow_h[:, par, e, :], w_he_h[:, e, :],
                            ow_l[:, par, e, :], w_he_l[:, e, :],
                            start=(e == 0), stop=(e == 7), inc=(1 if e == 7 else 0))
                    p.wait("scalar", eT)
                    pr2 = dc % 2
                    if owTok[pr2] is not None:
                        p.wait_tok("scalar", owTok[pr2])
                    tae = Ac(lambda a, pr2=pr2, ps=ps: a.copy(
                        out=evac[:, pr2, :], in_=ps[:, :]))
                    owPs[par] = tae[0]
                    if par == 0:
                        pf[3] = tae[0]
                    p.wait_tok("gpsimd", *tae)
                    tkg = dma("gpsimd", dG, oscr_d[dc, :, tc * 512:(tc + 1) * 512],
                              evac[:, pr2, :], accum=True)
                    owTok[pr2] = tkg[0]
                p.lab = "ffn.core"

            attn_core(
                lKh=lambda kt: kh.ap()[:, kt * 128:(kt + 1) * 128],
                lKl=lambda kt: kl.ap()[:, kt * 128:(kt + 1) * 128],
                rQh=lambda qg: qh.ap()[:, qg * 512:(qg + 1) * 512],
                rQl=lambda qg: ql.ap()[:, qg * 512:(qg + 1) * 512],
                vAh=lambda kt: vh.ap()[:, kt, 0:128],
                vAl=lambda kt: vl.ap()[:, kt, 0:128],
                cM=128, sums=True, causal=False, ctx_out_fn=ctx_out_f,
                psum4=pf)
        if s == 0 and "scrF" in tap_d:
            join()
            tap("scrF", oscr_d.ap())

    def st_final(s):
        p.lab = "final"
        join()
        for tt in range(TT):
            par = tt % 2
            if tt >= 2:
                p.wait("sync", eT)
                p.wait("sync", eV)
            tk1 = dma("sync", dR, fin_in[:, par, :, :],
                      oscr_d[:, :, tt * 128:(tt + 1) * 128].rearrange("c p u -> p c u"))
            tk2 = dma("sync", dX, fin_x[:, par, :], x_d[s, tt * 128:(tt + 1) * 128, :])
            p.wait_tok("tensor", *tk1)
            if tt >= 1:
                p.wait("tensor", eV)
            for c in range(8):
                T(lambda t, par=par, c=c: t.transpose(
                    out=ps_bigf[:, c * 128:(c + 1) * 128],
                    in_=fin_in[:, par, c, :], identity=id_sb[:]),
                  inc=(1 if c == 7 else 0))
            p.wait_tok("vector", *tk2)
            p.wait("vector", eT)
            if tt >= 2:
                p.wait("vector", dO)
            tv = V(lambda v, par=par: v.tensor_tensor(
                out=fin_o[:, par, :], in0=ps_bigf, in1=fin_x[:, par, :], op=ALU.add))
            p.wait_tok("sync", *tv)
            dma("sync", dO, out_d[s, tt * 128:(tt + 1) * 128, :], fin_o[:, par, :])

    stages = ["ln", "gates", "attn", "ffn", "final"]
    upto_i = stages.index(upto)
    nseq_eff = NSEQ if upto_i >= 4 else 1

    def seq0_tile(tt):
        ln_tile(0, tt)
        if upto_i >= 1:
            gates_tile(0, tt)

    st_wsplit(inter=[lambda tt=tt: ln_tile(0, tt) for tt in range(TT)])
    for s in range(nseq_eff):
        if s > 0:
            st_ln(s)
        if upto_i >= 1:
            st_gates(s)
        if upto_i >= 2:
            st_attn(s)
        if upto_i >= 3:
            st_ffn(s)
        if upto_i >= 4:
            st_final(s)

    p.wait("sync", dO)
    p.wait("sync", dG)
    p.wait("gpsimd", dO)

    with contextlib.ExitStack() as es:
        semh = {name: es.enter_context(nc.semaphore(name)) for name in p.cnt}
        block = es.enter_context(nc.Block())

        def runner(items):
            def body(eng):
                cur = [None]
                stack = [None]

                def setlab(lab):
                    if lab != stack[0]:
                        if stack[0] is not None:
                            cm = stack.pop()
                            cm.__exit__(None, None, None)
                            stack.append(None)
                        if lab is not None:
                            cm = nc.named_scope(lab)
                            cm.__enter__()
                            stack[-1] = None
                            stack.append(cm)
                        stack[0] = lab

                open_cm = [None, None]
                def swl(lab):
                    if lab != open_cm[0]:
                        if open_cm[1] is not None:
                            open_cm[1].__exit__(None, None, None)
                            open_cm[1] = None
                        if lab:
                            open_cm[1] = nc.named_scope(lab)
                            open_cm[1].__enter__()
                        open_cm[0] = lab
                for kind, a, b, lab in items:
                    swl(lab)
                    if kind == "wait":
                        eng.wait_ge(semh[a], b)
                    else:
                        ins = a(eng)
                        for sm, amt in b:
                            ins = ins.then_inc(semh[sm], amt)
                swl(None)
            return body

        for ename in ENGS:
            getattr(block, ename)(runner(p.items[ename]))
    return nc, tap_d, sbuf_left


def kernel(L0_attn_x, L0_attn_w_qkv, L0_attn_w_out, L0_ffn_q_w, L0_ffn_k_w,
           L0_ffn_v_w, L0_ffn_o_w, L0_ffn_sel_v_w, L0_ffn_sel_o_w,
           L0_ffn_route_scale, _taps=(), _upto="final"):
    consts = host_consts()
    nc, tap_d, _ = build(taps=_taps, upto=_upto)
    in_maps = []
    for c in range(NCORES):
        m = {
            "x": np.ascontiguousarray(np.asarray(L0_attn_x)[c * NSEQ:(c + 1) * NSEQ]),
            "w_qkv": np.asarray(L0_attn_w_qkv), "w_out": np.asarray(L0_attn_w_out),
            "q_w": np.asarray(L0_ffn_q_w), "k_w": np.asarray(L0_ffn_k_w),
            "v_w": np.asarray(L0_ffn_v_w), "o_w": np.asarray(L0_ffn_o_w),
            "sel_v": np.asarray(L0_ffn_sel_v_w), "sel_o": np.asarray(L0_ffn_sel_o_w),
            "route_scale": np.asarray(L0_ffn_route_scale),
        }
        m.update(consts)
        in_maps.append(m)
    res = run_bass_kernel_spmd(nc, in_maps, core_ids=list(range(NCORES)))
    out = np.concatenate([r["out"] for r in res.results], axis=0)
    if _taps:
        return out, {t: res.results[0]["tap_" + t] for t in _taps}
    return out


# version 10 (final)
# speedup vs baseline: 1.5793x; 1.1132x over previous
import math
import contextlib
import numpy as np
import ml_dtypes

import concourse.bass as bass
import concourse.mybir as mybir
from concourse.bass_utils import run_bass_kernel_spmd

F32 = mybir.dt.float32
BF16 = mybir.dt.bfloat16
AF = mybir.ActivationFunctionType
ALU = mybir.AluOpType
AX = mybir.AxisListType

B, S, D = 16, 2048, 1024
NCORES = 8
NSEQ = B // NCORES
DH = 64
FH, FE, FP = 8, 8, 128
SCALE_A = 1.0 / math.sqrt(DH)
SCALE_F = 1.0 / math.sqrt(FP)
EPS = 1e-5
NEG = -1.0e30
TT = S // 128
NQG = S // 512
ENGS = ("sync", "vector", "scalar", "tensor", "gpsimd")


def host_consts():
    ident = np.eye(128, dtype=np.float32)
    import jax
    import jax.numpy as jnp
    with jax.default_device(jax.devices("cpu")[0]):
        inv_freq = 1.0 / (10000.0 ** (jnp.arange(0, 64, 2, dtype=jnp.float32) / 64))
        t32 = jnp.arange(S, dtype=jnp.float32)
        freqs = t32[:, None] * inv_freq[None, :]
        cos32 = np.asarray(jnp.cos(freqs)).T.astype(np.float64)
        sin32 = np.asarray(jnp.sin(freqs)).T.astype(np.float64)
    cos_rep = np.tile(cos32, (4, 1))
    sin_rep = np.tile(sin32, (4, 1))
    sgn = np.ones((128, 1))
    sgn[0:32] = -1.0
    sgn[64:96] = -1.0
    sin_rep = sin_rep * sgn
    cos_hi = cos_rep.astype(np.float32)
    sin_hi = sin_rep.astype(np.float32)
    kp = np.arange(128)[:, None]
    qf = np.arange(512)[None, :]
    masks = np.zeros((4, 128, 512), dtype=np.float32)
    for o in range(4):
        masks[o] = np.where(kp <= qf - o * 128, 0.0, NEG)
    esel = np.kron(np.eye(8), np.ones((1, 128))).astype(np.float32)
    return {
        "c_ident": ident, "c_esel": esel,
        "c_cos_hi": cos_hi, "c_sin_hi": sin_hi,
        "c_masks": masks.astype(ml_dtypes.bfloat16),
    }


eT, eV, eA, eP = "eT", "eV", "eA", "eP"
dC, dX, dW, dF, dG, dO, dR = "dC", "dX", "dW", "dF", "dG", "dO", "dR"
dP1, dP2 = "dP1", "dP2"
ALL_SEMS = [eT, eV, eA, eP, dC, dX, dW, dF, dG, dO, dR, dP1, dP2]
OWN = {"tensor": eT, "vector": eV, "scalar": eA, "gpsimd": eP}


class Prog:

    def __init__(self):
        self.items = {e: [] for e in ENGS}
        self.cnt = {sm: 0 for sm in ALL_SEMS}
        self.lab = "pre"

    def op(self, eng, fn, incs=()):
        incs = tuple(incs)
        for sm, a in incs:
            self.cnt[sm] += a
        self.items[eng].append(("op", fn, incs, self.lab))
        return tuple((sm, self.cnt[sm]) for sm, _ in incs)

    def wait(self, eng, sem, v=None):
        if v is None:
            v = self.cnt[sem]
        if v > 0:
            self.items[eng].append(("wait", sem, v, self.lab))

    def tok(self, sem):
        return (sem, self.cnt[sem])

    def wait_tok(self, eng, *toks):
        for sem, v in toks:
            if v > 0:
                self.items[eng].append(("wait", sem, v, self.lab))


def build(taps=(), upto="final"):
    taps = set(taps)
    nc = bass.Bass("TRN2", target_bir_lowering=False, detect_race_conditions=False)

    x_d = nc.dram_tensor("x", [NSEQ, S, D], F32, kind="ExternalInput")
    wqkv_d = nc.dram_tensor("w_qkv", [D, 3 * D], F32, kind="ExternalInput")
    wout_d = nc.dram_tensor("w_out", [D, D], F32, kind="ExternalInput")
    qw_d = nc.dram_tensor("q_w", [FH * FP, D], F32, kind="ExternalInput")
    kw_d = nc.dram_tensor("k_w", [FH * FP, D], F32, kind="ExternalInput")
    vw_d = nc.dram_tensor("v_w", [FH * FE, D, FP], F32, kind="ExternalInput")
    ow_d = nc.dram_tensor("o_w", [FH * FE, FP, D], F32, kind="ExternalInput")
    selv_d = nc.dram_tensor("sel_v", [FH * FE, D], F32, kind="ExternalInput")
    selo_d = nc.dram_tensor("sel_o", [FH * FE, D], F32, kind="ExternalInput")
    rs_d = nc.dram_tensor("route_scale", [1], F32, kind="ExternalInput")
    ident_d = nc.dram_tensor("c_ident", [128, 128], F32, kind="ExternalInput")
    cos_hi_d = nc.dram_tensor("c_cos_hi", [128, S], F32, kind="ExternalInput")
    sin_hi_d = nc.dram_tensor("c_sin_hi", [128, S], F32, kind="ExternalInput")
    masks_d = nc.dram_tensor("c_masks", [4, 128, 512], BF16, kind="ExternalInput")
    esel_d = nc.dram_tensor("c_esel", [8, 1024], F32, kind="ExternalInput")
    out_d = nc.dram_tensor("out", [NSEQ, S, D], F32, kind="ExternalOutput")
    oscr_d = nc.dram_tensor("oscr", [8, 128, S], F32, kind="Internal")
    wqkvh_d = nc.dram_tensor("wqkvh", [D, 3 * D], BF16, kind="Internal")
    wqkvl_d = nc.dram_tensor("wqkvl", [D, 3 * D], BF16, kind="Internal")
    wouth_d = nc.dram_tensor("wouth", [D, D], BF16, kind="Internal")
    woutl_d = nc.dram_tensor("woutl", [D, D], BF16, kind="Internal")
    vwh_d = nc.dram_tensor("vwh", [FH * FE, D, FP], BF16, kind="Internal")
    vwl_d = nc.dram_tensor("vwl", [FH * FE, D, FP], BF16, kind="Internal")
    owh_d = nc.dram_tensor("owh", [FH * FE, FP, D], BF16, kind="Internal")
    owl_d = nc.dram_tensor("owl", [FH * FE, FP, D], BF16, kind="Internal")

    tap_d = {}
    tap_shapes = {
        "gv": [128, TT * 64], "go": [128, TT * 64],
        "qrot": [128, S], "krot": [128, S],
        "ctxa": [128, TT * 64], "scrA": [8, 128, S],
        "ctxf": [128, TT * 128], "scrF": [8, 128, S],
        "wqh": ([D, 3 * D], BF16), "vhb": ([128, TT * 130], BF16),
        "vlb": ([128, TT * 130], BF16), "cphb": ([128, 2048], BF16),
        "qhb": ([128, S], BF16), "qlb": ([128, S], BF16),
    }
    for t in taps:
        shp = tap_shapes[t]
        dt = F32
        if isinstance(shp, tuple):
            shp, dt = shp
        tap_d[t] = nc.dram_tensor("tap_" + t, shp, dt, kind="ExternalOutput")

    A = nc.alloc_sbuf_tensor
    xn_h = A("xn_h", [128, 8, S], BF16)
    xn_l = A("xn_l", [128, 8, S], BF16)
    cos_hi = A("cos_hi", [128, S], F32)
    sin_hi = A("sin_hi", [128, S], F32)
    gates_v = A("gates_v", [128, TT, 64], F32)
    gates_o = A("gates_o", [128, TT, 64], F32)
    selT_h = A("selT_h", [128, 2, 8, 64], BF16)
    selT_l = A("selT_l", [128, 2, 8, 64], BF16)
    masks_b = A("masks_b", [128, 4, 512], BF16)
    id_sb = A("id_sb", [128, 128], F32)
    iem9 = A("iem9", [128, 64], F32)
    rs_col = A("rs_col", [128, 1], F32)
    eps_col = A("eps_col", [128, 1], F32)
    ones_row = A("ones_row", [1, 128], F32)
    ones_b = A("ones_b", [128, 1], BF16)
    rs_sb = A("rs_sb", [1, 1], F32)
    sc = A("sc", [128, 32], F32)
    big1 = A("big1", [128, S], F32)
    big2 = A("big2", [128, S], F32)
    big3 = A("big3", [128, S], F32)
    big5 = A("big5", [128, 4096], F32)
    big8 = A("big8", [128, S], F32)
    qh = A("qh", [128, S], BF16)
    ql = A("ql", [128, S], BF16)
    kh = A("kh", [128, S], BF16)
    kl = A("kl", [128, S], BF16)
    vh = A("vh", [128, TT, 130], BF16)
    vl = A("vl", [128, TT, 130], BF16)
    wh = A("wh", [128, 4096], BF16)
    wl = A("wl", [128, 4096], BF16)
    whe_h = A("whe_h", [128, 4096], BF16)
    whe_l = A("whe_l", [128, 4096], BF16)
    pTh = A("pTh", [128, 3, 512], BF16)
    pTl = A("pTl", [128, 3, 512], BF16)
    sbuf_left = nc.sbuf_bytes_remaining

    gt = big3.ap()[:, 0:640]
    vacc = big5.ap()[:, 0:2080].rearrange("p (t u) -> p t u", t=TT)
    ln_x = big5.ap()[:, 0:2048].rearrange("p (b u) -> p b u", b=2)
    ln_s1 = big5.ap()[:, 2048:4096].rearrange("p (b u) -> p b u", b=2)
    fin_in = big5.ap()[:, 0:2048].rearrange("p (b c u) -> p b c u", b=2, c=8)
    fin_x = big5.ap()[:, 2048:4096].rearrange("p (b u) -> p b u", b=2)
    fin_o = big5.ap()[:, 0:2048].rearrange("p (b u) -> p b u", b=2)
    raw = big8.ap()
    sq_dump = big8.ap()[:, 0:1024]
    sel_raw64 = big8.ap()[0:64, 0:1024]
    shuf = big3.ap()
    ctxn_f = big3.ap().rearrange("p (t u) -> p t u", t=TT)
    ctxn = big3.ap()[:, 0:1024].rearrange("p (t u) -> p t u", t=TT)
    evac = big1.ap()[:, 0:1024].rearrange("p (b u) -> p b u", b=2)
    wq_h = wh.ap()[:, 0:3072].rearrange("p (c j u) -> p c j u", c=8, j=3)
    wq_l = wl.ap()[:, 0:3072].rearrange("p (c j u) -> p c j u", c=8, j=3)
    wout_h = wh.ap()[:, 3072:4096].rearrange("p (c u) -> p c u", c=8)
    wout_l = wl.ap()[:, 3072:4096].rearrange("p (c u) -> p c u", c=8)
    qwT_h = wh.ap()[:, 0:1024].rearrange("p (c u) -> p c u", c=8)
    qwT_l = wl.ap()[:, 0:1024].rearrange("p (c u) -> p c u", c=8)
    kwT_h = wh.ap()[:, 1024:2048].rearrange("p (c u) -> p c u", c=8)
    kwT_l = wl.ap()[:, 1024:2048].rearrange("p (c u) -> p c u", c=8)
    vw_h = wh.ap().rearrange("p (c e u) -> p c e u", c=8, e=4)
    vw_l = wl.ap().rearrange("p (c e u) -> p c e u", c=8, e=4)
    ow_h = wh.ap()[:, 0:2048].rearrange("p (b e u) -> p b e u", b=2, e=8)
    ow_l = wl.ap()[:, 0:2048].rearrange("p (b e u) -> p b e u", b=2, e=8)
    wstg_h = wh.ap().rearrange("p (b u) -> p b u", b=4)
    wstg_l = wl.ap().rearrange("p (b u) -> p b u", b=4)
    cph = whe_h.ap()[:, 0:2048]
    cpl = whe_l.ap()[:, 0:2048]
    ctxB_h = whe_h.ap()[0:64, 2048:4096]
    ctxB_l = whe_l.ap()[0:64, 2048:4096]
    w_he_h = whe_h.ap().rearrange("p (e u) -> p e u", e=8)
    w_he_l = whe_l.ap().rearrange("p (e u) -> p e u", e=8)

    ps_a = nc.alloc_psum_tensor("ps_a", [128, 512], F32)
    ps_b = nc.alloc_psum_tensor("ps_b", [128, 512], F32)
    ps_big = nc.alloc_psum_tensor("ps_big", [128, 2, 512], F32)
    ps_ctx = nc.alloc_psum_tensor("ps_ctx", [128, 4, 512], F32)
    ps_bigf = ps_big.ap().rearrange("p a u -> p (a u)")

    p = Prog()

    def T(fn, inc=1):
        return p.op("tensor", fn, ((eT, inc),) if inc else ())

    def V(fn, inc=1):
        return p.op("vector", fn, ((eV, inc),) if inc else ())

    def Ac(fn, inc=1):
        return p.op("scalar", fn, ((eA, inc),) if inc else ())

    def G(fn, inc=1):
        return p.op("gpsimd", fn, ((eP, inc),) if inc else ())

    def dma(queue_eng, sem, out, in_, accum=False):
        def fn(eng, out=out, in_=in_, accum=accum):
            if accum:
                return eng.dma_start(out=out, in_=in_, accum_op=ALU.add)
            return eng.dma_start(out=out, in_=in_)
        return p.op(queue_eng, fn, ((sem, 16),))

    def join():
        for e in ENGS:
            own = OWN.get(e)
            for sm in ALL_SEMS:
                if sm != own:
                    p.wait(e, sm)

    def tap(name, src_ap):
        if name in tap_d:
            dma("sync", dO, tap_d[name].ap(), src_ap)

    def MM3(out, lh, rh, ll, rl, start, stop, inc=0):
        T(lambda t, out=out, lh=lh, rh=rh, start=start: t.matmul(
            out=out, lhsT=lh, rhs=rh, start=start, stop=False), inc=0)
        T(lambda t, out=out, lh=lh, rl=rl: t.matmul(
            out=out, lhsT=lh, rhs=rl, start=False, stop=False), inc=0)
        return T(lambda t, out=out, ll=ll, rh=rh, stop=stop: t.matmul(
            out=out, lhsT=ll, rhs=rh, start=False, stop=stop), inc=inc)

    def castpair(dh, dl, src, inc=1):
        ta = Ac(lambda a, dh=dh, src=src: a.copy(out=dh, in_=src))
        p.wait_tok("vector", *ta)
        return V(lambda v, dh=dh, dl=dl, src=src: v.tensor_tensor(
            out=dl, in0=src, in1=dh, op=ALU.subtract), inc=inc)

    dma("sync", dC, id_sb[:], ident_d[:])
    dma("sync", dC, cos_hi[:], cos_hi_d[:])
    dma("sync", dC, sin_hi[:], sin_hi_d[:])
    dma("sync", dC, masks_b[:], masks_d.ap().rearrange("o p u -> p o u"))
    dma("sync", dC, rs_sb[:], rs_d.ap().rearrange("(a b) -> a b", a=1))

    G(lambda g: g.iota(out=iem9[:], pattern=[[0, 8], [1, 8]], base=-9,
                       channel_multiplier=0, allow_small_or_imprecise_dtypes=True))
    G(lambda g: g.memset(eps_col[:], EPS))
    G(lambda g: g.memset(ones_row[:], 1.0))
    G(lambda g: g.memset(ones_b[:], 1.0))

    p.wait("tensor", dC)
    p.wait("tensor", eP)
    T(lambda t: t.matmul(out=ps_a[:, 0:1], lhsT=ones_row[:], rhs=rs_sb[:],
                         start=True, stop=True))
    p.wait("vector", eT)
    V(lambda v: v.tensor_copy(out=rs_col[:], in_=ps_a[:, 0:1]))

    for si, sd in enumerate((selv_d, selo_d)):
        p.wait("sync", eT)
        tk = dma("sync", dC, sel_raw64, sd[:])
        p.wait_tok("tensor", *tk)
        p.wait("tensor", eV)
        for c in range(8):
            ps = ps_a if c % 2 == 0 else ps_b
            if si > 0 or c >= 2:
                p.wait("tensor", eV)
                p.wait("tensor", eA)
            T(lambda t, c=c, ps=ps: t.transpose(
                out=ps[0:128, 0:64], in_=sel_raw64[:, c * 128:(c + 1) * 128],
                identity=id_sb[0:64, 0:64]))
            p.wait("scalar", eT)
            castpair(selT_h[:, si, c, :], selT_l[:, si, c, :], ps[0:128, 0:64])
        p.wait("tensor", eV)
        p.wait("tensor", eA)

    def st_wsplit(inter=()):
        p.lab = "wsplit"
        specs = []
        for (srcd, dh_d, dl_d, rows, cols) in (
                (wqkv_d, wqkvh_d, wqkvl_d, D, 3 * D),
                (wout_d, wouth_d, woutl_d, D, D)):
            sv = srcd.ap().rearrange("(c p) n -> p c n", p=128)
            hv = dh_d.ap().rearrange("(c p) n -> p c n", p=128)
            lv = dl_d.ap().rearrange("(c p) n -> p c n", p=128)
            nb = cols // 1024
            for c in range(rows // 128):
                for b in range(nb):
                    sl = (slice(None), c, slice(b * 1024, (b + 1) * 1024))
                    specs.append((sv[sl], hv[sl], lv[sl], 1024))
        for (srcd, dh_d, dl_d) in ((vw_d, vwh_d, vwl_d), (ow_d, owh_d, owl_d)):
            sv = srcd.ap().rearrange("e a u -> (e a) u").rearrange(
                "(c p) n -> p c n", p=128)
            hv = dh_d.ap().rearrange("e a u -> (e a) u").rearrange(
                "(c p) n -> p c n", p=128)
            lv = dl_d.ap().rearrange("e a u -> (e a) u", ).rearrange(
                "(c p) n -> p c n", p=128)
            C = sv.shape[1]
            N = sv.shape[2]
            grp = max(1, 1024 // N)
            for c0 in range(0, C, grp):
                sl = (slice(None), slice(c0, c0 + grp), slice(None))
                specs.append((sv[sl], hv[sl], lv[sl], grp * N))
        subTok = [None] * 4
        whTok = [None] * 4
        wlTok = [None] * 4
        inter_q = list(inter)
        for i, (s_ap, h_ap, l_ap, L) in enumerate(specs):
            par = i % 4
            rq = ("sync", dC) if par % 2 == 0 else ("gpsimd", dP1)
            stgb = (big1 if par < 2 else big2).ap()[:, (par % 2) * 1024:]
            if subTok[par] is not None:
                p.wait_tok(rq[0], subTok[par])
            tkr = dma(rq[0], rq[1], stgb[:, 0:L], s_ap)
            hv2 = wstg_h[:, par, 0:L]
            lv2 = wstg_l[:, par, 0:L]
            p.wait_tok("scalar", *tkr)
            if whTok[par] is not None:
                p.wait_tok("scalar", whTok[par])
            ta = Ac(lambda a, hv2=hv2, st=stgb[:, 0:L]: a.copy(out=hv2, in_=st))
            p.wait_tok("vector", *ta)
            if wlTok[par] is not None:
                p.wait_tok("vector", wlTok[par])
            tv = V(lambda v, hv2=hv2, lv2=lv2, st=stgb[:, 0:L]: v.tensor_tensor(
                out=lv2, in0=st, in1=hv2, op=ALU.subtract))
            subTok[par] = tv[0]
            whTok[par] = dma("scalar", dP2, h_ap, hv2)[0]
            p.wait_tok("scalar", *tv)
            wlTok[par] = dma("scalar", dP2, l_ap, lv2)[0]
            if inter_q and i % 10 == 9:
                inter_q.pop(0)()
        while inter_q:
            inter_q.pop(0)()
        tap("wqh", wqkvh_d.ap())

    lnTok = [None, None]

    def ln_tile(s, tt):
            p.lab = "ln"
            if tt == 0:
                lnTok[0] = lnTok[1] = None
            par = tt % 2
            o = par * 8
            p.wait("sync", eV)
            xt = ln_x[:, par, :]
            tkx = dma("sync", dX, xt, x_d[s, tt * 128:(tt + 1) * 128, :])
            p.wait_tok("vector", *tkx)
            p.wait("vector", eT)
            V(lambda v, xt=xt, o=o: v.reduce_sum(out=sc[:, o:o + 1], in_=xt, axis=AX.X))
            p.wait("vector", eV)
            V(lambda v, o=o: v.tensor_scalar(out=sc[:, o + 1:o + 2], in0=sc[:, o:o + 1],
                                             scalar1=1.0 / D, scalar2=None, op0=ALU.mult))
            p.wait("vector", eV)
            dev = ln_s1[:, par, :]
            tkd = V(lambda v, xt=xt, dev=dev, o=o: v.tensor_scalar(
                out=dev, in0=xt, scalar1=sc[:, o + 1:o + 2], scalar2=None,
                op0=ALU.subtract))
            p.wait_tok("scalar", *tkd)
            Ac(lambda a, dev=dev, o=o: a.activation(
                out=sq_dump, in_=dev, func=AF.Square,
                accum_out=sc[:, o + 2:o + 3]))
            p.wait("scalar", eA)
            tka = Ac(lambda a, o=o: a.activation(
                out=sc[:, o + 5:o + 6], in_=sc[:, o + 2:o + 3], func=AF.Sqrt,
                scale=1.0 / D, bias=eps_col[:]))
            p.wait_tok("vector", *tka)
            V(lambda v, o=o: v.reciprocal(out=sc[:, o + 3:o + 4], in_=sc[:, o + 5:o + 6]))
            p.wait("vector", eV)
            V(lambda v, o=o: v.tensor_tensor(out=sc[:, o + 4:o + 5],
                                             in0=sc[:, o + 1:o + 2],
                                             in1=sc[:, o + 3:o + 4], op=ALU.mult))
            p.wait("vector", eV)
            tkn = V(lambda v, xt=xt, dev=dev, o=o: v.tensor_scalar(
                out=dev, in0=xt, scalar1=sc[:, o + 3:o + 4],
                scalar2=sc[:, o + 4:o + 5], op0=ALU.mult, op1=ALU.subtract))
            p.wait_tok("tensor", *tkn)
            for c in range(8):
                ps = ps_a if c % 2 == 0 else ps_b
                if lnTok[c % 2] is not None:
                    p.wait_tok("tensor", lnTok[c % 2])
                elif tt > 0 or c >= 2:
                    p.wait("tensor", eV)
                    p.wait("tensor", eA)
                T(lambda t, dev=dev, c=c, ps=ps: t.transpose(
                    out=ps[:, 0:128], in_=dev[:, c * 128:(c + 1) * 128],
                    identity=id_sb[:]))
                p.wait("scalar", eT)
                lnTok[c % 2] = castpair(
                    xn_h[:, c, tt * 128:(tt + 1) * 128],
                    xn_l[:, c, tt * 128:(tt + 1) * 128], ps[:, 0:128])[0]

    def st_ln(s):
        p.lab = "ln"
        join()
        for tt in range(TT):
            ln_tile(s, tt)

    def st_gates(s):
        p.lab = "gates"
        if s == 0:
            p.wait("tensor", eA)
            p.wait("tensor", eV)
        else:
            join()
        for tt in range(TT):
            gates_tile(s, tt)
        if s == 0 and ("gv" in tap_d or "go" in tap_d):
            join()
            tap("gv", gates_v.ap().rearrange("p t u -> p (t u)"))
            tap("go", gates_o.ap().rearrange("p t u -> p (t u)"))

    def b8(ap):
        return ap.rearrange("p (g e) -> p g e", e=1).broadcast_to([128, 8, 8])

    IE3 = iem9.ap().rearrange("p (g e) -> p g e", e=8)

    def emit_top2(si, gout, tt):
            Pa = gt[:, si * 64:(si + 1) * 64]
            P3v = Pa.rearrange("p (g e) -> p g e", e=8)
            M1 = gt[:, 192:200]
            I1 = gt[:, 200:208]
            M2 = gt[:, 208:216]
            I2 = gt[:, 216:224]
            SS = gt[:, 224:232]
            R = gt[:, 232:240]
            W1 = gt[:, 240:248]
            W2 = gt[:, 248:256]
            E1 = gt[:, 256:320].rearrange("p (g e) -> p g e", e=8)
            OH1 = gt[:, 320:384].rearrange("p (g e) -> p g e", e=8)
            PM = gt[:, 384:448]
            PM3 = PM.rearrange("p (g e) -> p g e", e=8)
            E2 = gt[:, 448:512].rearrange("p (g e) -> p g e", e=8)
            OH2 = gt[:, 512:576].rearrange("p (g e) -> p g e", e=8)
            CT = gt[:, 576:640].rearrange("p (g e) -> p g e", e=8)
            CTf = gt[:, 576:640]
            steps = [
                lambda v: v.reduce_max(out=M1, in_=P3v, axis=AX.X),
                lambda v: v.tensor_tensor(out=E1, in0=P3v, in1=b8(M1), op=ALU.is_equal),
                lambda v: v.tensor_tensor(out=CT, in0=E1, in1=IE3, op=ALU.mult),
                lambda v: v.tensor_reduce(out=I1, in_=CT, axis=AX.X, op=ALU.min),
                lambda v: v.tensor_tensor(out=OH1, in0=IE3, in1=b8(I1), op=ALU.is_equal),
                lambda v: v.scalar_tensor_tensor(
                    out=PM, in0=OH1.rearrange("p g e -> p (g e)"), scalar=-2.0,
                    in1=Pa, op0=ALU.mult, op1=ALU.add),
                lambda v: v.reduce_max(out=M2, in_=PM3, axis=AX.X),
                lambda v: v.tensor_tensor(out=E2, in0=PM3, in1=b8(M2), op=ALU.is_equal),
                lambda v: v.tensor_tensor(out=CT, in0=E2, in1=IE3, op=ALU.mult),
                lambda v: v.tensor_reduce(out=I2, in_=CT, axis=AX.X, op=ALU.min),
                lambda v: v.tensor_tensor(out=OH2, in0=IE3, in1=b8(I2), op=ALU.is_equal),
                lambda v: v.tensor_tensor(out=SS, in0=M1, in1=M2, op=ALU.add),
                lambda v: v.tensor_scalar(out=SS, in0=SS, scalar1=1e-9,
                                          scalar2=None, op0=ALU.max),
                lambda v: v.reciprocal(out=R, in_=SS),
                lambda v: v.tensor_tensor(out=W1, in0=M1, in1=R, op=ALU.mult),
                lambda v: v.tensor_scalar(out=W1, in0=W1, scalar1=rs_col[:],
                                          scalar2=None, op0=ALU.mult),
                lambda v: v.tensor_tensor(out=W2, in0=M2, in1=R, op=ALU.mult),
                lambda v: v.tensor_scalar(out=W2, in0=W2, scalar1=rs_col[:],
                                          scalar2=None, op0=ALU.mult),
                lambda v: v.tensor_tensor(out=CT, in0=OH1, in1=b8(W1), op=ALU.mult),
                lambda v: v.tensor_tensor(out=PM3, in0=OH2, in1=b8(W2), op=ALU.mult),
                lambda v: v.tensor_tensor(out=gout[:, tt, :], in0=CTf, in1=PM, op=ALU.add),
            ]
            for st_fn in steps:
                p.wait("vector", eV)
                V(st_fn, inc=1)

    def gates_tile(s, tt):
            p.lab = "gates"
            p.wait("tensor", eA)
            p.wait("tensor", eV)
            for si in range(2):
                psg = ps_a if si == 0 else ps_b
                for c in range(8):
                    MM3(psg[:, 0:64],
                        xn_h[:, c, tt * 128:(tt + 1) * 128],
                        selT_h[:, si, c, :],
                        xn_l[:, c, tt * 128:(tt + 1) * 128],
                        selT_l[:, si, c, :],
                        start=(c == 0), stop=(c == 7), inc=(1 if c == 7 else 0))
            p.wait("scalar", eT)
            p.wait("scalar", eV)
            Ac(lambda a: a.activation(out=gt[:, 0:64], in_=ps_a[:, 0:64],
                                      func=AF.Sigmoid), inc=0)
            Ac(lambda a: a.activation(out=gt[:, 64:128], in_=ps_b[:, 0:64],
                                      func=AF.Sigmoid))
            p.wait("vector", eA)
            for si, gout in ((0, gates_v), (1, gates_o)):
                emit_top2(si, gout, tt)

    def rope_one(dst):
        V(lambda v, dst=dst: v.tensor_tensor(out=dst, in0=raw, in1=cos_hi[:], op=ALU.mult), inc=0)
        V(lambda v: v.tensor_tensor(out=raw, in0=shuf, in1=sin_hi[:], op=ALU.mult), inc=0)
        V(lambda v, dst=dst: v.tensor_tensor(out=dst, in0=dst, in1=raw, op=ALU.add), inc=1)

    def attn_core(lKh, lKl, rQh, rQl, vAh, vAl, cM, sums, causal, ctx_out_fn,
                  psum4=None):
        ctxTok = [None, None, None]
        psumFree = psum4 if psum4 is not None else [None] * 4
        p.wait("tensor", eA)
        p.wait("tensor", eV)
        p.wait("scalar", eT)
        p.wait("vector", eT)

        def psS_of(kt):
            pb = kt % 4
            if pb < 2:
                return pb, ps_big.ap()[:, pb, :]
            if pb == 2:
                return pb, ps_ctx.ap()[:, 3, :]
            return pb, ps_a[:, :]

        def emit_S(qg, kt):
            pb, psS = psS_of(kt)
            if psumFree[pb] is not None:
                p.wait_tok("tensor", psumFree[pb])
            return MM3(psS, lKh(kt), rQh(qg), lKl(kt), rQl(qg),
                       start=True, stop=True, inc=1)

        DEPTH = 3
        for qg in range(NQG):
            nkt = (4 * qg + 4) if causal else TT
            p.wait("tensor", eV)
            sToks = {}
            for k0 in range(min(DEPTH, nkt)):
                sToks[k0] = emit_S(qg, k0)
            for kt in range(nkt):
                pb, psS = psS_of(kt)
                pr = kt % 3
                if kt + DEPTH < nkt:
                    sToks[kt + DEPTH] = emit_S(qg, kt + DEPTH)
                tb = sToks.pop(kt)
                lastS = tb
                o = kt - 4 * qg
                if causal and o >= 0:
                    p.wait_tok("vector", *tb)
                    lastS = V(lambda v, psS=psS, o=o: v.tensor_tensor(
                        out=psS, in0=psS, in1=masks_b[:, o, :], op=ALU.add))
                    p.wait_tok("scalar", *lastS)
                else:
                    p.wait_tok("scalar", *lastS)
                Ac(lambda a, psS=psS: a.activation(
                    out=psS, in_=psS, func=AF.Exp), inc=0)
                if ctxTok[pr] is not None:
                    p.wait_tok("scalar", ctxTok[pr])
                ta = Ac(lambda a, psS=psS, pr=pr: a.copy(out=pTh[:, pr, :], in_=psS))
                p.wait_tok("vector", *ta)
                if ctxTok[pr] is not None:
                    p.wait_tok("vector", ctxTok[pr])
                tl = V(lambda v, psS=psS, pr=pr: v.tensor_tensor(
                    out=pTl[:, pr, :], in0=psS, in1=pTh[:, pr, :], op=ALU.subtract))
                psumFree[pb] = tl[0]
                p.wait_tok("tensor", *ta)
                p.wait_tok("tensor", *tl)
                if kt == 0:
                    p.wait("tensor", eV)
                MM3(ps_ctx.ap()[0:cM, 0, :], vAh(kt), pTh[:, pr, :],
                    vAl(kt), pTl[:, pr, :],
                    start=(kt == 0), stop=(kt == nkt - 1),
                    inc=(0 if sums else 1))
                if sums:
                    T(lambda t, pr=pr, kt=kt, nkt=nkt: t.matmul(
                        out=ps_ctx.ap()[0:1, 1, :], lhsT=ones_b[:, 0:1],
                        rhs=pTh[:, pr, :], start=(kt == 0), stop=False), inc=0)
                    T(lambda t, pr=pr, kt=kt, nkt=nkt: t.matmul(
                        out=ps_ctx.ap()[0:1, 1, :], lhsT=ones_b[:, 0:1],
                        rhs=pTl[:, pr, :], start=False, stop=(kt == nkt - 1)), inc=1)
                ctxTok[pr] = p.tok(eT)
            ctx_out_fn(qg)

    def st_attn(s):
        p.lab = "attn"
        join()
        G(lambda g: g.memset(vh.ap()[:, :, 64:65], 1.0))
        G(lambda g: g.memset(vh.ap()[:, :, 129:130], 1.0))
        G(lambda g: g.memset(vl.ap()[:, :, 64:65], 0.0))
        G(lambda g: g.memset(vl.ap()[:, :, 129:130], 0.0))
        p.wait("tensor", eP)
        for hp in range(8):
            p.wait("sync", eT)
            tkw = None
            for j, c0 in enumerate((hp * 128, D + hp * 128, 2 * D + hp * 128)):
                dma("sync", dW, wq_h[:, :, j, :],
                    wqkvh_d.ap().rearrange("(c p) n -> p c n", p=128)[:, :, c0:c0 + 128])
                tkw = dma("sync", dW, wq_l[:, :, j, :],
                          wqkvl_d.ap().rearrange("(c p) n -> p c n", p=128)[:, :, c0:c0 + 128])
            p.wait_tok("tensor", *tkw)
            p.lab = "attn.proj"
            qkTok = [None, None]
            p.wait("tensor", eV)
            p.wait("tensor", eA)
            for j in (0, 1):
                scale = SCALE_A if j == 0 else 1.0
                p.wait("scalar", eV)
                p.wait("scalar", eT)
                for nt in range(NQG):
                    ps = ps_a if nt % 2 == 0 else ps_b
                    if qkTok[nt % 2] is not None:
                        p.wait_tok("tensor", qkTok[nt % 2])
                    for c in range(8):
                        MM3(ps[:, :], wq_h[:, c, j, :],
                            xn_h[:, c, nt * 512:(nt + 1) * 512],
                            wq_l[:, c, j, :],
                            xn_l[:, c, nt * 512:(nt + 1) * 512],
                            start=(c == 0), stop=(c == 7), inc=(1 if c == 7 else 0))
                    p.wait("scalar", eT)
                    tqa = Ac(lambda a, nt=nt, scale=scale, ps=ps: a.mul(
                        out=raw[:, nt * 512:(nt + 1) * 512], in_=ps[:, :], mul=scale))
                    qkTok[nt % 2] = tqa[0]
                p.wait("gpsimd", eA)
                p.wait("gpsimd", eV)
                p.wait("gpsimd", eT)
                for (d0, s0) in ((0, 32), (32, 0), (64, 96), (96, 64)):
                    dma("gpsimd", dG, big3.ap()[d0:d0 + 32, :], raw[s0:s0 + 32, :])
                p.wait("vector", dG)
                p.wait("vector", dW)
                p.wait("vector", eA)
                p.wait("vector", eT)
                p.wait("vector", eP)
                dstb = big1 if j == 0 else big2
                rope_one(dstb.ap())
                p.wait("scalar", eV)
                p.wait("scalar", eT)
                if j == 0:
                    castpair(qh[:], ql[:], dstb.ap())
                else:
                    castpair(kh[:], kl[:], dstb.ap())
            p.lab = "attn.vproj"
            vTok = [qkTok[0], qkTok[1]]
            for tt in range(TT):
                ps = ps_a if tt % 2 == 0 else ps_b
                if vTok[tt % 2] is not None:
                    p.wait_tok("tensor", vTok[tt % 2])
                for c in range(8):
                    MM3(ps[:, 0:128], xn_h[:, c, tt * 128:(tt + 1) * 128],
                        wq_h[:, c, 2, :],
                        xn_l[:, c, tt * 128:(tt + 1) * 128],
                        wq_l[:, c, 2, :],
                        start=(c == 0), stop=(c == 7), inc=(1 if c == 7 else 0))
                p.wait("scalar", eT)
                vh2 = vh.ap()[:, tt, :].rearrange("p (b u) -> p b u", b=2)[:, :, 0:64]
                vl2 = vl.ap()[:, tt, :].rearrange("p (b u) -> p b u", b=2)[:, :, 0:64]
                ps2 = ps[:, 0:128].rearrange("p (b u) -> p b u", b=2)
                vtk = castpair(vh2, vl2, ps2)
                vTok[tt % 2] = vtk[0]
            if s == 0 and hp == 0:
                join()
                tap("qrot", big1.ap())
                tap("krot", big2.ap())
                tap("qhb", qh.ap())
                tap("qlb", ql.ap())
                tap("vhb", vh.ap().rearrange("p t u -> p (t u)"))
                tap("vlb", vl.ap().rearrange("p t u -> p (t u)"))
            p.lab = "attn.core"
            for i in range(2):
                R0 = slice(64 * i, 64 * i + 64)
                cA = 65 * i

                def ctx_out(qg, i=i):
                    p.wait("vector", eT)
                    tv1 = V(lambda v: v.reciprocal(
                        out=big3.ap()[0:1, 0:512], in_=ps_ctx.ap()[64:65, 0, :]))
                    p.wait_tok("tensor", *tv1)
                    p.wait("tensor", eV)
                    p.wait("tensor", eA)
                    tr = T(lambda t: t.matmul(
                        out=ps_ctx.ap()[0:64, 1, :], lhsT=ones_row[0:1, 0:64],
                        rhs=big3.ap()[0:1, 0:512], start=True, stop=True))
                    p.wait_tok("scalar", *tr)
                    trc = Ac(lambda a: a.copy(out=big3.ap()[0:64, 1536:2048],
                                              in_=ps_ctx.ap()[0:64, 1, :]))
                    p.wait_tok("vector", *trc)
                    tv2 = V(lambda v: v.tensor_tensor(
                        out=big3.ap()[0:64, 512:1024], in0=ps_ctx.ap()[0:64, 0, :],
                        in1=big3.ap()[0:64, 1536:2048], op=ALU.mult))
                    p.wait("scalar", eV)
                    if i == 0:
                        castpair(cph[0:64, qg * 512:(qg + 1) * 512],
                                 cpl[0:64, qg * 512:(qg + 1) * 512],
                                 big3.ap()[0:64, 512:1024])
                    else:
                        castpair(ctxB_h[:, qg * 512:(qg + 1) * 512],
                                 ctxB_l[:, qg * 512:(qg + 1) * 512],
                                 big3.ap()[0:64, 512:1024])

                attn_core(
                    lKh=lambda kt, R0=R0: kh.ap()[R0, kt * 128:(kt + 1) * 128],
                    lKl=lambda kt, R0=R0: kl.ap()[R0, kt * 128:(kt + 1) * 128],
                    rQh=lambda qg, R0=R0: qh.ap()[R0, qg * 512:(qg + 1) * 512],
                    rQl=lambda qg, R0=R0: ql.ap()[R0, qg * 512:(qg + 1) * 512],
                    vAh=lambda kt, cA=cA: vh.ap()[:, kt, cA:cA + 65],
                    vAl=lambda kt, cA=cA: vl.ap()[:, kt, cA:cA + 65],
                    cM=65, sums=False, causal=True, ctx_out_fn=ctx_out)
                if i == 1:
                    p.wait("gpsimd", eV)
                    dma("gpsimd", dG, cph[64:128, :], ctxB_h[:, :])
                    dma("gpsimd", dG, cpl[64:128, :], ctxB_l[:, :])
            if s == 0 and hp == 0:
                join()
                tap("cphb", cph)
            p.lab = "attn.wout"
            p.wait("sync", eT)
            dma("sync", dW, wout_h,
                wouth_d[hp * 128:(hp + 1) * 128, :].rearrange("p (c u) -> p c u", c=8))
            tkwo = dma("sync", dW, wout_l,
                       woutl_d[hp * 128:(hp + 1) * 128, :].rearrange("p (c u) -> p c u", c=8))
            p.wait_tok("tensor", *tkwo)
            p.wait("tensor", dG)
            p.wait("tensor", eV)
            woTok = [None, None]
            woPs = [None, None]
            for dc in range(8):
                for jj in range(4):
                    pr2 = (dc * 4 + jj) % 2
                    if woPs[jj % 2] is not None:
                        p.wait_tok("tensor", woPs[jj % 2])
                    else:
                        p.wait("tensor", eA)
                    MM3(ps_big[:, jj % 2, :], wout_h[:, dc, :],
                        cph[:, jj * 512:(jj + 1) * 512],
                        wout_l[:, dc, :],
                        cpl[:, jj * 512:(jj + 1) * 512],
                        start=True, stop=True, inc=1)
                    p.wait("scalar", eT)
                    if woTok[pr2] is not None:
                        p.wait_tok("scalar", woTok[pr2])
                    tae = Ac(lambda a, pr2=pr2, jj=jj: a.copy(
                        out=evac[:, pr2, :], in_=ps_big[:, jj % 2, :]))
                    woPs[jj % 2] = tae[0]
                    p.wait_tok("gpsimd", *tae)
                    tkg = dma("gpsimd", dG, oscr_d[dc, :, jj * 512:(jj + 1) * 512],
                              evac[:, pr2, :], accum=(hp != 0))
                    woTok[pr2] = tkg[0]
        if s == 0 and "scrA" in tap_d:
            join()
            tap("scrA", oscr_d.ap())

    def st_ffn(s):
        p.lab = "ffn"
        join()
        G(lambda g: g.memset(vacc[:, :, 128:129], 1.0))
        p.wait("scalar", eP)
        p.wait("vector", eP)
        esel = big5.ap()[0:8, 2304:3328]
        dma("sync", dF, esel, esel_d[:])
        p.wait("tensor", dF)
        owTok = [None, None]
        owPs = [None, None]
        owMM = [None, None]
        for h in range(8):
            p.lab = "ffn.qkT"
            tpTok = [None, None]
            for wi, wd in enumerate((qw_d, kw_d)):
                p.wait("sync", eT)
                p.wait("sync", dG)
                tk = dma("sync", dF, raw[:, 0:1024], wd[h * 128:(h + 1) * 128, :])
                p.wait_tok("tensor", *tk)
                for c in range(8):
                    ps = ps_a if c % 2 == 0 else ps_b
                    if tpTok[c % 2] is not None:
                        p.wait_tok("tensor", tpTok[c % 2])
                    else:
                        p.wait("tensor", eV)
                        p.wait("tensor", eA)
                    T(lambda t, c=c, ps=ps: t.transpose(
                        out=ps[:, 0:128], in_=raw[:, c * 128:(c + 1) * 128],
                        identity=id_sb[:]))
                    p.wait("scalar", eT)
                    hv = (qwT_h if wi == 0 else kwT_h)[:, c, :]
                    lv = (qwT_l if wi == 0 else kwT_l)[:, c, :]
                    tpTok[c % 2] = castpair(hv, lv, ps[:, 0:128])[0]
            for tt in range(TT):
                ps = ps_a if tt % 2 == 0 else ps_b
                if tpTok[tt % 2] is not None:
                    p.wait_tok("tensor", tpTok[tt % 2])
                else:
                    p.wait("tensor", eV)
                    p.wait("tensor", eA)
                T(lambda t, tt=tt, ps=ps, h=h: t.transpose(
                    out=ps[0:8, 0:128], in_=gates_o[:, tt, h * 8:(h + 1) * 8],
                    identity=id_sb[:]))
                p.wait("vector", eT)
                dstg = big1 if tt < 8 else big2
                col = 1024 + (tt % 8) * 128
                tpTok[tt % 2] = V(lambda v, ps=ps, dstg=dstg, col=col: v.tensor_copy(
                    out=dstg.ap()[0:8, col:col + 128], in_=ps[0:8, 0:128]))[0]
            p.lab = "ffn.proj"
            prjTok = [tpTok[0], tpTok[1]]
            for wi in (0, 1):
                scale = SCALE_F if wi == 0 else 1.0
                for nt in range(NQG):
                    ps = ps_a if nt % 2 == 0 else ps_b
                    if prjTok[nt % 2] is not None:
                        p.wait_tok("tensor", prjTok[nt % 2])
                    else:
                        p.wait("tensor", eA)
                        p.wait("tensor", eV)
                    for c in range(8):
                        MM3(ps[:, :], (qwT_h if wi == 0 else kwT_h)[:, c, :],
                            xn_h[:, c, nt * 512:(nt + 1) * 512],
                            (qwT_l if wi == 0 else kwT_l)[:, c, :],
                            xn_l[:, c, nt * 512:(nt + 1) * 512],
                            start=(c == 0), stop=(c == 7), inc=(1 if c == 7 else 0))
                    p.wait("scalar", eT)
                    p.wait("scalar", eV)
                    hv = (qh if wi == 0 else kh).ap()[:, nt * 512:(nt + 1) * 512]
                    lv = (ql if wi == 0 else kl).ap()[:, nt * 512:(nt + 1) * 512]
                    ta = Ac(lambda a, hv=hv, ps=ps, scale=scale: a.mul(
                        out=hv, in_=ps[:, :], mul=scale))
                    p.wait_tok("vector", *ta)
                    prjTok[nt % 2] = V(
                        lambda v, hv=hv, lv=lv, ps=ps, scale=scale: v.scalar_tensor_tensor(
                            out=lv, in0=ps[:, :], scalar=scale, in1=hv,
                            op0=ALU.mult, op1=ALU.subtract))[0]
            p.lab = "ffn.xv"
            xvTok = [prjTok[0], prjTok[1]]
            for half in range(2):
                p.wait("sync", eT)
                for e4 in range(4):
                    dma("sync", dF, vw_h[:, :, e4, :],
                        vwh_d[h * 8 + half * 4 + e4, :, :].rearrange("(c p) u -> p c u", p=128))
                    tk = dma("sync", dF, vw_l[:, :, e4, :],
                             vwl_d[h * 8 + half * 4 + e4, :, :].rearrange("(c p) u -> p c u", p=128))
                p.wait_tok("tensor", *tk)
                for tt in range(TT):
                    ps = ps_a if tt % 2 == 0 else ps_b
                    if xvTok[tt % 2] is not None:
                        p.wait_tok("tensor", xvTok[tt % 2])
                    for c in range(8):
                        MM3(ps[:, :], xn_h[:, c, tt * 128:(tt + 1) * 128],
                            vw_h[:, c, :, :],
                            xn_l[:, c, tt * 128:(tt + 1) * 128],
                            vw_l[:, c, :, :],
                            start=(c == 0), stop=(c == 7), inc=(1 if c == 7 else 0))
                    p.wait("vector", eT)
                    for e4 in range(4):
                        he = h * 8 + half * 4 + e4
                        if e4 > 0:
                            p.wait("vector", eV)
                        if half == 0 and e4 == 0:
                            V(lambda v, tt=tt, e4=e4, he=he, ps=ps: v.tensor_scalar(
                                out=vacc[:, tt, 0:128],
                                in0=ps[:, e4 * 128:(e4 + 1) * 128],
                                scalar1=gates_v[:, tt, he:he + 1], scalar2=None,
                                op0=ALU.mult), inc=1)
                        else:
                            xvTok[tt % 2] = V(
                                lambda v, tt=tt, e4=e4, he=he, ps=ps: v.scalar_tensor_tensor(
                                    out=vacc[:, tt, 0:128],
                                    in0=ps[:, e4 * 128:(e4 + 1) * 128],
                                    scalar=gates_v[:, tt, he:he + 1],
                                    in1=vacc[:, tt, 0:128],
                                    op0=ALU.mult, op1=ALU.add), inc=1)[0]
            p.wait("scalar", eV)
            p.wait("scalar", eT)
            castpair(vh.ap()[:, :, 0:129], vl.ap()[:, :, 0:129], vacc[:, :, 0:129])

            p.lab = "ffn.core"

            def g_blk(qg):
                src = big1 if qg < 2 else big2
                c0 = 1024 + (qg % 2) * 512
                return src.ap()[0:8, c0:c0 + 512]

            pf = [None] * 4

            def ctx_out_f(qg):
                p.lab = "ffn.ow"
                p.wait("vector", eT)
                tv1 = V(lambda v: v.reciprocal(
                    out=big3.ap()[0:1, 0:512], in_=ps_ctx.ap()[0:1, 1, :]))
                p.wait_tok("tensor", *tv1)
                p.wait("tensor", eV)
                p.wait("tensor", eA)
                tr = T(lambda t: t.matmul(
                    out=ps_ctx.ap()[0:128, 2, :], lhsT=ones_row[0:1, 0:128],
                    rhs=big3.ap()[0:1, 0:512], start=True, stop=True))
                p.wait_tok("scalar", *tr)
                trc = Ac(lambda a: a.copy(out=big3.ap()[:, 1536:2048],
                                          in_=ps_ctx.ap()[0:128, 2, :]))
                p.wait_tok("vector", *trc)
                tcn = V(lambda v: v.tensor_tensor(
                    out=big3.ap()[:, 512:1024], in0=ps_ctx.ap()[0:128, 0, :],
                    in1=big3.ap()[:, 1536:2048], op=ALU.mult))
                p.wait("scalar", eT)
                for e in range(8):
                    p.wait("tensor", eV)
                    p.wait("tensor", eA)
                    tg = T(lambda t, e=e, qg=qg: t.matmul(
                        out=ps_ctx.ap()[0:128, 2, :], lhsT=esel[:, e * 128:(e + 1) * 128],
                        rhs=g_blk(qg), start=True, stop=True))
                    p.wait_tok("vector", *tg)
                    p.wait("vector", eA)
                    stg = big3.ap()[:, 1024:1536]
                    tv2 = V(lambda v, stg=stg: v.tensor_tensor(
                        out=stg, in0=big3.ap()[:, 512:1024],
                        in1=ps_ctx.ap()[0:128, 2, :], op=ALU.mult))
                    p.wait("scalar", eV)
                    castpair(w_he_h[:, e, :], w_he_l[:, e, :], stg)
                tc = qg
                for dc in range(8):
                    par = dc % 2
                    if owMM[par] is not None and not (qg == 0 and dc < 2):
                        p.wait_tok("sync", owMM[par])
                    else:
                        p.wait("sync", eT)
                    dma("sync", dF, ow_h[:, par, :, :],
                        owh_d[h * 8:(h + 1) * 8, :, dc * 128:(dc + 1) * 128]
                        .rearrange("e p u -> p e u"))
                    tk = dma("sync", dF, ow_l[:, par, :, :],
                             owl_d[h * 8:(h + 1) * 8, :, dc * 128:(dc + 1) * 128]
                             .rearrange("e p u -> p e u"))
                    p.wait_tok("tensor", *tk)
                    ps = ps_a if par == 0 else ps_b
                    if dc == 0 and pf[3] is not None:
                        p.wait_tok("tensor", pf[3])
                    if owPs[par] is not None:
                        p.wait_tok("tensor", owPs[par])
                    else:
                        p.wait("tensor", eV)
                        p.wait("tensor", eA)
                    for e in range(8):
                        tmm = MM3(ps[:, :], ow_h[:, par, e, :], w_he_h[:, e, :],
                                  ow_l[:, par, e, :], w_he_l[:, e, :],
                                  start=(e == 0), stop=(e == 7), inc=(1 if e == 7 else 0))
                    owMM[par] = tmm[0]
                    p.wait("scalar", eT)
                    pr2 = dc % 2
                    if owTok[pr2] is not None:
                        p.wait_tok("scalar", owTok[pr2])
                    tae = Ac(lambda a, pr2=pr2, ps=ps: a.copy(
                        out=evac[:, pr2, :], in_=ps[:, :]))
                    owPs[par] = tae[0]
                    if par == 0:
                        pf[3] = tae[0]
                    p.wait_tok("gpsimd", *tae)
                    tkg = dma("gpsimd", dG, oscr_d[dc, :, tc * 512:(tc + 1) * 512],
                              evac[:, pr2, :], accum=True)
                    owTok[pr2] = tkg[0]
                p.lab = "ffn.core"

            attn_core(
                lKh=lambda kt: kh.ap()[:, kt * 128:(kt + 1) * 128],
                lKl=lambda kt: kl.ap()[:, kt * 128:(kt + 1) * 128],
                rQh=lambda qg: qh.ap()[:, qg * 512:(qg + 1) * 512],
                rQl=lambda qg: ql.ap()[:, qg * 512:(qg + 1) * 512],
                vAh=lambda kt: vh.ap()[:, kt, 0:128],
                vAl=lambda kt: vl.ap()[:, kt, 0:128],
                cM=128, sums=True, causal=False, ctx_out_fn=ctx_out_f,
                psum4=pf)
        if s == 0 and "scrF" in tap_d:
            join()
            tap("scrF", oscr_d.ap())

    def st_final(s):
        p.lab = "final"
        join()
        for tt in range(TT):
            par = tt % 2
            if tt >= 2:
                p.wait("sync", eT)
                p.wait("sync", eV)
            tk1 = dma("sync", dR, fin_in[:, par, :, :],
                      oscr_d[:, :, tt * 128:(tt + 1) * 128].rearrange("c p u -> p c u"))
            tk2 = dma("sync", dX, fin_x[:, par, :], x_d[s, tt * 128:(tt + 1) * 128, :])
            p.wait_tok("tensor", *tk1)
            if tt >= 1:
                p.wait("tensor", eV)
            for c in range(8):
                T(lambda t, par=par, c=c: t.transpose(
                    out=ps_bigf[:, c * 128:(c + 1) * 128],
                    in_=fin_in[:, par, c, :], identity=id_sb[:]),
                  inc=(1 if c == 7 else 0))
            p.wait_tok("vector", *tk2)
            p.wait("vector", eT)
            if tt >= 2:
                p.wait("vector", dO)
            tv = V(lambda v, par=par: v.tensor_tensor(
                out=fin_o[:, par, :], in0=ps_bigf, in1=fin_x[:, par, :], op=ALU.add))
            p.wait_tok("sync", *tv)
            dma("sync", dO, out_d[s, tt * 128:(tt + 1) * 128, :], fin_o[:, par, :])

    stages = ["ln", "gates", "attn", "ffn", "final"]
    upto_i = stages.index(upto)
    nseq_eff = NSEQ if upto_i >= 4 else 1

    def seq0_tile(tt):
        ln_tile(0, tt)
        if upto_i >= 1:
            gates_tile(0, tt)

    st_wsplit(inter=[lambda tt=tt: ln_tile(0, tt) for tt in range(TT)])
    for s in range(nseq_eff):
        if s > 0:
            st_ln(s)
        if upto_i >= 1:
            st_gates(s)
        if upto_i >= 2:
            st_attn(s)
        if upto_i >= 3:
            st_ffn(s)
        if upto_i >= 4:
            st_final(s)

    p.wait("sync", dO)
    p.wait("sync", dG)
    p.wait("gpsimd", dO)

    with contextlib.ExitStack() as es:
        semh = {name: es.enter_context(nc.semaphore(name)) for name in p.cnt}
        block = es.enter_context(nc.Block())

        def runner(items):
            def body(eng):
                cur = [None]
                stack = [None]

                def setlab(lab):
                    if lab != stack[0]:
                        if stack[0] is not None:
                            cm = stack.pop()
                            cm.__exit__(None, None, None)
                            stack.append(None)
                        if lab is not None:
                            cm = nc.named_scope(lab)
                            cm.__enter__()
                            stack[-1] = None
                            stack.append(cm)
                        stack[0] = lab

                open_cm = [None, None]
                def swl(lab):
                    if lab != open_cm[0]:
                        if open_cm[1] is not None:
                            open_cm[1].__exit__(None, None, None)
                            open_cm[1] = None
                        if lab:
                            open_cm[1] = nc.named_scope(lab)
                            open_cm[1].__enter__()
                        open_cm[0] = lab
                for kind, a, b, lab in items:
                    swl(lab)
                    if kind == "wait":
                        eng.wait_ge(semh[a], b)
                    else:
                        ins = a(eng)
                        for sm, amt in b:
                            ins = ins.then_inc(semh[sm], amt)
                swl(None)
            return body

        for ename in ENGS:
            getattr(block, ename)(runner(p.items[ename]))
    return nc, tap_d, sbuf_left


def kernel(L0_attn_x, L0_attn_w_qkv, L0_attn_w_out, L0_ffn_q_w, L0_ffn_k_w,
           L0_ffn_v_w, L0_ffn_o_w, L0_ffn_sel_v_w, L0_ffn_sel_o_w,
           L0_ffn_route_scale, _taps=(), _upto="final"):
    consts = host_consts()
    nc, tap_d, _ = build(taps=_taps, upto=_upto)
    in_maps = []
    for c in range(NCORES):
        m = {
            "x": np.ascontiguousarray(np.asarray(L0_attn_x)[c * NSEQ:(c + 1) * NSEQ]),
            "w_qkv": np.asarray(L0_attn_w_qkv), "w_out": np.asarray(L0_attn_w_out),
            "q_w": np.asarray(L0_ffn_q_w), "k_w": np.asarray(L0_ffn_k_w),
            "v_w": np.asarray(L0_ffn_v_w), "o_w": np.asarray(L0_ffn_o_w),
            "sel_v": np.asarray(L0_ffn_sel_v_w), "sel_o": np.asarray(L0_ffn_sel_o_w),
            "route_scale": np.asarray(L0_ffn_route_scale),
        }
        m.update(consts)
        in_maps.append(m)
    res = run_bass_kernel_spmd(nc, in_maps, core_ids=list(range(NCORES)))
    out = np.concatenate([r["out"] for r in res.results], axis=0)
    if _taps:
        return out, {t: res.results[0]["tap_" + t] for t in _taps}
    return out
